# version 07 (final)
# baseline (speedup 1.0000x reference)
import math
import numpy as np
import ml_dtypes
import concourse.bass as bass
import concourse.mybir as mybir
from concourse.bass_utils import run_bass_kernel_spmd
from concourse.bass import IndirectOffsetOnAxis

F32 = mybir.dt.float32
BF16 = mybir.dt.bfloat16
I32 = mybir.dt.int32
U32 = mybir.dt.uint32
AF = mybir.ActivationFunctionType
ALU = mybir.AluOpType
BF = ml_dtypes.bfloat16

NCORES = 8
B = 4
S = 2048
D = 1024
T = B * S
NSLAB = D // 128

ENGINES = ("sync", "tensor", "vector", "scalar", "gpsimd")


_FUSE = {"on": False, "es_sem": None, "bar": None, "k": 0}


class XSem:
    def __init__(self, es, nc, name):
        self.h = es.enter_context(nc.semaphore(name))
        self.n = 0


class Pipe:

    def __init__(self, nc, es, name):
        self.nc = nc
        self.es = es
        self.sem_es = _FUSE["es_sem"] if _FUSE["on"] else es
        self.name = name
        self.s_cmp = self.sem_es.enter_context(nc.semaphore(name + "_c"))
        self.s_dma = self.sem_es.enter_context(nc.semaphore(name + "_d"))
        self.ncmp = 0
        self.ndma = 0
        self.xsems = []
        self.steps = {e: [] for e in ENGINES}

    def xsem(self, suffix):
        s = XSem(self.sem_es, self.nc, self.name + "_" + suffix)
        self.xsems.append(s)
        return s

    def at(self):
        return (self.ncmp, self.ndma)

    def step(self, engine, fn, dma=False, wait=None, tag="", inc=None, waits=()):
        w = wait if wait is not None else (self.ncmp, self.ndma)
        self.steps[engine].append((fn, dma, w, tuple(waits), inc))
        if inc is not None:
            inc.n += 16 if dma else 1
        elif dma:
            self.ndma += 16
        else:
            self.ncmp += 1
        return self.at()

    def emit(self):
        bar = _FUSE["bar"] if _FUSE["on"] else None
        bark = _FUSE["k"] if _FUSE["on"] else 0
        blk = self.nc.Block(self.name)
        with blk:
            for eng_name in ENGINES:
                lst = self.steps[eng_name]
                if not lst:
                    continue

                def body(e, lst=lst):
                    if bar is not None and bark > 0:
                        e.wait_ge(bar, bark)
                    for fn, dma, (wc, wd), xw, inc in lst:
                        if wc > 0:
                            e.wait_ge(self.s_cmp, wc)
                        if wd > 0:
                            e.wait_ge(self.s_dma, wd)
                        for xs, val in xw:
                            if val > 0:
                                e.wait_ge(xs.h, val)
                        ins = fn(e)
                        if inc is not None:
                            ins.then_inc(inc.h, 16 if dma else 1)
                        elif dma:
                            ins.then_inc(self.s_dma, 16)
                        else:
                            ins.then_inc(self.s_cmp, 1)

                getattr(blk, eng_name)(body)
            def fin(e):
                if bar is not None and bark > 0:
                    e.wait_ge(bar, bark)
                if self.ncmp:
                    e.wait_ge(self.s_cmp, self.ncmp)
                if self.ndma:
                    e.wait_ge(self.s_dma, self.ndma)
                for xs in self.xsems:
                    if xs.n:
                        e.wait_ge(xs.h, xs.n)
                if bar is not None:
                    e.sem_inc(bar, 1)
            blk.sync(fin)
        if _FUSE["on"]:
            _FUSE["k"] += 1


class IO:
    def __init__(self, nc, fused):
        self.nc = nc
        self.fused = fused
        self.aps = {}

    def inp(self, name, shape, dt):
        if name not in self.aps:
            self.aps[name] = self.nc.dram_tensor(name, shape, dt, kind="ExternalInput").ap()
        return self.aps[name]

    def out(self, name, shape, dt):
        if name not in self.aps:
            self.aps[name] = self.nc.dram_tensor(name, shape, dt, kind="ExternalOutput").ap()
        return self.aps[name]

    def stream(self, name, shape, dt, produce):
        if name not in self.aps:
            kind = "Internal" if self.fused else ("ExternalOutput" if produce else "ExternalInput")
            self.aps[name] = self.nc.dram_tensor(name, shape, dt, kind=kind).ap()
        return self.aps[name]


def slabs_ap(hbm_ap, c0, cw, nslab=NSLAB):
    return hbm_ap.rearrange("(s p) t -> p s t", p=128)[:, :, c0:c0 + cw]


def host_consts():
    c = {}
    c["c_ones_col_bf"] = np.ones((128, 1), BF)
    c["c_ones_row_f"] = np.ones((1, 128), np.float32)
    c["c_ident_bf"] = np.eye(128, dtype=BF)
    c["c_ident_f"] = np.eye(128, dtype=np.float32)
    c["c_ones64_f"] = np.ones((128, 64), np.float32)
    c["c_iota32_bc"] = np.broadcast_to(np.arange(32, dtype=np.float32), (128, 32)).copy()
    c["c_iota8_bc"] = np.broadcast_to(np.arange(8, dtype=np.float32), (128, 8)).copy()
    grp = np.zeros((128, 4), np.float32)
    for p in range(128):
        grp[p, p // 32] = 1.0
    c["c_grp_ones"] = grp
    c["c_grp_selT"] = np.ascontiguousarray(grp.T)
    A = np.zeros((128, 128), np.float32)
    for i in range(64):
        A[2 * i, 2 * i + 1] = -1.0
        A[2 * i + 1, 2 * i] = 1.0
    c["c_rot_lhsT"] = np.ascontiguousarray(A.T).astype(BF)
    SCALE_BASE = 512.0
    half = D // 2
    scale_vec = (np.arange(0, D, 2, dtype=np.float64) + 0.4 * D) / (1.4 * D)
    pos = np.arange(S, dtype=np.float64)
    scale = scale_vec[None, :] ** (pos[:, None] / SCALE_BASE)
    inv_freq = 1.0 / 10000.0 ** (np.arange(half, dtype=np.float64) / half)
    sinus = pos[:, None] * inv_freq[None, :]
    sn, cs = np.sin(sinus), np.cos(sinus)
    dup = lambda m: np.repeat(m, 2, axis=1)
    c["c_tqc"] = np.ascontiguousarray(dup(cs * scale).T).astype(BF)
    c["c_tqs"] = np.ascontiguousarray(dup(sn * scale).T).astype(BF)
    c["c_tkc"] = np.ascontiguousarray(dup(cs / scale).T).astype(BF)
    c["c_tks"] = np.ascontiguousarray(dup(sn / scale).T).astype(BF)
    tt = np.arange(128)
    expoT = (tt[None, :] - tt[:, None]).astype(np.float32)
    c["c_expoT"] = np.maximum(expoT, 0.0)
    c["c_cmask_scale"] = ((expoT >= 0).astype(np.float32) * (1.0 / math.sqrt(128.0)))
    c["c_iotaJ"] = np.broadcast_to(np.arange(128, dtype=np.float32), (128, 128)).copy()
    c["c_rev_col"] = (128.0 - np.arange(128, dtype=np.float32)).reshape(128, 1)
    c["c_iota_col"] = np.arange(128, dtype=np.float32).reshape(128, 1)
    L = np.zeros((32, 32), np.float32)
    for k_ in range(32):
        for m_ in range(32):
            if k_ < m_:
                L[k_, m_] = 1.0
    c["c_ltri32"] = L
    c["c_tile_iota"] = np.broadcast_to(512.0 * np.arange(96, dtype=np.float32), (32, 96)).copy()
    c["c_ones32col_f"] = np.ones((32, 1), np.float32)
    return c


def ln_stats_steps(P, sb, ps, xf_getter, cw, eps_ap, prefix):
    CW = cw
    xbf, sqbf, stat = sb[prefix + "xbf"], sb[prefix + "sqbf"], sb[prefix + "stat"]
    psx, psq, pbr, pbp = ps[prefix + "psx"], ps[prefix + "psq"], ps[prefix + "pbr"], ps[prefix + "pbp"]
    ones_col, ones_row = sb["ones_col_bf"], sb["ones_row_f"]
    xf = xf_getter

    def cast(e):
        e.tensor_copy(xbf[:], xf[:])
        e.drain()
        return e.tensor_tensor(sqbf[:], xbf[:], xbf[:], ALU.mult)
    P.step("vector", cast)

    def statmm(e):
        for s in range(NSLAB):
            e.matmul(psx[:], ones_col[:], xbf[:, s, :], start=(s == 0), stop=(s == NSLAB - 1))
        for s in range(NSLAB):
            ins = e.matmul(psq[:], ones_col[:], sqbf[:, s, :], start=(s == 0), stop=(s == NSLAB - 1))
        return ins
    P.step("tensor", statmm)

    def tiny(e):
        mu = stat[:, 0:CW]; m2 = stat[:, CW:2 * CW]; var = stat[:, 2 * CW:3 * CW]
        tmp = stat[:, 5 * CW:6 * CW]
        e.tensor_scalar_mul(mu[:], psx[:], 1.0 / D)
        e.tensor_scalar_mul(m2[:], psq[:], 1.0 / D)
        e.drain()
        e.tensor_tensor(tmp[:], mu[:], mu[:], ALU.mult)
        e.drain()
        return e.tensor_tensor(var[:], m2[:], tmp[:], ALU.subtract)
    P.step("vector", tiny)

    def rstd_fn(e):
        var = stat[:, 2 * CW:3 * CW]; rstd = stat[:, 3 * CW:4 * CW]
        e.activation(out=rstd[:], in_=var[:], func=AF.Ln, bias=eps_ap)
        e.drain()
        return e.activation(out=rstd[:], in_=rstd[:], func=AF.Exp, scale=-0.5)
    P.step("scalar", rstd_fn)

    def prod_fn(e):
        mu = stat[:, 0:CW]; rstd = stat[:, 3 * CW:4 * CW]; prod = stat[:, 4 * CW:5 * CW]
        return e.tensor_tensor(prod[:], mu[:], rstd[:], ALU.mult)
    P.step("vector", prod_fn)

    def bcast(e):
        rstd = stat[:, 3 * CW:4 * CW]; prod = stat[:, 4 * CW:5 * CW]
        e.matmul(pbr[:], ones_row[:], rstd[:], start=True, stop=True)
        return e.matmul(pbp[:], ones_row[:], prod[:], start=True, stop=True)
    P.step("tensor", bcast)


def build_p1a(nc, io, es, src, dst, eps=1e-5, name="p1a"):
    xT = io.inp(src, [D, T], F32) if src == "xT" else io.stream(src, [D, T], F32, produce=False)
    xnT = io.stream(dst, [D, T], BF16, produce=True)
    CW = 512
    NCH = T // CW
    P = Pipe(nc, es, name)
    t = lambda n, sh, dt: es.enter_context(nc.sbuf_tensor(name + "_" + n, sh, dt))
    pt = lambda n, sh: es.enter_context(nc.psum_tensor(name + "_" + n, sh, F32))
    sb = {
        "ones_col_bf": t("ones", [128, 1], BF16),
        "ones_row_f": t("onesr", [1, 128], F32),
        name + "xbf": t("xbf", [128, NSLAB, CW], BF16),
        name + "sqbf": t("sqbf", [128, NSLAB, CW], BF16),
        name + "stat": t("stat", [1, 6 * CW], F32),
    }
    ps = {
        name + "psx": pt("psx", [1, CW]),
        name + "psq": pt("psq", [1, CW]),
        name + "pbr": pt("pbr", [128, CW]),
        name + "pbp": pt("pbp", [128, CW]),
    }
    x_f = [t(f"x{i}", [128, NSLAB, CW], F32) for i in range(2)]
    xn_sb = [t(f"xn{i}", [128, NSLAB, CW], BF16) for i in range(2)]
    tmp_f = t("tmp", [128, CW], F32)
    kon = t("kon", [128, 1], F32)
    eps_ap = kon[0:1, 0:1]
    P.step("vector", lambda e: e.memset(kon[:, 0:1], float(eps)))

    P.step("sync", lambda e: e.dma_start(out=sb["ones_col_bf"][:], in_=io.inp("c_ones_col_bf", [128, 1], BF16)), dma=True)
    P.step("sync", lambda e: e.dma_start(out=sb["ones_row_f"][:], in_=io.inp("c_ones_row_f", [1, 128], F32)), dma=True)

    s_ld = P.xsem("ld"); s_st = P.xsem("st")
    st_gate = {}
    for c in range(NCH):
        P.step("sync", lambda e, c=c: e.dma_start(out=x_f[c % 2][:], in_=slabs_ap(xT, c * CW, CW)),
               dma=True, wait=(0, 0), inc=s_ld, waits=[(s_st, st_gate.get(c - 2, 0))])
        P.step("sync", lambda e: e.nop(), waits=[(s_ld, s_ld.n)])

        ln_stats_steps(P, sb, ps, x_f[c % 2], CW, eps_ap, name)

        def ap_fn(e, c=c):
            xf = x_f[c % 2]
            out = xn_sb[c % 2]
            for s in range(NSLAB):
                e.tensor_tensor(tmp_f[:], xf[:, s, :], ps[name + "pbr"][:], ALU.mult)
                e.drain()
                ins = e.tensor_tensor(out[:, s, :], tmp_f[:], ps[name + "pbp"][:], ALU.subtract)
                e.drain()
            return ins
        P.step("vector", ap_fn)
        P.step("gpsimd", lambda e, c=c: e.dma_start(out=slabs_ap(xnT, c * CW, CW), in_=xn_sb[c % 2][:]), dma=True, inc=s_st)
        st_gate[c] = s_st.n
    P.emit()


def load_weight_steps(P, io, sbuf_t, name, shape):
    ap = io.inp(name, list(shape), F32)
    P.step("gpsimd", lambda e: e.dma_start(
        out=sbuf_t[:], in_=ap.rearrange("(k p) m -> p k m", p=128)), dma=True,
        wait=(0, 0))


def vec_load_steps(P, io, sbuf_t, name, n):
    ap = io.inp(name, [n], F32)
    P.step("sync", lambda e: e.dma_start(
        out=sbuf_t[:], in_=ap.rearrange("(s p) -> p s", p=128),
        allow_slow_non_contiguous=True), dma=True,
        wait=(0, 0))


def build_p1b(nc, io, es, name="p1b"):
    xnT = io.stream("xn1T", [D, T], BF16, produce=False)
    kvT = io.stream("kvT", [D, T], F32, produce=True)
    CW = 512
    NCH = T // CW
    P = Pipe(nc, es, name)
    t = lambda n, sh, dt: es.enter_context(nc.sbuf_tensor(name + "_" + n, sh, dt))
    wk = t("wk", [128, NSLAB, D], BF16)
    wv = t("wv", [128, NSLAB, D], BF16)
    xn = [t(f"xn{i}", [128, NSLAB, CW], BF16) for i in range(2)]
    k_scr = t("kscr", [128, CW], F32)
    kv_sb = [t(f"kv{i}", [128, NSLAB, CW], F32) for i in range(2)]
    pk = es.enter_context(nc.psum_tensor(name + "_pk", [128, CW], F32))
    pv = es.enter_context(nc.psum_tensor(name + "_pv", [128, CW], F32))
    load_weight_steps(P, io, wk, "w_Wk", (D, D))
    load_weight_steps(P, io, wv, "w_Wv", (D, D))
    s_ld = P.xsem("ld"); s_st = P.xsem("st")
    st_gate = {}
    for c in range(NCH):
        P.step("sync", lambda e, c=c: e.dma_start(
            out=xn[c % 2][:], in_=slabs_ap(xnT, c * CW, CW)), dma=True,
            wait=(0, 0), inc=s_ld, waits=[(s_st, st_gate.get(c - 2, 0))])
        P.step("sync", lambda e: e.nop(), waits=[(s_ld, s_ld.n)])
        for s in range(NSLAB):
            def mmk(e, c=c, s=s):
                for k in range(NSLAB):
                    ins = e.matmul(pk[:], wk[:, k, s * 128:(s + 1) * 128],
                                   xn[c % 2][:, k, :], start=(k == 0), stop=(k == NSLAB - 1))
                return ins
            P.step("tensor", mmk)
            P.step("vector", lambda e, s=s: e.tensor_copy(k_scr[:], pk[:]))
            def mmv(e, c=c, s=s):
                for k in range(NSLAB):
                    ins = e.matmul(pv[:], wv[:, k, s * 128:(s + 1) * 128],
                                   xn[c % 2][:, k, :], start=(k == 0), stop=(k == NSLAB - 1))
                return ins
            P.step("tensor", mmv)
            P.step("vector", lambda e, c=c, s=s: e.tensor_tensor(
                kv_sb[c % 2][:, s, :], k_scr[:], pv[:], ALU.mult))
        P.step("gpsimd", lambda e, c=c: e.dma_start(
            out=slabs_ap(kvT, c * CW, CW), in_=kv_sb[c % 2][:]), dma=True, inc=s_st)
        st_gate[c] = s_st.n
    P.emit()


def build_p1c(nc, io, es, name="p1c"):
    xnT = io.stream("xn1T", [D, T], BF16, produce=False)
    lrT = io.stream("lrT", [D, T], F32, produce=True)
    thT = io.stream("thT", [D, T], F32, produce=True)
    CW = 512
    NCH = T // CW
    P = Pipe(nc, es, name)
    t = lambda n, sh, dt: es.enter_context(nc.sbuf_tensor(name + "_" + n, sh, dt))
    wa = t("wa", [128, NSLAB, 2 * D], BF16)
    xn = [t(f"xn{i}", [128, NSLAB, CW], BF16) for i in range(2)]
    ar = t("ar", [128, NSLAB, CW], F32)
    ai = t("ai", [128, NSLAB, CW], F32)
    t1 = t("t1", [128, NSLAB, CW], F32)
    t2 = t("t2", [128, NSLAB, CW], F32)
    lr_sb = [t(f"lr{i}", [128, NSLAB, CW], F32) for i in range(2)]
    th_sb = [t(f"th{i}", [128, NSLAB, CW], F32) for i in range(2)]
    kon = t("kon", [128, 1], F32)
    pa = es.enter_context(nc.psum_tensor(name + "_pa", [128, CW], F32))
    P.step("vector", lambda e: e.memset(kon[:, 0:1], 1.0))
    load_weight_steps(P, io, wa, "w_Wa", (D, 2 * D))
    s_ld = P.xsem("ld"); s_st = P.xsem("st")
    st_gate = {}
    for c in range(NCH):
        P.step("sync", lambda e, c=c: e.dma_start(
            out=xn[c % 2][:], in_=slabs_ap(xnT, c * CW, CW)), dma=True,
            wait=(0, 0), inc=s_ld, waits=[(s_st, st_gate.get(c - 2, 0))])
        P.step("sync", lambda e: e.nop(), waits=[(s_ld, s_ld.n)])
        for part, dst in ((0, ar), (1, ai)):
            for s in range(NSLAB):
                def mma(e, c=c, s=s, part=part):
                    for k in range(NSLAB):
                        ins = e.matmul(pa[:], wa[:, k, part * D + s * 128: part * D + (s + 1) * 128],
                                       xn[c % 2][:, k, :], start=(k == 0), stop=(k == NSLAB - 1))
                    return ins
                P.step("tensor", mma)
                P.step("vector", lambda e, s=s, dst=dst: e.tensor_copy(dst[:, s, :], pa[:]))
        def sq_fn(e):
            e.activation(out=t1[:], in_=ar[:], func=AF.Square)
            return e.activation(out=t2[:], in_=ai[:], func=AF.Square)
        P.step("scalar", sq_fn)
        P.step("vector", lambda e: e.tensor_tensor(t1[:], t1[:], t2[:], ALU.add))
        def lr_fn(e, c=c):
            e.activation(out=t1[:], in_=t1[:], func=AF.Sqrt)
            e.drain()
            e.activation(out=t1[:], in_=t1[:], func=AF.Exp, scale=-1.0)
            e.drain()
            return e.activation(out=lr_sb[c % 2][:], in_=t1[:], func=AF.Ln,
                                bias=kon[:, 0:1])
        P.step("scalar", lr_fn)
        P.step("vector", lambda e: e.reciprocal(t1[:], ar[:]))
        P.step("vector", lambda e: e.tensor_tensor(t1[:], ai[:], t1[:], ALU.mult))
        def at_fn(e):
            e.activation(out=t1[:], in_=t1[:], func=AF.Arctan)
            return e.activation(out=t2[:], in_=ai[:], func=AF.Sign)
        P.step("scalar", at_fn)
        def th_fn(e, c=c):
            e.tensor_single_scalar(ar[:], ar[:], 0.0, ALU.is_lt)
            e.drain()
            e.tensor_tensor(t2[:], ar[:], t2[:], ALU.mult)
            e.drain()
            return e.scalar_tensor_tensor(th_sb[c % 2][:], t2[:], math.pi, t1[:],
                                          ALU.mult, ALU.add)
        P.step("vector", th_fn)
        P.step("gpsimd", lambda e, c=c: e.dma_start(
            out=slabs_ap(lrT, c * CW, CW), in_=lr_sb[c % 2][:]), dma=True, inc=s_st)
        P.step("gpsimd", lambda e, c=c: e.dma_start(
            out=slabs_ap(thT, c * CW, CW), in_=th_sb[c % 2][:]), dma=True, inc=s_st)
        st_gate[c] = s_st.n
    P.emit()


TWO_PI = 2.0 * math.pi

def build_p1d(nc, io, es, name="p1d"):
    xnT = io.stream("xn1T", [D, T], BF16, produce=False)
    kvT = io.stream("kvT", [D, T], F32, produce=False)
    lrT = io.stream("lrT", [D, T], F32, produce=False)
    thT = io.stream("thT", [D, T], F32, produce=False)
    yrT = io.stream("yrT", [D, T], BF16, produce=True)
    gnab = io.stream("gnab", [B, 128, 16], F32, produce=True)
    CW = 256
    NCH_B = S // CW
    NSUB = CW // 64
    P = Pipe(nc, es, name)
    t = lambda n, sh, dt: es.enter_context(nc.sbuf_tensor(name + "_" + n, sh, dt))
    wq = t("wq", [128, NSLAB, D], BF16)
    xn = [t(f"xn{i}", [128, NSLAB, CW], BF16) for i in range(2)]
    kv = [t(f"kv{i}", [128, NSLAB, CW], F32) for i in range(2)]
    lr = [t(f"lr{i}", [128, NSLAB, CW], F32) for i in range(2)]
    th = [t(f"th{i}", [128, NSLAB, CW], F32) for i in range(2)]
    q_f = t("qf", [128, NSLAB, CW], F32)
    pos = t("pos", [128, NSLAB, CW], F32)
    phi = t("phi", [128, NSLAB, CW], F32)
    cs = t("cs", [128, NSLAB, CW], F32)
    sn = t("sn", [128, NSLAB, CW], F32)
    ew = t("ew", [128, NSLAB, CW], F32)
    CR = t("CR", [128, NSLAB, CW], F32)
    CIp = t("CIp", [128, NSLAB, CW], F32)
    u1 = t("u1", [128, NSLAB, CW], F32)
    u2 = t("u2", [128, NSLAB, CW], F32)
    hr = t("hr", [128, NSLAB, CW], F32)
    hi = t("hi", [128, NSLAB, CW], F32)
    ki = t("ki", [128, NSLAB, 64], I32)
    yr_sb = [t(f"yr{i}", [128, NSLAB, CW], BF16) for i in range(2)]
    ones64 = t("ones64", [128, 64], F32)
    car_r = t("car_r", [128, NSLAB, 1], F32)
    car_ni = t("car_ni", [128, NSLAB, 1], F32)
    carx = t("carx", [128, NSLAB, 4], F32)
    gsum3 = t("gsum3", [128, NSLAB, 3], F32)
    stats = t("stats", [128, NSLAB, 3, NCH_B], F32)
    gs_sb = t("gs", [4, 24], F32)
    gv_sb = t("gv", [4, 16], F32)
    ab_sb = t("ab", [128, 16], F32)
    gsc = t("gsc", [128, NSLAB], F32)
    gbi = t("gbi", [128, NSLAB], F32)
    grp_ones = t("grpo", [128, 4], F32)
    grp_selT = t("grps", [4, 128], F32)
    kon = t("kon", [128, 1], F32)
    pq = es.enter_context(nc.psum_tensor(name + "_pq", [128, CW], F32))
    pg = es.enter_context(nc.psum_tensor(name + "_pg", [4, 24], F32))
    pb = es.enter_context(nc.psum_tensor(name + "_pb", [128, 16], F32))

    P.step("vector", lambda e: e.memset(kon[:, 0:1], 1e-6))
    load_weight_steps(P, io, wq, "w_Wq", (D, D))
    vec_load_steps(P, io, gsc, "w_gn_scale", D)
    vec_load_steps(P, io, gbi, "w_gn_bias", D)
    P.step("sync", lambda e: e.dma_start(out=ones64[:], in_=io.inp("c_ones64_f", [128, 64], F32)), dma=True, wait=(0, 0))
    P.step("sync", lambda e: e.dma_start(out=grp_ones[:], in_=io.inp("c_grp_ones", [128, 4], F32)), dma=True, wait=(0, 0))
    P.step("sync", lambda e: e.dma_start(out=grp_selT[:], in_=io.inp("c_grp_selT", [4, 128], F32)), dma=True, wait=(0, 0))

    s_ld = P.xsem("ld"); s_st = P.xsem("st")
    st_gate = {}
    for b in range(B):
        P.step("vector", lambda e: (e.memset(car_r[:], 0.0), e.memset(car_ni[:], 0.0))[-1])
        for cc in range(NCH_B):
            c = b * NCH_B + cc
            buf = c % 2
            xw = [(s_st, st_gate.get(c - 2, 0))]
            P.step("sync", lambda e, c=c, buf=buf: e.dma_start(out=xn[buf][:], in_=slabs_ap(xnT, c * CW, CW)), dma=True, wait=(0, 0), inc=s_ld, waits=xw)
            P.step("sync", lambda e, c=c, buf=buf: e.dma_start(out=kv[buf][:], in_=slabs_ap(kvT, c * CW, CW)), dma=True, wait=(0, 0), inc=s_ld, waits=xw)
            P.step("sync", lambda e, c=c, buf=buf: e.dma_start(out=lr[buf][:], in_=slabs_ap(lrT, c * CW, CW)), dma=True, wait=(0, 0), inc=s_ld, waits=xw)
            P.step("sync", lambda e, c=c, buf=buf: e.dma_start(out=th[buf][:], in_=slabs_ap(thT, c * CW, CW)), dma=True, wait=(0, 0), inc=s_ld, waits=xw)
            P.step("sync", lambda e: e.nop(), waits=[(s_ld, s_ld.n)])
            for s in range(NSLAB):
                def mmq(e, buf=buf, s=s):
                    for k in range(NSLAB):
                        ins = e.matmul(pq[:], wq[:, k, s * 128:(s + 1) * 128],
                                       xn[buf][:, k, :], start=(k == 0), stop=(k == NSLAB - 1))
                    return ins
                P.step("tensor", mmq)
                P.step("vector", lambda e, s=s: e.tensor_copy(q_f[:, s, :], pq[:]))
            for j in range(NSUB):
                j0, j1 = j * 64, (j + 1) * 64
                def scan_pp(e, buf=buf, j0=j0, j1=j1):
                    for s in range(NSLAB):
                        e.tensor_tensor_scan(pos[:, s, j0:j1], ones64[:], lr[buf][:, s, j0:j1],
                                             0.0, ALU.mult, ALU.add)
                    for s in range(NSLAB):
                        ins = e.tensor_tensor_scan(phi[:, s, j0:j1], ones64[:], th[buf][:, s, j0:j1],
                                                   0.0, ALU.mult, ALU.add)
                    return ins
                P.step("vector", scan_pp)
                def rr_fn(e, j0=j0, j1=j1):
                    sl = (slice(None), slice(None), slice(j0, j1))
                    ki_ = ki[:, :, 0:64]
                    e.tensor_scalar_mul(u1[sl], phi[sl], 1.0 / TWO_PI)
                    e.drain()
                    e.tensor_copy(ki_, u1[sl])
                    e.drain()
                    e.tensor_copy(u2[sl], ki_)
                    e.drain()
                    e.tensor_tensor(u1[sl], u1[sl], u2[sl], ALU.subtract)
                    e.drain()
                    e.tensor_scalar_mul(u1[sl], u1[sl], TWO_PI)
                    e.tensor_scalar(u2[sl], phi[sl], 1.0 / TWO_PI, 0.25, ALU.mult, ALU.add)
                    e.drain()
                    e.tensor_copy(ki_, u2[sl])
                    e.drain()
                    e.tensor_copy(hr[:, :, j0:j1], ki_)
                    e.drain()
                    e.tensor_tensor(u2[sl], u2[sl], hr[:, :, j0:j1], ALU.subtract)
                    e.drain()
                    return e.tensor_scalar_mul(u2[sl], u2[sl], TWO_PI)
                P.step("vector", rr_fn)
                def trig_fn(e, j0=j0, j1=j1):
                    sl = (slice(None), slice(None), slice(j0, j1))
                    e.activation(out=sn[sl], in_=u1[sl], func=AF.Sin)
                    e.activation(out=cs[sl], in_=u2[sl], func=AF.Sin)
                    e.drain()
                    return e.activation(out=ew[sl], in_=pos[sl], func=AF.Exp)
                P.step("scalar", trig_fn)
                def terms_fn(e, buf=buf, j0=j0, j1=j1):
                    sl = (slice(None), slice(None), slice(j0, j1))
                    e.tensor_tensor(u1[sl], kv[buf][sl], ew[sl], ALU.mult)
                    e.drain()
                    e.tensor_tensor(u2[sl], u1[sl], cs[sl], ALU.mult)
                    return e.tensor_tensor(u1[sl], u1[sl], sn[sl], ALU.mult)
                P.step("vector", terms_fn)
                def scan_c(e, j0=j0, j1=j1):
                    for s in range(NSLAB):
                        e.tensor_tensor_scan(CR[:, s, j0:j1], ones64[:], u2[:, s, j0:j1],
                                             car_r[:, s, 0:1], ALU.mult, ALU.add)
                    for s in range(NSLAB):
                        ins = e.tensor_tensor_scan(CIp[:, s, j0:j1], ones64[:], u1[:, s, j0:j1],
                                                   car_ni[:, s, 0:1], ALU.mult, ALU.add)
                    return ins
                P.step("vector", scan_c)
                def eiend_fn(e, j1=j1):
                    return e.activation(out=carx[:, :, 0:1], in_=pos[:, :, j1 - 1:j1],
                                        func=AF.Exp, scale=-1.0)
                P.step("scalar", eiend_fn)
                def carry_fn(e, j1=j1):
                    je = slice(j1 - 1, j1)
                    sl = (slice(None), slice(None), je)
                    e.tensor_tensor(carx[:, :, 1:2], cs[sl], CR[sl], ALU.mult)
                    e.tensor_tensor(carx[:, :, 2:3], sn[sl], CIp[sl], ALU.mult)
                    e.drain()
                    e.tensor_tensor(carx[:, :, 1:2], carx[:, :, 1:2], carx[:, :, 2:3], ALU.add)
                    e.drain()
                    e.tensor_tensor(car_r[:, :, 0:1], carx[:, :, 0:1], carx[:, :, 1:2], ALU.mult)
                    e.tensor_tensor(carx[:, :, 1:2], cs[sl], CIp[sl], ALU.mult)
                    e.tensor_tensor(carx[:, :, 2:3], sn[sl], CR[sl], ALU.mult)
                    e.drain()
                    e.tensor_tensor(carx[:, :, 1:2], carx[:, :, 1:2], carx[:, :, 2:3], ALU.subtract)
                    e.drain()
                    return e.tensor_tensor(car_ni[:, :, 0:1], carx[:, :, 0:1], carx[:, :, 1:2], ALU.mult)
                P.step("vector", carry_fn)
            P.step("vector", lambda e: e.reciprocal(ew[:], ew[:]))
            def rot_fn(e):
                e.tensor_tensor(u1[:], cs[:], CR[:], ALU.mult)
                e.tensor_tensor(u2[:], sn[:], CIp[:], ALU.mult)
                e.drain()
                e.tensor_tensor(u1[:], u1[:], u2[:], ALU.add)
                e.drain()
                e.tensor_tensor(hr[:], ew[:], u1[:], ALU.mult)
                e.tensor_tensor(u1[:], sn[:], CR[:], ALU.mult)
                e.tensor_tensor(u2[:], cs[:], CIp[:], ALU.mult)
                e.drain()
                e.tensor_tensor(u1[:], u1[:], u2[:], ALU.subtract)
                e.drain()
                return e.tensor_tensor(hi[:], ew[:], u1[:], ALU.mult)
            P.step("vector", rot_fn)
            def y_fn(e, buf=buf):
                e.tensor_tensor(hr[:], q_f[:], hr[:], ALU.mult)
                e.tensor_tensor(hi[:], q_f[:], hi[:], ALU.mult)
                e.drain()
                return e.tensor_copy(yr_sb[buf][:], hr[:])
            P.step("vector", y_fn)
            def stat_fn(e, cc=cc):
                e.tensor_reduce(stats[:, :, 0, cc:cc + 1], hr[:], mybir.AxisListType.X, ALU.add)
                e.tensor_reduce(stats[:, :, 1, cc:cc + 1], hi[:], mybir.AxisListType.X, ALU.add)
                e.tensor_tensor(u1[:], hr[:], hr[:], ALU.mult)
                e.tensor_tensor(u2[:], hi[:], hi[:], ALU.mult)
                e.drain()
                e.tensor_tensor(u1[:], u1[:], u2[:], ALU.add)
                e.drain()
                return e.tensor_reduce(stats[:, :, 2, cc:cc + 1], u1[:], mybir.AxisListType.X, ALU.add)
            P.step("vector", stat_fn)
            P.step("gpsimd", lambda e, c=c, buf=buf: e.dma_start(
                out=slabs_ap(yrT, c * CW, CW), in_=yr_sb[buf][:]), dma=True, inc=s_st)
            st_gate[c] = s_st.n
        def gsum_fn(e):
            return e.tensor_reduce(gsum3[:], stats[:], mybir.AxisListType.X, ALU.add)
        P.step("vector", gsum_fn)
        def gmm_fn(e):
            return e.matmul(pg[:], grp_ones[:], gsum3[:].rearrange("p s c -> p (s c)"),
                            start=True, stop=True)
        P.step("tensor", gmm_fn)
        def gmath_fn(e):
            cnt = 1.0 / (S * 32)
            e.tensor_scalar_mul(gs_sb[:], pg[:], cnt)
            e.drain()
            g = gs_sb[:].rearrange("g (s c) -> g s c", c=3)
            mr = g[:, :, 0:1]; mi = g[:, :, 1:2]; msq = g[:, :, 2:3]
            gv = gv_sb[:].rearrange("g (s c) -> g s c", c=2)
            e.tensor_tensor(gv[:, :, 0:1], mr[:], mr[:], ALU.mult)
            e.tensor_tensor(gv[:, :, 1:2], mi[:], mi[:], ALU.mult)
            e.drain()
            e.tensor_tensor(gv[:, :, 1:2], gv[:, :, 0:1], gv[:, :, 1:2], ALU.add)
            e.drain()
            e.tensor_tensor(gv[:, :, 1:2], msq[:], gv[:, :, 1:2], ALU.subtract)
            e.drain()
            return e.tensor_copy(gv[:, :, 0:1], mr[:])
        P.step("vector", gmath_fn)
        def grstd_fn(e):
            gv = gv_sb[:].rearrange("g (s c) -> g s c", c=2)
            e.activation(out=gv[:, :, 1:2], in_=gv[:, :, 1:2], func=AF.Ln, bias=kon[0:4, 0:1])
            e.drain()
            return e.activation(out=gv[:, :, 1:2], in_=gv[:, :, 1:2], func=AF.Exp, scale=-0.5)
        P.step("scalar", grstd_fn)
        def gbc_fn(e):
            return e.matmul(pb[:], grp_selT[:], gv_sb[:], start=True, stop=True)
        P.step("tensor", gbc_fn)
        def ab_fn(e):
            pbv = pb[:].rearrange("p (s c) -> p s c", c=2)
            abv = ab_sb[:].rearrange("p (a s) -> p a s", a=2)
            e.tensor_tensor(abv[:, 0, :], pbv[:, :, 1], gsc[:], ALU.mult)
            e.drain()
            e.tensor_tensor(abv[:, 1, :], pbv[:, :, 0], abv[:, 0, :], ALU.mult)
            e.drain()
            return e.tensor_tensor(abv[:, 1, :], gbi[:], abv[:, 1, :], ALU.subtract)
        P.step("vector", ab_fn)
        P.step("gpsimd", lambda e, b=b: e.dma_start(out=gnab[b], in_=ab_sb[:]), dma=True)
    P.emit()


def build_p1e(nc, io, es, name="p1e"):
    xT = io.inp("xT", [D, T], F32)
    xnT = io.stream("xn1T", [D, T], BF16, produce=False)
    yrT = io.stream("yrT", [D, T], BF16, produce=False)
    gnab = io.stream("gnab", [B, 128, 16], F32, produce=False)
    vT = io.stream("v1T", [D, T], F32, produce=True)
    CW = 512
    NCH = T // CW
    P = Pipe(nc, es, name)
    t = lambda n, sh, dt: es.enter_context(nc.sbuf_tensor(name + "_" + n, sh, dt))
    pt = lambda n, sh: es.enter_context(nc.psum_tensor(name + "_" + n, sh, F32))
    wg = t("wgw", [128, NSLAB, D], BF16)
    wo = t("wow", [128, NSLAB, D], BF16)
    ab = t("abt", [128, B, 16], F32)
    xn = [t(f"xn{i}", [128, NSLAB, CW], BF16) for i in range(2)]
    yr = [t(f"yr{i}", [128, NSLAB, CW], BF16) for i in range(2)]
    x_f = [t(f"x{i}", [128, NSLAB, CW], F32) for i in range(2)]
    sg_bf = t("sg", [128, NSLAB, CW], BF16)
    zg_bf = t("zg", [128, NSLAB, CW], BF16)
    z_bf = t("z", [128, CW], BF16)
    v_sb = [t(f"v{i}", [128, NSLAB, CW], F32) for i in range(2)]
    tmp_f = t("tmp", [128, CW], F32)
    kon = t("kon", [128, 1], F32)
    sb = {
        "ones_col_bf": t("ones", [128, 1], BF16),
        "ones_row_f": t("onesr", [1, 128], F32),
        name + "xbf": t("xbf", [128, NSLAB, CW], BF16),
        name + "sqbf": t("sqbf", [128, NSLAB, CW], BF16),
        name + "stat": t("stat", [1, 6 * CW], F32),
    }
    ps = {
        name + "psx": pt("psx", [1, CW]),
        name + "psq": pt("psq", [1, CW]),
        name + "pbr": pt("pbr", [128, CW]),
        name + "pbp": pt("pbp", [128, CW]),
    }
    pgm = pt("pg", [128, CW])
    pom = pt("po", [128, CW])
    eps_ap = kon[0:1, 0:1]
    P.step("vector", lambda e: e.memset(kon[:, 0:1], 1e-5))
    P.step("sync", lambda e: e.dma_start(out=sb["ones_col_bf"][:], in_=io.inp("c_ones_col_bf", [128, 1], BF16)), dma=True, wait=(0, 0))
    P.step("sync", lambda e: e.dma_start(out=sb["ones_row_f"][:], in_=io.inp("c_ones_row_f", [1, 128], F32)), dma=True, wait=(0, 0))
    load_weight_steps(P, io, wg, "w_Wg", (D, D))
    load_weight_steps(P, io, wo, "w_Wo", (D, D))
    P.step("sync", lambda e: e.dma_start(out=ab[:], in_=gnab.rearrange("b p k -> p b k")), dma=True, wait=(0, 0))
    s_ld = P.xsem("ld"); s_st = P.xsem("st")
    st_gate = {}
    for c in range(NCH):
        b = (c * CW) // S
        buf = c % 2
        xw = [(s_st, st_gate.get(c - 2, 0))]
        P.step("sync", lambda e, c=c, buf=buf: e.dma_start(out=xn[buf][:], in_=slabs_ap(xnT, c * CW, CW)), dma=True, wait=(0, 0), inc=s_ld, waits=xw)
        P.step("sync", lambda e, c=c, buf=buf: e.dma_start(out=yr[buf][:], in_=slabs_ap(yrT, c * CW, CW)), dma=True, wait=(0, 0), inc=s_ld, waits=xw)
        P.step("sync", lambda e, c=c, buf=buf: e.dma_start(out=x_f[buf][:], in_=slabs_ap(xT, c * CW, CW)), dma=True, wait=(0, 0), inc=s_ld, waits=xw)
        P.step("sync", lambda e: e.nop(), waits=[(s_ld, s_ld.n)])
        ln_stats_steps(P, sb, ps, x_f[buf], CW, eps_ap, name)
        for s in range(NSLAB):
            def mmg(e, buf=buf, s=s):
                for k in range(NSLAB):
                    ins = e.matmul(pgm[:], wg[:, k, s * 128:(s + 1) * 128],
                                   xn[buf][:, k, :], start=(k == 0), stop=(k == NSLAB - 1))
                return ins
            P.step("tensor", mmg)
            P.step("scalar", lambda e, s=s: e.activation(out=sg_bf[:, s, :], in_=pgm[:], func=AF.Silu))
            def zg_fn(e, buf=buf, s=s, b=b):
                abv = ab[:, b, :].rearrange("p (a s) -> p a s", a=2)
                e.tensor_scalar(z_bf[:], yr[buf][:, s, :], abv[:, 0, s:s + 1],
                                abv[:, 1, s:s + 1], ALU.mult, ALU.add)
                e.drain()
                return e.tensor_tensor(zg_bf[:, s, :], z_bf[:], sg_bf[:, s, :], ALU.mult)
            P.step("vector", zg_fn)
        for s in range(NSLAB):
            def mmo(e, s=s):
                for k in range(NSLAB):
                    ins = e.matmul(pom[:], wo[:, k, s * 128:(s + 1) * 128],
                                   zg_bf[:, k, :], start=(k == 0), stop=(k == NSLAB - 1))
                return ins
            P.step("tensor", mmo)
            def v_fn(e, buf=buf, s=s):
                e.tensor_tensor(tmp_f[:], x_f[buf][:, s, :], ps[name + "pbr"][:], ALU.mult)
                e.drain()
                e.tensor_tensor(tmp_f[:], tmp_f[:], ps[name + "pbp"][:], ALU.subtract)
                e.drain()
                return e.tensor_tensor(v_sb[buf][:, s, :], tmp_f[:], pom[:], ALU.add)
            P.step("vector", v_fn)
        P.step("gpsimd", lambda e, c=c, buf=buf: e.dma_start(
            out=slabs_ap(vT, c * CW, CW), in_=v_sb[buf][:]), dma=True, inc=s_st)
        st_gate[c] = s_st.n
    P.emit()


def build_p1f(nc, io, es, half, name=None):
    name = name or f"p1f{half}"
    xT = io.inp("xT", [D, T], F32)
    vT = io.stream("v1T", [D, T], F32, produce=False)
    accT = io.stream("ffacc", [D, T], F32, produce=(half == 0))
    if half == 1:
        x1T = io.stream("x1T", [D, T], F32, produce=True)
    F = 4096
    FH = F // 2
    NFH = FH // 128
    CW = 512
    NCH = T // CW
    P = Pipe(nc, es, name)
    t = lambda n, sh, dt: es.enter_context(nc.sbuf_tensor(name + "_" + n, sh, dt))
    pt = lambda n, sh: es.enter_context(nc.psum_tensor(name + "_" + n, sh, F32))
    w1 = t("w1w", [128, NSLAB, FH], BF16)
    w2 = t("w2w", [128, NFH, D], BF16)
    b1v = t("b1v", [128, NFH], F32)
    b2v = t("b2v", [128, NSLAB], F32)
    l1s = t("l1s", [128, NSLAB], F32)
    l1b = t("l1b", [128, NSLAB], F32)
    l2s = t("l2s", [128, NSLAB], F32)
    l2b = t("l2b", [128, NSLAB], F32)
    v_f = [t(f"v{i}", [128, NSLAB, CW], F32) for i in range(2)]
    acc = [t("a0", [128, NSLAB, CW], F32)] * 2 if half == 1 else None
    x_f = [t("x0", [128, NSLAB, CW], F32)] * 2 if half == 1 else None
    x2_f = t("x2f", [128, NSLAB, CW], F32) if half == 0 else None
    x2_bf = t("x2bf", [128, NSLAB, CW], BF16)
    h_bf = t("hbf", [128, NFH, CW], BF16)
    s_bf = t("sbf", [128, CW], BF16)
    sq_b = t("sqb", [128, CW], BF16)
    u_f = t("uf", [128, NSLAB, CW], F32) if half == 1 else None
    o_sb = v_f
    tmp_f = t("tmp", [128, CW], F32)
    kon = t("kon", [128, 1], F32)
    sb = {
        "ones_col_bf": t("ones", [128, 1], BF16),
        "ones_row_f": t("onesr", [1, 128], F32),
        name + "xbf": t("xbf", [128, NSLAB, CW], BF16),
        name + "sqbf": t("sqbf", [128, NSLAB, CW], BF16),
        name + "stat": t("stat", [1, 6 * CW], F32),
    }
    ps = {
        name + "psx": pt("psx", [1, CW]),
        name + "psq": pt("psq", [1, CW]),
        name + "pbr": pt("pbr", [128, CW]),
        name + "pbp": pt("pbp", [128, CW]),
    }
    ph = pt("ph", [128, CW])
    pu = pt("pu", [128, CW])
    eps_ap = kon[0:1, 0:1]
    P.step("vector", lambda e: e.memset(kon[:, 0:1], 1e-6))
    P.step("sync", lambda e: e.dma_start(out=sb["ones_col_bf"][:], in_=io.inp("c_ones_col_bf", [128, 1], BF16)), dma=True, wait=(0, 0))
    P.step("sync", lambda e: e.dma_start(out=sb["ones_row_f"][:], in_=io.inp("c_ones_row_f", [1, 128], F32)), dma=True, wait=(0, 0))
    w1ap = io.inp("w_W1", [D, F], F32)
    P.step("gpsimd", lambda e: e.dma_start(
        out=w1[:], in_=w1ap.rearrange("(k p) m -> p k m", p=128)[:, :, half * FH:(half + 1) * FH]), dma=True, wait=(0, 0))
    w2ap = io.inp("w_W2", [F, D], F32)
    P.step("gpsimd", lambda e: e.dma_start(
        out=w2[:], in_=w2ap.rearrange("(k p) m -> p k m", p=128)[:, half * NFH:(half + 1) * NFH, :]), dma=True, wait=(0, 0))
    b1ap = io.inp("w_b1", [F], F32)
    P.step("sync", lambda e: e.dma_start(
        out=b1v[:], in_=b1ap.rearrange("(s p) -> p s", p=128)[:, half * NFH:(half + 1) * NFH],
        allow_slow_non_contiguous=True), dma=True, wait=(0, 0))
    vec_load_steps(P, io, b2v, "w_b2", D)
    vec_load_steps(P, io, l1s, "w_ln1_scale", D)
    vec_load_steps(P, io, l1b, "w_ln1_bias", D)
    vec_load_steps(P, io, l2s, "w_ln2_scale", D)
    vec_load_steps(P, io, l2b, "w_ln2_bias", D)
    store_pts = {}
    for c in range(NCH):
        buf = c % 2
        w = store_pts.get(c - 2)
        P.step("sync", lambda e, c=c, buf=buf: e.dma_start(out=v_f[buf][:], in_=slabs_ap(vT, c * CW, CW)), dma=True)
        if half == 1:
            P.step("sync", lambda e, c=c, buf=buf: e.dma_start(out=acc[buf][:], in_=slabs_ap(accT, c * CW, CW)), dma=True)
            P.step("sync", lambda e, c=c, buf=buf: e.dma_start(out=x_f[buf][:], in_=slabs_ap(xT, c * CW, CW)), dma=True)
        ln_stats_steps(P, sb, ps, v_f[buf], CW, eps_ap, name)
        for s in range(NSLAB):
            def x2_fn(e, buf=buf, s=s):
                e.tensor_tensor(tmp_f[:], v_f[buf][:, s, :], ps[name + "pbr"][:], ALU.mult)
                e.drain()
                e.tensor_tensor(tmp_f[:], tmp_f[:], ps[name + "pbp"][:], ALU.subtract)
                e.drain()
                if half == 0:
                    e.tensor_scalar(x2_f[:, s, :], tmp_f[:], l1s[:, s:s + 1], l1b[:, s:s + 1],
                                    ALU.mult, ALU.add)
                    e.drain()
                    return e.tensor_copy(x2_bf[:, s, :], x2_f[:, s, :])
                return e.tensor_scalar(x2_bf[:, s, :], tmp_f[:], l1s[:, s:s + 1], l1b[:, s:s + 1],
                                       ALU.mult, ALU.add)
            P.step("vector", x2_fn)
        for m in range(NFH):
            def mmh(e, m=m):
                for k in range(NSLAB):
                    ins = e.matmul(ph[:], w1[:, k, m * 128:(m + 1) * 128],
                                   x2_bf[:, k, :], start=(k == 0), stop=(k == NSLAB - 1))
                return ins
            P.step("tensor", mmh)
            def gel1(e, m=m):
                e.tensor_scalar(s_bf[:], ph[:], 1.0, b1v[:, m:m + 1], ALU.mult, ALU.add)
                e.drain()
                e.tensor_tensor(sq_b[:], s_bf[:], s_bf[:], ALU.mult)
                e.drain()
                e.tensor_tensor(sq_b[:], sq_b[:], s_bf[:], ALU.mult)
                e.drain()
                return e.scalar_tensor_tensor(sq_b[:], sq_b[:], 0.044715, s_bf[:], ALU.mult, ALU.add)
            P.step("vector", gel1)
            P.step("scalar", lambda e: e.activation(out=sq_b[:], in_=sq_b[:], func=AF.Sigmoid,
                                                    scale=1.5957691216057308))
            def gel2(e, m=m):
                return e.tensor_tensor(h_bf[:, m, :], s_bf[:], sq_b[:], ALU.mult)
            P.step("vector", gel2)
        for s in range(NSLAB):
            def mmu(e, s=s):
                for k in range(NFH):
                    ins = e.matmul(pu[:], w2[:, k, s * 128:(s + 1) * 128],
                                   h_bf[:, k, :], start=(k == 0), stop=(k == NFH - 1))
                return ins
            P.step("tensor", mmu)
            if half == 0:
                def acc_fn(e, buf=buf, s=s):
                    e.tensor_scalar(tmp_f[:], pu[:], 1.0, b2v[:, s:s + 1], ALU.mult, ALU.add)
                    e.drain()
                    return e.tensor_tensor(o_sb[buf][:, s, :], tmp_f[:], x2_f[:, s, :], ALU.add)
                P.step("vector", acc_fn)
            else:
                def u_fn(e, buf=buf, s=s):
                    return e.tensor_tensor(u_f[:, s, :], pu[:], acc[buf][:, s, :], ALU.add)
                P.step("vector", u_fn)
        if half == 0:
            P.step("gpsimd", lambda e, c=c, buf=buf: e.dma_start(
                out=slabs_ap(accT, c * CW, CW), in_=o_sb[buf][:]), dma=True)
            store_pts[c] = P.at()
            continue
        ln_stats_steps(P, sb, ps, u_f, CW, eps_ap, name)
        for s in range(NSLAB):
            def out_fn(e, buf=buf, s=s):
                e.tensor_tensor(tmp_f[:], u_f[:, s, :], ps[name + "pbr"][:], ALU.mult)
                e.drain()
                e.tensor_tensor(tmp_f[:], tmp_f[:], ps[name + "pbp"][:], ALU.subtract)
                e.drain()
                e.tensor_scalar(tmp_f[:], tmp_f[:], l2s[:, s:s + 1], l2b[:, s:s + 1],
                                ALU.mult, ALU.add)
                e.drain()
                return e.tensor_tensor(o_sb[buf][:, s, :], tmp_f[:], x_f[buf][:, s, :], ALU.add)
            P.step("vector", out_fn)
        P.step("gpsimd", lambda e, c=c, buf=buf: e.dma_start(
            out=slabs_ap(x1T, c * CW, CW), in_=o_sb[buf][:]), dma=True)
        store_pts[c] = P.at()
    P.emit()


def build_p2a(nc, io, es, name="p2a"):
    x1T = io.stream("x1T", [D, T], F32, produce=False)
    xnT = io.stream("xn2T", [D, T], BF16, produce=True)
    combT = io.stream("combT", [32, T], F32, produce=True)
    xtokT = io.stream("xn2tokT", [T, D], BF16, produce=True)
    selT = io.stream("selT", [T, 4], F32, produce=True)
    wgtT = io.stream("wgtT", [T, 4], F32, produce=True)
    CW = 512
    NCH = T // CW
    NT4 = CW // 128
    P = Pipe(nc, es, name)
    t = lambda n, sh, dt: es.enter_context(nc.sbuf_tensor(name + "_" + n, sh, dt))
    pt = lambda n, sh: es.enter_context(nc.psum_tensor(name + "_" + n, sh, F32))
    gw = t("gw", [128, NSLAB, 32], BF16)
    ebrow = t("ebrow", [1, 32], F32)
    ebc = t("ebc", [128, 32], F32)
    iota32 = t("iota32", [128, 32], F32)
    identf = t("identf", [128, 128], F32)
    x_f = [t(f"x{i}", [128, NSLAB, CW], F32) for i in range(2)]
    xn_sb = [t(f"xn{i}", [128, NSLAB, CW], BF16) for i in range(2)]
    sc_f = t("sc", [128, 32], F32)
    bi_f = t("bi", [128, 32], F32)
    vals = t("vals", [128, 8], F32)
    idxu = t("idxu", [128, 8], U32)
    idxf = t("idxf", [128, 8], F32)
    oh = t("oh", [128, 4, 32], F32)
    og = t("og", [128, 8], F32)
    comb_tok = t("ctok", [128, 32], F32)
    combt_sb = [t(f"cT{i}", [32, CW], F32) for i in range(2)]
    wk4 = t("wk4", [128, 4], F32)
    xtk = t("xtk", [128, 1024], BF16)
    identbb = t("identbb", [128, 128], BF16)
    tmp_f = t("tmp", [128, CW], F32)
    kon = t("kon", [128, 1], F32)
    sb = {
        "ones_col_bf": t("ones", [128, 1], BF16),
        "ones_row_f": t("onesr", [1, 128], F32),
        name + "xbf": t("xbf", [128, NSLAB, CW], BF16),
        name + "sqbf": t("sqbf", [128, NSLAB, CW], BF16),
        name + "stat": t("stat", [1, 6 * CW], F32),
    }
    ps = {
        name + "psx": pt("psx", [1, CW]),
        name + "psq": pt("psq", [1, CW]),
        name + "pbr": pt("pbr", [128, CW]),
        name + "pbp": pt("pbp", [128, CW]),
    }
    psc = pt("psc", [128, 32])
    pct = pt("pct", [32, 128])
    pxt = es.enter_context(nc.psum_tensor(name + "_pxt", [128, 128], BF16))
    eps_ap = kon[0:1, 0:1]
    P.step("vector", lambda e: e.memset(kon[:, 0:1], 1e-5))
    P.step("sync", lambda e: e.dma_start(out=sb["ones_col_bf"][:], in_=io.inp("c_ones_col_bf", [128, 1], BF16)), dma=True, wait=(0, 0))
    P.step("sync", lambda e: e.dma_start(out=sb["ones_row_f"][:], in_=io.inp("c_ones_row_f", [1, 128], F32)), dma=True, wait=(0, 0))
    P.step("sync", lambda e: e.dma_start(out=iota32[:], in_=io.inp("c_iota32_bc", [128, 32], F32)), dma=True, wait=(0, 0))
    P.step("sync", lambda e: e.dma_start(out=identf[:], in_=io.inp("c_ident_f", [128, 128], F32)), dma=True, wait=(0, 0))
    P.step("sync", lambda e: e.dma_start(out=identbb[:], in_=io.inp("c_ident_bf", [128, 128], BF16)), dma=True, wait=(0, 0))
    P.step("sync", lambda e: e.dma_start(out=ebrow[:], in_=io.inp("w_e_bias", [1, 32], F32)), dma=True, wait=(0, 0))
    P.step("gpsimd", lambda e: e.dma_start(out=gw[:], in_=io.inp("w_gate_wT", [D, 32], F32).rearrange("(k p) m -> p k m", p=128)), dma=True, wait=(0, 0))
    P.step("tensor", lambda e: e.matmul(psc[:], sb["ones_row_f"][:], ebrow[:], start=True, stop=True))
    P.step("vector", lambda e: e.tensor_copy(ebc[:], psc[:]))
    s_ld = P.xsem("ld"); s_st = P.xsem("st")
    st_gate = {}
    for c in range(NCH):
        buf = c % 2
        P.step("sync", lambda e, c=c, buf=buf: e.dma_start(out=x_f[buf][:], in_=slabs_ap(x1T, c * CW, CW)),
               dma=True, wait=(0, 0), inc=s_ld, waits=[(s_st, st_gate.get(c - 2, 0))])
        P.step("sync", lambda e: e.nop(), waits=[(s_ld, s_ld.n)])
        ln_stats_steps(P, sb, ps, x_f[buf], CW, eps_ap, name)
        def ap_fn(e, buf=buf):
            for s in range(NSLAB):
                e.tensor_tensor(tmp_f[:], x_f[buf][:, s, :], ps[name + "pbr"][:], ALU.mult)
                e.drain()
                ins = e.tensor_tensor(xn_sb[buf][:, s, :], tmp_f[:], ps[name + "pbp"][:], ALU.subtract)
                e.drain()
            return ins
        P.step("vector", ap_fn)
        for t4 in range(NT4):
            def mmsc(e, buf=buf, t4=t4):
                for k in range(NSLAB):
                    ins = e.matmul(psc[:], xn_sb[buf][:, k, t4 * 128:(t4 + 1) * 128],
                                   gw[:, k, :], start=(k == 0), stop=(k == NSLAB - 1))
                return ins
            P.step("tensor", mmsc)
            P.step("scalar", lambda e: e.activation(out=sc_f[:], in_=psc[:], func=AF.Sigmoid))
            def topk_fn(e):
                e.tensor_tensor(bi_f[:], sc_f[:], ebc[:], ALU.add)
                e.drain()
                e.max(vals[:], bi_f[:])
                e.drain()
                return e.max_index(idxu[:], vals[:], bi_f[:])
            P.step("vector", topk_fn)
            def wsel_fn(e):
                e.tensor_copy(idxf[:], idxu[:])
                e.drain()
                for kk in range(4):
                    e.tensor_scalar(oh[:, kk, :], iota32[:], idxf[:, kk:kk + 1], None, ALU.is_equal)
                e.drain()
                for kk in range(4):
                    e.tensor_tensor(oh[:, kk, :], oh[:, kk, :], sc_f[:], ALU.mult)
                e.drain()
                og3 = og[:].rearrange("p (k o) -> p k o", k=8)
                e.tensor_reduce(og3[:, 0:4, :], oh[:], mybir.AxisListType.X, ALU.add)
                e.drain()
                e.tensor_reduce(og[:, 4:5], og[:, 0:4], mybir.AxisListType.X, ALU.add)
                e.drain()
                return e.reciprocal(og[:, 5:6], og[:, 4:5])
            P.step("vector", wsel_fn)
            def comb_fn(e):
                e.memset(comb_tok[:], 0.0)
                e.drain()
                for kk in range(4):
                    e.scalar_tensor_tensor(comb_tok[:], oh[:, kk, :], og[0:128, 5:6], comb_tok[:],
                                           ALU.mult, ALU.add)
                    e.drain()
                e.tensor_scalar(wk4[:], og[:, 0:4], og[:, 5:6], None, ALU.mult)
                return e.nop()
            P.step("vector", comb_fn)
            P.step("tensor", lambda e, t4=t4: e.transpose(pct[:], comb_tok[:], identf[:]))
            P.step("vector", lambda e, buf=buf, t4=t4: e.tensor_copy(combt_sb[buf][:, t4 * 128:(t4 + 1) * 128], pct[:]))
            P.step("gpsimd", lambda e, c=c, t4=t4: e.dma_start(
                out=selT[c * CW + t4 * 128:c * CW + (t4 + 1) * 128, :], in_=idxf[:, 0:4]), dma=True)
            P.step("gpsimd", lambda e, c=c, t4=t4: e.dma_start(
                out=wgtT[c * CW + t4 * 128:c * CW + (t4 + 1) * 128, :], in_=wk4[:]), dma=True)
            for s in range(NSLAB):
                P.step("tensor", lambda e, buf=buf, t4=t4, s=s: e.transpose(
                    pxt[:], xn_sb[buf][:, s, t4 * 128:(t4 + 1) * 128], identbb[:]))
                P.step("vector", lambda e, s=s: e.tensor_copy(xtk[:, s * 128:(s + 1) * 128], pxt[:]))
            P.step("gpsimd", lambda e, c=c, t4=t4: e.dma_start(
                out=xtokT[c * CW + t4 * 128:c * CW + (t4 + 1) * 128, :], in_=xtk[:]), dma=True)
        P.step("gpsimd", lambda e, c=c, buf=buf: e.dma_start(
            out=slabs_ap(xnT, c * CW, CW), in_=xn_sb[buf][:]), dma=True, inc=s_st)
        P.step("gpsimd", lambda e, c=c, buf=buf: e.dma_start(
            out=combT.rearrange("e t -> e t")[:, c * CW:(c + 1) * CW], in_=combt_sb[buf][:]), dma=True, inc=s_st)
        st_gate[c] = s_st.n
    P.emit()


def build_p2b(nc, io, es, e0, yin, yout_name, final=False, name=None):
    name = name or f"p2e{e0}"
    xnT = io.stream("xn2T", [D, T], BF16, produce=False)
    combT = io.stream("combT", [32, T], F32, produce=False)
    yaccI = io.inp(yin, [D, T], BF16) if yin == "yacc0" else io.stream(yin, [D, T], BF16, produce=False)
    if final:
        x1T = io.stream("x1T", [D, T], F32, produce=False)
        x2T = io.stream("x2T", [D, T], F32, produce=True)
    else:
        yaccO = io.stream(yout_name, [D, T], BF16, produce=True)
    CW = 512
    NHALF = 4
    TH = T // NHALF
    NCH_H = TH // CW
    NI = 6
    P = Pipe(nc, es, name)
    t = lambda n, sh, dt: es.enter_context(nc.sbuf_tensor(name + "_" + n, sh, dt))
    pt = lambda n, sh: es.enter_context(nc.psum_tensor(name + "_" + n, sh, F32))
    w1s = [t(f"w1_{i}", [128, NSLAB, 768], BF16) for i in range(2)]
    w3s = [t(f"w3_{i}", [128, NSLAB, 768], BF16) for i in range(2)]
    w2s = [t(f"w2_{i}", [128, NI, D], BF16) for i in range(2)]
    xn_h = t("xnh", [128, NSLAB, TH], BF16)
    ya_h = t("yah", [128, NSLAB, TH], BF16)
    crow = t("crow", [1, TH], F32)
    bc_sb = t("bc", [128, CW], F32)
    sg_bf = t("sg", [128, CW], BF16)
    a_bf = t("abf", [128, NI, CW], BF16)
    x_f = t("xf", [128, NSLAB, CW], F32)
    o_f = t("of", [128, NSLAB, CW], F32)
    pbc = pt("pbc", [128, CW])
    pg = pt("pg", [128, CW])
    pu = pt("pu", [128, CW])
    py = pt("py", [128, CW])
    onesr = t("onesr", [1, 128], F32)
    P.step("sync", lambda e: e.dma_start(out=onesr[:], in_=io.inp("c_ones_row_f", [1, 128], F32)), dma=True, wait=(0, 0))
    w1ap = io.inp("w_moe_w1", [32, D, 768], F32)
    w3ap = io.inp("w_moe_w3", [32, D, 768], F32)
    w2ap = io.inp("w_moe_w2", [32, 768, D], F32)
    for h in range(NHALF):
        P.step("sync", lambda e, h=h: e.dma_start(out=xn_h[:], in_=slabs_ap(xnT, h * TH, TH)), dma=True)
        P.step("sync", lambda e, h=h: e.dma_start(out=ya_h[:], in_=slabs_ap(yaccI, h * TH, TH)), dma=True)
        for ei in range(8):
            eg = e0 + ei
            wb = ei % 2
            P.step("gpsimd", lambda e, eg=eg, wb=wb: e.dma_start(
                out=w1s[wb][:], in_=w1ap[eg].rearrange("(k p) m -> p k m", p=128)), dma=True)
            P.step("gpsimd", lambda e, eg=eg, wb=wb: e.dma_start(
                out=w3s[wb][:], in_=w3ap[eg].rearrange("(k p) m -> p k m", p=128)), dma=True)
            P.step("gpsimd", lambda e, eg=eg, wb=wb: e.dma_start(
                out=w2s[wb][:], in_=w2ap[eg].rearrange("(k p) m -> p k m", p=128)), dma=True)
            P.step("sync", lambda e, eg=eg, h=h: e.dma_start(
                out=crow[:], in_=combT[eg:eg + 1, h * TH:(h + 1) * TH]), dma=True)
            for c in range(NCH_H):
                def mmbc(e, c=c):
                    return e.matmul(pbc[:], onesr[:], crow[:, c * CW:(c + 1) * CW], start=True, stop=True)
                P.step("tensor", mmbc)
                P.step("vector", lambda e: e.tensor_copy(bc_sb[:], pbc[:]))
                for i in range(NI):
                    def mmg(e, c=c, i=i, wb=wb):
                        for k in range(NSLAB):
                            ins = e.matmul(pg[:], w1s[wb][:, k, i * 128:(i + 1) * 128],
                                           xn_h[:, k, c * CW:(c + 1) * CW], start=(k == 0), stop=(k == NSLAB - 1))
                        return ins
                    P.step("tensor", mmg)
                    P.step("scalar", lambda e: e.activation(out=sg_bf[:], in_=pg[:], func=AF.Silu))
                    def mmu(e, c=c, i=i, wb=wb):
                        for k in range(NSLAB):
                            ins = e.matmul(pu[:], w3s[wb][:, k, i * 128:(i + 1) * 128],
                                           xn_h[:, k, c * CW:(c + 1) * CW], start=(k == 0), stop=(k == NSLAB - 1))
                        return ins
                    P.step("tensor", mmu)
                    def a_fn(e, i=i):
                        e.tensor_tensor(a_bf[:, i, :], sg_bf[:], pu[:], ALU.mult)
                        e.drain()
                        return e.tensor_tensor(a_bf[:, i, :], a_bf[:, i, :], bc_sb[:], ALU.mult)
                    P.step("vector", a_fn)
                for s in range(NSLAB):
                    def mmy(e, s=s, wb=wb):
                        for k in range(NI):
                            ins = e.matmul(py[:], w2s[wb][:, k, s * 128:(s + 1) * 128],
                                           a_bf[:, k, :], start=(k == 0), stop=(k == NI - 1))
                        return ins
                    P.step("tensor", mmy)
                    def acc_fn(e, s=s, c=c):
                        sl = ya_h[:, s, c * CW:(c + 1) * CW]
                        return e.tensor_tensor(sl, sl, py[:], ALU.add)
                    P.step("vector", acc_fn)
        if not final:
            P.step("gpsimd", lambda e, h=h: e.dma_start(out=slabs_ap(yaccO, h * TH, TH), in_=ya_h[:]), dma=True)
        else:
            for c in range(NCH_H):
                P.step("sync", lambda e, h=h, c=c: e.dma_start(out=x_f[:], in_=slabs_ap(x1T, h * TH + c * CW, CW)), dma=True)
                def res_fn(e, c=c):
                    for s in range(NSLAB):
                        ins = e.tensor_tensor(o_f[:, s, :], x_f[:, s, :], ya_h[:, s, c * CW:(c + 1) * CW], ALU.add)
                    return ins
                P.step("vector", res_fn)
                P.step("gpsimd", lambda e, h=h, c=c: e.dma_start(out=slabs_ap(x2T, h * TH + c * CW, CW), in_=o_f[:]), dma=True)
    P.emit()


def build_p3a(nc, io, es, name="p3a"):
    x2T = io.stream("x2T", [D, T], F32, produce=False)
    xnT = io.stream("xn3T", [D, T], BF16, produce=True)
    qxT = io.stream("qxT", [D, T], BF16, produce=True)
    kxT = io.stream("kxT", [D, T], BF16, produce=True)
    vtokT = io.stream("vtokT", [T, 2048], BF16, produce=True)
    CW = 256
    NCH = T // CW
    P = Pipe(nc, es, name)
    t = lambda n, sh, dt: es.enter_context(nc.sbuf_tensor(name + "_" + n, sh, dt))
    pt = lambda n, sh, dt=F32: es.enter_context(nc.psum_tensor(name + "_" + n, sh, dt))
    wqkv = t("wqkv", [128, NSLAB, 4096], BF16)
    rotA = t("rotA", [128, 128], BF16)
    x_f = [t(f"x{i}", [128, NSLAB, CW], F32) for i in range(2)]
    xn_sb = [t(f"xn{i}", [128, NSLAB, CW], BF16) for i in range(2)]
    tq = [t("tq0", [128, NSLAB, 2, CW], BF16)] * 2
    tk = [t("tk0", [128, NSLAB, 2, CW], BF16)] * 2
    q_bf = t("qbf", [128, CW], BF16)
    t1_b = t("t1b", [128, CW], BF16)
    t2_b = t("t2b", [128, CW], BF16)
    qx_sb = [t(f"qx{i}", [128, NSLAB, CW], BF16) for i in range(2)]
    kx_sb = [t(f"kx{i}", [128, NSLAB, CW], BF16) for i in range(2)]
    v_bf = t("vbf", [128, 512], BF16)
    tmp_f = t("tmp", [128, CW], F32)
    kon = t("kon", [128, 1], F32)
    sb = {
        "ones_col_bf": t("ones", [128, 1], BF16),
        "ones_row_f": t("onesr", [1, 128], F32),
        name + "xbf": t("xbf", [128, NSLAB, CW], BF16),
        name + "sqbf": t("sqbf", [128, NSLAB, CW], BF16),
        name + "stat": t("stat", [1, 6 * CW], F32),
    }
    ps = {
        name + "psx": pt("psx", [1, CW]),
        name + "psq": pt("psq", [1, CW]),
        name + "pbr": pt("pbr", [128, CW]),
        name + "pbp": pt("pbp", [128, CW]),
    }
    pq = pt("pq", [128, CW])
    pr = pt("pr", [128, CW])
    pv = pt("pv", [128, 512])
    eps_ap = kon[0:1, 0:1]
    P.step("vector", lambda e: e.memset(kon[:, 0:1], 1e-5))
    P.step("sync", lambda e: e.dma_start(out=sb["ones_col_bf"][:], in_=io.inp("c_ones_col_bf", [128, 1], BF16)), dma=True, wait=(0, 0))
    P.step("sync", lambda e: e.dma_start(out=sb["ones_row_f"][:], in_=io.inp("c_ones_row_f", [1, 128], F32)), dma=True, wait=(0, 0))
    P.step("sync", lambda e: e.dma_start(out=rotA[:], in_=io.inp("c_rot_lhsT", [128, 128], BF16)), dma=True, wait=(0, 0))
    P.step("gpsimd", lambda e: e.dma_start(out=wqkv[:], in_=io.inp("w_qkvT", [D, 4096], F32).rearrange("(k p) m -> p k m", p=128)), dma=True, wait=(0, 0))
    tqc = io.inp("c_tqc", [D, S], BF16)
    tqs = io.inp("c_tqs", [D, S], BF16)
    tkc = io.inp("c_tkc", [D, S], BF16)
    tks = io.inp("c_tks", [D, S], BF16)
    s_ld = P.xsem("ld"); s_st = P.xsem("st")
    st_gate = {}
    for c in range(NCH):
        buf = c % 2
        loc = (c * CW) % S
        xw = [(s_st, st_gate.get(c - 2, 0))]
        P.step("sync", lambda e, c=c, buf=buf: e.dma_start(out=x_f[buf][:], in_=slabs_ap(x2T, c * CW, CW)), dma=True, wait=(0, 0), inc=s_ld, waits=xw)
        def tbl(dst, src, loc):
            return lambda e: e.dma_start(out=dst[:], in_=src.rearrange("(s p) t -> p s t", p=128)[:, :, loc:loc + CW])
        P.step("sync", lambda e: e.nop(), waits=[(s_ld, s_ld.n)])
        P.step("sync", lambda e, buf=buf, loc=loc: e.dma_start(out=tq[buf][:, :, 0, :], in_=tqc.rearrange("(s p) t -> p s t", p=128)[:, :, loc:loc + CW]), dma=True)
        P.step("sync", lambda e, buf=buf, loc=loc: e.dma_start(out=tq[buf][:, :, 1, :], in_=tqs.rearrange("(s p) t -> p s t", p=128)[:, :, loc:loc + CW]), dma=True)
        P.step("sync", lambda e, buf=buf, loc=loc: e.dma_start(out=tk[buf][:, :, 0, :], in_=tkc.rearrange("(s p) t -> p s t", p=128)[:, :, loc:loc + CW]), dma=True)
        P.step("sync", lambda e, buf=buf, loc=loc: e.dma_start(out=tk[buf][:, :, 1, :], in_=tks.rearrange("(s p) t -> p s t", p=128)[:, :, loc:loc + CW]), dma=True)
        ln_stats_steps(P, sb, ps, x_f[buf], CW, eps_ap, name)
        def ap_fn(e, buf=buf):
            for s in range(NSLAB):
                e.tensor_tensor(tmp_f[:], x_f[buf][:, s, :], ps[name + "pbr"][:], ALU.mult)
                e.drain()
                ins = e.tensor_tensor(xn_sb[buf][:, s, :], tmp_f[:], ps[name + "pbp"][:], ALU.subtract)
                e.drain()
            return ins
        P.step("vector", ap_fn)
        for which, off, tbl_sb, dst in (("q", 0, tq, qx_sb), ("k", D, tk, kx_sb)):
            for s in range(NSLAB):
                def mmq(e, buf=buf, s=s, off=off):
                    for k in range(NSLAB):
                        ins = e.matmul(pq[:], wqkv[:, k, off + s * 128:off + (s + 1) * 128],
                                       xn_sb[buf][:, k, :], start=(k == 0), stop=(k == NSLAB - 1))
                    return ins
                P.step("tensor", mmq)
                P.step("vector", lambda e: e.tensor_copy(q_bf[:], pq[:]))
                P.step("tensor", lambda e: e.matmul(pr[:], rotA[:], q_bf[:], start=True, stop=True))
                def xpos_fn(e, buf=buf, s=s, tbl_sb=tbl_sb, dst=dst):
                    e.tensor_tensor(t1_b[:], q_bf[:], tbl_sb[buf][:, s, 0, :], ALU.mult)
                    e.tensor_tensor(t2_b[:], pr[:], tbl_sb[buf][:, s, 1, :], ALU.mult)
                    e.drain()
                    return e.tensor_tensor(dst[buf][:, s, :], t1_b[:], t2_b[:], ALU.add)
                P.step("vector", xpos_fn)
        for t4 in range(CW // 128):
            for n in range(4):
                def mmv(e, buf=buf, t4=t4, n=n):
                    for k in range(NSLAB):
                        ins = e.matmul(pv[:], xn_sb[buf][:, k, t4 * 128:(t4 + 1) * 128],
                                       wqkv[:, k, 2048 + n * 512:2048 + (n + 1) * 512],
                                       start=(k == 0), stop=(k == NSLAB - 1))
                    return ins
                P.step("tensor", mmv)
                P.step("vector", lambda e: e.tensor_copy(v_bf[:], pv[:]))
                P.step("gpsimd", lambda e, c=c, t4=t4, n=n: e.dma_start(
                    out=vtokT[c * CW + t4 * 128:c * CW + (t4 + 1) * 128, n * 512:(n + 1) * 512],
                    in_=v_bf[:]), dma=True)
        P.step("gpsimd", lambda e, c=c, buf=buf: e.dma_start(out=slabs_ap(xnT, c * CW, CW), in_=xn_sb[buf][:]), dma=True, inc=s_st)
        P.step("gpsimd", lambda e, c=c, buf=buf: e.dma_start(out=slabs_ap(qxT, c * CW, CW), in_=qx_sb[buf][:]), dma=True, inc=s_st)
        P.step("gpsimd", lambda e, c=c, buf=buf: e.dma_start(out=slabs_ap(kxT, c * CW, CW), in_=kx_sb[buf][:]), dma=True, inc=s_st)
        st_gate[c] = s_st.n
    P.emit()


def build_p3b(nc, io, es, name="p3b"):
    x2T = io.stream("x2T", [D, T], F32, produce=False)
    xnT = io.stream("xn3T", [D, T], BF16, produce=False)
    qxT = io.stream("qxT", [D, T], BF16, produce=False)
    kxT = io.stream("kxT", [D, T], BF16, produce=False)
    vtokT = io.stream("vtokT", [T, 2048], BF16, produce=False)
    x3T = io.stream("x3T", [D, T], F32, produce=True)
    NH = 8
    HD = 128
    VD = 256
    CC = 128
    NCH_B = S // CC
    P = Pipe(nc, es, name)
    t = lambda n, sh, dt: es.enter_context(nc.sbuf_tensor(name + "_" + n, sh, dt))
    pt = lambda n, sh, dt=F32: es.enter_context(nc.psum_tensor(name + "_" + n, sh, dt))
    wgat = t("wgat", [128, NSLAB, 2048], BF16)
    wproj = t("wproj", [128, 16, D], BF16)
    identb = t("identb", [128, 128], BF16)
    expoT_s = t("expo", [128, 128], F32)
    cmask_s = t("cmask", [128, 128], F32)
    iotaJ_s = t("iotaJ", [128, 128], F32)
    revcol_s = t("revc", [128, 1], F32)
    k128_s = t("k128", [128, 1], F32)
    decrow = t("decrow", [1, 8], F32)
    lnd_bc = t("lnd", [128, 8], F32)
    dmT8 = t("dmT8", [128, NH, 128], BF16)
    dbc8 = t("dbc8", [128, NH, 128], BF16)
    wexp = t("wexp", [128, NH], F32)
    d128 = t("d128", [128, NH], F32)
    qx_b = t("qxb", [128, NSLAB, S], BF16)
    kx_b = t("kxb", [128, NSLAB, S], BF16)
    xn_c = t("xnc", [128, NSLAB, CC], BF16)
    v_tok = t("vtok", [128, 2048], BF16)
    gate_bf = t("gatebf", [128, 2048], BF16)
    gated_bf = t("gatedbf", [128, 2048], BF16)
    sc_bf = [t(f"scbf{i}", [128, 128], BF16) for i in range(2)]
    qdec_bf = [t(f"qdec{i}", [128, 128], BF16) for i in range(2)]
    k_tok = t("ktok", [128, 128], BF16)
    vw_bf = t("vwbf", [128, VD], BF16)
    o_f = t("of", [128, VD], F32)
    n_f = t("nf", [128, VD], F32)
    red = t("red", [128, 6], F32)
    S_sb = t("Ssb", [128, NH, VD], F32)
    S_bf = t("Sbf", [128, NH, VD], BF16)
    gT4 = t("gT4", [128, 16, 512], BF16)
    x_f = t("xf", [128, NSLAB, 512], F32)
    y_f = x_f
    onesr = t("onesr", [1, 128], F32)
    psc = [pt(f"psc{i}", [128, 128]) for i in range(2)]
    ppo = [pt(f"ppo{i}", [128, VD]) for i in range(2)]
    pS = pt("pS", [128, VD])
    pg = pt("pgg", [128, 512])
    pt_b = pt("ptb", [128, 128], BF16)
    py = pt("pyy", [128, 512])
    pbc8 = None
    P.step("sync", lambda e: e.dma_start(out=onesr[:], in_=io.inp("c_ones_row_f", [1, 128], F32)), dma=True, wait=(0, 0))
    P.step("sync", lambda e: e.dma_start(out=identb[:], in_=io.inp("c_ident_bf", [128, 128], BF16)), dma=True, wait=(0, 0))
    P.step("sync", lambda e: e.dma_start(out=expoT_s[:], in_=io.inp("c_expoT", [128, 128], F32)), dma=True, wait=(0, 0))
    P.step("sync", lambda e: e.dma_start(out=cmask_s[:], in_=io.inp("c_cmask_scale", [128, 128], F32)), dma=True, wait=(0, 0))
    P.step("sync", lambda e: e.dma_start(out=iotaJ_s[:], in_=io.inp("c_iotaJ", [128, 128], F32)), dma=True, wait=(0, 0))
    P.step("sync", lambda e: e.dma_start(out=revcol_s[:], in_=io.inp("c_rev_col", [128, 1], F32)), dma=True, wait=(0, 0))
    P.step("sync", lambda e: e.dma_start(out=decrow[:], in_=io.inp("w_decay", [1, 8], F32)), dma=True, wait=(0, 0))
    P.step("gpsimd", lambda e: e.dma_start(out=wgat[:], in_=io.inp("w_gatedT", [D, 2048], F32).rearrange("(k p) m -> p k m", p=128)), dma=True, wait=(0, 0))
    P.step("gpsimd", lambda e: e.dma_start(out=wproj[:], in_=io.inp("w_projT", [2048, D], F32).rearrange("(k p) m -> p k m", p=128)), dma=True, wait=(0, 0))
    P.step("vector", lambda e: e.memset(k128_s[:], 128.0))
    P.step("vector", lambda e: e.memset(red[:, 5:6], 1e-5))
    P.step("scalar", lambda e: e.activation(out=decrow[:], in_=decrow[:], func=AF.Ln))
    P.step("tensor", lambda e: e.matmul(psc[0][:, 0:8], onesr[:], decrow[:], start=True, stop=True))
    P.step("vector", lambda e: e.tensor_copy(lnd_bc[:], psc[0][:, 0:8]))
    for h in range(NH):
        def tab_fn(e, h=h):
            e.activation(out=dmT8[:, h, :], in_=expoT_s[:], func=AF.Exp, scale=lnd_bc[:, h:h + 1])
            e.activation(out=dbc8[:, h, :], in_=iotaJ_s[:], func=AF.Exp, scale=lnd_bc[:, h:h + 1])
            e.activation(out=wexp[:, h:h + 1], in_=revcol_s[:], func=AF.Exp, scale=lnd_bc[:, h:h + 1])
            return e.activation(out=d128[:, h:h + 1], in_=k128_s[:], func=AF.Exp, scale=lnd_bc[:, h:h + 1])
        P.step("scalar", tab_fn)
        def tab2_fn(e, h=h):
            e.tensor_tensor(dmT8[:, h, :], dmT8[:, h, :], cmask_s[:], ALU.mult)
            e.drain()
            return e.tensor_scalar_mul(dbc8[:, h, :], dbc8[:, h, :], 1.0 / math.sqrt(128.0))
        P.step("vector", tab2_fn)
    s_mm = P.xsem("mm")

    def pe(fn, waits=()):
        P.step("tensor", fn, inc=s_mm, waits=waits)
        return s_mm.n
    for b in range(B):
        P.step("sync", lambda e, b=b: e.dma_start(out=qx_b[:], in_=slabs_ap(qxT, b * S, S)), dma=True, waits=[(s_mm, s_mm.n)])
        P.step("sync", lambda e, b=b: e.dma_start(out=kx_b[:], in_=slabs_ap(kxT, b * S, S)), dma=True)
        P.step("vector", lambda e: e.memset(S_sb[:], 0.0), waits=[(s_mm, s_mm.n)])
        P.step("vector", lambda e: e.memset(S_bf[:], 0.0))
        for cc in range(NCH_B):
            t0 = b * S + cc * CC
            lo = cc * CC
            P.step("sync", lambda e, t0=t0: e.dma_start(out=v_tok[:], in_=vtokT[t0:t0 + CC, :]), dma=True, waits=[(s_mm, s_mm.n)])
            P.step("sync", lambda e, t0=t0: e.dma_start(out=xn_c[:], in_=slabs_ap(xnT, t0, CC)), dma=True)
            for n in range(4):
                def mmgate(e, n=n):
                    for k in range(NSLAB):
                        ins = e.matmul(pg[:], xn_c[:, k, :], wgat[:, k, n * 512:(n + 1) * 512],
                                       start=(k == 0), stop=(k == NSLAB - 1))
                    return ins
                gn = pe(mmgate)
                P.step("scalar", lambda e, n=n: e.activation(out=gate_bf[:, n * 512:(n + 1) * 512], in_=pg[:], func=AF.Silu), waits=[(s_mm, gn)])
            po_n = {}

            def F(h, lo=lo):
                p = h % 2
                def mmsc(e, h=h, lo=lo, p=p):
                    return e.matmul(psc[p][:], kx_b[:, h, lo:lo + CC], qx_b[:, h, lo:lo + CC], start=True, stop=True)
                scn = pe(mmsc)
                P.step("vector", lambda e, h=h, p=p: e.tensor_tensor(sc_bf[p][:], psc[p][:], dmT8[:, h, :], ALU.mult), waits=[(s_mm, scn)])
                P.step("vector", lambda e, h=h, lo=lo, p=p: e.tensor_tensor(qdec_bf[p][:], qx_b[:, h, lo:lo + CC], dbc8[:, h, :], ALU.mult))
                def mmpo(e, h=h, p=p):
                    e.matmul(ppo[p][:], sc_bf[p][:], v_tok[:, h * VD:(h + 1) * VD], start=True, stop=False)
                    return e.matmul(ppo[p][:], qdec_bf[p][:], S_bf[:, h, :], start=False, stop=True)
                po_n[h] = pe(mmpo)

            def Bk(h, lo=lo):
                p = h % 2
                def ln_fn(e, h=h, p=p):
                    e.tensor_copy(o_f[:], ppo[p][:])
                    e.drain()
                    e.tensor_reduce(red[:, 0:1], o_f[:], mybir.AxisListType.X, ALU.add)
                    e.drain()
                    return e.tensor_scalar_mul(red[:, 0:1], red[:, 0:1], 1.0 / VD)
                P.step("vector", ln_fn, waits=[(s_mm, po_n[h])])
                P.step("scalar", lambda e: e.activation(out=n_f[:], in_=o_f[:], func=AF.Square, accum_out=red[:, 1:2]))
                def ln2_fn(e, h=h):
                    e.tensor_scalar_mul(red[:, 1:2], red[:, 1:2], 1.0 / VD)
                    e.tensor_tensor(red[:, 2:3], red[:, 0:1], red[:, 0:1], ALU.mult)
                    e.drain()
                    return e.tensor_tensor(red[:, 1:2], red[:, 1:2], red[:, 2:3], ALU.subtract)
                P.step("vector", ln2_fn)
                def rstd_fn(e):
                    e.activation(out=red[:, 1:2], in_=red[:, 1:2], func=AF.Ln, bias=red[:, 5:6])
                    e.drain()
                    return e.activation(out=red[:, 1:2], in_=red[:, 1:2], func=AF.Exp, scale=-0.5)
                P.step("scalar", rstd_fn)
                def gated_fn(e, h=h):
                    e.tensor_tensor(red[:, 3:4], red[:, 0:1], red[:, 1:2], ALU.mult)
                    e.drain()
                    e.tensor_scalar(n_f[:], o_f[:], red[:, 1:2], red[:, 3:4], ALU.mult, ALU.subtract)
                    e.drain()
                    return e.tensor_tensor(gated_bf[:, h * VD:(h + 1) * VD], n_f[:], gate_bf[:, h * VD:(h + 1) * VD], ALU.mult)
                P.step("vector", gated_fn)
                tn = pe(lambda e, h=h, lo=lo: e.transpose(pt_b[:], kx_b[:, h, lo:lo + CC], identb[:]))
                P.step("vector", lambda e: e.tensor_copy(k_tok[:], pt_b[:]), waits=[(s_mm, tn)])
                P.step("vector", lambda e, h=h: e.tensor_scalar(vw_bf[:], v_tok[:, h * VD:(h + 1) * VD], wexp[:, h:h + 1], None, ALU.mult))
                sn = pe(lambda e: e.matmul(pS[:], k_tok[:], vw_bf[:], start=True, stop=True))
                def supd_fn(e, h=h):
                    e.scalar_tensor_tensor(S_sb[:, h, :], S_sb[:, h, :], d128[:, h:h + 1], pS[:],
                                           ALU.mult, ALU.add)
                    e.drain()
                    return e.tensor_copy(S_bf[:, h, :], S_sb[:, h, :])
                P.step("vector", supd_fn, waits=[(s_mm, sn)])
            F(0)
            for h in range(NH):
                if h + 1 < NH:
                    F(h + 1)
                Bk(h)
            for i in range(16):
                tn2 = pe(lambda e, i=i: e.transpose(pt_b[:], gated_bf[:, i * 128:(i + 1) * 128], identb[:]))
                P.step("vector", lambda e, i=i, cc=cc: e.tensor_copy(gT4[:, i, (cc % 4) * 128:(cc % 4 + 1) * 128], pt_b[:]), waits=[(s_mm, tn2)])
            if cc % 4 == 3:
                c0 = t0 + CC - 512
                P.step("sync", lambda e, c0=c0: e.dma_start(out=x_f[:], in_=slabs_ap(x2T, c0, 512)), dma=True)
                for m in range(NSLAB):
                    def mmproj(e, m=m):
                        for k in range(16):
                            ins = e.matmul(py[:], wproj[:, k, m * 128:(m + 1) * 128], gT4[:, k, :],
                                           start=(k == 0), stop=(k == 15))
                        return ins
                    pn = pe(mmproj)
                    P.step("vector", lambda e, m=m: e.tensor_tensor(x_f[:, m, :], x_f[:, m, :], py[:], ALU.add), waits=[(s_mm, pn)])
                P.step("gpsimd", lambda e, c0=c0: e.dma_start(out=slabs_ap(x3T, c0, 512), in_=y_f[:]), dma=True)
    P.emit()


def build_p4a(nc, io, es, name="p4a"):
    x3T = io.stream("x3T", [D, T], F32, produce=False)
    xnT = io.stream("xn4T", [D, T], BF16, produce=True)
    combT = io.stream("comb4T", [8, T], F32, produce=True)
    latT = io.stream("latT", [512, T], BF16, produce=True)
    CW = 512
    NCH = T // CW
    NT4 = CW // 128
    P = Pipe(nc, es, name)
    t = lambda n, sh, dt: es.enter_context(nc.sbuf_tensor(name + "_" + n, sh, dt))
    pt = lambda n, sh: es.enter_context(nc.psum_tensor(name + "_" + n, sh, F32))
    gw = t("gw", [128, NSLAB, 8], BF16)
    fc1 = t("fc1", [128, NSLAB, 512], BF16)
    cbrow = t("cbrow", [1, 8], F32)
    cbc = t("cbc", [128, 8], F32)
    iota8 = t("iota8", [128, 8], F32)
    identf = t("identf", [128, 128], F32)
    x_f = [t(f"x{i}", [128, NSLAB, CW], F32) for i in range(2)]
    xn_sb = [t(f"xn{i}", [128, NSLAB, CW], BF16) for i in range(2)]
    lat_sb = t("lat", [128, 4, CW], BF16)
    sc_f = t("sc", [128, 8], F32)
    sfc = t("sfc", [128, 8], F32)
    pad8 = t("pad8", [128, 16], F32)
    vals = t("vals", [128, 8], F32)
    idxu = t("idxu", [128, 8], U32)
    idxf = t("idxf", [128, 8], F32)
    oh = t("oh", [128, 2, 8], F32)
    og = t("og", [128, 8], F32)
    gs = t("gs", [128, 4], F32)
    comb_tok = t("ctok", [128, 8], F32)
    combt_sb = [t(f"cT{i}", [8, CW], F32) for i in range(2)]
    tmp_f = t("tmp", [128, CW], F32)
    kon = t("kon", [128, 1], F32)
    sb = {
        "ones_col_bf": t("ones", [128, 1], BF16),
        "ones_row_f": t("onesr", [1, 128], F32),
        name + "xbf": t("xbf", [128, NSLAB, CW], BF16),
        name + "sqbf": t("sqbf", [128, NSLAB, CW], BF16),
        name + "stat": t("stat", [1, 6 * CW], F32),
    }
    ps = {
        name + "psx": pt("psx", [1, CW]),
        name + "psq": pt("psq", [1, CW]),
        name + "pbr": pt("pbr", [128, CW]),
        name + "pbp": pt("pbp", [128, CW]),
    }
    psc = pt("psc", [128, 8])
    pct = pt("pct", [8, 128])
    plat = pt("plat", [128, CW])
    eps_ap = kon[0:1, 0:1]
    P.step("vector", lambda e: e.memset(kon[:, 0:1], 1e-5))
    P.step("sync", lambda e: e.dma_start(out=sb["ones_col_bf"][:], in_=io.inp("c_ones_col_bf", [128, 1], BF16)), dma=True, wait=(0, 0))
    P.step("sync", lambda e: e.dma_start(out=sb["ones_row_f"][:], in_=io.inp("c_ones_row_f", [1, 128], F32)), dma=True, wait=(0, 0))
    P.step("sync", lambda e: e.dma_start(out=iota8[:], in_=io.inp("c_iota8_bc", [128, 8], F32)), dma=True, wait=(0, 0))
    P.step("sync", lambda e: e.dma_start(out=identf[:], in_=io.inp("c_ident_f", [128, 128], F32)), dma=True, wait=(0, 0))
    P.step("sync", lambda e: e.dma_start(out=cbrow[:], in_=io.inp("w_corr_bias", [1, 8], F32)), dma=True, wait=(0, 0))
    P.step("gpsimd", lambda e: e.dma_start(out=gw[:], in_=io.inp("w_f_gateT", [D, 8], F32).rearrange("(k p) m -> p k m", p=128)), dma=True, wait=(0, 0))
    P.step("gpsimd", lambda e: e.dma_start(out=fc1[:], in_=io.inp("w_fc1T", [D, 512], F32).rearrange("(k p) m -> p k m", p=128)), dma=True, wait=(0, 0))
    P.step("tensor", lambda e: e.matmul(psc[:], sb["ones_row_f"][:], cbrow[:], start=True, stop=True))
    P.step("vector", lambda e: e.tensor_copy(cbc[:], psc[:]))
    s_ld = P.xsem("ld"); s_st = P.xsem("st")
    st_gate = {}
    for c in range(NCH):
        buf = c % 2
        P.step("sync", lambda e, c=c, buf=buf: e.dma_start(out=x_f[buf][:], in_=slabs_ap(x3T, c * CW, CW)),
               dma=True, wait=(0, 0), inc=s_ld, waits=[(s_st, st_gate.get(c - 2, 0))])
        P.step("sync", lambda e: e.nop(), waits=[(s_ld, s_ld.n)])
        ln_stats_steps(P, sb, ps, x_f[buf], CW, eps_ap, name)
        def ap_fn(e, buf=buf):
            for s in range(NSLAB):
                e.tensor_tensor(tmp_f[:], x_f[buf][:, s, :], ps[name + "pbr"][:], ALU.mult)
                e.drain()
                ins = e.tensor_tensor(xn_sb[buf][:, s, :], tmp_f[:], ps[name + "pbp"][:], ALU.subtract)
                e.drain()
            return ins
        P.step("vector", ap_fn)
        for m in range(4):
            def mml(e, buf=buf, m=m):
                for k in range(NSLAB):
                    ins = e.matmul(plat[:], fc1[:, k, m * 128:(m + 1) * 128],
                                   xn_sb[buf][:, k, :], start=(k == 0), stop=(k == NSLAB - 1))
                return ins
            P.step("tensor", mml)
            P.step("vector", lambda e, m=m: e.tensor_copy(lat_sb[:, m, :], plat[:]))
        P.step("gpsimd", lambda e, c=c: e.dma_start(
            out=latT.rearrange("(s p) t -> p s t", p=128, s=4)[:, :, c * CW:(c + 1) * CW],
            in_=lat_sb[:]), dma=True)
        for t4 in range(NT4):
            def mmsc(e, buf=buf, t4=t4):
                for k in range(NSLAB):
                    ins = e.matmul(psc[:], xn_sb[buf][:, k, t4 * 128:(t4 + 1) * 128],
                                   gw[:, k, :], start=(k == 0), stop=(k == NSLAB - 1))
                return ins
            P.step("tensor", mmsc)
            P.step("scalar", lambda e: e.activation(out=sc_f[:], in_=psc[:], func=AF.Sigmoid))
            def grp_fn(e):
                e.tensor_tensor(sfc[:], sc_f[:], cbc[:], ALU.add)
                e.memset(pad8[:], -3.0e38)
                e.drain()
                e.tensor_copy(pad8[:, 0:4], sfc[:, 0:4])
                e.tensor_copy(pad8[:, 8:12], sfc[:, 4:8])
                e.drain()
                e.max(vals[:], pad8[:, 0:8])
                e.drain()
                e.tensor_tensor(gs[:, 0:1], vals[:, 0:1], vals[:, 1:2], ALU.add)
                e.max(vals[:], pad8[:, 8:16])
                e.drain()
                e.tensor_tensor(gs[:, 1:2], vals[:, 0:1], vals[:, 1:2], ALU.add)
                e.drain()
                e.tensor_tensor(gs[:, 2:3], gs[:, 0:1], gs[:, 1:2], ALU.is_ge)
                e.drain()
                e.tensor_scalar(gs[:, 3:4], gs[:, 2:3], -1.0, 1.0, ALU.mult, ALU.add)
                e.drain()
                e.tensor_scalar(sfc[:, 0:4], sfc[:, 0:4], gs[:, 2:3], None, ALU.mult)
                e.tensor_scalar(sfc[:, 4:8], sfc[:, 4:8], gs[:, 3:4], None, ALU.mult)
                e.drain()
                e.max(vals[:], sfc[:])
                e.drain()
                return e.max_index(idxu[:], vals[:], sfc[:])
            P.step("vector", grp_fn)
            def wsel_fn(e):
                e.tensor_copy(idxf[:], idxu[:])
                e.drain()
                for kk in range(2):
                    e.tensor_scalar(oh[:, kk, :], iota8[:], idxf[:, kk:kk + 1], None, ALU.is_equal)
                e.drain()
                for kk in range(2):
                    e.tensor_tensor(oh[:, kk, :], oh[:, kk, :], sc_f[:], ALU.mult)
                e.drain()
                og3 = og[:].rearrange("p (k o) -> p k o", k=8)
                e.tensor_reduce(og3[:, 0:2, :], oh[:], mybir.AxisListType.X, ALU.add)
                e.drain()
                e.tensor_reduce(og[:, 4:5], og[:, 0:2], mybir.AxisListType.X, ALU.add)
                e.drain()
                e.tensor_scalar_add(og[:, 4:5], og[:, 4:5], 1e-20)
                e.drain()
                return e.reciprocal(og[:, 5:6], og[:, 4:5])
            P.step("vector", wsel_fn)
            def comb_fn(e):
                e.memset(comb_tok[:], 0.0)
                e.drain()
                for kk in range(2):
                    e.scalar_tensor_tensor(comb_tok[:], oh[:, kk, :], og[0:128, 5:6], comb_tok[:],
                                           ALU.mult, ALU.add)
                    e.drain()
                return e.nop()
            P.step("vector", comb_fn)
            P.step("tensor", lambda e: e.transpose(pct[:], comb_tok[:], identf[:]))
            P.step("vector", lambda e, buf=buf, t4=t4: e.tensor_copy(combt_sb[buf][:, t4 * 128:(t4 + 1) * 128], pct[:]))
        P.step("gpsimd", lambda e, c=c, buf=buf: e.dma_start(
            out=slabs_ap(xnT, c * CW, CW), in_=xn_sb[buf][:]), dma=True, inc=s_st)
        P.step("gpsimd", lambda e, c=c, buf=buf: e.dma_start(
            out=combT[:, c * CW:(c + 1) * CW], in_=combt_sb[buf][:]), dma=True, inc=s_st)
        st_gate[c] = s_st.n
    P.emit()


def build_p4b(nc, io, es, name="p4b"):
    latT = io.stream("latT", [512, T], BF16, produce=False)
    combT = io.stream("comb4T", [8, T], F32, produce=False)
    routedT = io.stream("routedT", [D, T], F32, produce=True)
    CW = 512
    NCH = T // CW
    P = Pipe(nc, es, name)
    t = lambda n, sh, dt: es.enter_context(nc.sbuf_tensor(name + "_" + n, sh, dt))
    pt = lambda n, sh: es.enter_context(nc.psum_tensor(name + "_" + n, sh, F32))
    w1 = t("w1", [128, 8, 4, 512], BF16)
    w2 = t("w2", [128, 8, 4, 512], BF16)
    fc2 = t("fc2", [128, 4, D], BF16)
    onesr = t("onesr", [1, 128], F32)
    lat_c = [t(f"lat{i}", [128, 4, CW], BF16) for i in range(2)]
    crow = t("crow", [1, 8 * CW], F32)
    bc_sb = t("bc", [128, CW], F32)
    r_bf = t("rbf", [128, 4, CW], BF16)
    racc = t("racc", [128, 4, CW], F32)
    rq_bf = t("rq", [128, 4, CW], BF16)
    ro_f = t("rof", [128, NSLAB, CW], F32)
    tmp_f = t("tmpq", [128, CW], F32)
    pbc = pt("pbc", [128, CW])
    ph = pt("ph", [128, CW])
    po = pt("po", [128, CW])
    w1ap = io.inp("w_f_w1T", [8, 512, 512], F32)
    w2ap = io.inp("w_f_w2T", [8, 512, 512], F32)
    P.step("sync", lambda e: e.dma_start(out=onesr[:], in_=io.inp("c_ones_row_f", [1, 128], F32)), dma=True, wait=(0, 0))
    for eg in range(8):
        P.step("gpsimd", lambda e, eg=eg: e.dma_start(
            out=w1[:, eg, :, :], in_=w1ap[eg].rearrange("(k p) m -> p k m", p=128)), dma=True, wait=(0, 0))
        P.step("gpsimd", lambda e, eg=eg: e.dma_start(
            out=w2[:, eg, :, :], in_=w2ap[eg].rearrange("(k p) m -> p k m", p=128)), dma=True, wait=(0, 0))
    P.step("gpsimd", lambda e: e.dma_start(
        out=fc2[:], in_=io.inp("w_fc2T", [512, D], F32).rearrange("(k p) m -> p k m", p=128)), dma=True, wait=(0, 0))
    s_ld = P.xsem("ld"); s_st = P.xsem("st")
    st_gate = {}
    for c in range(NCH):
        buf = c % 2
        P.step("sync", lambda e, c=c, buf=buf: e.dma_start(
            out=lat_c[buf][:], in_=latT.rearrange("(s p) t -> p s t", p=128, s=4)[:, :, c * CW:(c + 1) * CW]),
            dma=True, wait=(0, 0), inc=s_ld, waits=[(s_st, st_gate.get(c - 2, 0))])
        P.step("sync", lambda e: e.nop(), waits=[(s_ld, s_ld.n)])
        for eg in range(8):
            P.step("sync", lambda e, c=c, eg=eg: e.dma_start(
                out=crow[:, eg * CW:(eg + 1) * CW], in_=combT[eg:eg + 1, c * CW:(c + 1) * CW]), dma=True)
        P.step("vector", lambda e: e.memset(racc[:], 0.0))
        for eg in range(8):
            def mmbc(e, eg=eg):
                return e.matmul(pbc[:], onesr[:], crow[:, eg * CW:(eg + 1) * CW], start=True, stop=True)
            P.step("tensor", mmbc)
            P.step("vector", lambda e: e.tensor_copy(bc_sb[:], pbc[:]))
            for m in range(4):
                def mmh(e, eg=eg, m=m, buf=buf):
                    for k in range(4):
                        ins = e.matmul(ph[:], w1[:, eg, k, m * 128:(m + 1) * 128],
                                       lat_c[buf][:, k, :], start=(k == 0), stop=(k == 3))
                    return ins
                P.step("tensor", mmh)
                def relu2_fn(e, m=m):
                    e.tensor_relu(tmp_f[:], ph[:])
                    e.drain()
                    return e.tensor_tensor(r_bf[:, m, :], tmp_f[:], tmp_f[:], ALU.mult)
                P.step("vector", relu2_fn)
            for m in range(4):
                def mmo(e, eg=eg, m=m):
                    for k in range(4):
                        ins = e.matmul(po[:], w2[:, eg, k, m * 128:(m + 1) * 128],
                                       r_bf[:, k, :], start=(k == 0), stop=(k == 3))
                    return ins
                P.step("tensor", mmo)
                def racc_fn(e, m=m):
                    e.tensor_tensor(tmp_f[:], po[:], bc_sb[:], ALU.mult)
                    e.drain()
                    sl = racc[:, m, :]
                    return e.tensor_tensor(sl, sl, tmp_f[:], ALU.add)
                P.step("vector", racc_fn)
        P.step("vector", lambda e: e.tensor_copy(rq_bf[:], racc[:]))
        for s in range(NSLAB):
            def mmf(e, s=s):
                for k in range(4):
                    ins = e.matmul(po[:], fc2[:, k, s * 128:(s + 1) * 128],
                                   rq_bf[:, k, :], start=(k == 0), stop=(k == 3))
                return ins
            P.step("tensor", mmf)
            P.step("vector", lambda e, s=s: e.tensor_copy(ro_f[:, s, :], po[:]))
        P.step("gpsimd", lambda e, c=c: e.dma_start(
            out=slabs_ap(routedT, c * CW, CW), in_=ro_f[:]), dma=True, inc=s_st)
        st_gate[c] = s_st.n
    P.emit()


def build_p4c(nc, io, es, name="p4c"):
    x3T = io.stream("x3T", [D, T], F32, produce=False)
    xnT = io.stream("xn4T", [D, T], BF16, produce=False)
    routedT = io.stream("routedT", [D, T], F32, produce=False)
    outT = io.out("outT", [D, T], F32)
    SH = 4096
    NSH = SH // 128
    CW = 256
    NCH = T // CW
    P = Pipe(nc, es, name)
    t = lambda n, sh, dt: es.enter_context(nc.sbuf_tensor(name + "_" + n, sh, dt))
    pt = lambda n, sh: es.enter_context(nc.psum_tensor(name + "_" + n, sh, F32))
    up = t("up", [128, NSLAB, SH], BF16)
    down = t("down", [128, NSH, D], BF16)
    xn_c = [t(f"xn{i}", [128, NSLAB, CW], BF16) for i in range(2)]
    x_f = t("xf", [128, NSLAB, CW], F32)
    r_f = t("rf", [128, NSLAB, CW], F32)
    hs_bf = t("hs", [128, NSH, CW], BF16)
    o_f = t("of", [128, NSLAB, CW], F32)
    tmp_f = t("tmpc", [128, CW], F32)
    ph = pt("ph", [128, CW])
    po = pt("po", [128, CW])
    P.step("gpsimd", lambda e: e.dma_start(
        out=up[:], in_=io.inp("w_upT", [D, SH], F32).rearrange("(k p) m -> p k m", p=128)), dma=True, wait=(0, 0))
    P.step("gpsimd", lambda e: e.dma_start(
        out=down[:], in_=io.inp("w_downT", [SH, D], F32).rearrange("(k p) m -> p k m", p=128)), dma=True, wait=(0, 0))
    s_ld = P.xsem("ld"); s_st = P.xsem("st")
    st_gate = {}
    for c in range(NCH):
        buf = c % 2
        P.step("sync", lambda e, c=c, buf=buf: e.dma_start(out=xn_c[buf][:], in_=slabs_ap(xnT, c * CW, CW)),
               dma=True, wait=(0, 0), inc=s_ld, waits=[(s_st, st_gate.get(c - 2, 0))])
        P.step("sync", lambda e: e.nop(), waits=[(s_ld, s_ld.n)])
        P.step("sync", lambda e, c=c: e.dma_start(out=x_f[:], in_=slabs_ap(x3T, c * CW, CW)), dma=True)
        P.step("sync", lambda e, c=c: e.dma_start(out=r_f[:], in_=slabs_ap(routedT, c * CW, CW)), dma=True)
        for m in range(NSH):
            def mmh(e, m=m, buf=buf):
                for k in range(NSLAB):
                    ins = e.matmul(ph[:], up[:, k, m * 128:(m + 1) * 128],
                                   xn_c[buf][:, k, :], start=(k == 0), stop=(k == NSLAB - 1))
                return ins
            P.step("tensor", mmh)
            def relu2_fn(e, m=m):
                e.tensor_relu(tmp_f[:], ph[:])
                e.drain()
                return e.tensor_tensor(hs_bf[:, m, :], tmp_f[:], tmp_f[:], ALU.mult)
            P.step("vector", relu2_fn)
        for s in range(NSLAB):
            def mmo(e, s=s):
                for k in range(NSH):
                    ins = e.matmul(po[:], down[:, k, s * 128:(s + 1) * 128],
                                   hs_bf[:, k, :], start=(k == 0), stop=(k == NSH - 1))
                return ins
            P.step("tensor", mmo)
            def out_fn(e, s=s):
                e.tensor_tensor(o_f[:, s, :], x_f[:, s, :], r_f[:, s, :], ALU.add)
                e.drain()
                return e.tensor_tensor(o_f[:, s, :], o_f[:, s, :], po[:], ALU.add)
            P.step("vector", out_fn)
        P.step("gpsimd", lambda e, c=c: e.dma_start(
            out=slabs_ap(outT, c * CW, CW), in_=o_f[:]), dma=True, inc=s_st)
        st_gate[c] = s_st.n
    P.emit()


NSLOTS = 96 * 512
NTILES = 96
BIGF = 1.0e9

def build_p2m(nc, io, es, name="p2m"):
    combT = io.stream("combT", [32, T], F32, produce=False)
    selT = io.stream("selT", [T, 4], F32, produce=False)
    wgtT = io.stream("wgtT", [T, 4], F32, produce=False)
    metaT = io.stream("metaT", [NSLOTS, 4], F32, produce=True)
    gidxT = io.stream("gidxT", [128, 2 * NTILES], F32, produce=True)
    w1bf = io.stream("w1bfT", [32, D, 768], BF16, produce=True)
    w3bf = io.stream("w3bfT", [32, D, 768], BF16, produce=True)
    w2bf = io.stream("w2bfT", [32, 768, D], BF16, produce=True)
    P = Pipe(nc, es, name)
    t = lambda n, sh, dt: es.enter_context(nc.sbuf_tensor(name + "_" + n, sh, dt))
    pt = lambda n, sh, dt=F32: es.enter_context(nc.psum_tensor(name + "_" + n, sh, dt))
    comb = t("comb", [32, T], F32)
    m_m = t("mm", [32, T], F32)
    pos = t("pos", [32, T], F32)
    cnt = t("cnt", [32, 8], F32)
    base = t("base", [32, 1], F32)
    ones32 = t("ones32", [32, 1], F32)
    onesT = t("onesT", [32, T], F32)
    ltri = t("ltri", [32, 32], F32)
    tile_iota = t("tiota", [32, NTILES], F32)
    iota_col = t("icol", [128, 1], F32)
    onesr = t("onesr", [1, 128], F32)
    identf = t("identf", [128, 128], F32)
    eb_cmp = t("ebcmp", [32, NTILES], F32)
    eid_row = t("eidr", [1, NTILES], F32)
    gvec = t("gvec", [128, 2 * NTILES], F32)
    base_bc = t("basebc", [128, 32], F32)
    pos_tok = t("postok", [128, 32], F32)
    sel4 = t("sel4", [128, 4], F32)
    wg4 = t("wg4", [128, 4], F32)
    oh4 = t("oh4", [128, 32], F32)
    iota32 = t("iota32b", [128, 32], F32)
    meta_sc = t("metasc", [128, 4], F32)
    slot_i = t("sloti", [128, 1], I32)
    cnt_i = t("cnti", [32, 1], I32)
    red2 = t("red2", [128, 4], F32)
    init_t = t("initt", [128, 384 * 4], F32)
    wconv = [t(f"wconv{i}", [128, NSLAB, 768], BF16) for i in range(2)]
    wconv2 = [t(f"wconv2{i}", [128, 6, D], BF16) for i in range(2)]
    p96 = pt("p96", [1, NTILES])
    pb32 = pt("pb32", [128, 32])
    pbase = pt("pbase", [32, 1])
    ptp = pt("ptp", [128, 128])
    P.step("sync", lambda e: e.dma_start(out=onesr[:], in_=io.inp("c_ones_row_f", [1, 128], F32)), dma=True, wait=(0, 0))
    P.step("sync", lambda e: e.dma_start(out=identf[:], in_=io.inp("c_ident_f", [128, 128], F32)), dma=True, wait=(0, 0))
    P.step("sync", lambda e: e.dma_start(out=ltri[:], in_=io.inp("c_ltri32", [32, 32], F32)), dma=True, wait=(0, 0))
    P.step("sync", lambda e: e.dma_start(out=tile_iota[:], in_=io.inp("c_tile_iota", [32, NTILES], F32)), dma=True, wait=(0, 0))
    P.step("sync", lambda e: e.dma_start(out=iota_col[:], in_=io.inp("c_iota_col", [128, 1], F32)), dma=True, wait=(0, 0))
    P.step("sync", lambda e: e.dma_start(out=ones32[:], in_=io.inp("c_ones32col_f", [32, 1], F32)), dma=True, wait=(0, 0))
    P.step("sync", lambda e: e.dma_start(out=iota32[:], in_=io.inp("c_iota32_bc", [128, 32], F32)), dma=True, wait=(0, 0))
    P.step("sync", lambda e: e.dma_start(out=comb[:], in_=combT[:]), dma=True, wait=(0, 0))
    P.step("vector", lambda e: e.memset(onesT[:], 1.0))
    def scan_fn(e):
        e.tensor_single_scalar(m_m[:], comb[:], 0.0, ALU.is_gt)
        e.drain()
        e.tensor_tensor_scan(pos[:], onesT[:], m_m[:], 0.0, ALU.mult, ALU.add)
        e.drain()
        e.tensor_copy(cnt[:, 0:1], pos[:, T - 1:T])
        e.drain()
        return e.tensor_tensor(pos[:], pos[:], m_m[:], ALU.subtract)
    P.step("vector", scan_fn)
    def pc_fn(e):
        e.tensor_scalar(cnt[:, 1:2], cnt[:, 0:1], 1.0 / 512.0, 255.5 / 512.0, ALU.mult, ALU.add)
        e.drain()
        e.tensor_copy(cnt_i[:], cnt[:, 1:2])
        e.drain()
        e.tensor_copy(cnt[:, 3:4], cnt_i[:])
        e.drain()
        return e.tensor_scalar_mul(cnt[:, 3:4], cnt[:, 3:4], 512.0)
    P.step("vector", pc_fn)
    P.step("tensor", lambda e: e.matmul(pbase[:], ltri[:], cnt[:, 3:4], start=True, stop=True))
    P.step("vector", lambda e: e.tensor_copy(base[:], pbase[:]))
    def eid_fn(e):
        e.tensor_scalar(eb_cmp[:], tile_iota[:], base[:, 0:1], None, ALU.is_ge)
        return e.nop()
    P.step("vector", eid_fn)
    P.step("tensor", lambda e: e.matmul(p96[:], ones32[:], eb_cmp[:], start=True, stop=True))
    def eidr_fn(e):
        return e.tensor_scalar_add(eid_row[:], p96[:], -1.0)
    P.step("vector", eidr_fn)
    P.step("tensor", lambda e: e.matmul(ptp[:, 0:NTILES], onesr[:], eid_row[:], start=True, stop=True))
    def gvec_fn(e):
        e.tensor_scalar(gvec[:, 0:NTILES], ptp[:, 0:NTILES], 1024.0, None, ALU.mult)
        e.tensor_scalar(gvec[:, NTILES:2 * NTILES], ptp[:, 0:NTILES], 768.0, None, ALU.mult)
        e.drain()
        e.tensor_tensor(gvec[:, 0:NTILES], gvec[:, 0:NTILES], iota_col[:, 0:1].broadcast_to((128, NTILES)), ALU.add)
        return e.tensor_tensor(gvec[:, NTILES:2 * NTILES], gvec[:, NTILES:2 * NTILES], iota_col[:, 0:1].broadcast_to((128, NTILES)), ALU.add)
    P.step("vector", gvec_fn)
    P.step("gpsimd", lambda e: e.dma_start(out=gidxT[:], in_=gvec[:]), dma=True)
    baseRow = io.stream("baseRowT", [32, 1], F32, produce=True)
    P.step("gpsimd", lambda e: e.dma_start(out=baseRow[:], in_=base[:]), dma=True)
    P.step("sync", lambda e: e.dma_start(out=m_m[0:1, 0:32], in_=baseRow.rearrange("e o -> o e")), dma=True)
    P.step("tensor", lambda e: e.matmul(pb32[:], onesr[:], m_m[0:1, 0:32], start=True, stop=True))
    P.step("vector", lambda e: e.tensor_copy(base_bc[:], pb32[:]))
    def init_fn(e):
        ini4 = init_t[:].rearrange("p (b c) -> p b c", c=4)
        e.memset(init_t[:], 0.0)
        e.drain()
        e.memset(ini4[:, :, 2:3], float(4 * T))
        return e.nop()
    P.step("vector", init_fn)
    P.step("gpsimd", lambda e: e.dma_start(
        out=metaT.rearrange("(p b) c -> p (b c)", p=128),
        in_=init_t[:]), dma=True)
    s_lw1 = P.xsem("lw1"); s_sw1 = P.xsem("sw1")
    s_lw2 = P.xsem("lw2"); s_sw2 = P.xsem("sw2")
    wpairs = []
    for eg in range(32):
        wpairs.append(("w1", eg))
        wpairs.append(("w3", eg))
        wpairs.append(("w2", eg))
    wstate = {"n1": 0, "n2": 0}
    sw_hist1 = {}
    sw_hist2 = {}

    def emit_wpair():
        kind, eg = wpairs.pop(0)
        if kind == "w2":
            k = wstate["n2"]; wstate["n2"] += 1
            par = k % 2
            P.step("gpsimd", lambda e, eg=eg, par=par: e.dma_start(
                out=wconv2[par][:], in_=io.inp("w_moe_w2", [32, 768, D], F32)[eg].rearrange("(k p) m -> p k m", p=128)),
                dma=True, wait=(0, 0), inc=s_lw2, waits=[(s_sw2, sw_hist2.get(k - 2, 0))])
            lwn = s_lw2.n
            P.step("sync", lambda e, eg=eg, par=par: e.dma_start(
                out=w2bf[eg].rearrange("(k p) m -> p k m", p=128), in_=wconv2[par][:]),
                dma=True, wait=(0, 0), inc=s_sw2, waits=[(s_lw2, lwn)])
            sw_hist2[k] = s_sw2.n
        else:
            src_name = "w_moe_w1" if kind == "w1" else "w_moe_w3"
            dst = w1bf if kind == "w1" else w3bf
            k = wstate["n1"]; wstate["n1"] += 1
            par = k % 2
            P.step("gpsimd", lambda e, eg=eg, par=par, src_name=src_name: e.dma_start(
                out=wconv[par][:], in_=io.inp(src_name, [32, D, 768], F32)[eg].rearrange("(k p) m -> p k m", p=128)),
                dma=True, wait=(0, 0), inc=s_lw1, waits=[(s_sw1, sw_hist1.get(k - 2, 0))])
            lwn = s_lw1.n
            P.step("sync", lambda e, eg=eg, par=par, dst=dst: e.dma_start(
                out=dst[eg].rearrange("(k p) m -> p k m", p=128), in_=wconv[par][:]),
                dma=True, wait=(0, 0), inc=s_sw1, waits=[(s_lw1, lwn)])
            sw_hist1[k] = s_sw1.n

    for st in range(T // 128):
        if wpairs:
            emit_wpair()
        if st % 2 == 0 and wpairs:
            emit_wpair()
        t0 = st * 128
        P.step("sync", lambda e, t0=t0: e.dma_start(out=sel4[:], in_=selT[t0:t0 + 128, :]), dma=True)
        P.step("sync", lambda e, t0=t0: e.dma_start(out=wg4[:], in_=wgtT[t0:t0 + 128, :]), dma=True)
        P.step("tensor", lambda e, t0=t0: e.transpose(ptp[:, 0:32], pos[:, t0:t0 + 128], identf[0:32, 0:32]))
        P.step("vector", lambda e: e.tensor_copy(pos_tok[:], ptp[:, 0:32]))
        for kk in range(4):
            def meta_fn(e, kk=kk, t0=t0):
                e.tensor_scalar(oh4[:], iota32[:], sel4[:, kk:kk + 1], None, ALU.is_equal)
                e.drain()
                e.tensor_tensor(oh4[:], oh4[:], pos_tok[:], ALU.mult)
                e.drain()
                e.tensor_reduce(red2[:, 0:1], oh4[:], mybir.AxisListType.X, ALU.add)
                e.tensor_scalar(oh4[:], iota32[:], sel4[:, kk:kk + 1], None, ALU.is_equal)
                e.drain()
                e.tensor_tensor(oh4[:], oh4[:], base_bc[:], ALU.mult)
                e.drain()
                e.tensor_reduce(red2[:, 1:2], oh4[:], mybir.AxisListType.X, ALU.add)
                e.drain()
                e.tensor_tensor(red2[:, 0:1], red2[:, 0:1], red2[:, 1:2], ALU.add)
                e.drain()
                e.tensor_copy(meta_sc[:, 0:1], wg4[:, kk:kk + 1])
                e.tensor_scalar(meta_sc[:, 1:2], iota_col[:], 1.0, float(t0), ALU.mult, ALU.add)
                e.tensor_scalar(meta_sc[:, 2:3], iota_col[:], 1.0, float(kk * T + t0), ALU.mult, ALU.add)
                e.memset(meta_sc[:, 3:4], 0.0)
                e.drain()
                return e.tensor_copy(slot_i[:], red2[:, 0:1])
            P.step("vector", meta_fn)
            P.step("gpsimd", lambda e: e.indirect_dma_start(
                out=metaT, out_offset=IndirectOffsetOnAxis(slot_i[:, 0:1], 0),
                in_=meta_sc[:], in_offset=None), dma=True)
    while wpairs:
        emit_wpair()
    P.emit()


def build_p2s(nc, io, es, name="p2s"):
    xtokT = io.stream("xn2tokT", [T, D], BF16, produce=False)
    metaT = io.stream("metaT", [NSLOTS, 4], F32, produce=False)
    gidxT = io.stream("gidxT", [128, 2 * NTILES], F32, produce=False)
    w1bf = io.stream("w1bfT", [32, D, 768], BF16, produce=False)
    w3bf = io.stream("w3bfT", [32, D, 768], BF16, produce=False)
    w2bf = io.stream("w2bfT", [32, 768, D], BF16, produce=False)
    moe4 = io.stream("moe4T", [4 * T + 128, D], F32, produce=True)
    NI = 6
    P = Pipe(nc, es, name)
    t = lambda n, sh, dt: es.enter_context(nc.sbuf_tensor(name + "_" + n, sh, dt))
    pt = lambda n, sh, dt=F32: es.enter_context(nc.psum_tensor(name + "_" + n, sh, dt))
    identb = t("identb", [128, 128], BF16)
    gvec = t("gvec", [128, 2 * NTILES], F32)
    idx13f = t("idx13f", [128, NSLAB * NTILES], F32)
    idx13 = t("idx13", [128, NSLAB * NTILES], I32)
    idx2f = t("idx2f", [128, NI * NTILES], F32)
    idx2 = t("idx2", [128, NI * NTILES], I32)
    meta4 = [t(f"meta4{i}", [128, 4, 4], F32) for i in range(2)]
    tok_i = [t(f"toki{i}", [128, 4], I32) for i in range(2)]
    dst_i = [t(f"dsti{i}", [128, 4], I32) for i in range(2)]
    w1g = [t(f"w1g{i}", [128, NSLAB, 768], BF16) for i in range(2)]
    w3g = [t(f"w3g{i}", [128, NSLAB, 768], BF16) for i in range(2)]
    w2g = [t(f"w2g{i}", [128, NI, D], BF16) for i in range(2)]
    xga = [t(f"xga{i}", [128, 4, D], BF16) for i in range(2)]
    xgT = [t(f"xgT{i}", [128, NSLAB, 512], BF16) for i in range(2)]
    sg_bf = t("sg", [128, 512], BF16)
    a_bf = t("abf", [128, NI, 512], BF16)
    y_f = t("yf", [128, D], F32)
    ph = pt("ph", [128, 512])
    pu = pt("pu", [128, 512])
    pyA = pt("pyA", [128, 512])
    pyB = pt("pyB", [128, 512])
    pxt = pt("pxt", [128, 128], BF16)
    P.step("sync", lambda e: e.dma_start(out=identb[:], in_=io.inp("c_ident_bf", [128, 128], BF16)), dma=True, wait=(0, 0))
    P.step("sync", lambda e: e.dma_start(out=gvec[:], in_=gidxT[:]), dma=True, wait=(0, 0))
    def idx_fn(e):
        for k in range(NSLAB):
            e.tensor_scalar(idx13f[:, k * NTILES:(k + 1) * NTILES], gvec[:, 0:NTILES], 1.0, float(k * 128), ALU.mult, ALU.add)
        for k in range(NI):
            e.tensor_scalar(idx2f[:, k * NTILES:(k + 1) * NTILES], gvec[:, NTILES:2 * NTILES], 1.0, float(k * 128), ALU.mult, ALU.add)
        e.drain()
        e.tensor_copy(idx13[:], idx13f[:])
        e.drain()
        return e.tensor_copy(idx2[:], idx2f[:])
    P.step("vector", idx_fn)
    s_ld = P.xsem("ld")
    s_mm = P.xsem("mm")
    meta_n = {}
    ld_full = {}
    mm_end = {}

    def pe(fn, waits=()):
        P.step("tensor", fn, inc=s_mm, waits=waits)
        return s_mm.n

    def G(i):
        wb = i % 2
        P.step("gpsimd", lambda e, i=i, wb=wb: e.dma_start(
            out=meta4[wb][:], in_=metaT[i * 512:(i + 1) * 512, :].rearrange("(s p) c -> p s c", p=128)), dma=True, inc=s_ld,
            waits=[(s_mm, mm_end.get(i - 2, 0))])
        meta_n[i] = s_ld.n
        for k in range(NSLAB):
            P.step("gpsimd", lambda e, i=i, k=k, wb=wb: e.indirect_dma_start(
                out=w1g[wb][:, k, :], out_offset=None,
                in_=w1bf.rearrange("e k m -> (e k) m"),
                in_offset=IndirectOffsetOnAxis(idx13[:, k * NTILES + i:k * NTILES + i + 1], 0)), dma=True, inc=s_ld)
            P.step("gpsimd", lambda e, i=i, k=k, wb=wb: e.indirect_dma_start(
                out=w3g[wb][:, k, :], out_offset=None,
                in_=w3bf.rearrange("e k m -> (e k) m"),
                in_offset=IndirectOffsetOnAxis(idx13[:, k * NTILES + i:k * NTILES + i + 1], 0)), dma=True, inc=s_ld)
        for k in range(NI):
            P.step("gpsimd", lambda e, i=i, k=k, wb=wb: e.indirect_dma_start(
                out=w2g[wb][:, k, :], out_offset=None,
                in_=w2bf.rearrange("e k m -> (e k) m"),
                in_offset=IndirectOffsetOnAxis(idx2[:, k * NTILES + i:k * NTILES + i + 1], 0)), dma=True, inc=s_ld)
        def toks_fn(e, wb=wb):
            e.tensor_copy(tok_i[wb][:], meta4[wb][:, :, 1])
            return e.tensor_copy(dst_i[wb][:], meta4[wb][:, :, 2])
        P.step("vector", toks_fn, waits=[(s_ld, meta_n[i])])
        for s4 in range(4):
            P.step("gpsimd", lambda e, s4=s4, wb=wb: e.indirect_dma_start(
                out=xga[wb][:, s4, :], out_offset=None,
                in_=xtokT, in_offset=IndirectOffsetOnAxis(tok_i[wb][:, s4:s4 + 1], 0)), dma=True, inc=s_ld)
        ld_full[i] = s_ld.n

    def C(i):
        wb = i % 2
        first = True
        for s4 in range(4):
            for s in range(NSLAB):
                w = [(s_ld, ld_full[i])] if first else ()
                first = False
                tn = pe(lambda e, s4=s4, s=s, wb=wb: e.transpose(
                    pxt[:], xga[wb][:, s4, s * 128:(s + 1) * 128], identb[:]), waits=w)
                P.step("vector", lambda e, s4=s4, s=s, wb=wb: e.tensor_copy(
                    xgT[wb][:, s, s4 * 128:(s4 + 1) * 128], pxt[:]), waits=[(s_mm, tn)])
        for m in range(NI):
            def mmg(e, m=m, wb=wb):
                for k in range(NSLAB):
                    ins = e.matmul(ph[:], w1g[wb][:, k, m * 128:(m + 1) * 128],
                                   xgT[wb][:, k, :], start=(k == 0), stop=(k == NSLAB - 1))
                return ins
            gn = pe(mmg)
            P.step("scalar", lambda e: e.activation(out=sg_bf[:], in_=ph[:], func=AF.Silu), waits=[(s_mm, gn)])
            def mmu(e, m=m, wb=wb):
                for k in range(NSLAB):
                    ins = e.matmul(pu[:], w3g[wb][:, k, m * 128:(m + 1) * 128],
                                   xgT[wb][:, k, :], start=(k == 0), stop=(k == NSLAB - 1))
                return ins
            un = pe(mmu)
            P.step("vector", lambda e, m=m: e.tensor_tensor(a_bf[:, m, :], sg_bf[:], pu[:], ALU.mult), waits=[(s_mm, un)])
        for s4 in range(4):
            def mmy(e, s4=s4, wb=wb):
                for k in range(NI):
                    e.matmul(pyA[:], a_bf[:, k, s4 * 128:(s4 + 1) * 128],
                             w2g[wb][:, k, 0:512], start=(k == 0), stop=(k == NI - 1))
                for k in range(NI):
                    ins = e.matmul(pyB[:], a_bf[:, k, s4 * 128:(s4 + 1) * 128],
                                   w2g[wb][:, k, 512:1024], start=(k == 0), stop=(k == NI - 1))
                return ins
            yn = pe(mmy)
            def yev(e, s4=s4, wb=wb):
                e.tensor_scalar(y_f[:, 0:512], pyA[:], meta4[wb][:, s4, 0:1], None, ALU.mult)
                return e.tensor_scalar(y_f[:, 512:1024], pyB[:], meta4[wb][:, s4, 0:1], None, ALU.mult)
            P.step("vector", yev, waits=[(s_mm, yn)])
            P.step("gpsimd", lambda e, s4=s4, wb=wb: e.indirect_dma_start(
                out=moe4, out_offset=IndirectOffsetOnAxis(dst_i[wb][:, s4:s4 + 1], 0),
                in_=y_f[:], in_offset=None), dma=True)
        mm_end[i] = s_mm.n

    G(0)
    G(1)
    for i in range(NTILES):
        C(i)
        if i + 2 < NTILES:
            G(i + 2)
    P.emit()


def build_p2f(nc, io, es, name="p2f"):
    x1T = io.stream("x1T", [D, T], F32, produce=False)
    moe4 = io.stream("moe4T", [4 * T + 128, D], F32, produce=False)
    x2T = io.stream("x2T", [D, T], F32, produce=True)
    P = Pipe(nc, es, name)
    t = lambda n, sh, dt: es.enter_context(nc.sbuf_tensor(name + "_" + n, sh, dt))
    pt = lambda n, sh, dt=F32: es.enter_context(nc.psum_tensor(name + "_" + n, sh, dt))
    identf = t("identf", [128, 128], F32)
    mo = [t(f"mo{i}", [128, 4, D], F32) for i in range(2)]
    msum = [t(f"msum{i}", [128, D], F32) for i in range(2)]
    x_f = [t(f"xf{i}", [128, NSLAB, 128], F32) for i in range(2)]
    o_f = [t(f"of{i}", [128, NSLAB, 128], F32) for i in range(2)]
    ptp = pt("ptp", [128, 128])
    P.step("sync", lambda e: e.dma_start(out=identf[:], in_=io.inp("c_ident_f", [128, 128], F32)), dma=True, wait=(0, 0))
    s_ld = P.xsem("ld")
    ld_n = {}
    NST = T // 128

    def L(st):
        p = st % 2
        t0 = st * 128
        for r in range(4):
            P.step("sync", lambda e, t0=t0, r=r, p=p: e.dma_start(
                out=mo[p][:, r, :], in_=moe4[r * T + t0:r * T + t0 + 128, :]), dma=True, inc=s_ld)
        P.step("sync", lambda e, t0=t0, p=p: e.dma_start(out=x_f[p][:], in_=slabs_ap(x1T, t0, 128)), dma=True, inc=s_ld)
        ld_n[st] = s_ld.n

    def C(st):
        p = st % 2
        t0 = st * 128
        def sum_fn(e, p=p):
            e.tensor_tensor(msum[p][:], mo[p][:, 0, :], mo[p][:, 1, :], ALU.add)
            e.tensor_tensor(mo[p][:, 2, :], mo[p][:, 2, :], mo[p][:, 3, :], ALU.add)
            e.drain()
            return e.tensor_tensor(msum[p][:], msum[p][:], mo[p][:, 2, :], ALU.add)
        P.step("vector", sum_fn, waits=[(s_ld, ld_n[st])])
        for s in range(NSLAB):
            P.step("tensor", lambda e, s=s, p=p: e.transpose(ptp[:], msum[p][:, s * 128:(s + 1) * 128], identf[:]))
            def add_fn(e, s=s, p=p):
                return e.tensor_tensor(o_f[p][:, s, :], x_f[p][:, s, :], ptp[:], ALU.add)
            P.step("vector", add_fn)
        P.step("gpsimd", lambda e, t0=t0, p=p: e.dma_start(out=slabs_ap(x2T, t0, 128), in_=o_f[p][:]), dma=True)

    L(0)
    L(1)
    for st in range(NST):
        C(st)
        if st + 2 < NST:
            L(st + 2)
    P.emit()


def build_p3a(nc, io, es, name="p3a"):
    x2T = io.stream("x2T", [D, T], F32, produce=False)
    xnT = io.stream("xn3T", [D, T], BF16, produce=True)
    qxT = io.stream("qxT", [D, T], BF16, produce=True)
    kxT = io.stream("kxT", [D, T], BF16, produce=True)
    vtokT = io.stream("vtokT", [T, 2048], BF16, produce=True)
    CW = 256
    NCH = T // CW
    P = Pipe(nc, es, name)
    t = lambda n, sh, dt: es.enter_context(nc.sbuf_tensor(name + "_" + n, sh, dt))
    pt = lambda n, sh, dt=F32: es.enter_context(nc.psum_tensor(name + "_" + n, sh, dt))
    wqkv = t("wqkv", [128, NSLAB, 4096], BF16)
    rotA = t("rotA", [128, 128], BF16)
    x_f = [t(f"x{i}", [128, NSLAB, CW], F32) for i in range(2)]
    xn_sb = [t(f"xn{i}", [128, NSLAB, CW], BF16) for i in range(2)]
    tq = [t("tq0", [128, NSLAB, 2, CW], BF16)] * 2
    tk = [t("tk0", [128, NSLAB, 2, CW], BF16)] * 2
    q_bf = t("qbf", [128, CW], BF16)
    t1_b = t("t1b", [128, CW], BF16)
    t2_b = t("t2b", [128, CW], BF16)
    qx_sb = [t(f"qx{i}", [128, NSLAB, CW], BF16) for i in range(2)]
    kx_sb = [t(f"kx{i}", [128, NSLAB, CW], BF16) for i in range(2)]
    v_bf = t("vbf", [128, 512], BF16)
    tmp_f = t("tmp", [128, CW], F32)
    kon = t("kon", [128, 1], F32)
    sb = {
        "ones_col_bf": t("ones", [128, 1], BF16),
        "ones_row_f": t("onesr", [1, 128], F32),
        name + "xbf": t("xbf", [128, NSLAB, CW], BF16),
        name + "sqbf": t("sqbf", [128, NSLAB, CW], BF16),
        name + "stat": t("stat", [1, 6 * CW], F32),
    }
    ps = {
        name + "psx": pt("psx", [1, CW]),
        name + "psq": pt("psq", [1, CW]),
        name + "pbr": pt("pbr", [128, CW]),
        name + "pbp": pt("pbp", [128, CW]),
    }
    pq = pt("pq", [128, CW])
    pr = pt("pr", [128, CW])
    pv = pt("pv", [128, 512])
    eps_ap = kon[0:1, 0:1]
    P.step("vector", lambda e: e.memset(kon[:, 0:1], 1e-5))
    P.step("sync", lambda e: e.dma_start(out=sb["ones_col_bf"][:], in_=io.inp("c_ones_col_bf", [128, 1], BF16)), dma=True, wait=(0, 0))
    P.step("sync", lambda e: e.dma_start(out=sb["ones_row_f"][:], in_=io.inp("c_ones_row_f", [1, 128], F32)), dma=True, wait=(0, 0))
    P.step("sync", lambda e: e.dma_start(out=rotA[:], in_=io.inp("c_rot_lhsT", [128, 128], BF16)), dma=True, wait=(0, 0))
    P.step("gpsimd", lambda e: e.dma_start(out=wqkv[:], in_=io.inp("w_qkvT", [D, 4096], F32).rearrange("(k p) m -> p k m", p=128)), dma=True, wait=(0, 0))
    tqc = io.inp("c_tqc", [D, S], BF16)
    tqs = io.inp("c_tqs", [D, S], BF16)
    tkc = io.inp("c_tkc", [D, S], BF16)
    tks = io.inp("c_tks", [D, S], BF16)
    s_ld = P.xsem("ld"); s_st = P.xsem("st")
    st_gate = {}
    for c in range(NCH):
        buf = c % 2
        loc = (c * CW) % S
        xw = [(s_st, st_gate.get(c - 2, 0))]
        P.step("sync", lambda e, c=c, buf=buf: e.dma_start(out=x_f[buf][:], in_=slabs_ap(x2T, c * CW, CW)), dma=True, wait=(0, 0), inc=s_ld, waits=xw)
        def tbl(dst, src, loc):
            return lambda e: e.dma_start(out=dst[:], in_=src.rearrange("(s p) t -> p s t", p=128)[:, :, loc:loc + CW])
        P.step("sync", lambda e: e.nop(), waits=[(s_ld, s_ld.n)])
        P.step("sync", lambda e, buf=buf, loc=loc: e.dma_start(out=tq[buf][:, :, 0, :], in_=tqc.rearrange("(s p) t -> p s t", p=128)[:, :, loc:loc + CW]), dma=True)
        P.step("sync", lambda e, buf=buf, loc=loc: e.dma_start(out=tq[buf][:, :, 1, :], in_=tqs.rearrange("(s p) t -> p s t", p=128)[:, :, loc:loc + CW]), dma=True)
        P.step("sync", lambda e, buf=buf, loc=loc: e.dma_start(out=tk[buf][:, :, 0, :], in_=tkc.rearrange("(s p) t -> p s t", p=128)[:, :, loc:loc + CW]), dma=True)
        P.step("sync", lambda e, buf=buf, loc=loc: e.dma_start(out=tk[buf][:, :, 1, :], in_=tks.rearrange("(s p) t -> p s t", p=128)[:, :, loc:loc + CW]), dma=True)
        ln_stats_steps(P, sb, ps, x_f[buf], CW, eps_ap, name)
        def ap_fn(e, buf=buf):
            for s in range(NSLAB):
                e.tensor_tensor(tmp_f[:], x_f[buf][:, s, :], ps[name + "pbr"][:], ALU.mult)
                e.drain()
                ins = e.tensor_tensor(xn_sb[buf][:, s, :], tmp_f[:], ps[name + "pbp"][:], ALU.subtract)
                e.drain()
            return ins
        P.step("vector", ap_fn)
        for which, off, tbl_sb, dst in (("q", 0, tq, qx_sb), ("k", D, tk, kx_sb)):
            for s in range(NSLAB):
                def mmq(e, buf=buf, s=s, off=off):
                    for k in range(NSLAB):
                        ins = e.matmul(pq[:], wqkv[:, k, off + s * 128:off + (s + 1) * 128],
                                       xn_sb[buf][:, k, :], start=(k == 0), stop=(k == NSLAB - 1))
                    return ins
                P.step("tensor", mmq)
                P.step("vector", lambda e: e.tensor_copy(q_bf[:], pq[:]))
                P.step("tensor", lambda e: e.matmul(pr[:], rotA[:], q_bf[:], start=True, stop=True))
                def xpos_fn(e, buf=buf, s=s, tbl_sb=tbl_sb, dst=dst):
                    e.tensor_tensor(t1_b[:], q_bf[:], tbl_sb[buf][:, s, 0, :], ALU.mult)
                    e.tensor_tensor(t2_b[:], pr[:], tbl_sb[buf][:, s, 1, :], ALU.mult)
                    e.drain()
                    return e.tensor_tensor(dst[buf][:, s, :], t1_b[:], t2_b[:], ALU.add)
                P.step("vector", xpos_fn)
        for t4 in range(CW // 128):
            for n in range(4):
                def mmv(e, buf=buf, t4=t4, n=n):
                    for k in range(NSLAB):
                        ins = e.matmul(pv[:], xn_sb[buf][:, k, t4 * 128:(t4 + 1) * 128],
                                       wqkv[:, k, 2048 + n * 512:2048 + (n + 1) * 512],
                                       start=(k == 0), stop=(k == NSLAB - 1))
                    return ins
                P.step("tensor", mmv)
                P.step("vector", lambda e: e.tensor_copy(v_bf[:], pv[:]))
                P.step("gpsimd", lambda e, c=c, t4=t4, n=n: e.dma_start(
                    out=vtokT[c * CW + t4 * 128:c * CW + (t4 + 1) * 128, n * 512:(n + 1) * 512],
                    in_=v_bf[:]), dma=True)
        P.step("gpsimd", lambda e, c=c, buf=buf: e.dma_start(out=slabs_ap(xnT, c * CW, CW), in_=xn_sb[buf][:]), dma=True, inc=s_st)
        P.step("gpsimd", lambda e, c=c, buf=buf: e.dma_start(out=slabs_ap(qxT, c * CW, CW), in_=qx_sb[buf][:]), dma=True, inc=s_st)
        P.step("gpsimd", lambda e, c=c, buf=buf: e.dma_start(out=slabs_ap(kxT, c * CW, CW), in_=kx_sb[buf][:]), dma=True, inc=s_st)
        st_gate[c] = s_st.n
    P.emit()


def build_p3b(nc, io, es, name="p3b"):
    x2T = io.stream("x2T", [D, T], F32, produce=False)
    xnT = io.stream("xn3T", [D, T], BF16, produce=False)
    qxT = io.stream("qxT", [D, T], BF16, produce=False)
    kxT = io.stream("kxT", [D, T], BF16, produce=False)
    vtokT = io.stream("vtokT", [T, 2048], BF16, produce=False)
    x3T = io.stream("x3T", [D, T], F32, produce=True)
    NH = 8
    HD = 128
    VD = 256
    CC = 128
    NCH_B = S // CC
    P = Pipe(nc, es, name)
    t = lambda n, sh, dt: es.enter_context(nc.sbuf_tensor(name + "_" + n, sh, dt))
    pt = lambda n, sh, dt=F32: es.enter_context(nc.psum_tensor(name + "_" + n, sh, dt))
    wgat = t("wgat", [128, NSLAB, 2048], BF16)
    wproj = t("wproj", [128, 16, D], BF16)
    identb = t("identb", [128, 128], BF16)
    expoT_s = t("expo", [128, 128], F32)
    cmask_s = t("cmask", [128, 128], F32)
    iotaJ_s = t("iotaJ", [128, 128], F32)
    revcol_s = t("revc", [128, 1], F32)
    k128_s = t("k128", [128, 1], F32)
    decrow = t("decrow", [1, 8], F32)
    lnd_bc = t("lnd", [128, 8], F32)
    dmT8 = t("dmT8", [128, NH, 128], BF16)
    dbc8 = t("dbc8", [128, NH, 128], BF16)
    wexp = t("wexp", [128, NH], F32)
    d128 = t("d128", [128, NH], F32)
    qx_b = t("qxb", [128, NSLAB, S], BF16)
    kx_b = t("kxb", [128, NSLAB, S], BF16)
    xn_c = t("xnc", [128, NSLAB, CC], BF16)
    v_tok = t("vtok", [128, 2048], BF16)
    gate_bf = t("gatebf", [128, 2048], BF16)
    gated_bf = t("gatedbf", [128, 2048], BF16)
    sc_bf = [t(f"scbf{i}", [128, 128], BF16) for i in range(2)]
    qdec_bf = [t(f"qdec{i}", [128, 128], BF16) for i in range(2)]
    k_tok = t("ktok", [128, 128], BF16)
    vw_bf = t("vwbf", [128, VD], BF16)
    o_f = t("of", [128, VD], F32)
    n_f = t("nf", [128, VD], F32)
    red = t("red", [128, 6], F32)
    S_sb = t("Ssb", [128, NH, VD], F32)
    S_bf = t("Sbf", [128, NH, VD], BF16)
    gT4 = t("gT4", [128, 16, 512], BF16)
    x_f = t("xf", [128, NSLAB, 512], F32)
    y_f = x_f
    onesr = t("onesr", [1, 128], F32)
    psc = [pt(f"psc{i}", [128, 128]) for i in range(2)]
    ppo = [pt(f"ppo{i}", [128, VD]) for i in range(2)]
    pS = pt("pS", [128, VD])
    pg = pt("pgg", [128, 512])
    pt_b = pt("ptb", [128, 128], BF16)
    py = pt("pyy", [128, 512])
    pbc8 = None
    P.step("sync", lambda e: e.dma_start(out=onesr[:], in_=io.inp("c_ones_row_f", [1, 128], F32)), dma=True, wait=(0, 0))
    P.step("sync", lambda e: e.dma_start(out=identb[:], in_=io.inp("c_ident_bf", [128, 128], BF16)), dma=True, wait=(0, 0))
    P.step("sync", lambda e: e.dma_start(out=expoT_s[:], in_=io.inp("c_expoT", [128, 128], F32)), dma=True, wait=(0, 0))
    P.step("sync", lambda e: e.dma_start(out=cmask_s[:], in_=io.inp("c_cmask_scale", [128, 128], F32)), dma=True, wait=(0, 0))
    P.step("sync", lambda e: e.dma_start(out=iotaJ_s[:], in_=io.inp("c_iotaJ", [128, 128], F32)), dma=True, wait=(0, 0))
    P.step("sync", lambda e: e.dma_start(out=revcol_s[:], in_=io.inp("c_rev_col", [128, 1], F32)), dma=True, wait=(0, 0))
    P.step("sync", lambda e: e.dma_start(out=decrow[:], in_=io.inp("w_decay", [1, 8], F32)), dma=True, wait=(0, 0))
    P.step("gpsimd", lambda e: e.dma_start(out=wgat[:], in_=io.inp("w_gatedT", [D, 2048], F32).rearrange("(k p) m -> p k m", p=128)), dma=True, wait=(0, 0))
    P.step("gpsimd", lambda e: e.dma_start(out=wproj[:], in_=io.inp("w_projT", [2048, D], F32).rearrange("(k p) m -> p k m", p=128)), dma=True, wait=(0, 0))
    P.step("vector", lambda e: e.memset(k128_s[:], 128.0))
    P.step("vector", lambda e: e.memset(red[:, 5:6], 1e-5))
    P.step("scalar", lambda e: e.activation(out=decrow[:], in_=decrow[:], func=AF.Ln))
    P.step("tensor", lambda e: e.matmul(psc[0][:, 0:8], onesr[:], decrow[:], start=True, stop=True))
    P.step("vector", lambda e: e.tensor_copy(lnd_bc[:], psc[0][:, 0:8]))
    for h in range(NH):
        def tab_fn(e, h=h):
            e.activation(out=dmT8[:, h, :], in_=expoT_s[:], func=AF.Exp, scale=lnd_bc[:, h:h + 1])
            e.activation(out=dbc8[:, h, :], in_=iotaJ_s[:], func=AF.Exp, scale=lnd_bc[:, h:h + 1])
            e.activation(out=wexp[:, h:h + 1], in_=revcol_s[:], func=AF.Exp, scale=lnd_bc[:, h:h + 1])
            return e.activation(out=d128[:, h:h + 1], in_=k128_s[:], func=AF.Exp, scale=lnd_bc[:, h:h + 1])
        P.step("scalar", tab_fn)
        def tab2_fn(e, h=h):
            e.tensor_tensor(dmT8[:, h, :], dmT8[:, h, :], cmask_s[:], ALU.mult)
            e.drain()
            return e.tensor_scalar_mul(dbc8[:, h, :], dbc8[:, h, :], 1.0 / math.sqrt(128.0))
        P.step("vector", tab2_fn)
    for b in range(B):
        P.step("sync", lambda e, b=b: e.dma_start(out=qx_b[:], in_=slabs_ap(qxT, b * S, S)), dma=True)
        P.step("sync", lambda e, b=b: e.dma_start(out=kx_b[:], in_=slabs_ap(kxT, b * S, S)), dma=True)
        P.step("vector", lambda e: e.memset(S_sb[:], 0.0))
        P.step("vector", lambda e: e.memset(S_bf[:], 0.0))
        for cc in range(NCH_B):
            t0 = b * S + cc * CC
            lo = cc * CC
            P.step("sync", lambda e, t0=t0: e.dma_start(out=v_tok[:], in_=vtokT[t0:t0 + CC, :]), dma=True)
            P.step("sync", lambda e, t0=t0: e.dma_start(out=xn_c[:], in_=slabs_ap(xnT, t0, CC)), dma=True)
            for n in range(4):
                def mmgate(e, n=n):
                    for k in range(NSLAB):
                        ins = e.matmul(pg[:], xn_c[:, k, :], wgat[:, k, n * 512:(n + 1) * 512],
                                       start=(k == 0), stop=(k == NSLAB - 1))
                    return ins
                P.step("tensor", mmgate)
                P.step("scalar", lambda e, n=n: e.activation(out=gate_bf[:, n * 512:(n + 1) * 512], in_=pg[:], func=AF.Silu))
            def F(h, lo=lo):
                p = h % 2
                def mmsc(e, h=h, lo=lo, p=p):
                    return e.matmul(psc[p][:], kx_b[:, h, lo:lo + CC], qx_b[:, h, lo:lo + CC], start=True, stop=True)
                P.step("tensor", mmsc)
                P.step("vector", lambda e, h=h, p=p: e.tensor_tensor(sc_bf[p][:], psc[p][:], dmT8[:, h, :], ALU.mult))
                P.step("vector", lambda e, h=h, lo=lo, p=p: e.tensor_tensor(qdec_bf[p][:], qx_b[:, h, lo:lo + CC], dbc8[:, h, :], ALU.mult))
                def mmpo(e, h=h, p=p):
                    e.matmul(ppo[p][:], sc_bf[p][:], v_tok[:, h * VD:(h + 1) * VD], start=True, stop=False)
                    return e.matmul(ppo[p][:], qdec_bf[p][:], S_bf[:, h, :], start=False, stop=True)
                P.step("tensor", mmpo)

            def Bk(h, lo=lo):
                p = h % 2
                def ln_fn(e, h=h, p=p):
                    e.tensor_copy(o_f[:], ppo[p][:])
                    e.drain()
                    e.tensor_reduce(red[:, 0:1], o_f[:], mybir.AxisListType.X, ALU.add)
                    e.drain()
                    return e.tensor_scalar_mul(red[:, 0:1], red[:, 0:1], 1.0 / VD)
                P.step("vector", ln_fn)
                P.step("scalar", lambda e: e.activation(out=n_f[:], in_=o_f[:], func=AF.Square, accum_out=red[:, 1:2]))
                def ln2_fn(e, h=h):
                    e.tensor_scalar_mul(red[:, 1:2], red[:, 1:2], 1.0 / VD)
                    e.tensor_tensor(red[:, 2:3], red[:, 0:1], red[:, 0:1], ALU.mult)
                    e.drain()
                    return e.tensor_tensor(red[:, 1:2], red[:, 1:2], red[:, 2:3], ALU.subtract)
                P.step("vector", ln2_fn)
                def rstd_fn(e):
                    e.activation(out=red[:, 1:2], in_=red[:, 1:2], func=AF.Ln, bias=red[:, 5:6])
                    e.drain()
                    return e.activation(out=red[:, 1:2], in_=red[:, 1:2], func=AF.Exp, scale=-0.5)
                P.step("scalar", rstd_fn)
                def gated_fn(e, h=h):
                    e.tensor_tensor(red[:, 3:4], red[:, 0:1], red[:, 1:2], ALU.mult)
                    e.drain()
                    e.tensor_scalar(n_f[:], o_f[:], red[:, 1:2], red[:, 3:4], ALU.mult, ALU.subtract)
                    e.drain()
                    return e.tensor_tensor(gated_bf[:, h * VD:(h + 1) * VD], n_f[:], gate_bf[:, h * VD:(h + 1) * VD], ALU.mult)
                P.step("vector", gated_fn)
                P.step("tensor", lambda e, h=h, lo=lo: e.transpose(pt_b[:], kx_b[:, h, lo:lo + CC], identb[:]))
                P.step("vector", lambda e: e.tensor_copy(k_tok[:], pt_b[:]))
                P.step("vector", lambda e, h=h: e.tensor_scalar(vw_bf[:], v_tok[:, h * VD:(h + 1) * VD], wexp[:, h:h + 1], None, ALU.mult))
                P.step("tensor", lambda e: e.matmul(pS[:], k_tok[:], vw_bf[:], start=True, stop=True))
                def supd_fn(e, h=h):
                    e.scalar_tensor_tensor(S_sb[:, h, :], S_sb[:, h, :], d128[:, h:h + 1], pS[:],
                                           ALU.mult, ALU.add)
                    e.drain()
                    return e.tensor_copy(S_bf[:, h, :], S_sb[:, h, :])
                P.step("vector", supd_fn)
            F(0)
            for h in range(NH):
                if h + 1 < NH:
                    F(h + 1)
                Bk(h)
            for i in range(16):
                P.step("tensor", lambda e, i=i: e.transpose(pt_b[:], gated_bf[:, i * 128:(i + 1) * 128], identb[:]))
                P.step("vector", lambda e, i=i, cc=cc: e.tensor_copy(gT4[:, i, (cc % 4) * 128:(cc % 4 + 1) * 128], pt_b[:]))
            if cc % 4 == 3:
                c0 = t0 + CC - 512
                P.step("sync", lambda e, c0=c0: e.dma_start(out=x_f[:], in_=slabs_ap(x2T, c0, 512)), dma=True)
                for m in range(NSLAB):
                    def mmproj(e, m=m):
                        for k in range(16):
                            ins = e.matmul(py[:], wproj[:, k, m * 128:(m + 1) * 128], gT4[:, k, :],
                                           start=(k == 0), stop=(k == 15))
                        return ins
                    P.step("tensor", mmproj)
                    P.step("vector", lambda e, m=m: e.tensor_tensor(x_f[:, m, :], x_f[:, m, :], py[:], ALU.add))
                P.step("gpsimd", lambda e, c0=c0: e.dma_start(out=slabs_ap(x3T, c0, 512), in_=y_f[:]), dma=True)
    P.emit()


def build_p4a(nc, io, es, name="p4a"):
    x3T = io.stream("x3T", [D, T], F32, produce=False)
    xnT = io.stream("xn4T", [D, T], BF16, produce=True)
    combT = io.stream("comb4T", [8, T], F32, produce=True)
    latT = io.stream("latT", [512, T], BF16, produce=True)
    CW = 512
    NCH = T // CW
    NT4 = CW // 128
    P = Pipe(nc, es, name)
    t = lambda n, sh, dt: es.enter_context(nc.sbuf_tensor(name + "_" + n, sh, dt))
    pt = lambda n, sh: es.enter_context(nc.psum_tensor(name + "_" + n, sh, F32))
    gw = t("gw", [128, NSLAB, 8], BF16)
    fc1 = t("fc1", [128, NSLAB, 512], BF16)
    cbrow = t("cbrow", [1, 8], F32)
    cbc = t("cbc", [128, 8], F32)
    iota8 = t("iota8", [128, 8], F32)
    identf = t("identf", [128, 128], F32)
    x_f = [t(f"x{i}", [128, NSLAB, CW], F32) for i in range(2)]
    xn_sb = [t(f"xn{i}", [128, NSLAB, CW], BF16) for i in range(2)]
    lat_sb = t("lat", [128, 4, CW], BF16)
    sc_f = t("sc", [128, 8], F32)
    sfc = t("sfc", [128, 8], F32)
    pad8 = t("pad8", [128, 16], F32)
    vals = t("vals", [128, 8], F32)
    idxu = t("idxu", [128, 8], U32)
    idxf = t("idxf", [128, 8], F32)
    oh = t("oh", [128, 2, 8], F32)
    og = t("og", [128, 8], F32)
    gs = t("gs", [128, 4], F32)
    comb_tok = t("ctok", [128, 8], F32)
    combt_sb = [t(f"cT{i}", [8, CW], F32) for i in range(2)]
    tmp_f = t("tmp", [128, CW], F32)
    kon = t("kon", [128, 1], F32)
    sb = {
        "ones_col_bf": t("ones", [128, 1], BF16),
        "ones_row_f": t("onesr", [1, 128], F32),
        name + "xbf": t("xbf", [128, NSLAB, CW], BF16),
        name + "sqbf": t("sqbf", [128, NSLAB, CW], BF16),
        name + "stat": t("stat", [1, 6 * CW], F32),
    }
    ps = {
        name + "psx": pt("psx", [1, CW]),
        name + "psq": pt("psq", [1, CW]),
        name + "pbr": pt("pbr", [128, CW]),
        name + "pbp": pt("pbp", [128, CW]),
    }
    psc = pt("psc", [128, 8])
    pct = pt("pct", [8, 128])
    plat = pt("plat", [128, CW])
    eps_ap = kon[0:1, 0:1]
    P.step("vector", lambda e: e.memset(kon[:, 0:1], 1e-5))
    P.step("sync", lambda e: e.dma_start(out=sb["ones_col_bf"][:], in_=io.inp("c_ones_col_bf", [128, 1], BF16)), dma=True, wait=(0, 0))
    P.step("sync", lambda e: e.dma_start(out=sb["ones_row_f"][:], in_=io.inp("c_ones_row_f", [1, 128], F32)), dma=True, wait=(0, 0))
    P.step("sync", lambda e: e.dma_start(out=iota8[:], in_=io.inp("c_iota8_bc", [128, 8], F32)), dma=True, wait=(0, 0))
    P.step("sync", lambda e: e.dma_start(out=identf[:], in_=io.inp("c_ident_f", [128, 128], F32)), dma=True, wait=(0, 0))
    P.step("sync", lambda e: e.dma_start(out=cbrow[:], in_=io.inp("w_corr_bias", [1, 8], F32)), dma=True, wait=(0, 0))
    P.step("gpsimd", lambda e: e.dma_start(out=gw[:], in_=io.inp("w_f_gateT", [D, 8], F32).rearrange("(k p) m -> p k m", p=128)), dma=True, wait=(0, 0))
    P.step("gpsimd", lambda e: e.dma_start(out=fc1[:], in_=io.inp("w_fc1T", [D, 512], F32).rearrange("(k p) m -> p k m", p=128)), dma=True, wait=(0, 0))
    P.step("tensor", lambda e: e.matmul(psc[:], sb["ones_row_f"][:], cbrow[:], start=True, stop=True))
    P.step("vector", lambda e: e.tensor_copy(cbc[:], psc[:]))
    s_ld = P.xsem("ld"); s_st = P.xsem("st")
    st_gate = {}
    for c in range(NCH):
        buf = c % 2
        P.step("sync", lambda e, c=c, buf=buf: e.dma_start(out=x_f[buf][:], in_=slabs_ap(x3T, c * CW, CW)),
               dma=True, wait=(0, 0), inc=s_ld, waits=[(s_st, st_gate.get(c - 2, 0))])
        P.step("sync", lambda e: e.nop(), waits=[(s_ld, s_ld.n)])
        ln_stats_steps(P, sb, ps, x_f[buf], CW, eps_ap, name)
        def ap_fn(e, buf=buf):
            for s in range(NSLAB):
                e.tensor_tensor(tmp_f[:], x_f[buf][:, s, :], ps[name + "pbr"][:], ALU.mult)
                e.drain()
                ins = e.tensor_tensor(xn_sb[buf][:, s, :], tmp_f[:], ps[name + "pbp"][:], ALU.subtract)
                e.drain()
            return ins
        P.step("vector", ap_fn)
        for m in range(4):
            def mml(e, buf=buf, m=m):
                for k in range(NSLAB):
                    ins = e.matmul(plat[:], fc1[:, k, m * 128:(m + 1) * 128],
                                   xn_sb[buf][:, k, :], start=(k == 0), stop=(k == NSLAB - 1))
                return ins
            P.step("tensor", mml)
            P.step("vector", lambda e, m=m: e.tensor_copy(lat_sb[:, m, :], plat[:]))
        P.step("gpsimd", lambda e, c=c: e.dma_start(
            out=latT.rearrange("(s p) t -> p s t", p=128, s=4)[:, :, c * CW:(c + 1) * CW],
            in_=lat_sb[:]), dma=True)
        for t4 in range(NT4):
            def mmsc(e, buf=buf, t4=t4):
                for k in range(NSLAB):
                    ins = e.matmul(psc[:], xn_sb[buf][:, k, t4 * 128:(t4 + 1) * 128],
                                   gw[:, k, :], start=(k == 0), stop=(k == NSLAB - 1))
                return ins
            P.step("tensor", mmsc)
            P.step("scalar", lambda e: e.activation(out=sc_f[:], in_=psc[:], func=AF.Sigmoid))
            def grp_fn(e):
                e.tensor_tensor(sfc[:], sc_f[:], cbc[:], ALU.add)
                e.memset(pad8[:], -3.0e38)
                e.drain()
                e.tensor_copy(pad8[:, 0:4], sfc[:, 0:4])
                e.tensor_copy(pad8[:, 8:12], sfc[:, 4:8])
                e.drain()
                e.max(vals[:], pad8[:, 0:8])
                e.drain()
                e.tensor_tensor(gs[:, 0:1], vals[:, 0:1], vals[:, 1:2], ALU.add)
                e.max(vals[:], pad8[:, 8:16])
                e.drain()
                e.tensor_tensor(gs[:, 1:2], vals[:, 0:1], vals[:, 1:2], ALU.add)
                e.drain()
                e.tensor_tensor(gs[:, 2:3], gs[:, 0:1], gs[:, 1:2], ALU.is_ge)
                e.drain()
                e.tensor_scalar(gs[:, 3:4], gs[:, 2:3], -1.0, 1.0, ALU.mult, ALU.add)
                e.drain()
                e.tensor_scalar(sfc[:, 0:4], sfc[:, 0:4], gs[:, 2:3], None, ALU.mult)
                e.tensor_scalar(sfc[:, 4:8], sfc[:, 4:8], gs[:, 3:4], None, ALU.mult)
                e.drain()
                e.max(vals[:], sfc[:])
                e.drain()
                return e.max_index(idxu[:], vals[:], sfc[:])
            P.step("vector", grp_fn)
            def wsel_fn(e):
                e.tensor_copy(idxf[:], idxu[:])
                e.drain()
                for kk in range(2):
                    e.tensor_scalar(oh[:, kk, :], iota8[:], idxf[:, kk:kk + 1], None, ALU.is_equal)
                e.drain()
                for kk in range(2):
                    e.tensor_tensor(oh[:, kk, :], oh[:, kk, :], sc_f[:], ALU.mult)
                e.drain()
                og3 = og[:].rearrange("p (k o) -> p k o", k=8)
                e.tensor_reduce(og3[:, 0:2, :], oh[:], mybir.AxisListType.X, ALU.add)
                e.drain()
                e.tensor_reduce(og[:, 4:5], og[:, 0:2], mybir.AxisListType.X, ALU.add)
                e.drain()
                e.tensor_scalar_add(og[:, 4:5], og[:, 4:5], 1e-20)
                e.drain()
                return e.reciprocal(og[:, 5:6], og[:, 4:5])
            P.step("vector", wsel_fn)
            def comb_fn(e):
                e.memset(comb_tok[:], 0.0)
                e.drain()
                for kk in range(2):
                    e.scalar_tensor_tensor(comb_tok[:], oh[:, kk, :], og[0:128, 5:6], comb_tok[:],
                                           ALU.mult, ALU.add)
                    e.drain()
                return e.nop()
            P.step("vector", comb_fn)
            P.step("tensor", lambda e: e.transpose(pct[:], comb_tok[:], identf[:]))
            P.step("vector", lambda e, buf=buf, t4=t4: e.tensor_copy(combt_sb[buf][:, t4 * 128:(t4 + 1) * 128], pct[:]))
        P.step("gpsimd", lambda e, c=c, buf=buf: e.dma_start(
            out=slabs_ap(xnT, c * CW, CW), in_=xn_sb[buf][:]), dma=True, inc=s_st)
        P.step("gpsimd", lambda e, c=c, buf=buf: e.dma_start(
            out=combT[:, c * CW:(c + 1) * CW], in_=combt_sb[buf][:]), dma=True, inc=s_st)
        st_gate[c] = s_st.n
    P.emit()


def build_p4b(nc, io, es, name="p4b"):
    latT = io.stream("latT", [512, T], BF16, produce=False)
    combT = io.stream("comb4T", [8, T], F32, produce=False)
    routedT = io.stream("routedT", [D, T], F32, produce=True)
    CW = 512
    NCH = T // CW
    P = Pipe(nc, es, name)
    t = lambda n, sh, dt: es.enter_context(nc.sbuf_tensor(name + "_" + n, sh, dt))
    pt = lambda n, sh: es.enter_context(nc.psum_tensor(name + "_" + n, sh, F32))
    w1 = t("w1", [128, 8, 4, 512], BF16)
    w2 = t("w2", [128, 8, 4, 512], BF16)
    fc2 = t("fc2", [128, 4, D], BF16)
    onesr = t("onesr", [1, 128], F32)
    lat_c = [t(f"lat{i}", [128, 4, CW], BF16) for i in range(2)]
    crow = t("crow", [1, 8 * CW], F32)
    bc_sb = t("bc", [128, CW], F32)
    r_bf = t("rbf", [128, 4, CW], BF16)
    racc = t("racc", [128, 4, CW], F32)
    rq_bf = t("rq", [128, 4, CW], BF16)
    ro_f = t("rof", [128, NSLAB, CW], F32)
    tmp_f = t("tmpq", [128, CW], F32)
    pbc = pt("pbc", [128, CW])
    ph = pt("ph", [128, CW])
    po = pt("po", [128, CW])
    w1ap = io.inp("w_f_w1T", [8, 512, 512], F32)
    w2ap = io.inp("w_f_w2T", [8, 512, 512], F32)
    P.step("sync", lambda e: e.dma_start(out=onesr[:], in_=io.inp("c_ones_row_f", [1, 128], F32)), dma=True, wait=(0, 0))
    for eg in range(8):
        P.step("gpsimd", lambda e, eg=eg: e.dma_start(
            out=w1[:, eg, :, :], in_=w1ap[eg].rearrange("(k p) m -> p k m", p=128)), dma=True, wait=(0, 0))
        P.step("gpsimd", lambda e, eg=eg: e.dma_start(
            out=w2[:, eg, :, :], in_=w2ap[eg].rearrange("(k p) m -> p k m", p=128)), dma=True, wait=(0, 0))
    P.step("gpsimd", lambda e: e.dma_start(
        out=fc2[:], in_=io.inp("w_fc2T", [512, D], F32).rearrange("(k p) m -> p k m", p=128)), dma=True, wait=(0, 0))
    s_ld = P.xsem("ld"); s_st = P.xsem("st")
    st_gate = {}
    for c in range(NCH):
        buf = c % 2
        P.step("sync", lambda e, c=c, buf=buf: e.dma_start(
            out=lat_c[buf][:], in_=latT.rearrange("(s p) t -> p s t", p=128, s=4)[:, :, c * CW:(c + 1) * CW]),
            dma=True, wait=(0, 0), inc=s_ld, waits=[(s_st, st_gate.get(c - 2, 0))])
        P.step("sync", lambda e: e.nop(), waits=[(s_ld, s_ld.n)])
        for eg in range(8):
            P.step("sync", lambda e, c=c, eg=eg: e.dma_start(
                out=crow[:, eg * CW:(eg + 1) * CW], in_=combT[eg:eg + 1, c * CW:(c + 1) * CW]), dma=True)
        P.step("vector", lambda e: e.memset(racc[:], 0.0))
        for eg in range(8):
            def mmbc(e, eg=eg):
                return e.matmul(pbc[:], onesr[:], crow[:, eg * CW:(eg + 1) * CW], start=True, stop=True)
            P.step("tensor", mmbc)
            P.step("vector", lambda e: e.tensor_copy(bc_sb[:], pbc[:]))
            for m in range(4):
                def mmh(e, eg=eg, m=m, buf=buf):
                    for k in range(4):
                        ins = e.matmul(ph[:], w1[:, eg, k, m * 128:(m + 1) * 128],
                                       lat_c[buf][:, k, :], start=(k == 0), stop=(k == 3))
                    return ins
                P.step("tensor", mmh)
                def relu2_fn(e, m=m):
                    e.tensor_relu(tmp_f[:], ph[:])
                    e.drain()
                    return e.tensor_tensor(r_bf[:, m, :], tmp_f[:], tmp_f[:], ALU.mult)
                P.step("vector", relu2_fn)
            for m in range(4):
                def mmo(e, eg=eg, m=m):
                    for k in range(4):
                        ins = e.matmul(po[:], w2[:, eg, k, m * 128:(m + 1) * 128],
                                       r_bf[:, k, :], start=(k == 0), stop=(k == 3))
                    return ins
                P.step("tensor", mmo)
                def racc_fn(e, m=m):
                    e.tensor_tensor(tmp_f[:], po[:], bc_sb[:], ALU.mult)
                    e.drain()
                    sl = racc[:, m, :]
                    return e.tensor_tensor(sl, sl, tmp_f[:], ALU.add)
                P.step("vector", racc_fn)
        P.step("vector", lambda e: e.tensor_copy(rq_bf[:], racc[:]))
        for s in range(NSLAB):
            def mmf(e, s=s):
                for k in range(4):
                    ins = e.matmul(po[:], fc2[:, k, s * 128:(s + 1) * 128],
                                   rq_bf[:, k, :], start=(k == 0), stop=(k == 3))
                return ins
            P.step("tensor", mmf)
            P.step("vector", lambda e, s=s: e.tensor_copy(ro_f[:, s, :], po[:]))
        P.step("gpsimd", lambda e, c=c: e.dma_start(
            out=slabs_ap(routedT, c * CW, CW), in_=ro_f[:]), dma=True, inc=s_st)
        st_gate[c] = s_st.n
    P.emit()


def build_p4c(nc, io, es, name="p4c"):
    x3T = io.stream("x3T", [D, T], F32, produce=False)
    xnT = io.stream("xn4T", [D, T], BF16, produce=False)
    routedT = io.stream("routedT", [D, T], F32, produce=False)
    outT = io.out("outT", [D, T], F32)
    SH = 4096
    NSH = SH // 128
    CW = 256
    NCH = T // CW
    P = Pipe(nc, es, name)
    t = lambda n, sh, dt: es.enter_context(nc.sbuf_tensor(name + "_" + n, sh, dt))
    pt = lambda n, sh: es.enter_context(nc.psum_tensor(name + "_" + n, sh, F32))
    up = t("up", [128, NSLAB, SH], BF16)
    down = t("down", [128, NSH, D], BF16)
    xn_c = [t(f"xn{i}", [128, NSLAB, CW], BF16) for i in range(2)]
    x_f = t("xf", [128, NSLAB, CW], F32)
    r_f = t("rf", [128, NSLAB, CW], F32)
    hs_bf = t("hs", [128, NSH, CW], BF16)
    o_f = t("of", [128, NSLAB, CW], F32)
    tmp_f = t("tmpc", [128, CW], F32)
    ph = pt("ph", [128, CW])
    po = pt("po", [128, CW])
    P.step("gpsimd", lambda e: e.dma_start(
        out=up[:], in_=io.inp("w_upT", [D, SH], F32).rearrange("(k p) m -> p k m", p=128)), dma=True, wait=(0, 0))
    P.step("gpsimd", lambda e: e.dma_start(
        out=down[:], in_=io.inp("w_downT", [SH, D], F32).rearrange("(k p) m -> p k m", p=128)), dma=True, wait=(0, 0))
    s_ld = P.xsem("ld"); s_st = P.xsem("st")
    st_gate = {}
    for c in range(NCH):
        buf = c % 2
        P.step("sync", lambda e, c=c, buf=buf: e.dma_start(out=xn_c[buf][:], in_=slabs_ap(xnT, c * CW, CW)),
               dma=True, wait=(0, 0), inc=s_ld, waits=[(s_st, st_gate.get(c - 2, 0))])
        P.step("sync", lambda e: e.nop(), waits=[(s_ld, s_ld.n)])
        P.step("sync", lambda e, c=c: e.dma_start(out=x_f[:], in_=slabs_ap(x3T, c * CW, CW)), dma=True)
        P.step("sync", lambda e, c=c: e.dma_start(out=r_f[:], in_=slabs_ap(routedT, c * CW, CW)), dma=True)
        for m in range(NSH):
            def mmh(e, m=m, buf=buf):
                for k in range(NSLAB):
                    ins = e.matmul(ph[:], up[:, k, m * 128:(m + 1) * 128],
                                   xn_c[buf][:, k, :], start=(k == 0), stop=(k == NSLAB - 1))
                return ins
            P.step("tensor", mmh)
            def relu2_fn(e, m=m):
                e.tensor_relu(tmp_f[:], ph[:])
                e.drain()
                return e.tensor_tensor(hs_bf[:, m, :], tmp_f[:], tmp_f[:], ALU.mult)
            P.step("vector", relu2_fn)
        for s in range(NSLAB):
            def mmo(e, s=s):
                for k in range(NSH):
                    ins = e.matmul(po[:], down[:, k, s * 128:(s + 1) * 128],
                                   hs_bf[:, k, :], start=(k == 0), stop=(k == NSH - 1))
                return ins
            P.step("tensor", mmo)
            def out_fn(e, s=s):
                e.tensor_tensor(o_f[:, s, :], x_f[:, s, :], r_f[:, s, :], ALU.add)
                e.drain()
                return e.tensor_tensor(o_f[:, s, :], o_f[:, s, :], po[:], ALU.add)
            P.step("vector", out_fn)
        P.step("gpsimd", lambda e, c=c: e.dma_start(
            out=slabs_ap(outT, c * CW, CW), in_=o_f[:]), dma=True, inc=s_st)
        st_gate[c] = s_st.n
    P.emit()


NSLOTS = 96 * 512
NTILES = 96
BIGF = 1.0e9

def build_p2m(nc, io, es, name="p2m"):
    combT = io.stream("combT", [32, T], F32, produce=False)
    selT = io.stream("selT", [T, 4], F32, produce=False)
    wgtT = io.stream("wgtT", [T, 4], F32, produce=False)
    metaT = io.stream("metaT", [NSLOTS, 4], F32, produce=True)
    gidxT = io.stream("gidxT", [128, 2 * NTILES], F32, produce=True)
    w1bf = io.stream("w1bfT", [32, D, 768], BF16, produce=True)
    w3bf = io.stream("w3bfT", [32, D, 768], BF16, produce=True)
    w2bf = io.stream("w2bfT", [32, 768, D], BF16, produce=True)
    P = Pipe(nc, es, name)
    t = lambda n, sh, dt: es.enter_context(nc.sbuf_tensor(name + "_" + n, sh, dt))
    pt = lambda n, sh, dt=F32: es.enter_context(nc.psum_tensor(name + "_" + n, sh, dt))
    comb = t("comb", [32, T], F32)
    m_m = t("mm", [32, T], F32)
    pos = t("pos", [32, T], F32)
    cnt = t("cnt", [32, 8], F32)
    base = t("base", [32, 1], F32)
    ones32 = t("ones32", [32, 1], F32)
    onesT = t("onesT", [32, T], F32)
    ltri = t("ltri", [32, 32], F32)
    tile_iota = t("tiota", [32, NTILES], F32)
    iota_col = t("icol", [128, 1], F32)
    onesr = t("onesr", [1, 128], F32)
    identf = t("identf", [128, 128], F32)
    eb_cmp = t("ebcmp", [32, NTILES], F32)
    eid_row = t("eidr", [1, NTILES], F32)
    gvec = t("gvec", [128, 2 * NTILES], F32)
    base_bc = t("basebc", [128, 32], F32)
    pos_tok = t("postok", [128, 32], F32)
    sel4 = t("sel4", [128, 4], F32)
    wg4 = t("wg4", [128, 4], F32)
    oh4 = t("oh4", [128, 32], F32)
    iota32 = t("iota32b", [128, 32], F32)
    meta_sc = t("metasc", [128, 4], F32)
    slot_i = t("sloti", [128, 1], I32)
    cnt_i = t("cnti", [32, 1], I32)
    red2 = t("red2", [128, 4], F32)
    init_t = t("initt", [128, 384 * 4], F32)
    wconv = [t(f"wconv{i}", [128, NSLAB, 768], BF16) for i in range(2)]
    wconv2 = [t(f"wconv2{i}", [128, 6, D], BF16) for i in range(2)]
    p96 = pt("p96", [1, NTILES])
    pb32 = pt("pb32", [128, 32])
    pbase = pt("pbase", [32, 1])
    ptp = pt("ptp", [128, 128])
    P.step("sync", lambda e: e.dma_start(out=onesr[:], in_=io.inp("c_ones_row_f", [1, 128], F32)), dma=True, wait=(0, 0))
    P.step("sync", lambda e: e.dma_start(out=identf[:], in_=io.inp("c_ident_f", [128, 128], F32)), dma=True, wait=(0, 0))
    P.step("sync", lambda e: e.dma_start(out=ltri[:], in_=io.inp("c_ltri32", [32, 32], F32)), dma=True, wait=(0, 0))
    P.step("sync", lambda e: e.dma_start(out=tile_iota[:], in_=io.inp("c_tile_iota", [32, NTILES], F32)), dma=True, wait=(0, 0))
    P.step("sync", lambda e: e.dma_start(out=iota_col[:], in_=io.inp("c_iota_col", [128, 1], F32)), dma=True, wait=(0, 0))
    P.step("sync", lambda e: e.dma_start(out=ones32[:], in_=io.inp("c_ones32col_f", [32, 1], F32)), dma=True, wait=(0, 0))
    P.step("sync", lambda e: e.dma_start(out=iota32[:], in_=io.inp("c_iota32_bc", [128, 32], F32)), dma=True, wait=(0, 0))
    P.step("sync", lambda e: e.dma_start(out=comb[:], in_=combT[:]), dma=True, wait=(0, 0))
    P.step("vector", lambda e: e.memset(onesT[:], 1.0))
    def scan_fn(e):
        e.tensor_single_scalar(m_m[:], comb[:], 0.0, ALU.is_gt)
        e.drain()
        e.tensor_tensor_scan(pos[:], onesT[:], m_m[:], 0.0, ALU.mult, ALU.add)
        e.drain()
        e.tensor_copy(cnt[:, 0:1], pos[:, T - 1:T])
        e.drain()
        return e.tensor_tensor(pos[:], pos[:], m_m[:], ALU.subtract)
    P.step("vector", scan_fn)
    def pc_fn(e):
        e.tensor_scalar(cnt[:, 1:2], cnt[:, 0:1], 1.0 / 512.0, 255.5 / 512.0, ALU.mult, ALU.add)
        e.drain()
        e.tensor_copy(cnt_i[:], cnt[:, 1:2])
        e.drain()
        e.tensor_copy(cnt[:, 3:4], cnt_i[:])
        e.drain()
        return e.tensor_scalar_mul(cnt[:, 3:4], cnt[:, 3:4], 512.0)
    P.step("vector", pc_fn)
    P.step("tensor", lambda e: e.matmul(pbase[:], ltri[:], cnt[:, 3:4], start=True, stop=True))
    P.step("vector", lambda e: e.tensor_copy(base[:], pbase[:]))
    def eid_fn(e):
        e.tensor_scalar(eb_cmp[:], tile_iota[:], base[:, 0:1], None, ALU.is_ge)
        return e.nop()
    P.step("vector", eid_fn)
    P.step("tensor", lambda e: e.matmul(p96[:], ones32[:], eb_cmp[:], start=True, stop=True))
    def eidr_fn(e):
        return e.tensor_scalar_add(eid_row[:], p96[:], -1.0)
    P.step("vector", eidr_fn)
    P.step("tensor", lambda e: e.matmul(ptp[:, 0:NTILES], onesr[:], eid_row[:], start=True, stop=True))
    def gvec_fn(e):
        e.tensor_scalar(gvec[:, 0:NTILES], ptp[:, 0:NTILES], 1024.0, None, ALU.mult)
        e.tensor_scalar(gvec[:, NTILES:2 * NTILES], ptp[:, 0:NTILES], 768.0, None, ALU.mult)
        e.drain()
        e.tensor_tensor(gvec[:, 0:NTILES], gvec[:, 0:NTILES], iota_col[:, 0:1].broadcast_to((128, NTILES)), ALU.add)
        return e.tensor_tensor(gvec[:, NTILES:2 * NTILES], gvec[:, NTILES:2 * NTILES], iota_col[:, 0:1].broadcast_to((128, NTILES)), ALU.add)
    P.step("vector", gvec_fn)
    P.step("gpsimd", lambda e: e.dma_start(out=gidxT[:], in_=gvec[:]), dma=True)
    baseRow = io.stream("baseRowT", [32, 1], F32, produce=True)
    P.step("gpsimd", lambda e: e.dma_start(out=baseRow[:], in_=base[:]), dma=True)
    P.step("sync", lambda e: e.dma_start(out=m_m[0:1, 0:32], in_=baseRow.rearrange("e o -> o e")), dma=True)
    P.step("tensor", lambda e: e.matmul(pb32[:], onesr[:], m_m[0:1, 0:32], start=True, stop=True))
    P.step("vector", lambda e: e.tensor_copy(base_bc[:], pb32[:]))
    def init_fn(e):
        ini4 = init_t[:].rearrange("p (b c) -> p b c", c=4)
        e.memset(init_t[:], 0.0)
        e.drain()
        e.memset(ini4[:, :, 2:3], float(4 * T))
        return e.nop()
    P.step("vector", init_fn)
    P.step("gpsimd", lambda e: e.dma_start(
        out=metaT.rearrange("(p b) c -> p (b c)", p=128),
        in_=init_t[:]), dma=True)
    s_lw1 = P.xsem("lw1"); s_sw1 = P.xsem("sw1")
    s_lw2 = P.xsem("lw2"); s_sw2 = P.xsem("sw2")
    wpairs = []
    for eg in range(32):
        wpairs.append(("w1", eg))
        wpairs.append(("w3", eg))
        wpairs.append(("w2", eg))
    wstate = {"n1": 0, "n2": 0}
    sw_hist1 = {}
    sw_hist2 = {}

    def emit_wpair():
        kind, eg = wpairs.pop(0)
        if kind == "w2":
            k = wstate["n2"]; wstate["n2"] += 1
            par = k % 2
            P.step("gpsimd", lambda e, eg=eg, par=par: e.dma_start(
                out=wconv2[par][:], in_=io.inp("w_moe_w2", [32, 768, D], F32)[eg].rearrange("(k p) m -> p k m", p=128)),
                dma=True, wait=(0, 0), inc=s_lw2, waits=[(s_sw2, sw_hist2.get(k - 2, 0))])
            lwn = s_lw2.n
            P.step("sync", lambda e, eg=eg, par=par: e.dma_start(
                out=w2bf[eg].rearrange("(k p) m -> p k m", p=128), in_=wconv2[par][:]),
                dma=True, wait=(0, 0), inc=s_sw2, waits=[(s_lw2, lwn)])
            sw_hist2[k] = s_sw2.n
        else:
            src_name = "w_moe_w1" if kind == "w1" else "w_moe_w3"
            dst = w1bf if kind == "w1" else w3bf
            k = wstate["n1"]; wstate["n1"] += 1
            par = k % 2
            P.step("gpsimd", lambda e, eg=eg, par=par, src_name=src_name: e.dma_start(
                out=wconv[par][:], in_=io.inp(src_name, [32, D, 768], F32)[eg].rearrange("(k p) m -> p k m", p=128)),
                dma=True, wait=(0, 0), inc=s_lw1, waits=[(s_sw1, sw_hist1.get(k - 2, 0))])
            lwn = s_lw1.n
            P.step("sync", lambda e, eg=eg, par=par, dst=dst: e.dma_start(
                out=dst[eg].rearrange("(k p) m -> p k m", p=128), in_=wconv[par][:]),
                dma=True, wait=(0, 0), inc=s_sw1, waits=[(s_lw1, lwn)])
            sw_hist1[k] = s_sw1.n

    for st in range(T // 128):
        if wpairs:
            emit_wpair()
        if st % 2 == 0 and wpairs:
            emit_wpair()
        t0 = st * 128
        P.step("sync", lambda e, t0=t0: e.dma_start(out=sel4[:], in_=selT[t0:t0 + 128, :]), dma=True)
        P.step("sync", lambda e, t0=t0: e.dma_start(out=wg4[:], in_=wgtT[t0:t0 + 128, :]), dma=True)
        P.step("tensor", lambda e, t0=t0: e.transpose(ptp[:, 0:32], pos[:, t0:t0 + 128], identf[0:32, 0:32]))
        P.step("vector", lambda e: e.tensor_copy(pos_tok[:], ptp[:, 0:32]))
        for kk in range(4):
            def meta_fn(e, kk=kk, t0=t0):
                e.tensor_scalar(oh4[:], iota32[:], sel4[:, kk:kk + 1], None, ALU.is_equal)
                e.drain()
                e.tensor_tensor(oh4[:], oh4[:], pos_tok[:], ALU.mult)
                e.drain()
                e.tensor_reduce(red2[:, 0:1], oh4[:], mybir.AxisListType.X, ALU.add)
                e.tensor_scalar(oh4[:], iota32[:], sel4[:, kk:kk + 1], None, ALU.is_equal)
                e.drain()
                e.tensor_tensor(oh4[:], oh4[:], base_bc[:], ALU.mult)
                e.drain()
                e.tensor_reduce(red2[:, 1:2], oh4[:], mybir.AxisListType.X, ALU.add)
                e.drain()
                e.tensor_tensor(red2[:, 0:1], red2[:, 0:1], red2[:, 1:2], ALU.add)
                e.drain()
                e.tensor_copy(meta_sc[:, 0:1], wg4[:, kk:kk + 1])
                e.tensor_scalar(meta_sc[:, 1:2], iota_col[:], 1.0, float(t0), ALU.mult, ALU.add)
                e.tensor_scalar(meta_sc[:, 2:3], iota_col[:], 1.0, float(kk * T + t0), ALU.mult, ALU.add)
                e.memset(meta_sc[:, 3:4], 0.0)
                e.drain()
                return e.tensor_copy(slot_i[:], red2[:, 0:1])
            P.step("vector", meta_fn)
            P.step("gpsimd", lambda e: e.indirect_dma_start(
                out=metaT, out_offset=IndirectOffsetOnAxis(slot_i[:, 0:1], 0),
                in_=meta_sc[:], in_offset=None), dma=True)
    while wpairs:
        emit_wpair()
    P.emit()


def build_p2s(nc, io, es, name="p2s"):
    xtokT = io.stream("xn2tokT", [T, D], BF16, produce=False)
    metaT = io.stream("metaT", [NSLOTS, 4], F32, produce=False)
    gidxT = io.stream("gidxT", [128, 2 * NTILES], F32, produce=False)
    w1bf = io.stream("w1bfT", [32, D, 768], BF16, produce=False)
    w3bf = io.stream("w3bfT", [32, D, 768], BF16, produce=False)
    w2bf = io.stream("w2bfT", [32, 768, D], BF16, produce=False)
    moe4 = io.stream("moe4T", [4 * T + 128, D], F32, produce=True)
    NI = 6
    P = Pipe(nc, es, name)
    t = lambda n, sh, dt: es.enter_context(nc.sbuf_tensor(name + "_" + n, sh, dt))
    pt = lambda n, sh, dt=F32: es.enter_context(nc.psum_tensor(name + "_" + n, sh, dt))
    identb = t("identb", [128, 128], BF16)
    gvec = t("gvec", [128, 2 * NTILES], F32)
    idx13f = t("idx13f", [128, NSLAB * NTILES], F32)
    idx13 = t("idx13", [128, NSLAB * NTILES], I32)
    idx2f = t("idx2f", [128, NI * NTILES], F32)
    idx2 = t("idx2", [128, NI * NTILES], I32)
    meta4 = [t(f"meta4{i}", [128, 4, 4], F32) for i in range(2)]
    tok_i = [t(f"toki{i}", [128, 4], I32) for i in range(2)]
    dst_i = [t(f"dsti{i}", [128, 4], I32) for i in range(2)]
    w1g = [t(f"w1g{i}", [128, NSLAB, 768], BF16) for i in range(2)]
    w3g = [t(f"w3g{i}", [128, NSLAB, 768], BF16) for i in range(2)]
    w2g = [t(f"w2g{i}", [128, NI, D], BF16) for i in range(2)]
    xga = [t(f"xga{i}", [128, 4, D], BF16) for i in range(2)]
    xgT = [t(f"xgT{i}", [128, NSLAB, 512], BF16) for i in range(2)]
    sg_bf = t("sg", [128, 512], BF16)
    a_bf = t("abf", [128, NI, 512], BF16)
    y_f = t("yf", [128, D], F32)
    ph = pt("ph", [128, 512])
    pu = pt("pu", [128, 512])
    pyA = pt("pyA", [128, 512])
    pyB = pt("pyB", [128, 512])
    pxt = pt("pxt", [128, 128], BF16)
    P.step("sync", lambda e: e.dma_start(out=identb[:], in_=io.inp("c_ident_bf", [128, 128], BF16)), dma=True, wait=(0, 0))
    P.step("sync", lambda e: e.dma_start(out=gvec[:], in_=gidxT[:]), dma=True, wait=(0, 0))
    def idx_fn(e):
        for k in range(NSLAB):
            e.tensor_scalar(idx13f[:, k * NTILES:(k + 1) * NTILES], gvec[:, 0:NTILES], 1.0, float(k * 128), ALU.mult, ALU.add)
        for k in range(NI):
            e.tensor_scalar(idx2f[:, k * NTILES:(k + 1) * NTILES], gvec[:, NTILES:2 * NTILES], 1.0, float(k * 128), ALU.mult, ALU.add)
        e.drain()
        e.tensor_copy(idx13[:], idx13f[:])
        e.drain()
        return e.tensor_copy(idx2[:], idx2f[:])
    P.step("vector", idx_fn)
    s_ld = P.xsem("ld")
    meta_n = {}
    ld_full = {}

    def G(i):
        wb = i % 2
        P.step("gpsimd", lambda e, i=i, wb=wb: e.dma_start(
            out=meta4[wb][:], in_=metaT[i * 512:(i + 1) * 512, :].rearrange("(s p) c -> p s c", p=128)), dma=True, inc=s_ld)
        meta_n[i] = s_ld.n
        for k in range(NSLAB):
            P.step("gpsimd", lambda e, i=i, k=k, wb=wb: e.indirect_dma_start(
                out=w1g[wb][:, k, :], out_offset=None,
                in_=w1bf.rearrange("e k m -> (e k) m"),
                in_offset=IndirectOffsetOnAxis(idx13[:, k * NTILES + i:k * NTILES + i + 1], 0)), dma=True, inc=s_ld)
            P.step("gpsimd", lambda e, i=i, k=k, wb=wb: e.indirect_dma_start(
                out=w3g[wb][:, k, :], out_offset=None,
                in_=w3bf.rearrange("e k m -> (e k) m"),
                in_offset=IndirectOffsetOnAxis(idx13[:, k * NTILES + i:k * NTILES + i + 1], 0)), dma=True, inc=s_ld)
        for k in range(NI):
            P.step("gpsimd", lambda e, i=i, k=k, wb=wb: e.indirect_dma_start(
                out=w2g[wb][:, k, :], out_offset=None,
                in_=w2bf.rearrange("e k m -> (e k) m"),
                in_offset=IndirectOffsetOnAxis(idx2[:, k * NTILES + i:k * NTILES + i + 1], 0)), dma=True, inc=s_ld)
        def toks_fn(e, wb=wb):
            e.tensor_copy(tok_i[wb][:], meta4[wb][:, :, 1])
            return e.tensor_copy(dst_i[wb][:], meta4[wb][:, :, 2])
        P.step("vector", toks_fn, waits=[(s_ld, meta_n[i])])
        for s4 in range(4):
            P.step("gpsimd", lambda e, s4=s4, wb=wb: e.indirect_dma_start(
                out=xga[wb][:, s4, :], out_offset=None,
                in_=xtokT, in_offset=IndirectOffsetOnAxis(tok_i[wb][:, s4:s4 + 1], 0)), dma=True, inc=s_ld)
        ld_full[i] = s_ld.n

    def C(i):
        wb = i % 2
        first = True
        for s4 in range(4):
            for s in range(NSLAB):
                w = [(s_ld, ld_full[i])] if first else ()
                first = False
                P.step("tensor", lambda e, s4=s4, s=s, wb=wb: e.transpose(
                    pxt[:], xga[wb][:, s4, s * 128:(s + 1) * 128], identb[:]), waits=w)
                P.step("vector", lambda e, s4=s4, s=s, wb=wb: e.tensor_copy(
                    xgT[wb][:, s, s4 * 128:(s4 + 1) * 128], pxt[:]))
        for m in range(NI):
            def mmg(e, m=m, wb=wb):
                for k in range(NSLAB):
                    ins = e.matmul(ph[:], w1g[wb][:, k, m * 128:(m + 1) * 128],
                                   xgT[wb][:, k, :], start=(k == 0), stop=(k == NSLAB - 1))
                return ins
            P.step("tensor", mmg)
            P.step("scalar", lambda e: e.activation(out=sg_bf[:], in_=ph[:], func=AF.Silu))
            def mmu(e, m=m, wb=wb):
                for k in range(NSLAB):
                    ins = e.matmul(pu[:], w3g[wb][:, k, m * 128:(m + 1) * 128],
                                   xgT[wb][:, k, :], start=(k == 0), stop=(k == NSLAB - 1))
                return ins
            P.step("tensor", mmu)
            P.step("vector", lambda e, m=m: e.tensor_tensor(a_bf[:, m, :], sg_bf[:], pu[:], ALU.mult))
        for s4 in range(4):
            def mmy(e, s4=s4, wb=wb):
                for k in range(NI):
                    e.matmul(pyA[:], a_bf[:, k, s4 * 128:(s4 + 1) * 128],
                             w2g[wb][:, k, 0:512], start=(k == 0), stop=(k == NI - 1))
                for k in range(NI):
                    ins = e.matmul(pyB[:], a_bf[:, k, s4 * 128:(s4 + 1) * 128],
                                   w2g[wb][:, k, 512:1024], start=(k == 0), stop=(k == NI - 1))
                return ins
            P.step("tensor", mmy)
            def yev(e, s4=s4, wb=wb):
                e.tensor_scalar(y_f[:, 0:512], pyA[:], meta4[wb][:, s4, 0:1], None, ALU.mult)
                return e.tensor_scalar(y_f[:, 512:1024], pyB[:], meta4[wb][:, s4, 0:1], None, ALU.mult)
            P.step("vector", yev)
            P.step("gpsimd", lambda e, s4=s4, wb=wb: e.indirect_dma_start(
                out=moe4, out_offset=IndirectOffsetOnAxis(dst_i[wb][:, s4:s4 + 1], 0),
                in_=y_f[:], in_offset=None), dma=True)

    G(0)
    G(1)
    for i in range(NTILES):
        C(i)
        if i + 2 < NTILES:
            G(i + 2)
    P.emit()


def build_p2f(nc, io, es, name="p2f"):
    x1T = io.stream("x1T", [D, T], F32, produce=False)
    moe4 = io.stream("moe4T", [4 * T + 128, D], F32, produce=False)
    x2T = io.stream("x2T", [D, T], F32, produce=True)
    P = Pipe(nc, es, name)
    t = lambda n, sh, dt: es.enter_context(nc.sbuf_tensor(name + "_" + n, sh, dt))
    pt = lambda n, sh, dt=F32: es.enter_context(nc.psum_tensor(name + "_" + n, sh, dt))
    identf = t("identf", [128, 128], F32)
    mo = t("mo", [128, 4, D], F32)
    msum = t("msum", [128, D], F32)
    x_f = t("xf", [128, NSLAB, 128], F32)
    o_f = t("of", [128, NSLAB, 128], F32)
    ptp = pt("ptp", [128, 128])
    P.step("sync", lambda e: e.dma_start(out=identf[:], in_=io.inp("c_ident_f", [128, 128], F32)), dma=True, wait=(0, 0))
    for st in range(T // 128):
        t0 = st * 128
        for r in range(4):
            P.step("sync", lambda e, t0=t0, r=r: e.dma_start(
                out=mo[:, r, :], in_=moe4[r * T + t0:r * T + t0 + 128, :]), dma=True)
        P.step("sync", lambda e, t0=t0: e.dma_start(out=x_f[:], in_=slabs_ap(x1T, t0, 128)), dma=True)
        def sum_fn(e):
            e.tensor_tensor(msum[:], mo[:, 0, :], mo[:, 1, :], ALU.add)
            e.tensor_tensor(mo[:, 2, :], mo[:, 2, :], mo[:, 3, :], ALU.add)
            e.drain()
            return e.tensor_tensor(msum[:], msum[:], mo[:, 2, :], ALU.add)
        P.step("vector", sum_fn)
        for s in range(NSLAB):
            P.step("tensor", lambda e, s=s: e.transpose(ptp[:], msum[:, s * 128:(s + 1) * 128], identf[:]))
            def add_fn(e, s=s):
                return e.tensor_tensor(o_f[:, s, :], x_f[:, s, :], ptp[:], ALU.add)
            P.step("vector", add_fn)
        P.step("gpsimd", lambda e, t0=t0: e.dma_start(out=slabs_ap(x2T, t0, 128), in_=o_f[:]), dma=True)
    P.emit()


def _build_launch(build_fns, fused=False):
    nc = bass.Bass("TRN2", target_bir_lowering=False, debug=False)
    from contextlib import ExitStack
    io = IO(nc, fused)
    if not fused:
        with ExitStack() as es:
            for fn in build_fns:
                fn(nc, io, es)
        return nc
    with ExitStack() as es_sem:
        _FUSE["on"] = True
        _FUSE["es_sem"] = es_sem
        _FUSE["bar"] = es_sem.enter_context(nc.semaphore("fusebar"))
        _FUSE["k"] = 0
        try:
            for fn in build_fns:
                with ExitStack() as es:
                    fn(nc, io, es)
        finally:
            _FUSE["on"] = False
            _FUSE["es_sem"] = None
            _FUSE["bar"] = None
            _FUSE["k"] = 0
    return nc


FUSED = True


def _run_fused(in_maps):
    fns = [f for n, (fs, o) in ALL_LAUNCHES for f in fs]
    nc = _build_launch(fns, fused=True)
    need = _declared_inputs(nc)
    maps = [{k: m[k] for k in need} for m in in_maps]
    res = run_bass_kernel_spmd(nc, maps, core_ids=list(range(NCORES)))
    return [{"outT": np.asarray(res.results[c]["outT"])} for c in range(NCORES)]


def _shard_inputs(inputs, consts):
    x = inputs["L0_attn_x"]
    base = {}
    base.update(consts)
    for k in ["Wq", "Wk", "Wv", "Wa", "Wg", "Wo", "W1", "W2"]:
        base["w_" + k] = np.ascontiguousarray(inputs["L0_attn_" + k])
    for k in ["gn_scale", "gn_bias", "ln1_scale", "ln1_bias", "b1", "b2",
              "ln2_scale", "ln2_bias"]:
        base["w_" + k] = np.ascontiguousarray(inputs["L0_attn_" + k])
    base["w_gate_wT"] = np.ascontiguousarray(inputs["L0_ffn_gate_w"].T)
    base["w_moe_w1"] = np.ascontiguousarray(inputs["L0_ffn_w1"])
    base["w_moe_w3"] = np.ascontiguousarray(inputs["L0_ffn_w3"])
    base["w_moe_w2"] = np.ascontiguousarray(inputs["L0_ffn_w2"])
    base["w_e_bias"] = np.ascontiguousarray(inputs["L0_ffn_e_bias"].reshape(1, 32))
    base["w_qkvT"] = np.ascontiguousarray(inputs["L1_attn_w_qkv"].T)
    base["w_gatedT"] = np.ascontiguousarray(inputs["L1_attn_w_gated"].T)
    base["w_projT"] = np.ascontiguousarray(inputs["L1_attn_w_proj"].T)
    base["w_decay"] = np.ascontiguousarray(inputs["L1_attn_decay"].reshape(1, 8))
    base["w_f_gateT"] = np.ascontiguousarray(inputs["L1_ffn_gate"].T)
    base["w_corr_bias"] = np.ascontiguousarray(inputs["L1_ffn_corr_bias"].reshape(1, 8))
    base["w_f_w1T"] = np.ascontiguousarray(inputs["L1_ffn_w1"].transpose(0, 2, 1))
    base["w_f_w2T"] = np.ascontiguousarray(inputs["L1_ffn_w2"].transpose(0, 2, 1))
    base["w_fc1T"] = np.ascontiguousarray(inputs["L1_ffn_fc1_w"].T)
    base["w_fc2T"] = np.ascontiguousarray(inputs["L1_ffn_fc2_w"].T)
    base["w_upT"] = np.ascontiguousarray(inputs["L1_ffn_up_w"].T)
    base["w_downT"] = np.ascontiguousarray(inputs["L1_ffn_down_w"].T)
    in_maps = []
    for c in range(NCORES):
        m = dict(base)
        m["xT"] = np.ascontiguousarray(
            x[B * c:B * (c + 1)].transpose(2, 0, 1).reshape(D, T))
        in_maps.append(m)
    return in_maps


ALL_LAUNCHES = [
    ("p1a", ([lambda nc, io, es: build_p1a(nc, io, es, "xT", "xn1T")], ["xn1T"])),
    ("p1b", ([build_p1b], ["kvT"])),
    ("p1c", ([build_p1c], ["lrT", "thT"])),
    ("p1d", ([build_p1d], ["yrT", "gnab"])),
    ("p1e", ([build_p1e], ["v1T"])),
    ("p1f0", ([lambda nc, io, es: build_p1f(nc, io, es, 0)], ["ffacc"])),
    ("p1f1", ([lambda nc, io, es: build_p1f(nc, io, es, 1)], ["x1T"])),
    ("p2a", ([build_p2a], ["xn2T", "combT", "xn2tokT", "selT", "wgtT"])),
    ("p2m", ([build_p2m], ["metaT", "gidxT", "w1bfT", "w3bfT", "w2bfT", "baseRowT"])),
    ("p2s", ([build_p2s], ["moe4T"])),
    ("p2f", ([build_p2f], ["x2T"])),
    ("p3a", ([build_p3a], ["xn3T", "qxT", "kxT", "vtokT"])),
    ("p3b", ([build_p3b], ["x3T"])),
    ("p4a", ([build_p4a], ["xn4T", "comb4T", "latT"])),
    ("p4b", ([build_p4b], ["routedT"])),
    ("p4c", ([build_p4c], ["outT"])),
]
LAUNCHES = [(n, f, o) for n, (f, o) in ALL_LAUNCHES]


def _declared_inputs(nc):
    names = set()
    for alloc in nc.m.functions[0].allocations:
        if isinstance(alloc, mybir.MemoryLocationSet) and alloc.kind == "ExternalInput":
            names.add(alloc.memorylocations[0].name)
    if nc.partition_id_tensor is not None:
        names.discard(nc.partition_id_tensor.name)
    return names


def _run_launches(launches, in_maps):
    carry = [dict() for _ in range(NCORES)]
    for name, fns, outs in launches:
        nc = _build_launch(fns)
        need = _declared_inputs(nc)
        maps = []
        for c in range(NCORES):
            m = {}
            for k in need:
                if k in carry[c]:
                    m[k] = carry[c][k]
                elif k in in_maps[c]:
                    m[k] = in_maps[c][k]
                else:
                    raise KeyError(f"launch {name}: missing input {k}")
            maps.append(m)
        res = run_bass_kernel_spmd(nc, maps, core_ids=list(range(NCORES)))
        for c in range(NCORES):
            for o in outs:
                carry[c][o] = np.asarray(res.results[c][o])
    return carry


def kernel(**inputs):
    consts = host_consts()
    in_maps = _shard_inputs(inputs, consts)
    if FUSED:
        carry = _run_fused(in_maps)
    else:
        carry = _run_launches(LAUNCHES, in_maps)
    outs = []
    for c in range(NCORES):
        o = np.ascontiguousarray(carry[c]["outT"].astype(np.float32))
        outs.append(o.reshape(D, B, S).transpose(1, 2, 0))
    return np.concatenate(outs, axis=0)


# version 08
# speedup vs baseline: 1.0024x; 1.0024x over previous
import math
import numpy as np
import ml_dtypes
import concourse.bass as bass
import concourse.mybir as mybir
from concourse.bass_utils import run_bass_kernel_spmd
from concourse.bass import IndirectOffsetOnAxis

F32 = mybir.dt.float32
BF16 = mybir.dt.bfloat16
I32 = mybir.dt.int32
U32 = mybir.dt.uint32
AF = mybir.ActivationFunctionType
ALU = mybir.AluOpType
BF = ml_dtypes.bfloat16

NCORES = 8
B = 4
S = 2048
D = 1024
T = B * S
NSLAB = D // 128

ENGINES = ("sync", "tensor", "vector", "scalar", "gpsimd")


_FUSE = {"on": False, "es_sem": None, "bar": None, "k": 0}


class XSem:
    def __init__(self, es, nc, name):
        self.h = es.enter_context(nc.semaphore(name))
        self.n = 0


class Pipe:

    def __init__(self, nc, es, name):
        self.nc = nc
        self.es = es
        self.sem_es = _FUSE["es_sem"] if _FUSE["on"] else es
        self.name = name
        self.s_cmp = self.sem_es.enter_context(nc.semaphore(name + "_c"))
        self.s_dma = self.sem_es.enter_context(nc.semaphore(name + "_d"))
        self.ncmp = 0
        self.ndma = 0
        self.xsems = []
        self.steps = {e: [] for e in ENGINES}

    def xsem(self, suffix):
        s = XSem(self.sem_es, self.nc, self.name + "_" + suffix)
        self.xsems.append(s)
        return s

    def at(self):
        return (self.ncmp, self.ndma)

    def step(self, engine, fn, dma=False, wait=None, tag="", inc=None, waits=()):
        w = wait if wait is not None else (self.ncmp, self.ndma)
        self.steps[engine].append((fn, dma, w, tuple(waits), inc))
        if inc is not None:
            inc.n += 16 if dma else 1
        elif dma:
            self.ndma += 16
        else:
            self.ncmp += 1
        return self.at()

    def emit(self):
        bar = _FUSE["bar"] if _FUSE["on"] else None
        bark = _FUSE["k"] if _FUSE["on"] else 0
        blk = self.nc.Block(self.name)
        with blk:
            for eng_name in ENGINES:
                lst = self.steps[eng_name]
                if not lst:
                    continue

                def body(e, lst=lst):
                    if bar is not None and bark > 0:
                        e.wait_ge(bar, bark)
                    for fn, dma, (wc, wd), xw, inc in lst:
                        if wc > 0:
                            e.wait_ge(self.s_cmp, wc)
                        if wd > 0:
                            e.wait_ge(self.s_dma, wd)
                        for xs, val in xw:
                            if val > 0:
                                e.wait_ge(xs.h, val)
                        ins = fn(e)
                        if inc is not None:
                            ins.then_inc(inc.h, 16 if dma else 1)
                        elif dma:
                            ins.then_inc(self.s_dma, 16)
                        else:
                            ins.then_inc(self.s_cmp, 1)

                getattr(blk, eng_name)(body)
            def fin(e):
                if bar is not None and bark > 0:
                    e.wait_ge(bar, bark)
                if self.ncmp:
                    e.wait_ge(self.s_cmp, self.ncmp)
                if self.ndma:
                    e.wait_ge(self.s_dma, self.ndma)
                for xs in self.xsems:
                    if xs.n:
                        e.wait_ge(xs.h, xs.n)
                if bar is not None:
                    e.sem_inc(bar, 1)
            blk.sync(fin)
        if _FUSE["on"]:
            _FUSE["k"] += 1


class IO:
    def __init__(self, nc, fused):
        self.nc = nc
        self.fused = fused
        self.aps = {}

    def inp(self, name, shape, dt):
        if name not in self.aps:
            self.aps[name] = self.nc.dram_tensor(name, shape, dt, kind="ExternalInput").ap()
        return self.aps[name]

    def out(self, name, shape, dt):
        if name not in self.aps:
            self.aps[name] = self.nc.dram_tensor(name, shape, dt, kind="ExternalOutput").ap()
        return self.aps[name]

    def stream(self, name, shape, dt, produce):
        if name not in self.aps:
            kind = "Internal" if self.fused else ("ExternalOutput" if produce else "ExternalInput")
            self.aps[name] = self.nc.dram_tensor(name, shape, dt, kind=kind).ap()
        return self.aps[name]


def slabs_ap(hbm_ap, c0, cw, nslab=NSLAB):
    return hbm_ap.rearrange("(s p) t -> p s t", p=128)[:, :, c0:c0 + cw]


def host_consts():
    c = {}
    c["c_ones_col_bf"] = np.ones((128, 1), BF)
    c["c_ones_row_f"] = np.ones((1, 128), np.float32)
    c["c_ident_bf"] = np.eye(128, dtype=BF)
    c["c_ident_f"] = np.eye(128, dtype=np.float32)
    c["c_ones64_f"] = np.ones((128, 64), np.float32)
    c["c_iota32_bc"] = np.broadcast_to(np.arange(32, dtype=np.float32), (128, 32)).copy()
    c["c_iota8_bc"] = np.broadcast_to(np.arange(8, dtype=np.float32), (128, 8)).copy()
    grp = np.zeros((128, 4), np.float32)
    for p in range(128):
        grp[p, p // 32] = 1.0
    c["c_grp_ones"] = grp
    c["c_grp_selT"] = np.ascontiguousarray(grp.T)
    A = np.zeros((128, 128), np.float32)
    for i in range(64):
        A[2 * i, 2 * i + 1] = -1.0
        A[2 * i + 1, 2 * i] = 1.0
    c["c_rot_lhsT"] = np.ascontiguousarray(A.T).astype(BF)
    SCALE_BASE = 512.0
    half = D // 2
    scale_vec = (np.arange(0, D, 2, dtype=np.float64) + 0.4 * D) / (1.4 * D)
    pos = np.arange(S, dtype=np.float64)
    scale = scale_vec[None, :] ** (pos[:, None] / SCALE_BASE)
    inv_freq = 1.0 / 10000.0 ** (np.arange(half, dtype=np.float64) / half)
    sinus = pos[:, None] * inv_freq[None, :]
    sn, cs = np.sin(sinus), np.cos(sinus)
    dup = lambda m: np.repeat(m, 2, axis=1)
    c["c_tqc"] = np.ascontiguousarray(dup(cs * scale).T).astype(BF)
    c["c_tqs"] = np.ascontiguousarray(dup(sn * scale).T).astype(BF)
    c["c_tkc"] = np.ascontiguousarray(dup(cs / scale).T).astype(BF)
    c["c_tks"] = np.ascontiguousarray(dup(sn / scale).T).astype(BF)
    tt = np.arange(128)
    expoT = (tt[None, :] - tt[:, None]).astype(np.float32)
    c["c_expoT"] = np.maximum(expoT, 0.0)
    c["c_cmask_scale"] = ((expoT >= 0).astype(np.float32) * (1.0 / math.sqrt(128.0)))
    c["c_iotaJ"] = np.broadcast_to(np.arange(128, dtype=np.float32), (128, 128)).copy()
    c["c_rev_col"] = (128.0 - np.arange(128, dtype=np.float32)).reshape(128, 1)
    c["c_iota_col"] = np.arange(128, dtype=np.float32).reshape(128, 1)
    L = np.zeros((32, 32), np.float32)
    for k_ in range(32):
        for m_ in range(32):
            if k_ < m_:
                L[k_, m_] = 1.0
    c["c_ltri32"] = L
    c["c_tile_iota"] = np.broadcast_to(512.0 * np.arange(96, dtype=np.float32), (32, 96)).copy()
    c["c_ones32col_f"] = np.ones((32, 1), np.float32)
    return c


def ln_stats_steps(P, sb, ps, xf_getter, cw, eps_ap, prefix):
    CW = cw
    xbf, sqbf, stat = sb[prefix + "xbf"], sb[prefix + "sqbf"], sb[prefix + "stat"]
    psx, psq, pbr, pbp = ps[prefix + "psx"], ps[prefix + "psq"], ps[prefix + "pbr"], ps[prefix + "pbp"]
    ones_col, ones_row = sb["ones_col_bf"], sb["ones_row_f"]
    xf = xf_getter

    def cast(e):
        e.tensor_copy(xbf[:], xf[:])
        e.drain()
        return e.tensor_tensor(sqbf[:], xbf[:], xbf[:], ALU.mult)
    P.step("vector", cast)

    def statmm(e):
        for s in range(NSLAB):
            e.matmul(psx[:], ones_col[:], xbf[:, s, :], start=(s == 0), stop=(s == NSLAB - 1))
        for s in range(NSLAB):
            ins = e.matmul(psq[:], ones_col[:], sqbf[:, s, :], start=(s == 0), stop=(s == NSLAB - 1))
        return ins
    P.step("tensor", statmm)

    def tiny(e):
        mu = stat[:, 0:CW]; m2 = stat[:, CW:2 * CW]; var = stat[:, 2 * CW:3 * CW]
        tmp = stat[:, 5 * CW:6 * CW]
        e.tensor_scalar_mul(mu[:], psx[:], 1.0 / D)
        e.tensor_scalar_mul(m2[:], psq[:], 1.0 / D)
        e.drain()
        e.tensor_tensor(tmp[:], mu[:], mu[:], ALU.mult)
        e.drain()
        return e.tensor_tensor(var[:], m2[:], tmp[:], ALU.subtract)
    P.step("vector", tiny)

    def rstd_fn(e):
        var = stat[:, 2 * CW:3 * CW]; rstd = stat[:, 3 * CW:4 * CW]
        e.activation(out=rstd[:], in_=var[:], func=AF.Ln, bias=eps_ap)
        e.drain()
        return e.activation(out=rstd[:], in_=rstd[:], func=AF.Exp, scale=-0.5)
    P.step("scalar", rstd_fn)

    def prod_fn(e):
        mu = stat[:, 0:CW]; rstd = stat[:, 3 * CW:4 * CW]; prod = stat[:, 4 * CW:5 * CW]
        return e.tensor_tensor(prod[:], mu[:], rstd[:], ALU.mult)
    P.step("vector", prod_fn)

    def bcast(e):
        rstd = stat[:, 3 * CW:4 * CW]; prod = stat[:, 4 * CW:5 * CW]
        e.matmul(pbr[:], ones_row[:], rstd[:], start=True, stop=True)
        return e.matmul(pbp[:], ones_row[:], prod[:], start=True, stop=True)
    P.step("tensor", bcast)


def build_p1a(nc, io, es, src, dst, eps=1e-5, name="p1a"):
    xT = io.inp(src, [D, T], F32) if src == "xT" else io.stream(src, [D, T], F32, produce=False)
    xnT = io.stream(dst, [D, T], BF16, produce=True)
    CW = 512
    NCH = T // CW
    P = Pipe(nc, es, name)
    t = lambda n, sh, dt: es.enter_context(nc.sbuf_tensor(name + "_" + n, sh, dt))
    pt = lambda n, sh: es.enter_context(nc.psum_tensor(name + "_" + n, sh, F32))
    sb = {
        "ones_col_bf": t("ones", [128, 1], BF16),
        "ones_row_f": t("onesr", [1, 128], F32),
        name + "xbf": t("xbf", [128, NSLAB, CW], BF16),
        name + "sqbf": t("sqbf", [128, NSLAB, CW], BF16),
        name + "stat": t("stat", [1, 6 * CW], F32),
    }
    ps = {
        name + "psx": pt("psx", [1, CW]),
        name + "psq": pt("psq", [1, CW]),
        name + "pbr": pt("pbr", [128, CW]),
        name + "pbp": pt("pbp", [128, CW]),
    }
    x_f = [t(f"x{i}", [128, NSLAB, CW], F32) for i in range(2)]
    xn_sb = [t(f"xn{i}", [128, NSLAB, CW], BF16) for i in range(2)]
    tmp_f = t("tmp", [128, CW], F32)
    kon = t("kon", [128, 1], F32)
    eps_ap = kon[0:1, 0:1]
    P.step("vector", lambda e: e.memset(kon[:, 0:1], float(eps)))

    P.step("sync", lambda e: e.dma_start(out=sb["ones_col_bf"][:], in_=io.inp("c_ones_col_bf", [128, 1], BF16)), dma=True)
    P.step("sync", lambda e: e.dma_start(out=sb["ones_row_f"][:], in_=io.inp("c_ones_row_f", [1, 128], F32)), dma=True)

    s_ld = P.xsem("ld"); s_st = P.xsem("st")
    st_gate = {}
    for c in range(NCH):
        P.step("sync", lambda e, c=c: e.dma_start(out=x_f[c % 2][:], in_=slabs_ap(xT, c * CW, CW)),
               dma=True, wait=(0, 0), inc=s_ld, waits=[(s_st, st_gate.get(c - 2, 0))])
        P.step("sync", lambda e: e.nop(), waits=[(s_ld, s_ld.n)])

        ln_stats_steps(P, sb, ps, x_f[c % 2], CW, eps_ap, name)

        def ap_fn(e, c=c):
            xf = x_f[c % 2]
            out = xn_sb[c % 2]
            for s in range(NSLAB):
                e.tensor_tensor(tmp_f[:], xf[:, s, :], ps[name + "pbr"][:], ALU.mult)
                e.drain()
                ins = e.tensor_tensor(out[:, s, :], tmp_f[:], ps[name + "pbp"][:], ALU.subtract)
                e.drain()
            return ins
        P.step("vector", ap_fn)
        P.step("gpsimd", lambda e, c=c: e.dma_start(out=slabs_ap(xnT, c * CW, CW), in_=xn_sb[c % 2][:]), dma=True, inc=s_st)
        st_gate[c] = s_st.n
    P.emit()


def load_weight_steps(P, io, sbuf_t, name, shape):
    ap = io.inp(name, list(shape), F32)
    P.step("gpsimd", lambda e: e.dma_start(
        out=sbuf_t[:], in_=ap.rearrange("(k p) m -> p k m", p=128)), dma=True,
        wait=(0, 0))


def vec_load_steps(P, io, sbuf_t, name, n):
    ap = io.inp(name, [n], F32)
    P.step("sync", lambda e: e.dma_start(
        out=sbuf_t[:], in_=ap.rearrange("(s p) -> p s", p=128),
        allow_slow_non_contiguous=True), dma=True,
        wait=(0, 0))


def build_p1b(nc, io, es, name="p1b"):
    xnT = io.stream("xn1T", [D, T], BF16, produce=False)
    kvT = io.stream("kvT", [D, T], F32, produce=True)
    CW = 512
    NCH = T // CW
    P = Pipe(nc, es, name)
    t = lambda n, sh, dt: es.enter_context(nc.sbuf_tensor(name + "_" + n, sh, dt))
    wk = t("wk", [128, NSLAB, D], BF16)
    wv = t("wv", [128, NSLAB, D], BF16)
    xn = [t(f"xn{i}", [128, NSLAB, CW], BF16) for i in range(2)]
    k_scr = t("kscr", [128, CW], F32)
    kv_sb = [t(f"kv{i}", [128, NSLAB, CW], F32) for i in range(2)]
    pk = es.enter_context(nc.psum_tensor(name + "_pk", [128, CW], F32))
    pv = es.enter_context(nc.psum_tensor(name + "_pv", [128, CW], F32))
    load_weight_steps(P, io, wk, "w_Wk", (D, D))
    load_weight_steps(P, io, wv, "w_Wv", (D, D))
    s_ld = P.xsem("ld"); s_st = P.xsem("st")
    st_gate = {}
    for c in range(NCH):
        P.step("sync", lambda e, c=c: e.dma_start(
            out=xn[c % 2][:], in_=slabs_ap(xnT, c * CW, CW)), dma=True,
            wait=(0, 0), inc=s_ld, waits=[(s_st, st_gate.get(c - 2, 0))])
        P.step("sync", lambda e: e.nop(), waits=[(s_ld, s_ld.n)])
        for s in range(NSLAB):
            def mmk(e, c=c, s=s):
                for k in range(NSLAB):
                    ins = e.matmul(pk[:], wk[:, k, s * 128:(s + 1) * 128],
                                   xn[c % 2][:, k, :], start=(k == 0), stop=(k == NSLAB - 1))
                return ins
            P.step("tensor", mmk)
            P.step("vector", lambda e, s=s: e.tensor_copy(k_scr[:], pk[:]))
            def mmv(e, c=c, s=s):
                for k in range(NSLAB):
                    ins = e.matmul(pv[:], wv[:, k, s * 128:(s + 1) * 128],
                                   xn[c % 2][:, k, :], start=(k == 0), stop=(k == NSLAB - 1))
                return ins
            P.step("tensor", mmv)
            P.step("vector", lambda e, c=c, s=s: e.tensor_tensor(
                kv_sb[c % 2][:, s, :], k_scr[:], pv[:], ALU.mult))
        P.step("gpsimd", lambda e, c=c: e.dma_start(
            out=slabs_ap(kvT, c * CW, CW), in_=kv_sb[c % 2][:]), dma=True, inc=s_st)
        st_gate[c] = s_st.n
    P.emit()


def build_p1c(nc, io, es, name="p1c"):
    xnT = io.stream("xn1T", [D, T], BF16, produce=False)
    lrT = io.stream("lrT", [D, T], F32, produce=True)
    thT = io.stream("thT", [D, T], F32, produce=True)
    CW = 512
    NCH = T // CW
    P = Pipe(nc, es, name)
    t = lambda n, sh, dt: es.enter_context(nc.sbuf_tensor(name + "_" + n, sh, dt))
    wa = t("wa", [128, NSLAB, 2 * D], BF16)
    xn = [t(f"xn{i}", [128, NSLAB, CW], BF16) for i in range(2)]
    ar = t("ar", [128, NSLAB, CW], F32)
    ai = t("ai", [128, NSLAB, CW], F32)
    t1 = t("t1", [128, NSLAB, CW], F32)
    t2 = t("t2", [128, NSLAB, CW], F32)
    lr_sb = [t(f"lr{i}", [128, NSLAB, CW], F32) for i in range(2)]
    th_sb = [t(f"th{i}", [128, NSLAB, CW], F32) for i in range(2)]
    kon = t("kon", [128, 1], F32)
    pa = es.enter_context(nc.psum_tensor(name + "_pa", [128, CW], F32))
    P.step("vector", lambda e: e.memset(kon[:, 0:1], 1.0))
    load_weight_steps(P, io, wa, "w_Wa", (D, 2 * D))
    s_ld = P.xsem("ld"); s_st = P.xsem("st")
    st_gate = {}
    for c in range(NCH):
        P.step("sync", lambda e, c=c: e.dma_start(
            out=xn[c % 2][:], in_=slabs_ap(xnT, c * CW, CW)), dma=True,
            wait=(0, 0), inc=s_ld, waits=[(s_st, st_gate.get(c - 2, 0))])
        P.step("sync", lambda e: e.nop(), waits=[(s_ld, s_ld.n)])
        for part, dst in ((0, ar), (1, ai)):
            for s in range(NSLAB):
                def mma(e, c=c, s=s, part=part):
                    for k in range(NSLAB):
                        ins = e.matmul(pa[:], wa[:, k, part * D + s * 128: part * D + (s + 1) * 128],
                                       xn[c % 2][:, k, :], start=(k == 0), stop=(k == NSLAB - 1))
                    return ins
                P.step("tensor", mma)
                P.step("vector", lambda e, s=s, dst=dst: e.tensor_copy(dst[:, s, :], pa[:]))
        def sq_fn(e):
            e.activation(out=t1[:], in_=ar[:], func=AF.Square)
            return e.activation(out=t2[:], in_=ai[:], func=AF.Square)
        P.step("scalar", sq_fn)
        P.step("vector", lambda e: e.tensor_tensor(t1[:], t1[:], t2[:], ALU.add))
        def lr_fn(e, c=c):
            e.activation(out=t1[:], in_=t1[:], func=AF.Sqrt)
            e.drain()
            e.activation(out=t1[:], in_=t1[:], func=AF.Exp, scale=-1.0)
            e.drain()
            return e.activation(out=lr_sb[c % 2][:], in_=t1[:], func=AF.Ln,
                                bias=kon[:, 0:1])
        P.step("scalar", lr_fn)
        P.step("vector", lambda e: e.reciprocal(t1[:], ar[:]))
        P.step("vector", lambda e: e.tensor_tensor(t1[:], ai[:], t1[:], ALU.mult))
        def at_fn(e):
            e.activation(out=t1[:], in_=t1[:], func=AF.Arctan)
            return e.activation(out=t2[:], in_=ai[:], func=AF.Sign)
        P.step("scalar", at_fn)
        def th_fn(e, c=c):
            e.tensor_single_scalar(ar[:], ar[:], 0.0, ALU.is_lt)
            e.drain()
            e.tensor_tensor(t2[:], ar[:], t2[:], ALU.mult)
            e.drain()
            return e.scalar_tensor_tensor(th_sb[c % 2][:], t2[:], math.pi, t1[:],
                                          ALU.mult, ALU.add)
        P.step("vector", th_fn)
        P.step("gpsimd", lambda e, c=c: e.dma_start(
            out=slabs_ap(lrT, c * CW, CW), in_=lr_sb[c % 2][:]), dma=True, inc=s_st)
        P.step("gpsimd", lambda e, c=c: e.dma_start(
            out=slabs_ap(thT, c * CW, CW), in_=th_sb[c % 2][:]), dma=True, inc=s_st)
        st_gate[c] = s_st.n
    P.emit()


TWO_PI = 2.0 * math.pi

def build_p1d(nc, io, es, name="p1d"):
    xnT = io.stream("xn1T", [D, T], BF16, produce=False)
    kvT = io.stream("kvT", [D, T], F32, produce=False)
    lrT = io.stream("lrT", [D, T], F32, produce=False)
    thT = io.stream("thT", [D, T], F32, produce=False)
    yrT = io.stream("yrT", [D, T], BF16, produce=True)
    gnab = io.stream("gnab", [B, 128, 16], F32, produce=True)
    CW = 256
    NCH_B = S // CW
    NSUB = CW // 64
    P = Pipe(nc, es, name)
    t = lambda n, sh, dt: es.enter_context(nc.sbuf_tensor(name + "_" + n, sh, dt))
    wq = t("wq", [128, NSLAB, D], BF16)
    xn = [t(f"xn{i}", [128, NSLAB, CW], BF16) for i in range(2)]
    kv = [t(f"kv{i}", [128, NSLAB, CW], F32) for i in range(2)]
    lr = [t(f"lr{i}", [128, NSLAB, CW], F32) for i in range(2)]
    th = [t(f"th{i}", [128, NSLAB, CW], F32) for i in range(2)]
    q_f = t("qf", [128, NSLAB, CW], F32)
    pos = t("pos", [128, NSLAB, CW], F32)
    phi = t("phi", [128, NSLAB, CW], F32)
    cs = t("cs", [128, NSLAB, CW], F32)
    sn = t("sn", [128, NSLAB, CW], F32)
    ew = t("ew", [128, NSLAB, CW], F32)
    CR = t("CR", [128, NSLAB, CW], F32)
    CIp = t("CIp", [128, NSLAB, CW], F32)
    u1 = t("u1", [128, NSLAB, CW], F32)
    u2 = t("u2", [128, NSLAB, CW], F32)
    hr = t("hr", [128, NSLAB, CW], F32)
    hi = t("hi", [128, NSLAB, CW], F32)
    ki = t("ki", [128, NSLAB, 64], I32)
    yr_sb = [t(f"yr{i}", [128, NSLAB, CW], BF16) for i in range(2)]
    ones64 = t("ones64", [128, 64], F32)
    car_r = t("car_r", [128, NSLAB, 1], F32)
    car_ni = t("car_ni", [128, NSLAB, 1], F32)
    carx = t("carx", [128, NSLAB, 4], F32)
    gsum3 = t("gsum3", [128, NSLAB, 3], F32)
    stats = t("stats", [128, NSLAB, 3, NCH_B], F32)
    gs_sb = t("gs", [4, 24], F32)
    gv_sb = t("gv", [4, 16], F32)
    ab_sb = t("ab", [128, 16], F32)
    gsc = t("gsc", [128, NSLAB], F32)
    gbi = t("gbi", [128, NSLAB], F32)
    grp_ones = t("grpo", [128, 4], F32)
    grp_selT = t("grps", [4, 128], F32)
    kon = t("kon", [128, 1], F32)
    pq = es.enter_context(nc.psum_tensor(name + "_pq", [128, CW], F32))
    pg = es.enter_context(nc.psum_tensor(name + "_pg", [4, 24], F32))
    pb = es.enter_context(nc.psum_tensor(name + "_pb", [128, 16], F32))

    P.step("vector", lambda e: e.memset(kon[:, 0:1], 1e-6))
    load_weight_steps(P, io, wq, "w_Wq", (D, D))
    vec_load_steps(P, io, gsc, "w_gn_scale", D)
    vec_load_steps(P, io, gbi, "w_gn_bias", D)
    P.step("sync", lambda e: e.dma_start(out=ones64[:], in_=io.inp("c_ones64_f", [128, 64], F32)), dma=True, wait=(0, 0))
    P.step("sync", lambda e: e.dma_start(out=grp_ones[:], in_=io.inp("c_grp_ones", [128, 4], F32)), dma=True, wait=(0, 0))
    P.step("sync", lambda e: e.dma_start(out=grp_selT[:], in_=io.inp("c_grp_selT", [4, 128], F32)), dma=True, wait=(0, 0))

    s_ld = P.xsem("ld"); s_st = P.xsem("st")
    st_gate = {}
    for b in range(B):
        P.step("vector", lambda e: (e.memset(car_r[:], 0.0), e.memset(car_ni[:], 0.0))[-1])
        for cc in range(NCH_B):
            c = b * NCH_B + cc
            buf = c % 2
            xw = [(s_st, st_gate.get(c - 2, 0))]
            P.step("sync", lambda e, c=c, buf=buf: e.dma_start(out=xn[buf][:], in_=slabs_ap(xnT, c * CW, CW)), dma=True, wait=(0, 0), inc=s_ld, waits=xw)
            P.step("sync", lambda e, c=c, buf=buf: e.dma_start(out=kv[buf][:], in_=slabs_ap(kvT, c * CW, CW)), dma=True, wait=(0, 0), inc=s_ld, waits=xw)
            P.step("sync", lambda e, c=c, buf=buf: e.dma_start(out=lr[buf][:], in_=slabs_ap(lrT, c * CW, CW)), dma=True, wait=(0, 0), inc=s_ld, waits=xw)
            P.step("sync", lambda e, c=c, buf=buf: e.dma_start(out=th[buf][:], in_=slabs_ap(thT, c * CW, CW)), dma=True, wait=(0, 0), inc=s_ld, waits=xw)
            P.step("sync", lambda e: e.nop(), waits=[(s_ld, s_ld.n)])
            for s in range(NSLAB):
                def mmq(e, buf=buf, s=s):
                    for k in range(NSLAB):
                        ins = e.matmul(pq[:], wq[:, k, s * 128:(s + 1) * 128],
                                       xn[buf][:, k, :], start=(k == 0), stop=(k == NSLAB - 1))
                    return ins
                P.step("tensor", mmq)
                P.step("vector", lambda e, s=s: e.tensor_copy(q_f[:, s, :], pq[:]))
            for j in range(NSUB):
                j0, j1 = j * 64, (j + 1) * 64
                def scan_pp(e, buf=buf, j0=j0, j1=j1):
                    for s in range(NSLAB):
                        e.tensor_tensor_scan(pos[:, s, j0:j1], ones64[:], lr[buf][:, s, j0:j1],
                                             0.0, ALU.mult, ALU.add)
                    for s in range(NSLAB):
                        ins = e.tensor_tensor_scan(phi[:, s, j0:j1], ones64[:], th[buf][:, s, j0:j1],
                                                   0.0, ALU.mult, ALU.add)
                    return ins
                P.step("vector", scan_pp)
                def rr_fn(e, j0=j0, j1=j1):
                    sl = (slice(None), slice(None), slice(j0, j1))
                    ki_ = ki[:, :, 0:64]
                    e.tensor_scalar_mul(u1[sl], phi[sl], 1.0 / TWO_PI)
                    e.drain()
                    e.tensor_copy(ki_, u1[sl])
                    e.drain()
                    e.tensor_copy(u2[sl], ki_)
                    e.drain()
                    e.tensor_tensor(u1[sl], u1[sl], u2[sl], ALU.subtract)
                    e.drain()
                    e.tensor_scalar_mul(u1[sl], u1[sl], TWO_PI)
                    e.tensor_scalar(u2[sl], phi[sl], 1.0 / TWO_PI, 0.25, ALU.mult, ALU.add)
                    e.drain()
                    e.tensor_copy(ki_, u2[sl])
                    e.drain()
                    e.tensor_copy(hr[:, :, j0:j1], ki_)
                    e.drain()
                    e.tensor_tensor(u2[sl], u2[sl], hr[:, :, j0:j1], ALU.subtract)
                    e.drain()
                    return e.tensor_scalar_mul(u2[sl], u2[sl], TWO_PI)
                P.step("vector", rr_fn)
                def trig_fn(e, j0=j0, j1=j1):
                    sl = (slice(None), slice(None), slice(j0, j1))
                    e.activation(out=sn[sl], in_=u1[sl], func=AF.Sin)
                    e.activation(out=cs[sl], in_=u2[sl], func=AF.Sin)
                    e.drain()
                    return e.activation(out=ew[sl], in_=pos[sl], func=AF.Exp)
                P.step("scalar", trig_fn)
                def terms_fn(e, buf=buf, j0=j0, j1=j1):
                    sl = (slice(None), slice(None), slice(j0, j1))
                    e.tensor_tensor(u1[sl], kv[buf][sl], ew[sl], ALU.mult)
                    e.drain()
                    e.tensor_tensor(u2[sl], u1[sl], cs[sl], ALU.mult)
                    return e.tensor_tensor(u1[sl], u1[sl], sn[sl], ALU.mult)
                P.step("vector", terms_fn)
                def scan_c(e, j0=j0, j1=j1):
                    for s in range(NSLAB):
                        e.tensor_tensor_scan(CR[:, s, j0:j1], ones64[:], u2[:, s, j0:j1],
                                             car_r[:, s, 0:1], ALU.mult, ALU.add)
                    for s in range(NSLAB):
                        ins = e.tensor_tensor_scan(CIp[:, s, j0:j1], ones64[:], u1[:, s, j0:j1],
                                                   car_ni[:, s, 0:1], ALU.mult, ALU.add)
                    return ins
                P.step("vector", scan_c)
                def eiend_fn(e, j1=j1):
                    return e.activation(out=carx[:, :, 0:1], in_=pos[:, :, j1 - 1:j1],
                                        func=AF.Exp, scale=-1.0)
                P.step("scalar", eiend_fn)
                def carry_fn(e, j1=j1):
                    je = slice(j1 - 1, j1)
                    sl = (slice(None), slice(None), je)
                    e.tensor_tensor(carx[:, :, 1:2], cs[sl], CR[sl], ALU.mult)
                    e.tensor_tensor(carx[:, :, 2:3], sn[sl], CIp[sl], ALU.mult)
                    e.drain()
                    e.tensor_tensor(carx[:, :, 1:2], carx[:, :, 1:2], carx[:, :, 2:3], ALU.add)
                    e.drain()
                    e.tensor_tensor(car_r[:, :, 0:1], carx[:, :, 0:1], carx[:, :, 1:2], ALU.mult)
                    e.tensor_tensor(carx[:, :, 1:2], cs[sl], CIp[sl], ALU.mult)
                    e.tensor_tensor(carx[:, :, 2:3], sn[sl], CR[sl], ALU.mult)
                    e.drain()
                    e.tensor_tensor(carx[:, :, 1:2], carx[:, :, 1:2], carx[:, :, 2:3], ALU.subtract)
                    e.drain()
                    return e.tensor_tensor(car_ni[:, :, 0:1], carx[:, :, 0:1], carx[:, :, 1:2], ALU.mult)
                P.step("vector", carry_fn)
            P.step("vector", lambda e: e.reciprocal(ew[:], ew[:]))
            def rot_fn(e):
                e.tensor_tensor(u1[:], cs[:], CR[:], ALU.mult)
                e.tensor_tensor(u2[:], sn[:], CIp[:], ALU.mult)
                e.drain()
                e.tensor_tensor(u1[:], u1[:], u2[:], ALU.add)
                e.drain()
                e.tensor_tensor(hr[:], ew[:], u1[:], ALU.mult)
                e.tensor_tensor(u1[:], sn[:], CR[:], ALU.mult)
                e.tensor_tensor(u2[:], cs[:], CIp[:], ALU.mult)
                e.drain()
                e.tensor_tensor(u1[:], u1[:], u2[:], ALU.subtract)
                e.drain()
                return e.tensor_tensor(hi[:], ew[:], u1[:], ALU.mult)
            P.step("vector", rot_fn)
            def y_fn(e, buf=buf):
                e.tensor_tensor(hr[:], q_f[:], hr[:], ALU.mult)
                e.tensor_tensor(hi[:], q_f[:], hi[:], ALU.mult)
                e.drain()
                return e.tensor_copy(yr_sb[buf][:], hr[:])
            P.step("vector", y_fn)
            def stat_fn(e, cc=cc):
                e.tensor_reduce(stats[:, :, 0, cc:cc + 1], hr[:], mybir.AxisListType.X, ALU.add)
                e.tensor_reduce(stats[:, :, 1, cc:cc + 1], hi[:], mybir.AxisListType.X, ALU.add)
                e.tensor_tensor(u1[:], hr[:], hr[:], ALU.mult)
                e.tensor_tensor(u2[:], hi[:], hi[:], ALU.mult)
                e.drain()
                e.tensor_tensor(u1[:], u1[:], u2[:], ALU.add)
                e.drain()
                return e.tensor_reduce(stats[:, :, 2, cc:cc + 1], u1[:], mybir.AxisListType.X, ALU.add)
            P.step("vector", stat_fn)
            P.step("gpsimd", lambda e, c=c, buf=buf: e.dma_start(
                out=slabs_ap(yrT, c * CW, CW), in_=yr_sb[buf][:]), dma=True, inc=s_st)
            st_gate[c] = s_st.n
        def gsum_fn(e):
            return e.tensor_reduce(gsum3[:], stats[:], mybir.AxisListType.X, ALU.add)
        P.step("vector", gsum_fn)
        def gmm_fn(e):
            return e.matmul(pg[:], grp_ones[:], gsum3[:].rearrange("p s c -> p (s c)"),
                            start=True, stop=True)
        P.step("tensor", gmm_fn)
        def gmath_fn(e):
            cnt = 1.0 / (S * 32)
            e.tensor_scalar_mul(gs_sb[:], pg[:], cnt)
            e.drain()
            g = gs_sb[:].rearrange("g (s c) -> g s c", c=3)
            mr = g[:, :, 0:1]; mi = g[:, :, 1:2]; msq = g[:, :, 2:3]
            gv = gv_sb[:].rearrange("g (s c) -> g s c", c=2)
            e.tensor_tensor(gv[:, :, 0:1], mr[:], mr[:], ALU.mult)
            e.tensor_tensor(gv[:, :, 1:2], mi[:], mi[:], ALU.mult)
            e.drain()
            e.tensor_tensor(gv[:, :, 1:2], gv[:, :, 0:1], gv[:, :, 1:2], ALU.add)
            e.drain()
            e.tensor_tensor(gv[:, :, 1:2], msq[:], gv[:, :, 1:2], ALU.subtract)
            e.drain()
            return e.tensor_copy(gv[:, :, 0:1], mr[:])
        P.step("vector", gmath_fn)
        def grstd_fn(e):
            gv = gv_sb[:].rearrange("g (s c) -> g s c", c=2)
            e.activation(out=gv[:, :, 1:2], in_=gv[:, :, 1:2], func=AF.Ln, bias=kon[0:4, 0:1])
            e.drain()
            return e.activation(out=gv[:, :, 1:2], in_=gv[:, :, 1:2], func=AF.Exp, scale=-0.5)
        P.step("scalar", grstd_fn)
        def gbc_fn(e):
            return e.matmul(pb[:], grp_selT[:], gv_sb[:], start=True, stop=True)
        P.step("tensor", gbc_fn)
        def ab_fn(e):
            pbv = pb[:].rearrange("p (s c) -> p s c", c=2)
            abv = ab_sb[:].rearrange("p (a s) -> p a s", a=2)
            e.tensor_tensor(abv[:, 0, :], pbv[:, :, 1], gsc[:], ALU.mult)
            e.drain()
            e.tensor_tensor(abv[:, 1, :], pbv[:, :, 0], abv[:, 0, :], ALU.mult)
            e.drain()
            return e.tensor_tensor(abv[:, 1, :], gbi[:], abv[:, 1, :], ALU.subtract)
        P.step("vector", ab_fn)
        P.step("gpsimd", lambda e, b=b: e.dma_start(out=gnab[b], in_=ab_sb[:]), dma=True)
    P.emit()


def build_p1e(nc, io, es, name="p1e"):
    xT = io.inp("xT", [D, T], F32)
    xnT = io.stream("xn1T", [D, T], BF16, produce=False)
    yrT = io.stream("yrT", [D, T], BF16, produce=False)
    gnab = io.stream("gnab", [B, 128, 16], F32, produce=False)
    vT = io.stream("v1T", [D, T], F32, produce=True)
    CW = 512
    NCH = T // CW
    P = Pipe(nc, es, name)
    t = lambda n, sh, dt: es.enter_context(nc.sbuf_tensor(name + "_" + n, sh, dt))
    pt = lambda n, sh: es.enter_context(nc.psum_tensor(name + "_" + n, sh, F32))
    wg = t("wgw", [128, NSLAB, D], BF16)
    wo = t("wow", [128, NSLAB, D], BF16)
    ab = t("abt", [128, B, 16], F32)
    xn = [t(f"xn{i}", [128, NSLAB, CW], BF16) for i in range(2)]
    yr = [t(f"yr{i}", [128, NSLAB, CW], BF16) for i in range(2)]
    x_f = [t(f"x{i}", [128, NSLAB, CW], F32) for i in range(2)]
    sg_bf = t("sg", [128, NSLAB, CW], BF16)
    zg_bf = t("zg", [128, NSLAB, CW], BF16)
    z_bf = t("z", [128, CW], BF16)
    v_sb = [t(f"v{i}", [128, NSLAB, CW], F32) for i in range(2)]
    tmp_f = t("tmp", [128, CW], F32)
    kon = t("kon", [128, 1], F32)
    sb = {
        "ones_col_bf": t("ones", [128, 1], BF16),
        "ones_row_f": t("onesr", [1, 128], F32),
        name + "xbf": t("xbf", [128, NSLAB, CW], BF16),
        name + "sqbf": t("sqbf", [128, NSLAB, CW], BF16),
        name + "stat": t("stat", [1, 6 * CW], F32),
    }
    ps = {
        name + "psx": pt("psx", [1, CW]),
        name + "psq": pt("psq", [1, CW]),
        name + "pbr": pt("pbr", [128, CW]),
        name + "pbp": pt("pbp", [128, CW]),
    }
    pgm = pt("pg", [128, CW])
    pom = pt("po", [128, CW])
    eps_ap = kon[0:1, 0:1]
    P.step("vector", lambda e: e.memset(kon[:, 0:1], 1e-5))
    P.step("sync", lambda e: e.dma_start(out=sb["ones_col_bf"][:], in_=io.inp("c_ones_col_bf", [128, 1], BF16)), dma=True, wait=(0, 0))
    P.step("sync", lambda e: e.dma_start(out=sb["ones_row_f"][:], in_=io.inp("c_ones_row_f", [1, 128], F32)), dma=True, wait=(0, 0))
    load_weight_steps(P, io, wg, "w_Wg", (D, D))
    load_weight_steps(P, io, wo, "w_Wo", (D, D))
    P.step("sync", lambda e: e.dma_start(out=ab[:], in_=gnab.rearrange("b p k -> p b k")), dma=True, wait=(0, 0))
    s_ld = P.xsem("ld"); s_st = P.xsem("st")
    st_gate = {}
    for c in range(NCH):
        b = (c * CW) // S
        buf = c % 2
        xw = [(s_st, st_gate.get(c - 2, 0))]
        P.step("sync", lambda e, c=c, buf=buf: e.dma_start(out=xn[buf][:], in_=slabs_ap(xnT, c * CW, CW)), dma=True, wait=(0, 0), inc=s_ld, waits=xw)
        P.step("sync", lambda e, c=c, buf=buf: e.dma_start(out=yr[buf][:], in_=slabs_ap(yrT, c * CW, CW)), dma=True, wait=(0, 0), inc=s_ld, waits=xw)
        P.step("sync", lambda e, c=c, buf=buf: e.dma_start(out=x_f[buf][:], in_=slabs_ap(xT, c * CW, CW)), dma=True, wait=(0, 0), inc=s_ld, waits=xw)
        P.step("sync", lambda e: e.nop(), waits=[(s_ld, s_ld.n)])
        ln_stats_steps(P, sb, ps, x_f[buf], CW, eps_ap, name)
        for s in range(NSLAB):
            def mmg(e, buf=buf, s=s):
                for k in range(NSLAB):
                    ins = e.matmul(pgm[:], wg[:, k, s * 128:(s + 1) * 128],
                                   xn[buf][:, k, :], start=(k == 0), stop=(k == NSLAB - 1))
                return ins
            P.step("tensor", mmg)
            P.step("scalar", lambda e, s=s: e.activation(out=sg_bf[:, s, :], in_=pgm[:], func=AF.Silu))
            def zg_fn(e, buf=buf, s=s, b=b):
                abv = ab[:, b, :].rearrange("p (a s) -> p a s", a=2)
                e.tensor_scalar(z_bf[:], yr[buf][:, s, :], abv[:, 0, s:s + 1],
                                abv[:, 1, s:s + 1], ALU.mult, ALU.add)
                e.drain()
                return e.tensor_tensor(zg_bf[:, s, :], z_bf[:], sg_bf[:, s, :], ALU.mult)
            P.step("vector", zg_fn)
        for s in range(NSLAB):
            def mmo(e, s=s):
                for k in range(NSLAB):
                    ins = e.matmul(pom[:], wo[:, k, s * 128:(s + 1) * 128],
                                   zg_bf[:, k, :], start=(k == 0), stop=(k == NSLAB - 1))
                return ins
            P.step("tensor", mmo)
            def v_fn(e, buf=buf, s=s):
                e.tensor_tensor(tmp_f[:], x_f[buf][:, s, :], ps[name + "pbr"][:], ALU.mult)
                e.drain()
                e.tensor_tensor(tmp_f[:], tmp_f[:], ps[name + "pbp"][:], ALU.subtract)
                e.drain()
                return e.tensor_tensor(v_sb[buf][:, s, :], tmp_f[:], pom[:], ALU.add)
            P.step("vector", v_fn)
        P.step("gpsimd", lambda e, c=c, buf=buf: e.dma_start(
            out=slabs_ap(vT, c * CW, CW), in_=v_sb[buf][:]), dma=True, inc=s_st)
        st_gate[c] = s_st.n
    P.emit()


def build_p1f(nc, io, es, half, name=None):
    name = name or f"p1f{half}"
    xT = io.inp("xT", [D, T], F32)
    vT = io.stream("v1T", [D, T], F32, produce=False)
    accT = io.stream("ffacc", [D, T], F32, produce=(half == 0))
    if half == 1:
        x1T = io.stream("x1T", [D, T], F32, produce=True)
    F = 4096
    FH = F // 2
    NFH = FH // 128
    CW = 512
    NCH = T // CW
    P = Pipe(nc, es, name)
    t = lambda n, sh, dt: es.enter_context(nc.sbuf_tensor(name + "_" + n, sh, dt))
    pt = lambda n, sh: es.enter_context(nc.psum_tensor(name + "_" + n, sh, F32))
    w1 = t("w1w", [128, NSLAB, FH], BF16)
    w2 = t("w2w", [128, NFH, D], BF16)
    b1v = t("b1v", [128, NFH], F32)
    b2v = t("b2v", [128, NSLAB], F32)
    l1s = t("l1s", [128, NSLAB], F32)
    l1b = t("l1b", [128, NSLAB], F32)
    l2s = t("l2s", [128, NSLAB], F32)
    l2b = t("l2b", [128, NSLAB], F32)
    v_f = [t(f"v{i}", [128, NSLAB, CW], F32) for i in range(2)]
    acc = [t("a0", [128, NSLAB, CW], F32)] * 2 if half == 1 else None
    x_f = [t("x0", [128, NSLAB, CW], F32)] * 2 if half == 1 else None
    x2_f = t("x2f", [128, NSLAB, CW], F32) if half == 0 else None
    x2_bf = t("x2bf", [128, NSLAB, CW], BF16)
    h_bf = t("hbf", [128, NFH, CW], BF16)
    s_bf = t("sbf", [128, CW], BF16)
    sq_b = t("sqb", [128, CW], BF16)
    u_f = t("uf", [128, NSLAB, CW], F32) if half == 1 else None
    o_sb = v_f
    tmp_f = t("tmp", [128, CW], F32)
    kon = t("kon", [128, 1], F32)
    sb = {
        "ones_col_bf": t("ones", [128, 1], BF16),
        "ones_row_f": t("onesr", [1, 128], F32),
        name + "xbf": t("xbf", [128, NSLAB, CW], BF16),
        name + "sqbf": t("sqbf", [128, NSLAB, CW], BF16),
        name + "stat": t("stat", [1, 6 * CW], F32),
    }
    ps = {
        name + "psx": pt("psx", [1, CW]),
        name + "psq": pt("psq", [1, CW]),
        name + "pbr": pt("pbr", [128, CW]),
        name + "pbp": pt("pbp", [128, CW]),
    }
    ph = pt("ph", [128, CW])
    pu = pt("pu", [128, CW])
    eps_ap = kon[0:1, 0:1]
    P.step("vector", lambda e: e.memset(kon[:, 0:1], 1e-6))
    P.step("sync", lambda e: e.dma_start(out=sb["ones_col_bf"][:], in_=io.inp("c_ones_col_bf", [128, 1], BF16)), dma=True, wait=(0, 0))
    P.step("sync", lambda e: e.dma_start(out=sb["ones_row_f"][:], in_=io.inp("c_ones_row_f", [1, 128], F32)), dma=True, wait=(0, 0))
    w1ap = io.inp("w_W1", [D, F], F32)
    P.step("gpsimd", lambda e: e.dma_start(
        out=w1[:], in_=w1ap.rearrange("(k p) m -> p k m", p=128)[:, :, half * FH:(half + 1) * FH]), dma=True, wait=(0, 0))
    w2ap = io.inp("w_W2", [F, D], F32)
    P.step("gpsimd", lambda e: e.dma_start(
        out=w2[:], in_=w2ap.rearrange("(k p) m -> p k m", p=128)[:, half * NFH:(half + 1) * NFH, :]), dma=True, wait=(0, 0))
    b1ap = io.inp("w_b1", [F], F32)
    P.step("sync", lambda e: e.dma_start(
        out=b1v[:], in_=b1ap.rearrange("(s p) -> p s", p=128)[:, half * NFH:(half + 1) * NFH],
        allow_slow_non_contiguous=True), dma=True, wait=(0, 0))
    vec_load_steps(P, io, b2v, "w_b2", D)
    vec_load_steps(P, io, l1s, "w_ln1_scale", D)
    vec_load_steps(P, io, l1b, "w_ln1_bias", D)
    vec_load_steps(P, io, l2s, "w_ln2_scale", D)
    vec_load_steps(P, io, l2b, "w_ln2_bias", D)
    store_pts = {}
    for c in range(NCH):
        buf = c % 2
        w = store_pts.get(c - 2)
        P.step("sync", lambda e, c=c, buf=buf: e.dma_start(out=v_f[buf][:], in_=slabs_ap(vT, c * CW, CW)), dma=True)
        if half == 1:
            P.step("sync", lambda e, c=c, buf=buf: e.dma_start(out=acc[buf][:], in_=slabs_ap(accT, c * CW, CW)), dma=True)
            P.step("sync", lambda e, c=c, buf=buf: e.dma_start(out=x_f[buf][:], in_=slabs_ap(xT, c * CW, CW)), dma=True)
        ln_stats_steps(P, sb, ps, v_f[buf], CW, eps_ap, name)
        for s in range(NSLAB):
            def x2_fn(e, buf=buf, s=s):
                e.tensor_tensor(tmp_f[:], v_f[buf][:, s, :], ps[name + "pbr"][:], ALU.mult)
                e.drain()
                e.tensor_tensor(tmp_f[:], tmp_f[:], ps[name + "pbp"][:], ALU.subtract)
                e.drain()
                if half == 0:
                    e.tensor_scalar(x2_f[:, s, :], tmp_f[:], l1s[:, s:s + 1], l1b[:, s:s + 1],
                                    ALU.mult, ALU.add)
                    e.drain()
                    return e.tensor_copy(x2_bf[:, s, :], x2_f[:, s, :])
                return e.tensor_scalar(x2_bf[:, s, :], tmp_f[:], l1s[:, s:s + 1], l1b[:, s:s + 1],
                                       ALU.mult, ALU.add)
            P.step("vector", x2_fn)
        for m in range(NFH):
            def mmh(e, m=m):
                for k in range(NSLAB):
                    ins = e.matmul(ph[:], w1[:, k, m * 128:(m + 1) * 128],
                                   x2_bf[:, k, :], start=(k == 0), stop=(k == NSLAB - 1))
                return ins
            P.step("tensor", mmh)
            def gel1(e, m=m):
                e.tensor_scalar(s_bf[:], ph[:], 1.0, b1v[:, m:m + 1], ALU.mult, ALU.add)
                e.drain()
                e.tensor_tensor(sq_b[:], s_bf[:], s_bf[:], ALU.mult)
                e.drain()
                e.tensor_tensor(sq_b[:], sq_b[:], s_bf[:], ALU.mult)
                e.drain()
                return e.scalar_tensor_tensor(sq_b[:], sq_b[:], 0.044715, s_bf[:], ALU.mult, ALU.add)
            P.step("vector", gel1)
            P.step("scalar", lambda e: e.activation(out=sq_b[:], in_=sq_b[:], func=AF.Sigmoid,
                                                    scale=1.5957691216057308))
            def gel2(e, m=m):
                return e.tensor_tensor(h_bf[:, m, :], s_bf[:], sq_b[:], ALU.mult)
            P.step("vector", gel2)
        for s in range(NSLAB):
            def mmu(e, s=s):
                for k in range(NFH):
                    ins = e.matmul(pu[:], w2[:, k, s * 128:(s + 1) * 128],
                                   h_bf[:, k, :], start=(k == 0), stop=(k == NFH - 1))
                return ins
            P.step("tensor", mmu)
            if half == 0:
                def acc_fn(e, buf=buf, s=s):
                    e.tensor_scalar(tmp_f[:], pu[:], 1.0, b2v[:, s:s + 1], ALU.mult, ALU.add)
                    e.drain()
                    return e.tensor_tensor(o_sb[buf][:, s, :], tmp_f[:], x2_f[:, s, :], ALU.add)
                P.step("vector", acc_fn)
            else:
                def u_fn(e, buf=buf, s=s):
                    return e.tensor_tensor(u_f[:, s, :], pu[:], acc[buf][:, s, :], ALU.add)
                P.step("vector", u_fn)
        if half == 0:
            P.step("gpsimd", lambda e, c=c, buf=buf: e.dma_start(
                out=slabs_ap(accT, c * CW, CW), in_=o_sb[buf][:]), dma=True)
            store_pts[c] = P.at()
            continue
        ln_stats_steps(P, sb, ps, u_f, CW, eps_ap, name)
        for s in range(NSLAB):
            def out_fn(e, buf=buf, s=s):
                e.tensor_tensor(tmp_f[:], u_f[:, s, :], ps[name + "pbr"][:], ALU.mult)
                e.drain()
                e.tensor_tensor(tmp_f[:], tmp_f[:], ps[name + "pbp"][:], ALU.subtract)
                e.drain()
                e.tensor_scalar(tmp_f[:], tmp_f[:], l2s[:, s:s + 1], l2b[:, s:s + 1],
                                ALU.mult, ALU.add)
                e.drain()
                return e.tensor_tensor(o_sb[buf][:, s, :], tmp_f[:], x_f[buf][:, s, :], ALU.add)
            P.step("vector", out_fn)
        P.step("gpsimd", lambda e, c=c, buf=buf: e.dma_start(
            out=slabs_ap(x1T, c * CW, CW), in_=o_sb[buf][:]), dma=True)
        store_pts[c] = P.at()
    P.emit()


def build_p2a(nc, io, es, name="p2a"):
    x1T = io.stream("x1T", [D, T], F32, produce=False)
    xnT = io.stream("xn2T", [D, T], BF16, produce=True)
    combT = io.stream("combT", [32, T], F32, produce=True)
    xtokT = io.stream("xn2tokT", [T, D], BF16, produce=True)
    selT = io.stream("selT", [T, 4], F32, produce=True)
    wgtT = io.stream("wgtT", [T, 4], F32, produce=True)
    CW = 512
    NCH = T // CW
    NT4 = CW // 128
    P = Pipe(nc, es, name)
    t = lambda n, sh, dt: es.enter_context(nc.sbuf_tensor(name + "_" + n, sh, dt))
    pt = lambda n, sh: es.enter_context(nc.psum_tensor(name + "_" + n, sh, F32))
    gw = t("gw", [128, NSLAB, 32], BF16)
    ebrow = t("ebrow", [1, 32], F32)
    ebc = t("ebc", [128, 32], F32)
    iota32 = t("iota32", [128, 32], F32)
    identf = t("identf", [128, 128], F32)
    x_f = [t(f"x{i}", [128, NSLAB, CW], F32) for i in range(2)]
    xn_sb = [t(f"xn{i}", [128, NSLAB, CW], BF16) for i in range(2)]
    sc_f = t("sc", [128, 32], F32)
    bi_f = t("bi", [128, 32], F32)
    vals = t("vals", [128, 8], F32)
    idxu = t("idxu", [128, 8], U32)
    idxf = t("idxf", [128, 8], F32)
    oh = t("oh", [128, 4, 32], F32)
    og = t("og", [128, 8], F32)
    comb_tok = t("ctok", [128, 32], F32)
    combt_sb = [t(f"cT{i}", [32, CW], F32) for i in range(2)]
    wk4 = t("wk4", [128, 4], F32)
    xtk = t("xtk", [128, 1024], BF16)
    identbb = t("identbb", [128, 128], BF16)
    tmp_f = t("tmp", [128, CW], F32)
    kon = t("kon", [128, 1], F32)
    sb = {
        "ones_col_bf": t("ones", [128, 1], BF16),
        "ones_row_f": t("onesr", [1, 128], F32),
        name + "xbf": t("xbf", [128, NSLAB, CW], BF16),
        name + "sqbf": t("sqbf", [128, NSLAB, CW], BF16),
        name + "stat": t("stat", [1, 6 * CW], F32),
    }
    ps = {
        name + "psx": pt("psx", [1, CW]),
        name + "psq": pt("psq", [1, CW]),
        name + "pbr": pt("pbr", [128, CW]),
        name + "pbp": pt("pbp", [128, CW]),
    }
    psc = pt("psc", [128, 32])
    pct = pt("pct", [32, 128])
    pxt = es.enter_context(nc.psum_tensor(name + "_pxt", [128, 128], BF16))
    eps_ap = kon[0:1, 0:1]
    P.step("vector", lambda e: e.memset(kon[:, 0:1], 1e-5))
    P.step("sync", lambda e: e.dma_start(out=sb["ones_col_bf"][:], in_=io.inp("c_ones_col_bf", [128, 1], BF16)), dma=True, wait=(0, 0))
    P.step("sync", lambda e: e.dma_start(out=sb["ones_row_f"][:], in_=io.inp("c_ones_row_f", [1, 128], F32)), dma=True, wait=(0, 0))
    P.step("sync", lambda e: e.dma_start(out=iota32[:], in_=io.inp("c_iota32_bc", [128, 32], F32)), dma=True, wait=(0, 0))
    P.step("sync", lambda e: e.dma_start(out=identf[:], in_=io.inp("c_ident_f", [128, 128], F32)), dma=True, wait=(0, 0))
    P.step("sync", lambda e: e.dma_start(out=identbb[:], in_=io.inp("c_ident_bf", [128, 128], BF16)), dma=True, wait=(0, 0))
    P.step("sync", lambda e: e.dma_start(out=ebrow[:], in_=io.inp("w_e_bias", [1, 32], F32)), dma=True, wait=(0, 0))
    P.step("gpsimd", lambda e: e.dma_start(out=gw[:], in_=io.inp("w_gate_wT", [D, 32], F32).rearrange("(k p) m -> p k m", p=128)), dma=True, wait=(0, 0))
    P.step("tensor", lambda e: e.matmul(psc[:], sb["ones_row_f"][:], ebrow[:], start=True, stop=True))
    P.step("vector", lambda e: e.tensor_copy(ebc[:], psc[:]))
    s_ld = P.xsem("ld"); s_st = P.xsem("st")
    st_gate = {}
    for c in range(NCH):
        buf = c % 2
        P.step("sync", lambda e, c=c, buf=buf: e.dma_start(out=x_f[buf][:], in_=slabs_ap(x1T, c * CW, CW)),
               dma=True, wait=(0, 0), inc=s_ld, waits=[(s_st, st_gate.get(c - 2, 0))])
        P.step("sync", lambda e: e.nop(), waits=[(s_ld, s_ld.n)])
        ln_stats_steps(P, sb, ps, x_f[buf], CW, eps_ap, name)
        def ap_fn(e, buf=buf):
            for s in range(NSLAB):
                e.tensor_tensor(tmp_f[:], x_f[buf][:, s, :], ps[name + "pbr"][:], ALU.mult)
                e.drain()
                ins = e.tensor_tensor(xn_sb[buf][:, s, :], tmp_f[:], ps[name + "pbp"][:], ALU.subtract)
                e.drain()
            return ins
        P.step("vector", ap_fn)
        for t4 in range(NT4):
            def mmsc(e, buf=buf, t4=t4):
                for k in range(NSLAB):
                    ins = e.matmul(psc[:], xn_sb[buf][:, k, t4 * 128:(t4 + 1) * 128],
                                   gw[:, k, :], start=(k == 0), stop=(k == NSLAB - 1))
                return ins
            P.step("tensor", mmsc)
            P.step("scalar", lambda e: e.activation(out=sc_f[:], in_=psc[:], func=AF.Sigmoid))
            def topk_fn(e):
                e.tensor_tensor(bi_f[:], sc_f[:], ebc[:], ALU.add)
                e.drain()
                e.max(vals[:], bi_f[:])
                e.drain()
                return e.max_index(idxu[:], vals[:], bi_f[:])
            P.step("vector", topk_fn)
            def wsel_fn(e):
                e.tensor_copy(idxf[:], idxu[:])
                e.drain()
                for kk in range(4):
                    e.tensor_scalar(oh[:, kk, :], iota32[:], idxf[:, kk:kk + 1], None, ALU.is_equal)
                e.drain()
                for kk in range(4):
                    e.tensor_tensor(oh[:, kk, :], oh[:, kk, :], sc_f[:], ALU.mult)
                e.drain()
                og3 = og[:].rearrange("p (k o) -> p k o", k=8)
                e.tensor_reduce(og3[:, 0:4, :], oh[:], mybir.AxisListType.X, ALU.add)
                e.drain()
                e.tensor_reduce(og[:, 4:5], og[:, 0:4], mybir.AxisListType.X, ALU.add)
                e.drain()
                return e.reciprocal(og[:, 5:6], og[:, 4:5])
            P.step("vector", wsel_fn)
            def comb_fn(e):
                e.memset(comb_tok[:], 0.0)
                e.drain()
                for kk in range(4):
                    e.scalar_tensor_tensor(comb_tok[:], oh[:, kk, :], og[0:128, 5:6], comb_tok[:],
                                           ALU.mult, ALU.add)
                    e.drain()
                e.tensor_scalar(wk4[:], og[:, 0:4], og[:, 5:6], None, ALU.mult)
                return e.nop()
            P.step("vector", comb_fn)
            P.step("tensor", lambda e, t4=t4: e.transpose(pct[:], comb_tok[:], identf[:]))
            P.step("vector", lambda e, buf=buf, t4=t4: e.tensor_copy(combt_sb[buf][:, t4 * 128:(t4 + 1) * 128], pct[:]))
            P.step("gpsimd", lambda e, c=c, t4=t4: e.dma_start(
                out=selT[c * CW + t4 * 128:c * CW + (t4 + 1) * 128, :], in_=idxf[:, 0:4]), dma=True)
            P.step("gpsimd", lambda e, c=c, t4=t4: e.dma_start(
                out=wgtT[c * CW + t4 * 128:c * CW + (t4 + 1) * 128, :], in_=wk4[:]), dma=True)
            for s in range(NSLAB):
                P.step("tensor", lambda e, buf=buf, t4=t4, s=s: e.transpose(
                    pxt[:], xn_sb[buf][:, s, t4 * 128:(t4 + 1) * 128], identbb[:]))
                P.step("vector", lambda e, s=s: e.tensor_copy(xtk[:, s * 128:(s + 1) * 128], pxt[:]))
            P.step("gpsimd", lambda e, c=c, t4=t4: e.dma_start(
                out=xtokT[c * CW + t4 * 128:c * CW + (t4 + 1) * 128, :], in_=xtk[:]), dma=True)
        P.step("gpsimd", lambda e, c=c, buf=buf: e.dma_start(
            out=slabs_ap(xnT, c * CW, CW), in_=xn_sb[buf][:]), dma=True, inc=s_st)
        P.step("gpsimd", lambda e, c=c, buf=buf: e.dma_start(
            out=combT.rearrange("e t -> e t")[:, c * CW:(c + 1) * CW], in_=combt_sb[buf][:]), dma=True, inc=s_st)
        st_gate[c] = s_st.n
    P.emit()


def build_p2b(nc, io, es, e0, yin, yout_name, final=False, name=None):
    name = name or f"p2e{e0}"
    xnT = io.stream("xn2T", [D, T], BF16, produce=False)
    combT = io.stream("combT", [32, T], F32, produce=False)
    yaccI = io.inp(yin, [D, T], BF16) if yin == "yacc0" else io.stream(yin, [D, T], BF16, produce=False)
    if final:
        x1T = io.stream("x1T", [D, T], F32, produce=False)
        x2T = io.stream("x2T", [D, T], F32, produce=True)
    else:
        yaccO = io.stream(yout_name, [D, T], BF16, produce=True)
    CW = 512
    NHALF = 4
    TH = T // NHALF
    NCH_H = TH // CW
    NI = 6
    P = Pipe(nc, es, name)
    t = lambda n, sh, dt: es.enter_context(nc.sbuf_tensor(name + "_" + n, sh, dt))
    pt = lambda n, sh: es.enter_context(nc.psum_tensor(name + "_" + n, sh, F32))
    w1s = [t(f"w1_{i}", [128, NSLAB, 768], BF16) for i in range(2)]
    w3s = [t(f"w3_{i}", [128, NSLAB, 768], BF16) for i in range(2)]
    w2s = [t(f"w2_{i}", [128, NI, D], BF16) for i in range(2)]
    xn_h = t("xnh", [128, NSLAB, TH], BF16)
    ya_h = t("yah", [128, NSLAB, TH], BF16)
    crow = t("crow", [1, TH], F32)
    bc_sb = t("bc", [128, CW], F32)
    sg_bf = t("sg", [128, CW], BF16)
    a_bf = t("abf", [128, NI, CW], BF16)
    x_f = t("xf", [128, NSLAB, CW], F32)
    o_f = t("of", [128, NSLAB, CW], F32)
    pbc = pt("pbc", [128, CW])
    pg = pt("pg", [128, CW])
    pu = pt("pu", [128, CW])
    py = pt("py", [128, CW])
    onesr = t("onesr", [1, 128], F32)
    P.step("sync", lambda e: e.dma_start(out=onesr[:], in_=io.inp("c_ones_row_f", [1, 128], F32)), dma=True, wait=(0, 0))
    w1ap = io.inp("w_moe_w1", [32, D, 768], F32)
    w3ap = io.inp("w_moe_w3", [32, D, 768], F32)
    w2ap = io.inp("w_moe_w2", [32, 768, D], F32)
    for h in range(NHALF):
        P.step("sync", lambda e, h=h: e.dma_start(out=xn_h[:], in_=slabs_ap(xnT, h * TH, TH)), dma=True)
        P.step("sync", lambda e, h=h: e.dma_start(out=ya_h[:], in_=slabs_ap(yaccI, h * TH, TH)), dma=True)
        for ei in range(8):
            eg = e0 + ei
            wb = ei % 2
            P.step("gpsimd", lambda e, eg=eg, wb=wb: e.dma_start(
                out=w1s[wb][:], in_=w1ap[eg].rearrange("(k p) m -> p k m", p=128)), dma=True)
            P.step("gpsimd", lambda e, eg=eg, wb=wb: e.dma_start(
                out=w3s[wb][:], in_=w3ap[eg].rearrange("(k p) m -> p k m", p=128)), dma=True)
            P.step("gpsimd", lambda e, eg=eg, wb=wb: e.dma_start(
                out=w2s[wb][:], in_=w2ap[eg].rearrange("(k p) m -> p k m", p=128)), dma=True)
            P.step("sync", lambda e, eg=eg, h=h: e.dma_start(
                out=crow[:], in_=combT[eg:eg + 1, h * TH:(h + 1) * TH]), dma=True)
            for c in range(NCH_H):
                def mmbc(e, c=c):
                    return e.matmul(pbc[:], onesr[:], crow[:, c * CW:(c + 1) * CW], start=True, stop=True)
                P.step("tensor", mmbc)
                P.step("vector", lambda e: e.tensor_copy(bc_sb[:], pbc[:]))
                for i in range(NI):
                    def mmg(e, c=c, i=i, wb=wb):
                        for k in range(NSLAB):
                            ins = e.matmul(pg[:], w1s[wb][:, k, i * 128:(i + 1) * 128],
                                           xn_h[:, k, c * CW:(c + 1) * CW], start=(k == 0), stop=(k == NSLAB - 1))
                        return ins
                    P.step("tensor", mmg)
                    P.step("scalar", lambda e: e.activation(out=sg_bf[:], in_=pg[:], func=AF.Silu))
                    def mmu(e, c=c, i=i, wb=wb):
                        for k in range(NSLAB):
                            ins = e.matmul(pu[:], w3s[wb][:, k, i * 128:(i + 1) * 128],
                                           xn_h[:, k, c * CW:(c + 1) * CW], start=(k == 0), stop=(k == NSLAB - 1))
                        return ins
                    P.step("tensor", mmu)
                    def a_fn(e, i=i):
                        e.tensor_tensor(a_bf[:, i, :], sg_bf[:], pu[:], ALU.mult)
                        e.drain()
                        return e.tensor_tensor(a_bf[:, i, :], a_bf[:, i, :], bc_sb[:], ALU.mult)
                    P.step("vector", a_fn)
                for s in range(NSLAB):
                    def mmy(e, s=s, wb=wb):
                        for k in range(NI):
                            ins = e.matmul(py[:], w2s[wb][:, k, s * 128:(s + 1) * 128],
                                           a_bf[:, k, :], start=(k == 0), stop=(k == NI - 1))
                        return ins
                    P.step("tensor", mmy)
                    def acc_fn(e, s=s, c=c):
                        sl = ya_h[:, s, c * CW:(c + 1) * CW]
                        return e.tensor_tensor(sl, sl, py[:], ALU.add)
                    P.step("vector", acc_fn)
        if not final:
            P.step("gpsimd", lambda e, h=h: e.dma_start(out=slabs_ap(yaccO, h * TH, TH), in_=ya_h[:]), dma=True)
        else:
            for c in range(NCH_H):
                P.step("sync", lambda e, h=h, c=c: e.dma_start(out=x_f[:], in_=slabs_ap(x1T, h * TH + c * CW, CW)), dma=True)
                def res_fn(e, c=c):
                    for s in range(NSLAB):
                        ins = e.tensor_tensor(o_f[:, s, :], x_f[:, s, :], ya_h[:, s, c * CW:(c + 1) * CW], ALU.add)
                    return ins
                P.step("vector", res_fn)
                P.step("gpsimd", lambda e, h=h, c=c: e.dma_start(out=slabs_ap(x2T, h * TH + c * CW, CW), in_=o_f[:]), dma=True)
    P.emit()


def build_p3a(nc, io, es, name="p3a"):
    x2T = io.stream("x2T", [D, T], F32, produce=False)
    xnT = io.stream("xn3T", [D, T], BF16, produce=True)
    qxT = io.stream("qxT", [D, T], BF16, produce=True)
    kxT = io.stream("kxT", [D, T], BF16, produce=True)
    vtokT = io.stream("vtokT", [T, 2048], BF16, produce=True)
    CW = 256
    NCH = T // CW
    P = Pipe(nc, es, name)
    t = lambda n, sh, dt: es.enter_context(nc.sbuf_tensor(name + "_" + n, sh, dt))
    pt = lambda n, sh, dt=F32: es.enter_context(nc.psum_tensor(name + "_" + n, sh, dt))
    wqkv = t("wqkv", [128, NSLAB, 4096], BF16)
    rotA = t("rotA", [128, 128], BF16)
    x_f = [t(f"x{i}", [128, NSLAB, CW], F32) for i in range(2)]
    xn_sb = [t(f"xn{i}", [128, NSLAB, CW], BF16) for i in range(2)]
    tq = [t("tq0", [128, NSLAB, 2, CW], BF16)] * 2
    tk = [t("tk0", [128, NSLAB, 2, CW], BF16)] * 2
    q_bf = t("qbf", [128, CW], BF16)
    t1_b = t("t1b", [128, CW], BF16)
    t2_b = t("t2b", [128, CW], BF16)
    qx_sb = [t(f"qx{i}", [128, NSLAB, CW], BF16) for i in range(2)]
    kx_sb = [t(f"kx{i}", [128, NSLAB, CW], BF16) for i in range(2)]
    v_bf = t("vbf", [128, 512], BF16)
    tmp_f = t("tmp", [128, CW], F32)
    kon = t("kon", [128, 1], F32)
    sb = {
        "ones_col_bf": t("ones", [128, 1], BF16),
        "ones_row_f": t("onesr", [1, 128], F32),
        name + "xbf": t("xbf", [128, NSLAB, CW], BF16),
        name + "sqbf": t("sqbf", [128, NSLAB, CW], BF16),
        name + "stat": t("stat", [1, 6 * CW], F32),
    }
    ps = {
        name + "psx": pt("psx", [1, CW]),
        name + "psq": pt("psq", [1, CW]),
        name + "pbr": pt("pbr", [128, CW]),
        name + "pbp": pt("pbp", [128, CW]),
    }
    pq = pt("pq", [128, CW])
    pr = pt("pr", [128, CW])
    pv = pt("pv", [128, 512])
    eps_ap = kon[0:1, 0:1]
    P.step("vector", lambda e: e.memset(kon[:, 0:1], 1e-5))
    P.step("sync", lambda e: e.dma_start(out=sb["ones_col_bf"][:], in_=io.inp("c_ones_col_bf", [128, 1], BF16)), dma=True, wait=(0, 0))
    P.step("sync", lambda e: e.dma_start(out=sb["ones_row_f"][:], in_=io.inp("c_ones_row_f", [1, 128], F32)), dma=True, wait=(0, 0))
    P.step("sync", lambda e: e.dma_start(out=rotA[:], in_=io.inp("c_rot_lhsT", [128, 128], BF16)), dma=True, wait=(0, 0))
    P.step("gpsimd", lambda e: e.dma_start(out=wqkv[:], in_=io.inp("w_qkvT", [D, 4096], F32).rearrange("(k p) m -> p k m", p=128)), dma=True, wait=(0, 0))
    tqc = io.inp("c_tqc", [D, S], BF16)
    tqs = io.inp("c_tqs", [D, S], BF16)
    tkc = io.inp("c_tkc", [D, S], BF16)
    tks = io.inp("c_tks", [D, S], BF16)
    s_ld = P.xsem("ld"); s_st = P.xsem("st")
    st_gate = {}
    for c in range(NCH):
        buf = c % 2
        loc = (c * CW) % S
        xw = [(s_st, st_gate.get(c - 2, 0))]
        P.step("sync", lambda e, c=c, buf=buf: e.dma_start(out=x_f[buf][:], in_=slabs_ap(x2T, c * CW, CW)), dma=True, wait=(0, 0), inc=s_ld, waits=xw)
        def tbl(dst, src, loc):
            return lambda e: e.dma_start(out=dst[:], in_=src.rearrange("(s p) t -> p s t", p=128)[:, :, loc:loc + CW])
        P.step("sync", lambda e: e.nop(), waits=[(s_ld, s_ld.n)])
        P.step("sync", lambda e, buf=buf, loc=loc: e.dma_start(out=tq[buf][:, :, 0, :], in_=tqc.rearrange("(s p) t -> p s t", p=128)[:, :, loc:loc + CW]), dma=True)
        P.step("sync", lambda e, buf=buf, loc=loc: e.dma_start(out=tq[buf][:, :, 1, :], in_=tqs.rearrange("(s p) t -> p s t", p=128)[:, :, loc:loc + CW]), dma=True)
        P.step("sync", lambda e, buf=buf, loc=loc: e.dma_start(out=tk[buf][:, :, 0, :], in_=tkc.rearrange("(s p) t -> p s t", p=128)[:, :, loc:loc + CW]), dma=True)
        P.step("sync", lambda e, buf=buf, loc=loc: e.dma_start(out=tk[buf][:, :, 1, :], in_=tks.rearrange("(s p) t -> p s t", p=128)[:, :, loc:loc + CW]), dma=True)
        ln_stats_steps(P, sb, ps, x_f[buf], CW, eps_ap, name)
        def ap_fn(e, buf=buf):
            for s in range(NSLAB):
                e.tensor_tensor(tmp_f[:], x_f[buf][:, s, :], ps[name + "pbr"][:], ALU.mult)
                e.drain()
                ins = e.tensor_tensor(xn_sb[buf][:, s, :], tmp_f[:], ps[name + "pbp"][:], ALU.subtract)
                e.drain()
            return ins
        P.step("vector", ap_fn)
        for which, off, tbl_sb, dst in (("q", 0, tq, qx_sb), ("k", D, tk, kx_sb)):
            for s in range(NSLAB):
                def mmq(e, buf=buf, s=s, off=off):
                    for k in range(NSLAB):
                        ins = e.matmul(pq[:], wqkv[:, k, off + s * 128:off + (s + 1) * 128],
                                       xn_sb[buf][:, k, :], start=(k == 0), stop=(k == NSLAB - 1))
                    return ins
                P.step("tensor", mmq)
                P.step("vector", lambda e: e.tensor_copy(q_bf[:], pq[:]))
                P.step("tensor", lambda e: e.matmul(pr[:], rotA[:], q_bf[:], start=True, stop=True))
                def xpos_fn(e, buf=buf, s=s, tbl_sb=tbl_sb, dst=dst):
                    e.tensor_tensor(t1_b[:], q_bf[:], tbl_sb[buf][:, s, 0, :], ALU.mult)
                    e.tensor_tensor(t2_b[:], pr[:], tbl_sb[buf][:, s, 1, :], ALU.mult)
                    e.drain()
                    return e.tensor_tensor(dst[buf][:, s, :], t1_b[:], t2_b[:], ALU.add)
                P.step("vector", xpos_fn)
        for t4 in range(CW // 128):
            for n in range(4):
                def mmv(e, buf=buf, t4=t4, n=n):
                    for k in range(NSLAB):
                        ins = e.matmul(pv[:], xn_sb[buf][:, k, t4 * 128:(t4 + 1) * 128],
                                       wqkv[:, k, 2048 + n * 512:2048 + (n + 1) * 512],
                                       start=(k == 0), stop=(k == NSLAB - 1))
                    return ins
                P.step("tensor", mmv)
                P.step("vector", lambda e: e.tensor_copy(v_bf[:], pv[:]))
                P.step("gpsimd", lambda e, c=c, t4=t4, n=n: e.dma_start(
                    out=vtokT[c * CW + t4 * 128:c * CW + (t4 + 1) * 128, n * 512:(n + 1) * 512],
                    in_=v_bf[:]), dma=True)
        P.step("gpsimd", lambda e, c=c, buf=buf: e.dma_start(out=slabs_ap(xnT, c * CW, CW), in_=xn_sb[buf][:]), dma=True, inc=s_st)
        P.step("gpsimd", lambda e, c=c, buf=buf: e.dma_start(out=slabs_ap(qxT, c * CW, CW), in_=qx_sb[buf][:]), dma=True, inc=s_st)
        P.step("gpsimd", lambda e, c=c, buf=buf: e.dma_start(out=slabs_ap(kxT, c * CW, CW), in_=kx_sb[buf][:]), dma=True, inc=s_st)
        st_gate[c] = s_st.n
    P.emit()


def build_p3b(nc, io, es, name="p3b"):
    x2T = io.stream("x2T", [D, T], F32, produce=False)
    xnT = io.stream("xn3T", [D, T], BF16, produce=False)
    qxT = io.stream("qxT", [D, T], BF16, produce=False)
    kxT = io.stream("kxT", [D, T], BF16, produce=False)
    vtokT = io.stream("vtokT", [T, 2048], BF16, produce=False)
    x3T = io.stream("x3T", [D, T], F32, produce=True)
    NH = 8
    HD = 128
    VD = 256
    CC = 128
    NCH_B = S // CC
    P = Pipe(nc, es, name)
    t = lambda n, sh, dt: es.enter_context(nc.sbuf_tensor(name + "_" + n, sh, dt))
    pt = lambda n, sh, dt=F32: es.enter_context(nc.psum_tensor(name + "_" + n, sh, dt))
    wgat = t("wgat", [128, NSLAB, 2048], BF16)
    wproj = t("wproj", [128, 16, D], BF16)
    identb = t("identb", [128, 128], BF16)
    expoT_s = t("expo", [128, 128], F32)
    cmask_s = t("cmask", [128, 128], F32)
    iotaJ_s = t("iotaJ", [128, 128], F32)
    revcol_s = t("revc", [128, 1], F32)
    k128_s = t("k128", [128, 1], F32)
    decrow = t("decrow", [1, 8], F32)
    lnd_bc = t("lnd", [128, 8], F32)
    dmT8 = t("dmT8", [128, NH, 128], BF16)
    dbc8 = t("dbc8", [128, NH, 128], BF16)
    wexp = t("wexp", [128, NH], F32)
    d128 = t("d128", [128, NH], F32)
    qx_b = t("qxb", [128, NSLAB, S], BF16)
    kx_b = t("kxb", [128, NSLAB, S], BF16)
    xn_c = t("xnc", [128, NSLAB, CC], BF16)
    v_tok = t("vtok", [128, 2048], BF16)
    gate_bf = t("gatebf", [128, 2048], BF16)
    gated_bf = t("gatedbf", [128, 2048], BF16)
    sc_bf = [t(f"scbf{i}", [128, 128], BF16) for i in range(2)]
    qdec_bf = [t(f"qdec{i}", [128, 128], BF16) for i in range(2)]
    k_tok = t("ktok", [128, 128], BF16)
    vw_bf = t("vwbf", [128, VD], BF16)
    o_f = t("of", [128, VD], F32)
    n_f = t("nf", [128, VD], F32)
    red = t("red", [128, 6], F32)
    S_sb = t("Ssb", [128, NH, VD], F32)
    S_bf = t("Sbf", [128, NH, VD], BF16)
    gT4 = t("gT4", [128, 16, 512], BF16)
    x_f = t("xf", [128, NSLAB, 512], F32)
    y_f = x_f
    onesr = t("onesr", [1, 128], F32)
    psc = [pt(f"psc{i}", [128, 128]) for i in range(2)]
    ppo = [pt(f"ppo{i}", [128, VD]) for i in range(2)]
    pS = pt("pS", [128, VD])
    pg = pt("pgg", [128, 512])
    pt_b = pt("ptb", [128, 128], BF16)
    py = pt("pyy", [128, 512])
    pbc8 = None
    P.step("sync", lambda e: e.dma_start(out=onesr[:], in_=io.inp("c_ones_row_f", [1, 128], F32)), dma=True, wait=(0, 0))
    P.step("sync", lambda e: e.dma_start(out=identb[:], in_=io.inp("c_ident_bf", [128, 128], BF16)), dma=True, wait=(0, 0))
    P.step("sync", lambda e: e.dma_start(out=expoT_s[:], in_=io.inp("c_expoT", [128, 128], F32)), dma=True, wait=(0, 0))
    P.step("sync", lambda e: e.dma_start(out=cmask_s[:], in_=io.inp("c_cmask_scale", [128, 128], F32)), dma=True, wait=(0, 0))
    P.step("sync", lambda e: e.dma_start(out=iotaJ_s[:], in_=io.inp("c_iotaJ", [128, 128], F32)), dma=True, wait=(0, 0))
    P.step("sync", lambda e: e.dma_start(out=revcol_s[:], in_=io.inp("c_rev_col", [128, 1], F32)), dma=True, wait=(0, 0))
    P.step("sync", lambda e: e.dma_start(out=decrow[:], in_=io.inp("w_decay", [1, 8], F32)), dma=True, wait=(0, 0))
    P.step("gpsimd", lambda e: e.dma_start(out=wgat[:], in_=io.inp("w_gatedT", [D, 2048], F32).rearrange("(k p) m -> p k m", p=128)), dma=True, wait=(0, 0))
    P.step("gpsimd", lambda e: e.dma_start(out=wproj[:], in_=io.inp("w_projT", [2048, D], F32).rearrange("(k p) m -> p k m", p=128)), dma=True, wait=(0, 0))
    P.step("vector", lambda e: e.memset(k128_s[:], 128.0))
    P.step("vector", lambda e: e.memset(red[:, 5:6], 1e-5))
    P.step("scalar", lambda e: e.activation(out=decrow[:], in_=decrow[:], func=AF.Ln))
    P.step("tensor", lambda e: e.matmul(psc[0][:, 0:8], onesr[:], decrow[:], start=True, stop=True))
    P.step("vector", lambda e: e.tensor_copy(lnd_bc[:], psc[0][:, 0:8]))
    for h in range(NH):
        def tab_fn(e, h=h):
            e.activation(out=dmT8[:, h, :], in_=expoT_s[:], func=AF.Exp, scale=lnd_bc[:, h:h + 1])
            e.activation(out=dbc8[:, h, :], in_=iotaJ_s[:], func=AF.Exp, scale=lnd_bc[:, h:h + 1])
            e.activation(out=wexp[:, h:h + 1], in_=revcol_s[:], func=AF.Exp, scale=lnd_bc[:, h:h + 1])
            return e.activation(out=d128[:, h:h + 1], in_=k128_s[:], func=AF.Exp, scale=lnd_bc[:, h:h + 1])
        P.step("scalar", tab_fn)
        def tab2_fn(e, h=h):
            e.tensor_tensor(dmT8[:, h, :], dmT8[:, h, :], cmask_s[:], ALU.mult)
            e.drain()
            return e.tensor_scalar_mul(dbc8[:, h, :], dbc8[:, h, :], 1.0 / math.sqrt(128.0))
        P.step("vector", tab2_fn)
    s_mm = P.xsem("mm")

    def pe(fn, waits=()):
        P.step("tensor", fn, inc=s_mm, waits=waits)
        return s_mm.n
    for b in range(B):
        P.step("sync", lambda e, b=b: e.dma_start(out=qx_b[:], in_=slabs_ap(qxT, b * S, S)), dma=True, waits=[(s_mm, s_mm.n)])
        P.step("sync", lambda e, b=b: e.dma_start(out=kx_b[:], in_=slabs_ap(kxT, b * S, S)), dma=True)
        P.step("vector", lambda e: e.memset(S_sb[:], 0.0), waits=[(s_mm, s_mm.n)])
        P.step("vector", lambda e: e.memset(S_bf[:], 0.0))
        for cc in range(NCH_B):
            t0 = b * S + cc * CC
            lo = cc * CC
            P.step("sync", lambda e, t0=t0: e.dma_start(out=v_tok[:], in_=vtokT[t0:t0 + CC, :]), dma=True, waits=[(s_mm, s_mm.n)])
            P.step("sync", lambda e, t0=t0: e.dma_start(out=xn_c[:], in_=slabs_ap(xnT, t0, CC)), dma=True)
            for n in range(4):
                def mmgate(e, n=n):
                    for k in range(NSLAB):
                        ins = e.matmul(pg[:], xn_c[:, k, :], wgat[:, k, n * 512:(n + 1) * 512],
                                       start=(k == 0), stop=(k == NSLAB - 1))
                    return ins
                gn = pe(mmgate)
                P.step("scalar", lambda e, n=n: e.activation(out=gate_bf[:, n * 512:(n + 1) * 512], in_=pg[:], func=AF.Silu), waits=[(s_mm, gn)])
            po_n = {}

            def F(h, lo=lo):
                p = h % 2
                def mmsc(e, h=h, lo=lo, p=p):
                    return e.matmul(psc[p][:], kx_b[:, h, lo:lo + CC], qx_b[:, h, lo:lo + CC], start=True, stop=True)
                scn = pe(mmsc)
                P.step("vector", lambda e, h=h, p=p: e.tensor_tensor(sc_bf[p][:], psc[p][:], dmT8[:, h, :], ALU.mult), waits=[(s_mm, scn)])
                P.step("vector", lambda e, h=h, lo=lo, p=p: e.tensor_tensor(qdec_bf[p][:], qx_b[:, h, lo:lo + CC], dbc8[:, h, :], ALU.mult))
                def mmpo(e, h=h, p=p):
                    e.matmul(ppo[p][:], sc_bf[p][:], v_tok[:, h * VD:(h + 1) * VD], start=True, stop=False)
                    return e.matmul(ppo[p][:], qdec_bf[p][:], S_bf[:, h, :], start=False, stop=True)
                po_n[h] = pe(mmpo)

            def Bk(h, lo=lo):
                p = h % 2
                def ln_fn(e, h=h, p=p):
                    e.tensor_copy(o_f[:], ppo[p][:])
                    e.drain()
                    e.tensor_reduce(red[:, 0:1], o_f[:], mybir.AxisListType.X, ALU.add)
                    e.drain()
                    return e.tensor_scalar_mul(red[:, 0:1], red[:, 0:1], 1.0 / VD)
                P.step("vector", ln_fn, waits=[(s_mm, po_n[h])])
                P.step("scalar", lambda e: e.activation(out=n_f[:], in_=o_f[:], func=AF.Square, accum_out=red[:, 1:2]))
                def ln2_fn(e, h=h):
                    e.tensor_scalar_mul(red[:, 1:2], red[:, 1:2], 1.0 / VD)
                    e.tensor_tensor(red[:, 2:3], red[:, 0:1], red[:, 0:1], ALU.mult)
                    e.drain()
                    return e.tensor_tensor(red[:, 1:2], red[:, 1:2], red[:, 2:3], ALU.subtract)
                P.step("vector", ln2_fn)
                def rstd_fn(e):
                    e.activation(out=red[:, 1:2], in_=red[:, 1:2], func=AF.Ln, bias=red[:, 5:6])
                    e.drain()
                    return e.activation(out=red[:, 1:2], in_=red[:, 1:2], func=AF.Exp, scale=-0.5)
                P.step("scalar", rstd_fn)
                def gated_fn(e, h=h):
                    e.tensor_tensor(red[:, 3:4], red[:, 0:1], red[:, 1:2], ALU.mult)
                    e.drain()
                    e.tensor_scalar(n_f[:], o_f[:], red[:, 1:2], red[:, 3:4], ALU.mult, ALU.subtract)
                    e.drain()
                    return e.tensor_tensor(gated_bf[:, h * VD:(h + 1) * VD], n_f[:], gate_bf[:, h * VD:(h + 1) * VD], ALU.mult)
                P.step("vector", gated_fn)
                tn = pe(lambda e, h=h, lo=lo: e.transpose(pt_b[:], kx_b[:, h, lo:lo + CC], identb[:]))
                P.step("vector", lambda e: e.tensor_copy(k_tok[:], pt_b[:]), waits=[(s_mm, tn)])
                P.step("vector", lambda e, h=h: e.tensor_scalar(vw_bf[:], v_tok[:, h * VD:(h + 1) * VD], wexp[:, h:h + 1], None, ALU.mult))
                sn = pe(lambda e: e.matmul(pS[:], k_tok[:], vw_bf[:], start=True, stop=True))
                def supd_fn(e, h=h):
                    e.scalar_tensor_tensor(S_sb[:, h, :], S_sb[:, h, :], d128[:, h:h + 1], pS[:],
                                           ALU.mult, ALU.add)
                    e.drain()
                    return e.tensor_copy(S_bf[:, h, :], S_sb[:, h, :])
                P.step("vector", supd_fn, waits=[(s_mm, sn)])
            F(0)
            for h in range(NH):
                if h + 1 < NH:
                    F(h + 1)
                Bk(h)
            for i in range(16):
                tn2 = pe(lambda e, i=i: e.transpose(pt_b[:], gated_bf[:, i * 128:(i + 1) * 128], identb[:]))
                P.step("vector", lambda e, i=i, cc=cc: e.tensor_copy(gT4[:, i, (cc % 4) * 128:(cc % 4 + 1) * 128], pt_b[:]), waits=[(s_mm, tn2)])
            if cc % 4 == 3:
                c0 = t0 + CC - 512
                P.step("sync", lambda e, c0=c0: e.dma_start(out=x_f[:], in_=slabs_ap(x2T, c0, 512)), dma=True)
                for m in range(NSLAB):
                    def mmproj(e, m=m):
                        for k in range(16):
                            ins = e.matmul(py[:], wproj[:, k, m * 128:(m + 1) * 128], gT4[:, k, :],
                                           start=(k == 0), stop=(k == 15))
                        return ins
                    pn = pe(mmproj)
                    P.step("vector", lambda e, m=m: e.tensor_tensor(x_f[:, m, :], x_f[:, m, :], py[:], ALU.add), waits=[(s_mm, pn)])
                P.step("gpsimd", lambda e, c0=c0: e.dma_start(out=slabs_ap(x3T, c0, 512), in_=y_f[:]), dma=True)
    P.emit()


def build_p4a(nc, io, es, name="p4a"):
    x3T = io.stream("x3T", [D, T], F32, produce=False)
    xnT = io.stream("xn4T", [D, T], BF16, produce=True)
    combT = io.stream("comb4T", [8, T], F32, produce=True)
    latT = io.stream("latT", [512, T], BF16, produce=True)
    CW = 512
    NCH = T // CW
    NT4 = CW // 128
    P = Pipe(nc, es, name)
    t = lambda n, sh, dt: es.enter_context(nc.sbuf_tensor(name + "_" + n, sh, dt))
    pt = lambda n, sh: es.enter_context(nc.psum_tensor(name + "_" + n, sh, F32))
    gw = t("gw", [128, NSLAB, 8], BF16)
    fc1 = t("fc1", [128, NSLAB, 512], BF16)
    cbrow = t("cbrow", [1, 8], F32)
    cbc = t("cbc", [128, 8], F32)
    iota8 = t("iota8", [128, 8], F32)
    identf = t("identf", [128, 128], F32)
    x_f = [t(f"x{i}", [128, NSLAB, CW], F32) for i in range(2)]
    xn_sb = [t(f"xn{i}", [128, NSLAB, CW], BF16) for i in range(2)]
    lat_sb = t("lat", [128, 4, CW], BF16)
    sc_f = t("sc", [128, 8], F32)
    sfc = t("sfc", [128, 8], F32)
    pad8 = t("pad8", [128, 16], F32)
    vals = t("vals", [128, 8], F32)
    idxu = t("idxu", [128, 8], U32)
    idxf = t("idxf", [128, 8], F32)
    oh = t("oh", [128, 2, 8], F32)
    og = t("og", [128, 8], F32)
    gs = t("gs", [128, 4], F32)
    comb_tok = t("ctok", [128, 8], F32)
    combt_sb = [t(f"cT{i}", [8, CW], F32) for i in range(2)]
    tmp_f = t("tmp", [128, CW], F32)
    kon = t("kon", [128, 1], F32)
    sb = {
        "ones_col_bf": t("ones", [128, 1], BF16),
        "ones_row_f": t("onesr", [1, 128], F32),
        name + "xbf": t("xbf", [128, NSLAB, CW], BF16),
        name + "sqbf": t("sqbf", [128, NSLAB, CW], BF16),
        name + "stat": t("stat", [1, 6 * CW], F32),
    }
    ps = {
        name + "psx": pt("psx", [1, CW]),
        name + "psq": pt("psq", [1, CW]),
        name + "pbr": pt("pbr", [128, CW]),
        name + "pbp": pt("pbp", [128, CW]),
    }
    psc = pt("psc", [128, 8])
    pct = pt("pct", [8, 128])
    plat = pt("plat", [128, CW])
    eps_ap = kon[0:1, 0:1]
    P.step("vector", lambda e: e.memset(kon[:, 0:1], 1e-5))
    P.step("sync", lambda e: e.dma_start(out=sb["ones_col_bf"][:], in_=io.inp("c_ones_col_bf", [128, 1], BF16)), dma=True, wait=(0, 0))
    P.step("sync", lambda e: e.dma_start(out=sb["ones_row_f"][:], in_=io.inp("c_ones_row_f", [1, 128], F32)), dma=True, wait=(0, 0))
    P.step("sync", lambda e: e.dma_start(out=iota8[:], in_=io.inp("c_iota8_bc", [128, 8], F32)), dma=True, wait=(0, 0))
    P.step("sync", lambda e: e.dma_start(out=identf[:], in_=io.inp("c_ident_f", [128, 128], F32)), dma=True, wait=(0, 0))
    P.step("sync", lambda e: e.dma_start(out=cbrow[:], in_=io.inp("w_corr_bias", [1, 8], F32)), dma=True, wait=(0, 0))
    P.step("gpsimd", lambda e: e.dma_start(out=gw[:], in_=io.inp("w_f_gateT", [D, 8], F32).rearrange("(k p) m -> p k m", p=128)), dma=True, wait=(0, 0))
    P.step("gpsimd", lambda e: e.dma_start(out=fc1[:], in_=io.inp("w_fc1T", [D, 512], F32).rearrange("(k p) m -> p k m", p=128)), dma=True, wait=(0, 0))
    P.step("tensor", lambda e: e.matmul(psc[:], sb["ones_row_f"][:], cbrow[:], start=True, stop=True))
    P.step("vector", lambda e: e.tensor_copy(cbc[:], psc[:]))
    s_ld = P.xsem("ld"); s_st = P.xsem("st")
    st_gate = {}
    for c in range(NCH):
        buf = c % 2
        P.step("sync", lambda e, c=c, buf=buf: e.dma_start(out=x_f[buf][:], in_=slabs_ap(x3T, c * CW, CW)),
               dma=True, wait=(0, 0), inc=s_ld, waits=[(s_st, st_gate.get(c - 2, 0))])
        P.step("sync", lambda e: e.nop(), waits=[(s_ld, s_ld.n)])
        ln_stats_steps(P, sb, ps, x_f[buf], CW, eps_ap, name)
        def ap_fn(e, buf=buf):
            for s in range(NSLAB):
                e.tensor_tensor(tmp_f[:], x_f[buf][:, s, :], ps[name + "pbr"][:], ALU.mult)
                e.drain()
                ins = e.tensor_tensor(xn_sb[buf][:, s, :], tmp_f[:], ps[name + "pbp"][:], ALU.subtract)
                e.drain()
            return ins
        P.step("vector", ap_fn)
        for m in range(4):
            def mml(e, buf=buf, m=m):
                for k in range(NSLAB):
                    ins = e.matmul(plat[:], fc1[:, k, m * 128:(m + 1) * 128],
                                   xn_sb[buf][:, k, :], start=(k == 0), stop=(k == NSLAB - 1))
                return ins
            P.step("tensor", mml)
            P.step("vector", lambda e, m=m: e.tensor_copy(lat_sb[:, m, :], plat[:]))
        P.step("gpsimd", lambda e, c=c: e.dma_start(
            out=latT.rearrange("(s p) t -> p s t", p=128, s=4)[:, :, c * CW:(c + 1) * CW],
            in_=lat_sb[:]), dma=True)
        for t4 in range(NT4):
            def mmsc(e, buf=buf, t4=t4):
                for k in range(NSLAB):
                    ins = e.matmul(psc[:], xn_sb[buf][:, k, t4 * 128:(t4 + 1) * 128],
                                   gw[:, k, :], start=(k == 0), stop=(k == NSLAB - 1))
                return ins
            P.step("tensor", mmsc)
            P.step("scalar", lambda e: e.activation(out=sc_f[:], in_=psc[:], func=AF.Sigmoid))
            def grp_fn(e):
                e.tensor_tensor(sfc[:], sc_f[:], cbc[:], ALU.add)
                e.memset(pad8[:], -3.0e38)
                e.drain()
                e.tensor_copy(pad8[:, 0:4], sfc[:, 0:4])
                e.tensor_copy(pad8[:, 8:12], sfc[:, 4:8])
                e.drain()
                e.max(vals[:], pad8[:, 0:8])
                e.drain()
                e.tensor_tensor(gs[:, 0:1], vals[:, 0:1], vals[:, 1:2], ALU.add)
                e.max(vals[:], pad8[:, 8:16])
                e.drain()
                e.tensor_tensor(gs[:, 1:2], vals[:, 0:1], vals[:, 1:2], ALU.add)
                e.drain()
                e.tensor_tensor(gs[:, 2:3], gs[:, 0:1], gs[:, 1:2], ALU.is_ge)
                e.drain()
                e.tensor_scalar(gs[:, 3:4], gs[:, 2:3], -1.0, 1.0, ALU.mult, ALU.add)
                e.drain()
                e.tensor_scalar(sfc[:, 0:4], sfc[:, 0:4], gs[:, 2:3], None, ALU.mult)
                e.tensor_scalar(sfc[:, 4:8], sfc[:, 4:8], gs[:, 3:4], None, ALU.mult)
                e.drain()
                e.max(vals[:], sfc[:])
                e.drain()
                return e.max_index(idxu[:], vals[:], sfc[:])
            P.step("vector", grp_fn)
            def wsel_fn(e):
                e.tensor_copy(idxf[:], idxu[:])
                e.drain()
                for kk in range(2):
                    e.tensor_scalar(oh[:, kk, :], iota8[:], idxf[:, kk:kk + 1], None, ALU.is_equal)
                e.drain()
                for kk in range(2):
                    e.tensor_tensor(oh[:, kk, :], oh[:, kk, :], sc_f[:], ALU.mult)
                e.drain()
                og3 = og[:].rearrange("p (k o) -> p k o", k=8)
                e.tensor_reduce(og3[:, 0:2, :], oh[:], mybir.AxisListType.X, ALU.add)
                e.drain()
                e.tensor_reduce(og[:, 4:5], og[:, 0:2], mybir.AxisListType.X, ALU.add)
                e.drain()
                e.tensor_scalar_add(og[:, 4:5], og[:, 4:5], 1e-20)
                e.drain()
                return e.reciprocal(og[:, 5:6], og[:, 4:5])
            P.step("vector", wsel_fn)
            def comb_fn(e):
                e.memset(comb_tok[:], 0.0)
                e.drain()
                for kk in range(2):
                    e.scalar_tensor_tensor(comb_tok[:], oh[:, kk, :], og[0:128, 5:6], comb_tok[:],
                                           ALU.mult, ALU.add)
                    e.drain()
                return e.nop()
            P.step("vector", comb_fn)
            P.step("tensor", lambda e: e.transpose(pct[:], comb_tok[:], identf[:]))
            P.step("vector", lambda e, buf=buf, t4=t4: e.tensor_copy(combt_sb[buf][:, t4 * 128:(t4 + 1) * 128], pct[:]))
        P.step("gpsimd", lambda e, c=c, buf=buf: e.dma_start(
            out=slabs_ap(xnT, c * CW, CW), in_=xn_sb[buf][:]), dma=True, inc=s_st)
        P.step("gpsimd", lambda e, c=c, buf=buf: e.dma_start(
            out=combT[:, c * CW:(c + 1) * CW], in_=combt_sb[buf][:]), dma=True, inc=s_st)
        st_gate[c] = s_st.n
    P.emit()


def build_p4b(nc, io, es, name="p4b"):
    latT = io.stream("latT", [512, T], BF16, produce=False)
    combT = io.stream("comb4T", [8, T], F32, produce=False)
    routedT = io.stream("routedT", [D, T], F32, produce=True)
    CW = 512
    NCH = T // CW
    P = Pipe(nc, es, name)
    t = lambda n, sh, dt: es.enter_context(nc.sbuf_tensor(name + "_" + n, sh, dt))
    pt = lambda n, sh: es.enter_context(nc.psum_tensor(name + "_" + n, sh, F32))
    w1 = t("w1", [128, 8, 4, 512], BF16)
    w2 = t("w2", [128, 8, 4, 512], BF16)
    fc2 = t("fc2", [128, 4, D], BF16)
    onesr = t("onesr", [1, 128], F32)
    lat_c = [t(f"lat{i}", [128, 4, CW], BF16) for i in range(2)]
    crow = t("crow", [1, 8 * CW], F32)
    bc_sb = t("bc", [128, CW], F32)
    r_bf = t("rbf", [128, 4, CW], BF16)
    racc = t("racc", [128, 4, CW], F32)
    rq_bf = t("rq", [128, 4, CW], BF16)
    ro_f = t("rof", [128, NSLAB, CW], F32)
    tmp_f = t("tmpq", [128, CW], F32)
    pbc = pt("pbc", [128, CW])
    ph = pt("ph", [128, CW])
    po = pt("po", [128, CW])
    w1ap = io.inp("w_f_w1T", [8, 512, 512], F32)
    w2ap = io.inp("w_f_w2T", [8, 512, 512], F32)
    P.step("sync", lambda e: e.dma_start(out=onesr[:], in_=io.inp("c_ones_row_f", [1, 128], F32)), dma=True, wait=(0, 0))
    for eg in range(8):
        P.step("gpsimd", lambda e, eg=eg: e.dma_start(
            out=w1[:, eg, :, :], in_=w1ap[eg].rearrange("(k p) m -> p k m", p=128)), dma=True, wait=(0, 0))
        P.step("gpsimd", lambda e, eg=eg: e.dma_start(
            out=w2[:, eg, :, :], in_=w2ap[eg].rearrange("(k p) m -> p k m", p=128)), dma=True, wait=(0, 0))
    P.step("gpsimd", lambda e: e.dma_start(
        out=fc2[:], in_=io.inp("w_fc2T", [512, D], F32).rearrange("(k p) m -> p k m", p=128)), dma=True, wait=(0, 0))
    s_ld = P.xsem("ld"); s_st = P.xsem("st")
    st_gate = {}
    for c in range(NCH):
        buf = c % 2
        P.step("sync", lambda e, c=c, buf=buf: e.dma_start(
            out=lat_c[buf][:], in_=latT.rearrange("(s p) t -> p s t", p=128, s=4)[:, :, c * CW:(c + 1) * CW]),
            dma=True, wait=(0, 0), inc=s_ld, waits=[(s_st, st_gate.get(c - 2, 0))])
        P.step("sync", lambda e: e.nop(), waits=[(s_ld, s_ld.n)])
        for eg in range(8):
            P.step("sync", lambda e, c=c, eg=eg: e.dma_start(
                out=crow[:, eg * CW:(eg + 1) * CW], in_=combT[eg:eg + 1, c * CW:(c + 1) * CW]), dma=True)
        P.step("vector", lambda e: e.memset(racc[:], 0.0))
        for eg in range(8):
            def mmbc(e, eg=eg):
                return e.matmul(pbc[:], onesr[:], crow[:, eg * CW:(eg + 1) * CW], start=True, stop=True)
            P.step("tensor", mmbc)
            P.step("vector", lambda e: e.tensor_copy(bc_sb[:], pbc[:]))
            for m in range(4):
                def mmh(e, eg=eg, m=m, buf=buf):
                    for k in range(4):
                        ins = e.matmul(ph[:], w1[:, eg, k, m * 128:(m + 1) * 128],
                                       lat_c[buf][:, k, :], start=(k == 0), stop=(k == 3))
                    return ins
                P.step("tensor", mmh)
                def relu2_fn(e, m=m):
                    e.tensor_relu(tmp_f[:], ph[:])
                    e.drain()
                    return e.tensor_tensor(r_bf[:, m, :], tmp_f[:], tmp_f[:], ALU.mult)
                P.step("vector", relu2_fn)
            for m in range(4):
                def mmo(e, eg=eg, m=m):
                    for k in range(4):
                        ins = e.matmul(po[:], w2[:, eg, k, m * 128:(m + 1) * 128],
                                       r_bf[:, k, :], start=(k == 0), stop=(k == 3))
                    return ins
                P.step("tensor", mmo)
                def racc_fn(e, m=m):
                    e.tensor_tensor(tmp_f[:], po[:], bc_sb[:], ALU.mult)
                    e.drain()
                    sl = racc[:, m, :]
                    return e.tensor_tensor(sl, sl, tmp_f[:], ALU.add)
                P.step("vector", racc_fn)
        P.step("vector", lambda e: e.tensor_copy(rq_bf[:], racc[:]))
        for s in range(NSLAB):
            def mmf(e, s=s):
                for k in range(4):
                    ins = e.matmul(po[:], fc2[:, k, s * 128:(s + 1) * 128],
                                   rq_bf[:, k, :], start=(k == 0), stop=(k == 3))
                return ins
            P.step("tensor", mmf)
            P.step("vector", lambda e, s=s: e.tensor_copy(ro_f[:, s, :], po[:]))
        P.step("gpsimd", lambda e, c=c: e.dma_start(
            out=slabs_ap(routedT, c * CW, CW), in_=ro_f[:]), dma=True, inc=s_st)
        st_gate[c] = s_st.n
    P.emit()


def build_p4c(nc, io, es, name="p4c"):
    x3T = io.stream("x3T", [D, T], F32, produce=False)
    xnT = io.stream("xn4T", [D, T], BF16, produce=False)
    routedT = io.stream("routedT", [D, T], F32, produce=False)
    outT = io.out("outT", [D, T], F32)
    SH = 4096
    NSH = SH // 128
    CW = 256
    NCH = T // CW
    P = Pipe(nc, es, name)
    t = lambda n, sh, dt: es.enter_context(nc.sbuf_tensor(name + "_" + n, sh, dt))
    pt = lambda n, sh: es.enter_context(nc.psum_tensor(name + "_" + n, sh, F32))
    up = t("up", [128, NSLAB, SH], BF16)
    down = t("down", [128, NSH, D], BF16)
    xn_c = [t(f"xn{i}", [128, NSLAB, CW], BF16) for i in range(2)]
    x_f = t("xf", [128, NSLAB, CW], F32)
    r_f = t("rf", [128, NSLAB, CW], F32)
    hs_bf = t("hs", [128, NSH, CW], BF16)
    o_f = t("of", [128, NSLAB, CW], F32)
    tmp_f = t("tmpc", [128, CW], F32)
    ph = pt("ph", [128, CW])
    po = pt("po", [128, CW])
    P.step("gpsimd", lambda e: e.dma_start(
        out=up[:], in_=io.inp("w_upT", [D, SH], F32).rearrange("(k p) m -> p k m", p=128)), dma=True, wait=(0, 0))
    P.step("gpsimd", lambda e: e.dma_start(
        out=down[:], in_=io.inp("w_downT", [SH, D], F32).rearrange("(k p) m -> p k m", p=128)), dma=True, wait=(0, 0))
    s_ld = P.xsem("ld"); s_st = P.xsem("st")
    st_gate = {}
    for c in range(NCH):
        buf = c % 2
        P.step("sync", lambda e, c=c, buf=buf: e.dma_start(out=xn_c[buf][:], in_=slabs_ap(xnT, c * CW, CW)),
               dma=True, wait=(0, 0), inc=s_ld, waits=[(s_st, st_gate.get(c - 2, 0))])
        P.step("sync", lambda e: e.nop(), waits=[(s_ld, s_ld.n)])
        P.step("sync", lambda e, c=c: e.dma_start(out=x_f[:], in_=slabs_ap(x3T, c * CW, CW)), dma=True)
        P.step("sync", lambda e, c=c: e.dma_start(out=r_f[:], in_=slabs_ap(routedT, c * CW, CW)), dma=True)
        for m in range(NSH):
            def mmh(e, m=m, buf=buf):
                for k in range(NSLAB):
                    ins = e.matmul(ph[:], up[:, k, m * 128:(m + 1) * 128],
                                   xn_c[buf][:, k, :], start=(k == 0), stop=(k == NSLAB - 1))
                return ins
            P.step("tensor", mmh)
            def relu2_fn(e, m=m):
                e.tensor_relu(tmp_f[:], ph[:])
                e.drain()
                return e.tensor_tensor(hs_bf[:, m, :], tmp_f[:], tmp_f[:], ALU.mult)
            P.step("vector", relu2_fn)
        for s in range(NSLAB):
            def mmo(e, s=s):
                for k in range(NSH):
                    ins = e.matmul(po[:], down[:, k, s * 128:(s + 1) * 128],
                                   hs_bf[:, k, :], start=(k == 0), stop=(k == NSH - 1))
                return ins
            P.step("tensor", mmo)
            def out_fn(e, s=s):
                e.tensor_tensor(o_f[:, s, :], x_f[:, s, :], r_f[:, s, :], ALU.add)
                e.drain()
                return e.tensor_tensor(o_f[:, s, :], o_f[:, s, :], po[:], ALU.add)
            P.step("vector", out_fn)
        P.step("gpsimd", lambda e, c=c: e.dma_start(
            out=slabs_ap(outT, c * CW, CW), in_=o_f[:]), dma=True, inc=s_st)
        st_gate[c] = s_st.n
    P.emit()


NSLOTS = 96 * 512
NTILES = 96
BIGF = 1.0e9

def build_p2m(nc, io, es, name="p2m"):
    combT = io.stream("combT", [32, T], F32, produce=False)
    selT = io.stream("selT", [T, 4], F32, produce=False)
    wgtT = io.stream("wgtT", [T, 4], F32, produce=False)
    metaT = io.stream("metaT", [NSLOTS, 4], F32, produce=True)
    gidxT = io.stream("gidxT", [128, 2 * NTILES], F32, produce=True)
    w1bf = io.stream("w1bfT", [32, D, 768], BF16, produce=True)
    w3bf = io.stream("w3bfT", [32, D, 768], BF16, produce=True)
    w2bf = io.stream("w2bfT", [32, 768, D], BF16, produce=True)
    P = Pipe(nc, es, name)
    t = lambda n, sh, dt: es.enter_context(nc.sbuf_tensor(name + "_" + n, sh, dt))
    pt = lambda n, sh, dt=F32: es.enter_context(nc.psum_tensor(name + "_" + n, sh, dt))
    comb = t("comb", [32, T], F32)
    m_m = t("mm", [32, T], F32)
    pos = t("pos", [32, T], F32)
    cnt = t("cnt", [32, 8], F32)
    base = t("base", [32, 1], F32)
    ones32 = t("ones32", [32, 1], F32)
    onesT = t("onesT", [32, T], F32)
    ltri = t("ltri", [32, 32], F32)
    tile_iota = t("tiota", [32, NTILES], F32)
    iota_col = t("icol", [128, 1], F32)
    onesr = t("onesr", [1, 128], F32)
    identf = t("identf", [128, 128], F32)
    eb_cmp = t("ebcmp", [32, NTILES], F32)
    eid_row = t("eidr", [1, NTILES], F32)
    gvec = t("gvec", [128, 2 * NTILES], F32)
    base_bc = t("basebc", [128, 32], F32)
    pos_tok = t("postok", [128, 32], F32)
    sel4 = t("sel4", [128, 4], F32)
    wg4 = t("wg4", [128, 4], F32)
    oh4 = t("oh4", [128, 32], F32)
    iota32 = t("iota32b", [128, 32], F32)
    meta_sc = t("metasc", [128, 4], F32)
    slot_i = t("sloti", [128, 1], I32)
    cnt_i = t("cnti", [32, 1], I32)
    red2 = t("red2", [128, 4], F32)
    init_t = t("initt", [128, 384 * 4], F32)
    wconv = [t(f"wconv{i}", [128, NSLAB, 768], BF16) for i in range(2)]
    wconv2 = [t(f"wconv2{i}", [128, 6, D], BF16) for i in range(2)]
    p96 = pt("p96", [1, NTILES])
    pb32 = pt("pb32", [128, 32])
    pbase = pt("pbase", [32, 1])
    ptp = pt("ptp", [128, 128])
    P.step("sync", lambda e: e.dma_start(out=onesr[:], in_=io.inp("c_ones_row_f", [1, 128], F32)), dma=True, wait=(0, 0))
    P.step("sync", lambda e: e.dma_start(out=identf[:], in_=io.inp("c_ident_f", [128, 128], F32)), dma=True, wait=(0, 0))
    P.step("sync", lambda e: e.dma_start(out=ltri[:], in_=io.inp("c_ltri32", [32, 32], F32)), dma=True, wait=(0, 0))
    P.step("sync", lambda e: e.dma_start(out=tile_iota[:], in_=io.inp("c_tile_iota", [32, NTILES], F32)), dma=True, wait=(0, 0))
    P.step("sync", lambda e: e.dma_start(out=iota_col[:], in_=io.inp("c_iota_col", [128, 1], F32)), dma=True, wait=(0, 0))
    P.step("sync", lambda e: e.dma_start(out=ones32[:], in_=io.inp("c_ones32col_f", [32, 1], F32)), dma=True, wait=(0, 0))
    P.step("sync", lambda e: e.dma_start(out=iota32[:], in_=io.inp("c_iota32_bc", [128, 32], F32)), dma=True, wait=(0, 0))
    P.step("sync", lambda e: e.dma_start(out=comb[:], in_=combT[:]), dma=True, wait=(0, 0))
    P.step("vector", lambda e: e.memset(onesT[:], 1.0))
    def scan_fn(e):
        e.tensor_single_scalar(m_m[:], comb[:], 0.0, ALU.is_gt)
        e.drain()
        e.tensor_tensor_scan(pos[:], onesT[:], m_m[:], 0.0, ALU.mult, ALU.add)
        e.drain()
        e.tensor_copy(cnt[:, 0:1], pos[:, T - 1:T])
        e.drain()
        return e.tensor_tensor(pos[:], pos[:], m_m[:], ALU.subtract)
    P.step("vector", scan_fn)
    def pc_fn(e):
        e.tensor_scalar(cnt[:, 1:2], cnt[:, 0:1], 1.0 / 512.0, 255.5 / 512.0, ALU.mult, ALU.add)
        e.drain()
        e.tensor_copy(cnt_i[:], cnt[:, 1:2])
        e.drain()
        e.tensor_copy(cnt[:, 3:4], cnt_i[:])
        e.drain()
        return e.tensor_scalar_mul(cnt[:, 3:4], cnt[:, 3:4], 512.0)
    P.step("vector", pc_fn)
    P.step("tensor", lambda e: e.matmul(pbase[:], ltri[:], cnt[:, 3:4], start=True, stop=True))
    P.step("vector", lambda e: e.tensor_copy(base[:], pbase[:]))
    def eid_fn(e):
        e.tensor_scalar(eb_cmp[:], tile_iota[:], base[:, 0:1], None, ALU.is_ge)
        return e.nop()
    P.step("vector", eid_fn)
    P.step("tensor", lambda e: e.matmul(p96[:], ones32[:], eb_cmp[:], start=True, stop=True))
    def eidr_fn(e):
        return e.tensor_scalar_add(eid_row[:], p96[:], -1.0)
    P.step("vector", eidr_fn)
    P.step("tensor", lambda e: e.matmul(ptp[:, 0:NTILES], onesr[:], eid_row[:], start=True, stop=True))
    def gvec_fn(e):
        e.tensor_scalar(gvec[:, 0:NTILES], ptp[:, 0:NTILES], 1024.0, None, ALU.mult)
        e.tensor_scalar(gvec[:, NTILES:2 * NTILES], ptp[:, 0:NTILES], 768.0, None, ALU.mult)
        e.drain()
        e.tensor_tensor(gvec[:, 0:NTILES], gvec[:, 0:NTILES], iota_col[:, 0:1].broadcast_to((128, NTILES)), ALU.add)
        return e.tensor_tensor(gvec[:, NTILES:2 * NTILES], gvec[:, NTILES:2 * NTILES], iota_col[:, 0:1].broadcast_to((128, NTILES)), ALU.add)
    P.step("vector", gvec_fn)
    P.step("gpsimd", lambda e: e.dma_start(out=gidxT[:], in_=gvec[:]), dma=True)
    baseRow = io.stream("baseRowT", [32, 1], F32, produce=True)
    P.step("gpsimd", lambda e: e.dma_start(out=baseRow[:], in_=base[:]), dma=True)
    P.step("sync", lambda e: e.dma_start(out=m_m[0:1, 0:32], in_=baseRow.rearrange("e o -> o e")), dma=True)
    P.step("tensor", lambda e: e.matmul(pb32[:], onesr[:], m_m[0:1, 0:32], start=True, stop=True))
    P.step("vector", lambda e: e.tensor_copy(base_bc[:], pb32[:]))
    def init_fn(e):
        ini4 = init_t[:].rearrange("p (b c) -> p b c", c=4)
        e.memset(init_t[:], 0.0)
        e.drain()
        e.memset(ini4[:, :, 2:3], float(4 * T))
        return e.nop()
    P.step("vector", init_fn)
    P.step("gpsimd", lambda e: e.dma_start(
        out=metaT.rearrange("(p b) c -> p (b c)", p=128),
        in_=init_t[:]), dma=True)
    s_lw1 = P.xsem("lw1"); s_sw1 = P.xsem("sw1")
    s_lw2 = P.xsem("lw2"); s_sw2 = P.xsem("sw2")
    wpairs = []
    for eg in range(32):
        wpairs.append(("w1", eg))
        wpairs.append(("w3", eg))
        wpairs.append(("w2", eg))
    wstate = {"n1": 0, "n2": 0}
    sw_hist1 = {}
    sw_hist2 = {}

    def emit_wpair():
        kind, eg = wpairs.pop(0)
        if kind == "w2":
            k = wstate["n2"]; wstate["n2"] += 1
            par = k % 2
            P.step("gpsimd", lambda e, eg=eg, par=par: e.dma_start(
                out=wconv2[par][:], in_=io.inp("w_moe_w2", [32, 768, D], F32)[eg].rearrange("(k p) m -> p k m", p=128)),
                dma=True, wait=(0, 0), inc=s_lw2, waits=[(s_sw2, sw_hist2.get(k - 2, 0))])
            lwn = s_lw2.n
            P.step("sync", lambda e, eg=eg, par=par: e.dma_start(
                out=w2bf[eg].rearrange("(k p) m -> p k m", p=128), in_=wconv2[par][:]),
                dma=True, wait=(0, 0), inc=s_sw2, waits=[(s_lw2, lwn)])
            sw_hist2[k] = s_sw2.n
        else:
            src_name = "w_moe_w1" if kind == "w1" else "w_moe_w3"
            dst = w1bf if kind == "w1" else w3bf
            k = wstate["n1"]; wstate["n1"] += 1
            par = k % 2
            P.step("gpsimd", lambda e, eg=eg, par=par, src_name=src_name: e.dma_start(
                out=wconv[par][:], in_=io.inp(src_name, [32, D, 768], F32)[eg].rearrange("(k p) m -> p k m", p=128)),
                dma=True, wait=(0, 0), inc=s_lw1, waits=[(s_sw1, sw_hist1.get(k - 2, 0))])
            lwn = s_lw1.n
            P.step("sync", lambda e, eg=eg, par=par, dst=dst: e.dma_start(
                out=dst[eg].rearrange("(k p) m -> p k m", p=128), in_=wconv[par][:]),
                dma=True, wait=(0, 0), inc=s_sw1, waits=[(s_lw1, lwn)])
            sw_hist1[k] = s_sw1.n

    for st in range(T // 128):
        if wpairs:
            emit_wpair()
        if st % 2 == 0 and wpairs:
            emit_wpair()
        t0 = st * 128
        P.step("sync", lambda e, t0=t0: e.dma_start(out=sel4[:], in_=selT[t0:t0 + 128, :]), dma=True)
        P.step("sync", lambda e, t0=t0: e.dma_start(out=wg4[:], in_=wgtT[t0:t0 + 128, :]), dma=True)
        P.step("tensor", lambda e, t0=t0: e.transpose(ptp[:, 0:32], pos[:, t0:t0 + 128], identf[0:32, 0:32]))
        P.step("vector", lambda e: e.tensor_copy(pos_tok[:], ptp[:, 0:32]))
        for kk in range(4):
            def meta_fn(e, kk=kk, t0=t0):
                e.tensor_scalar(oh4[:], iota32[:], sel4[:, kk:kk + 1], None, ALU.is_equal)
                e.drain()
                e.tensor_tensor(oh4[:], oh4[:], pos_tok[:], ALU.mult)
                e.drain()
                e.tensor_reduce(red2[:, 0:1], oh4[:], mybir.AxisListType.X, ALU.add)
                e.tensor_scalar(oh4[:], iota32[:], sel4[:, kk:kk + 1], None, ALU.is_equal)
                e.drain()
                e.tensor_tensor(oh4[:], oh4[:], base_bc[:], ALU.mult)
                e.drain()
                e.tensor_reduce(red2[:, 1:2], oh4[:], mybir.AxisListType.X, ALU.add)
                e.drain()
                e.tensor_tensor(red2[:, 0:1], red2[:, 0:1], red2[:, 1:2], ALU.add)
                e.drain()
                e.tensor_copy(meta_sc[:, 0:1], wg4[:, kk:kk + 1])
                e.tensor_scalar(meta_sc[:, 1:2], iota_col[:], 1.0, float(t0), ALU.mult, ALU.add)
                e.tensor_scalar(meta_sc[:, 2:3], iota_col[:], 1.0, float(kk * T + t0), ALU.mult, ALU.add)
                e.memset(meta_sc[:, 3:4], 0.0)
                e.drain()
                return e.tensor_copy(slot_i[:], red2[:, 0:1])
            P.step("vector", meta_fn)
            P.step("gpsimd", lambda e: e.indirect_dma_start(
                out=metaT, out_offset=IndirectOffsetOnAxis(slot_i[:, 0:1], 0),
                in_=meta_sc[:], in_offset=None), dma=True)
    while wpairs:
        emit_wpair()
    P.emit()


def build_p2s(nc, io, es, name="p2s"):
    xtokT = io.stream("xn2tokT", [T, D], BF16, produce=False)
    metaT = io.stream("metaT", [NSLOTS, 4], F32, produce=False)
    gidxT = io.stream("gidxT", [128, 2 * NTILES], F32, produce=False)
    w1bf = io.stream("w1bfT", [32, D, 768], BF16, produce=False)
    w3bf = io.stream("w3bfT", [32, D, 768], BF16, produce=False)
    w2bf = io.stream("w2bfT", [32, 768, D], BF16, produce=False)
    moe4 = io.stream("moe4T", [4 * T + 128, D], F32, produce=True)
    NI = 6
    P = Pipe(nc, es, name)
    t = lambda n, sh, dt: es.enter_context(nc.sbuf_tensor(name + "_" + n, sh, dt))
    pt = lambda n, sh, dt=F32: es.enter_context(nc.psum_tensor(name + "_" + n, sh, dt))
    identb = t("identb", [128, 128], BF16)
    gvec = t("gvec", [128, 2 * NTILES], F32)
    idx13f = t("idx13f", [128, NSLAB * NTILES], F32)
    idx13 = t("idx13", [128, NSLAB * NTILES], I32)
    idx2f = t("idx2f", [128, NI * NTILES], F32)
    idx2 = t("idx2", [128, NI * NTILES], I32)
    meta4 = [t(f"meta4{i}", [128, 4, 4], F32) for i in range(2)]
    tok_i = [t(f"toki{i}", [128, 4], I32) for i in range(2)]
    dst_i = [t(f"dsti{i}", [128, 4], I32) for i in range(2)]
    w1g = [t(f"w1g{i}", [128, NSLAB, 768], BF16) for i in range(2)]
    w3g = [t(f"w3g{i}", [128, NSLAB, 768], BF16) for i in range(2)]
    w2g = [t(f"w2g{i}", [128, NI, D], BF16) for i in range(2)]
    xga = [t(f"xga{i}", [128, 4, D], BF16) for i in range(2)]
    xgT = [t(f"xgT{i}", [128, NSLAB, 512], BF16) for i in range(2)]
    sg_bf = t("sg", [128, 512], BF16)
    a_bf = t("abf", [128, NI, 512], BF16)
    y_f = t("yf", [128, D], F32)
    ph = pt("ph", [128, 512])
    pu = pt("pu", [128, 512])
    pyA = pt("pyA", [128, 512])
    pyB = pt("pyB", [128, 512])
    pxt = [pt(f"pxt{i}", [128, 128], BF16) for i in range(2)]
    P.step("sync", lambda e: e.dma_start(out=identb[:], in_=io.inp("c_ident_bf", [128, 128], BF16)), dma=True, wait=(0, 0))
    P.step("sync", lambda e: e.dma_start(out=gvec[:], in_=gidxT[:]), dma=True, wait=(0, 0))
    def idx_fn(e):
        for k in range(NSLAB):
            e.tensor_scalar(idx13f[:, k * NTILES:(k + 1) * NTILES], gvec[:, 0:NTILES], 1.0, float(k * 128), ALU.mult, ALU.add)
        for k in range(NI):
            e.tensor_scalar(idx2f[:, k * NTILES:(k + 1) * NTILES], gvec[:, NTILES:2 * NTILES], 1.0, float(k * 128), ALU.mult, ALU.add)
        e.drain()
        e.tensor_copy(idx13[:], idx13f[:])
        e.drain()
        return e.tensor_copy(idx2[:], idx2f[:])
    P.step("vector", idx_fn)
    s_ld = P.xsem("ld")
    s_mm = P.xsem("mm")
    meta_n = {}
    ld_full = {}
    mm_end = {}

    def pe(fn, waits=()):
        P.step("tensor", fn, inc=s_mm, waits=waits)
        return s_mm.n

    def G(i):
        wb = i % 2
        P.step("gpsimd", lambda e, i=i, wb=wb: e.dma_start(
            out=meta4[wb][:], in_=metaT[i * 512:(i + 1) * 512, :].rearrange("(s p) c -> p s c", p=128)), dma=True, inc=s_ld,
            waits=[(s_mm, mm_end.get(i - 2, 0))])
        meta_n[i] = s_ld.n
        for k in range(NSLAB):
            P.step("gpsimd", lambda e, i=i, k=k, wb=wb: e.indirect_dma_start(
                out=w1g[wb][:, k, :], out_offset=None,
                in_=w1bf.rearrange("e k m -> (e k) m"),
                in_offset=IndirectOffsetOnAxis(idx13[:, k * NTILES + i:k * NTILES + i + 1], 0)), dma=True, inc=s_ld)
            P.step("gpsimd", lambda e, i=i, k=k, wb=wb: e.indirect_dma_start(
                out=w3g[wb][:, k, :], out_offset=None,
                in_=w3bf.rearrange("e k m -> (e k) m"),
                in_offset=IndirectOffsetOnAxis(idx13[:, k * NTILES + i:k * NTILES + i + 1], 0)), dma=True, inc=s_ld)
        for k in range(NI):
            P.step("gpsimd", lambda e, i=i, k=k, wb=wb: e.indirect_dma_start(
                out=w2g[wb][:, k, :], out_offset=None,
                in_=w2bf.rearrange("e k m -> (e k) m"),
                in_offset=IndirectOffsetOnAxis(idx2[:, k * NTILES + i:k * NTILES + i + 1], 0)), dma=True, inc=s_ld)
        def toks_fn(e, wb=wb):
            e.tensor_copy(tok_i[wb][:], meta4[wb][:, :, 1])
            return e.tensor_copy(dst_i[wb][:], meta4[wb][:, :, 2])
        P.step("vector", toks_fn, waits=[(s_ld, meta_n[i])])
        for s4 in range(4):
            P.step("gpsimd", lambda e, s4=s4, wb=wb: e.indirect_dma_start(
                out=xga[wb][:, s4, :], out_offset=None,
                in_=xtokT, in_offset=IndirectOffsetOnAxis(tok_i[wb][:, s4:s4 + 1], 0)), dma=True, inc=s_ld)
        ld_full[i] = s_ld.n

    def C(i):
        wb = i % 2
        first = True
        for s4 in range(4):
            for s2 in range(NSLAB // 2):
                w = [(s_ld, ld_full[i])] if first else ()
                first = False
                pe(lambda e, s4=s4, s=2 * s2, wb=wb: e.transpose(
                    pxt[0][:], xga[wb][:, s4, s * 128:(s + 1) * 128], identb[:]), waits=w)
                tn = pe(lambda e, s4=s4, s=2 * s2 + 1, wb=wb: e.transpose(
                    pxt[1][:], xga[wb][:, s4, s * 128:(s + 1) * 128], identb[:]))
                def cp2(e, s4=s4, s2=s2, wb=wb):
                    e.tensor_copy(xgT[wb][:, 2 * s2, s4 * 128:(s4 + 1) * 128], pxt[0][:])
                    return e.tensor_copy(xgT[wb][:, 2 * s2 + 1, s4 * 128:(s4 + 1) * 128], pxt[1][:])
                P.step("vector", cp2, waits=[(s_mm, tn)])
        for m in range(NI):
            def mmg(e, m=m, wb=wb):
                for k in range(NSLAB):
                    ins = e.matmul(ph[:], w1g[wb][:, k, m * 128:(m + 1) * 128],
                                   xgT[wb][:, k, :], start=(k == 0), stop=(k == NSLAB - 1))
                return ins
            gn = pe(mmg)
            P.step("scalar", lambda e: e.activation(out=sg_bf[:], in_=ph[:], func=AF.Silu), waits=[(s_mm, gn)])
            def mmu(e, m=m, wb=wb):
                for k in range(NSLAB):
                    ins = e.matmul(pu[:], w3g[wb][:, k, m * 128:(m + 1) * 128],
                                   xgT[wb][:, k, :], start=(k == 0), stop=(k == NSLAB - 1))
                return ins
            un = pe(mmu)
            P.step("vector", lambda e, m=m: e.tensor_tensor(a_bf[:, m, :], sg_bf[:], pu[:], ALU.mult), waits=[(s_mm, un)])
        for s4 in range(4):
            def mmy(e, s4=s4, wb=wb):
                for k in range(NI):
                    e.matmul(pyA[:], a_bf[:, k, s4 * 128:(s4 + 1) * 128],
                             w2g[wb][:, k, 0:512], start=(k == 0), stop=(k == NI - 1))
                for k in range(NI):
                    ins = e.matmul(pyB[:], a_bf[:, k, s4 * 128:(s4 + 1) * 128],
                                   w2g[wb][:, k, 512:1024], start=(k == 0), stop=(k == NI - 1))
                return ins
            yn = pe(mmy)
            def yev(e, s4=s4, wb=wb):
                e.tensor_scalar(y_f[:, 0:512], pyA[:], meta4[wb][:, s4, 0:1], None, ALU.mult)
                return e.tensor_scalar(y_f[:, 512:1024], pyB[:], meta4[wb][:, s4, 0:1], None, ALU.mult)
            P.step("vector", yev, waits=[(s_mm, yn)])
            P.step("gpsimd", lambda e, s4=s4, wb=wb: e.indirect_dma_start(
                out=moe4, out_offset=IndirectOffsetOnAxis(dst_i[wb][:, s4:s4 + 1], 0),
                in_=y_f[:], in_offset=None), dma=True)
        mm_end[i] = s_mm.n

    G(0)
    G(1)
    for i in range(NTILES):
        C(i)
        if i + 2 < NTILES:
            G(i + 2)
    P.emit()


def build_p2f(nc, io, es, name="p2f"):
    x1T = io.stream("x1T", [D, T], F32, produce=False)
    moe4 = io.stream("moe4T", [4 * T + 128, D], F32, produce=False)
    x2T = io.stream("x2T", [D, T], F32, produce=True)
    P = Pipe(nc, es, name)
    t = lambda n, sh, dt: es.enter_context(nc.sbuf_tensor(name + "_" + n, sh, dt))
    pt = lambda n, sh, dt=F32: es.enter_context(nc.psum_tensor(name + "_" + n, sh, dt))
    identf = t("identf", [128, 128], F32)
    mo = [t(f"mo{i}", [128, 4, D], F32) for i in range(2)]
    msum = [t(f"msum{i}", [128, D], F32) for i in range(2)]
    x_f = [t(f"xf{i}", [128, NSLAB, 128], F32) for i in range(2)]
    o_f = [t(f"of{i}", [128, NSLAB, 128], F32) for i in range(2)]
    ptp = pt("ptp", [128, 128])
    P.step("sync", lambda e: e.dma_start(out=identf[:], in_=io.inp("c_ident_f", [128, 128], F32)), dma=True, wait=(0, 0))
    s_ld = P.xsem("ld")
    ld_n = {}
    NST = T // 128

    def L(st):
        p = st % 2
        t0 = st * 128
        for r in range(4):
            P.step("sync", lambda e, t0=t0, r=r, p=p: e.dma_start(
                out=mo[p][:, r, :], in_=moe4[r * T + t0:r * T + t0 + 128, :]), dma=True, inc=s_ld)
        P.step("sync", lambda e, t0=t0, p=p: e.dma_start(out=x_f[p][:], in_=slabs_ap(x1T, t0, 128)), dma=True, inc=s_ld)
        ld_n[st] = s_ld.n

    def C(st):
        p = st % 2
        t0 = st * 128
        def sum_fn(e, p=p):
            e.tensor_tensor(msum[p][:], mo[p][:, 0, :], mo[p][:, 1, :], ALU.add)
            e.tensor_tensor(mo[p][:, 2, :], mo[p][:, 2, :], mo[p][:, 3, :], ALU.add)
            e.drain()
            return e.tensor_tensor(msum[p][:], msum[p][:], mo[p][:, 2, :], ALU.add)
        P.step("vector", sum_fn, waits=[(s_ld, ld_n[st])])
        for s in range(NSLAB):
            P.step("tensor", lambda e, s=s, p=p: e.transpose(ptp[:], msum[p][:, s * 128:(s + 1) * 128], identf[:]))
            def add_fn(e, s=s, p=p):
                return e.tensor_tensor(o_f[p][:, s, :], x_f[p][:, s, :], ptp[:], ALU.add)
            P.step("vector", add_fn)
        P.step("gpsimd", lambda e, t0=t0, p=p: e.dma_start(out=slabs_ap(x2T, t0, 128), in_=o_f[p][:]), dma=True)

    L(0)
    L(1)
    for st in range(NST):
        C(st)
        if st + 2 < NST:
            L(st + 2)
    P.emit()


def build_p3a(nc, io, es, name="p3a"):
    x2T = io.stream("x2T", [D, T], F32, produce=False)
    xnT = io.stream("xn3T", [D, T], BF16, produce=True)
    qxT = io.stream("qxT", [D, T], BF16, produce=True)
    kxT = io.stream("kxT", [D, T], BF16, produce=True)
    vtokT = io.stream("vtokT", [T, 2048], BF16, produce=True)
    CW = 256
    NCH = T // CW
    P = Pipe(nc, es, name)
    t = lambda n, sh, dt: es.enter_context(nc.sbuf_tensor(name + "_" + n, sh, dt))
    pt = lambda n, sh, dt=F32: es.enter_context(nc.psum_tensor(name + "_" + n, sh, dt))
    wqkv = t("wqkv", [128, NSLAB, 4096], BF16)
    rotA = t("rotA", [128, 128], BF16)
    x_f = [t(f"x{i}", [128, NSLAB, CW], F32) for i in range(2)]
    xn_sb = [t(f"xn{i}", [128, NSLAB, CW], BF16) for i in range(2)]
    tq = [t("tq0", [128, NSLAB, 2, CW], BF16)] * 2
    tk = [t("tk0", [128, NSLAB, 2, CW], BF16)] * 2
    q_bf = t("qbf", [128, CW], BF16)
    t1_b = t("t1b", [128, CW], BF16)
    t2_b = t("t2b", [128, CW], BF16)
    qx_sb = [t(f"qx{i}", [128, NSLAB, CW], BF16) for i in range(2)]
    kx_sb = [t(f"kx{i}", [128, NSLAB, CW], BF16) for i in range(2)]
    v_bf = t("vbf", [128, 512], BF16)
    tmp_f = t("tmp", [128, CW], F32)
    kon = t("kon", [128, 1], F32)
    sb = {
        "ones_col_bf": t("ones", [128, 1], BF16),
        "ones_row_f": t("onesr", [1, 128], F32),
        name + "xbf": t("xbf", [128, NSLAB, CW], BF16),
        name + "sqbf": t("sqbf", [128, NSLAB, CW], BF16),
        name + "stat": t("stat", [1, 6 * CW], F32),
    }
    ps = {
        name + "psx": pt("psx", [1, CW]),
        name + "psq": pt("psq", [1, CW]),
        name + "pbr": pt("pbr", [128, CW]),
        name + "pbp": pt("pbp", [128, CW]),
    }
    pq = pt("pq", [128, CW])
    pr = pt("pr", [128, CW])
    pv = pt("pv", [128, 512])
    eps_ap = kon[0:1, 0:1]
    P.step("vector", lambda e: e.memset(kon[:, 0:1], 1e-5))
    P.step("sync", lambda e: e.dma_start(out=sb["ones_col_bf"][:], in_=io.inp("c_ones_col_bf", [128, 1], BF16)), dma=True, wait=(0, 0))
    P.step("sync", lambda e: e.dma_start(out=sb["ones_row_f"][:], in_=io.inp("c_ones_row_f", [1, 128], F32)), dma=True, wait=(0, 0))
    P.step("sync", lambda e: e.dma_start(out=rotA[:], in_=io.inp("c_rot_lhsT", [128, 128], BF16)), dma=True, wait=(0, 0))
    P.step("gpsimd", lambda e: e.dma_start(out=wqkv[:], in_=io.inp("w_qkvT", [D, 4096], F32).rearrange("(k p) m -> p k m", p=128)), dma=True, wait=(0, 0))
    tqc = io.inp("c_tqc", [D, S], BF16)
    tqs = io.inp("c_tqs", [D, S], BF16)
    tkc = io.inp("c_tkc", [D, S], BF16)
    tks = io.inp("c_tks", [D, S], BF16)
    s_ld = P.xsem("ld"); s_st = P.xsem("st")
    st_gate = {}
    for c in range(NCH):
        buf = c % 2
        loc = (c * CW) % S
        xw = [(s_st, st_gate.get(c - 2, 0))]
        P.step("sync", lambda e, c=c, buf=buf: e.dma_start(out=x_f[buf][:], in_=slabs_ap(x2T, c * CW, CW)), dma=True, wait=(0, 0), inc=s_ld, waits=xw)
        def tbl(dst, src, loc):
            return lambda e: e.dma_start(out=dst[:], in_=src.rearrange("(s p) t -> p s t", p=128)[:, :, loc:loc + CW])
        P.step("sync", lambda e: e.nop(), waits=[(s_ld, s_ld.n)])
        P.step("sync", lambda e, buf=buf, loc=loc: e.dma_start(out=tq[buf][:, :, 0, :], in_=tqc.rearrange("(s p) t -> p s t", p=128)[:, :, loc:loc + CW]), dma=True)
        P.step("sync", lambda e, buf=buf, loc=loc: e.dma_start(out=tq[buf][:, :, 1, :], in_=tqs.rearrange("(s p) t -> p s t", p=128)[:, :, loc:loc + CW]), dma=True)
        P.step("sync", lambda e, buf=buf, loc=loc: e.dma_start(out=tk[buf][:, :, 0, :], in_=tkc.rearrange("(s p) t -> p s t", p=128)[:, :, loc:loc + CW]), dma=True)
        P.step("sync", lambda e, buf=buf, loc=loc: e.dma_start(out=tk[buf][:, :, 1, :], in_=tks.rearrange("(s p) t -> p s t", p=128)[:, :, loc:loc + CW]), dma=True)
        ln_stats_steps(P, sb, ps, x_f[buf], CW, eps_ap, name)
        def ap_fn(e, buf=buf):
            for s in range(NSLAB):
                e.tensor_tensor(tmp_f[:], x_f[buf][:, s, :], ps[name + "pbr"][:], ALU.mult)
                e.drain()
                ins = e.tensor_tensor(xn_sb[buf][:, s, :], tmp_f[:], ps[name + "pbp"][:], ALU.subtract)
                e.drain()
            return ins
        P.step("vector", ap_fn)
        for which, off, tbl_sb, dst in (("q", 0, tq, qx_sb), ("k", D, tk, kx_sb)):
            for s in range(NSLAB):
                def mmq(e, buf=buf, s=s, off=off):
                    for k in range(NSLAB):
                        ins = e.matmul(pq[:], wqkv[:, k, off + s * 128:off + (s + 1) * 128],
                                       xn_sb[buf][:, k, :], start=(k == 0), stop=(k == NSLAB - 1))
                    return ins
                P.step("tensor", mmq)
                P.step("vector", lambda e: e.tensor_copy(q_bf[:], pq[:]))
                P.step("tensor", lambda e: e.matmul(pr[:], rotA[:], q_bf[:], start=True, stop=True))
                def xpos_fn(e, buf=buf, s=s, tbl_sb=tbl_sb, dst=dst):
                    e.tensor_tensor(t1_b[:], q_bf[:], tbl_sb[buf][:, s, 0, :], ALU.mult)
                    e.tensor_tensor(t2_b[:], pr[:], tbl_sb[buf][:, s, 1, :], ALU.mult)
                    e.drain()
                    return e.tensor_tensor(dst[buf][:, s, :], t1_b[:], t2_b[:], ALU.add)
                P.step("vector", xpos_fn)
        for t4 in range(CW // 128):
            for n in range(4):
                def mmv(e, buf=buf, t4=t4, n=n):
                    for k in range(NSLAB):
                        ins = e.matmul(pv[:], xn_sb[buf][:, k, t4 * 128:(t4 + 1) * 128],
                                       wqkv[:, k, 2048 + n * 512:2048 + (n + 1) * 512],
                                       start=(k == 0), stop=(k == NSLAB - 1))
                    return ins
                P.step("tensor", mmv)
                P.step("vector", lambda e: e.tensor_copy(v_bf[:], pv[:]))
                P.step("gpsimd", lambda e, c=c, t4=t4, n=n: e.dma_start(
                    out=vtokT[c * CW + t4 * 128:c * CW + (t4 + 1) * 128, n * 512:(n + 1) * 512],
                    in_=v_bf[:]), dma=True)
        P.step("gpsimd", lambda e, c=c, buf=buf: e.dma_start(out=slabs_ap(xnT, c * CW, CW), in_=xn_sb[buf][:]), dma=True, inc=s_st)
        P.step("gpsimd", lambda e, c=c, buf=buf: e.dma_start(out=slabs_ap(qxT, c * CW, CW), in_=qx_sb[buf][:]), dma=True, inc=s_st)
        P.step("gpsimd", lambda e, c=c, buf=buf: e.dma_start(out=slabs_ap(kxT, c * CW, CW), in_=kx_sb[buf][:]), dma=True, inc=s_st)
        st_gate[c] = s_st.n
    P.emit()


def build_p3b(nc, io, es, name="p3b"):
    x2T = io.stream("x2T", [D, T], F32, produce=False)
    xnT = io.stream("xn3T", [D, T], BF16, produce=False)
    qxT = io.stream("qxT", [D, T], BF16, produce=False)
    kxT = io.stream("kxT", [D, T], BF16, produce=False)
    vtokT = io.stream("vtokT", [T, 2048], BF16, produce=False)
    x3T = io.stream("x3T", [D, T], F32, produce=True)
    NH = 8
    HD = 128
    VD = 256
    CC = 128
    NCH_B = S // CC
    P = Pipe(nc, es, name)
    t = lambda n, sh, dt: es.enter_context(nc.sbuf_tensor(name + "_" + n, sh, dt))
    pt = lambda n, sh, dt=F32: es.enter_context(nc.psum_tensor(name + "_" + n, sh, dt))
    wgat = t("wgat", [128, NSLAB, 2048], BF16)
    wproj = t("wproj", [128, 16, D], BF16)
    identb = t("identb", [128, 128], BF16)
    expoT_s = t("expo", [128, 128], F32)
    cmask_s = t("cmask", [128, 128], F32)
    iotaJ_s = t("iotaJ", [128, 128], F32)
    revcol_s = t("revc", [128, 1], F32)
    k128_s = t("k128", [128, 1], F32)
    decrow = t("decrow", [1, 8], F32)
    lnd_bc = t("lnd", [128, 8], F32)
    dmT8 = t("dmT8", [128, NH, 128], BF16)
    dbc8 = t("dbc8", [128, NH, 128], BF16)
    wexp = t("wexp", [128, NH], F32)
    d128 = t("d128", [128, NH], F32)
    qx_b = t("qxb", [128, NSLAB, S], BF16)
    kx_b = t("kxb", [128, NSLAB, S], BF16)
    xn_c = t("xnc", [128, NSLAB, CC], BF16)
    v_tok = t("vtok", [128, 2048], BF16)
    gate_bf = t("gatebf", [128, 2048], BF16)
    gated_bf = t("gatedbf", [128, 2048], BF16)
    sc_bf = [t(f"scbf{i}", [128, 128], BF16) for i in range(2)]
    qdec_bf = [t(f"qdec{i}", [128, 128], BF16) for i in range(2)]
    k_tok = t("ktok", [128, 128], BF16)
    vw_bf = t("vwbf", [128, VD], BF16)
    o_f = t("of", [128, VD], F32)
    n_f = t("nf", [128, VD], F32)
    red = t("red", [128, 6], F32)
    S_sb = t("Ssb", [128, NH, VD], F32)
    S_bf = t("Sbf", [128, NH, VD], BF16)
    gT4 = t("gT4", [128, 16, 512], BF16)
    x_f = t("xf", [128, NSLAB, 512], F32)
    y_f = x_f
    onesr = t("onesr", [1, 128], F32)
    psc = [pt(f"psc{i}", [128, 128]) for i in range(2)]
    ppo = [pt(f"ppo{i}", [128, VD]) for i in range(2)]
    pS = pt("pS", [128, VD])
    pg = pt("pgg", [128, 512])
    pt_b = pt("ptb", [128, 128], BF16)
    py = pt("pyy", [128, 512])
    pbc8 = None
    P.step("sync", lambda e: e.dma_start(out=onesr[:], in_=io.inp("c_ones_row_f", [1, 128], F32)), dma=True, wait=(0, 0))
    P.step("sync", lambda e: e.dma_start(out=identb[:], in_=io.inp("c_ident_bf", [128, 128], BF16)), dma=True, wait=(0, 0))
    P.step("sync", lambda e: e.dma_start(out=expoT_s[:], in_=io.inp("c_expoT", [128, 128], F32)), dma=True, wait=(0, 0))
    P.step("sync", lambda e: e.dma_start(out=cmask_s[:], in_=io.inp("c_cmask_scale", [128, 128], F32)), dma=True, wait=(0, 0))
    P.step("sync", lambda e: e.dma_start(out=iotaJ_s[:], in_=io.inp("c_iotaJ", [128, 128], F32)), dma=True, wait=(0, 0))
    P.step("sync", lambda e: e.dma_start(out=revcol_s[:], in_=io.inp("c_rev_col", [128, 1], F32)), dma=True, wait=(0, 0))
    P.step("sync", lambda e: e.dma_start(out=decrow[:], in_=io.inp("w_decay", [1, 8], F32)), dma=True, wait=(0, 0))
    P.step("gpsimd", lambda e: e.dma_start(out=wgat[:], in_=io.inp("w_gatedT", [D, 2048], F32).rearrange("(k p) m -> p k m", p=128)), dma=True, wait=(0, 0))
    P.step("gpsimd", lambda e: e.dma_start(out=wproj[:], in_=io.inp("w_projT", [2048, D], F32).rearrange("(k p) m -> p k m", p=128)), dma=True, wait=(0, 0))
    P.step("vector", lambda e: e.memset(k128_s[:], 128.0))
    P.step("vector", lambda e: e.memset(red[:, 5:6], 1e-5))
    P.step("scalar", lambda e: e.activation(out=decrow[:], in_=decrow[:], func=AF.Ln))
    P.step("tensor", lambda e: e.matmul(psc[0][:, 0:8], onesr[:], decrow[:], start=True, stop=True))
    P.step("vector", lambda e: e.tensor_copy(lnd_bc[:], psc[0][:, 0:8]))
    for h in range(NH):
        def tab_fn(e, h=h):
            e.activation(out=dmT8[:, h, :], in_=expoT_s[:], func=AF.Exp, scale=lnd_bc[:, h:h + 1])
            e.activation(out=dbc8[:, h, :], in_=iotaJ_s[:], func=AF.Exp, scale=lnd_bc[:, h:h + 1])
            e.activation(out=wexp[:, h:h + 1], in_=revcol_s[:], func=AF.Exp, scale=lnd_bc[:, h:h + 1])
            return e.activation(out=d128[:, h:h + 1], in_=k128_s[:], func=AF.Exp, scale=lnd_bc[:, h:h + 1])
        P.step("scalar", tab_fn)
        def tab2_fn(e, h=h):
            e.tensor_tensor(dmT8[:, h, :], dmT8[:, h, :], cmask_s[:], ALU.mult)
            e.drain()
            return e.tensor_scalar_mul(dbc8[:, h, :], dbc8[:, h, :], 1.0 / math.sqrt(128.0))
        P.step("vector", tab2_fn)
    for b in range(B):
        P.step("sync", lambda e, b=b: e.dma_start(out=qx_b[:], in_=slabs_ap(qxT, b * S, S)), dma=True)
        P.step("sync", lambda e, b=b: e.dma_start(out=kx_b[:], in_=slabs_ap(kxT, b * S, S)), dma=True)
        P.step("vector", lambda e: e.memset(S_sb[:], 0.0))
        P.step("vector", lambda e: e.memset(S_bf[:], 0.0))
        for cc in range(NCH_B):
            t0 = b * S + cc * CC
            lo = cc * CC
            P.step("sync", lambda e, t0=t0: e.dma_start(out=v_tok[:], in_=vtokT[t0:t0 + CC, :]), dma=True)
            P.step("sync", lambda e, t0=t0: e.dma_start(out=xn_c[:], in_=slabs_ap(xnT, t0, CC)), dma=True)
            for n in range(4):
                def mmgate(e, n=n):
                    for k in range(NSLAB):
                        ins = e.matmul(pg[:], xn_c[:, k, :], wgat[:, k, n * 512:(n + 1) * 512],
                                       start=(k == 0), stop=(k == NSLAB - 1))
                    return ins
                P.step("tensor", mmgate)
                P.step("scalar", lambda e, n=n: e.activation(out=gate_bf[:, n * 512:(n + 1) * 512], in_=pg[:], func=AF.Silu))
            def F(h, lo=lo):
                p = h % 2
                def mmsc(e, h=h, lo=lo, p=p):
                    return e.matmul(psc[p][:], kx_b[:, h, lo:lo + CC], qx_b[:, h, lo:lo + CC], start=True, stop=True)
                P.step("tensor", mmsc)
                P.step("vector", lambda e, h=h, p=p: e.tensor_tensor(sc_bf[p][:], psc[p][:], dmT8[:, h, :], ALU.mult))
                P.step("vector", lambda e, h=h, lo=lo, p=p: e.tensor_tensor(qdec_bf[p][:], qx_b[:, h, lo:lo + CC], dbc8[:, h, :], ALU.mult))
                def mmpo(e, h=h, p=p):
                    e.matmul(ppo[p][:], sc_bf[p][:], v_tok[:, h * VD:(h + 1) * VD], start=True, stop=False)
                    return e.matmul(ppo[p][:], qdec_bf[p][:], S_bf[:, h, :], start=False, stop=True)
                P.step("tensor", mmpo)

            def Bk(h, lo=lo):
                p = h % 2
                def ln_fn(e, h=h, p=p):
                    e.tensor_copy(o_f[:], ppo[p][:])
                    e.drain()
                    e.tensor_reduce(red[:, 0:1], o_f[:], mybir.AxisListType.X, ALU.add)
                    e.drain()
                    return e.tensor_scalar_mul(red[:, 0:1], red[:, 0:1], 1.0 / VD)
                P.step("vector", ln_fn)
                P.step("scalar", lambda e: e.activation(out=n_f[:], in_=o_f[:], func=AF.Square, accum_out=red[:, 1:2]))
                def ln2_fn(e, h=h):
                    e.tensor_scalar_mul(red[:, 1:2], red[:, 1:2], 1.0 / VD)
                    e.tensor_tensor(red[:, 2:3], red[:, 0:1], red[:, 0:1], ALU.mult)
                    e.drain()
                    return e.tensor_tensor(red[:, 1:2], red[:, 1:2], red[:, 2:3], ALU.subtract)
                P.step("vector", ln2_fn)
                def rstd_fn(e):
                    e.activation(out=red[:, 1:2], in_=red[:, 1:2], func=AF.Ln, bias=red[:, 5:6])
                    e.drain()
                    return e.activation(out=red[:, 1:2], in_=red[:, 1:2], func=AF.Exp, scale=-0.5)
                P.step("scalar", rstd_fn)
                def gated_fn(e, h=h):
                    e.tensor_tensor(red[:, 3:4], red[:, 0:1], red[:, 1:2], ALU.mult)
                    e.drain()
                    e.tensor_scalar(n_f[:], o_f[:], red[:, 1:2], red[:, 3:4], ALU.mult, ALU.subtract)
                    e.drain()
                    return e.tensor_tensor(gated_bf[:, h * VD:(h + 1) * VD], n_f[:], gate_bf[:, h * VD:(h + 1) * VD], ALU.mult)
                P.step("vector", gated_fn)
                P.step("tensor", lambda e, h=h, lo=lo: e.transpose(pt_b[:], kx_b[:, h, lo:lo + CC], identb[:]))
                P.step("vector", lambda e: e.tensor_copy(k_tok[:], pt_b[:]))
                P.step("vector", lambda e, h=h: e.tensor_scalar(vw_bf[:], v_tok[:, h * VD:(h + 1) * VD], wexp[:, h:h + 1], None, ALU.mult))
                P.step("tensor", lambda e: e.matmul(pS[:], k_tok[:], vw_bf[:], start=True, stop=True))
                def supd_fn(e, h=h):
                    e.scalar_tensor_tensor(S_sb[:, h, :], S_sb[:, h, :], d128[:, h:h + 1], pS[:],
                                           ALU.mult, ALU.add)
                    e.drain()
                    return e.tensor_copy(S_bf[:, h, :], S_sb[:, h, :])
                P.step("vector", supd_fn)
            F(0)
            for h in range(NH):
                if h + 1 < NH:
                    F(h + 1)
                Bk(h)
            for i in range(16):
                P.step("tensor", lambda e, i=i: e.transpose(pt_b[:], gated_bf[:, i * 128:(i + 1) * 128], identb[:]))
                P.step("vector", lambda e, i=i, cc=cc: e.tensor_copy(gT4[:, i, (cc % 4) * 128:(cc % 4 + 1) * 128], pt_b[:]))
            if cc % 4 == 3:
                c0 = t0 + CC - 512
                P.step("sync", lambda e, c0=c0: e.dma_start(out=x_f[:], in_=slabs_ap(x2T, c0, 512)), dma=True)
                for m in range(NSLAB):
                    def mmproj(e, m=m):
                        for k in range(16):
                            ins = e.matmul(py[:], wproj[:, k, m * 128:(m + 1) * 128], gT4[:, k, :],
                                           start=(k == 0), stop=(k == 15))
                        return ins
                    P.step("tensor", mmproj)
                    P.step("vector", lambda e, m=m: e.tensor_tensor(x_f[:, m, :], x_f[:, m, :], py[:], ALU.add))
                P.step("gpsimd", lambda e, c0=c0: e.dma_start(out=slabs_ap(x3T, c0, 512), in_=y_f[:]), dma=True)
    P.emit()


def build_p4a(nc, io, es, name="p4a"):
    x3T = io.stream("x3T", [D, T], F32, produce=False)
    xnT = io.stream("xn4T", [D, T], BF16, produce=True)
    combT = io.stream("comb4T", [8, T], F32, produce=True)
    latT = io.stream("latT", [512, T], BF16, produce=True)
    CW = 512
    NCH = T // CW
    NT4 = CW // 128
    P = Pipe(nc, es, name)
    t = lambda n, sh, dt: es.enter_context(nc.sbuf_tensor(name + "_" + n, sh, dt))
    pt = lambda n, sh: es.enter_context(nc.psum_tensor(name + "_" + n, sh, F32))
    gw = t("gw", [128, NSLAB, 8], BF16)
    fc1 = t("fc1", [128, NSLAB, 512], BF16)
    cbrow = t("cbrow", [1, 8], F32)
    cbc = t("cbc", [128, 8], F32)
    iota8 = t("iota8", [128, 8], F32)
    identf = t("identf", [128, 128], F32)
    x_f = [t(f"x{i}", [128, NSLAB, CW], F32) for i in range(2)]
    xn_sb = [t(f"xn{i}", [128, NSLAB, CW], BF16) for i in range(2)]
    lat_sb = t("lat", [128, 4, CW], BF16)
    sc_f = t("sc", [128, 8], F32)
    sfc = t("sfc", [128, 8], F32)
    pad8 = t("pad8", [128, 16], F32)
    vals = t("vals", [128, 8], F32)
    idxu = t("idxu", [128, 8], U32)
    idxf = t("idxf", [128, 8], F32)
    oh = t("oh", [128, 2, 8], F32)
    og = t("og", [128, 8], F32)
    gs = t("gs", [128, 4], F32)
    comb_tok = t("ctok", [128, 8], F32)
    combt_sb = [t(f"cT{i}", [8, CW], F32) for i in range(2)]
    tmp_f = t("tmp", [128, CW], F32)
    kon = t("kon", [128, 1], F32)
    sb = {
        "ones_col_bf": t("ones", [128, 1], BF16),
        "ones_row_f": t("onesr", [1, 128], F32),
        name + "xbf": t("xbf", [128, NSLAB, CW], BF16),
        name + "sqbf": t("sqbf", [128, NSLAB, CW], BF16),
        name + "stat": t("stat", [1, 6 * CW], F32),
    }
    ps = {
        name + "psx": pt("psx", [1, CW]),
        name + "psq": pt("psq", [1, CW]),
        name + "pbr": pt("pbr", [128, CW]),
        name + "pbp": pt("pbp", [128, CW]),
    }
    psc = pt("psc", [128, 8])
    pct = pt("pct", [8, 128])
    plat = pt("plat", [128, CW])
    eps_ap = kon[0:1, 0:1]
    P.step("vector", lambda e: e.memset(kon[:, 0:1], 1e-5))
    P.step("sync", lambda e: e.dma_start(out=sb["ones_col_bf"][:], in_=io.inp("c_ones_col_bf", [128, 1], BF16)), dma=True, wait=(0, 0))
    P.step("sync", lambda e: e.dma_start(out=sb["ones_row_f"][:], in_=io.inp("c_ones_row_f", [1, 128], F32)), dma=True, wait=(0, 0))
    P.step("sync", lambda e: e.dma_start(out=iota8[:], in_=io.inp("c_iota8_bc", [128, 8], F32)), dma=True, wait=(0, 0))
    P.step("sync", lambda e: e.dma_start(out=identf[:], in_=io.inp("c_ident_f", [128, 128], F32)), dma=True, wait=(0, 0))
    P.step("sync", lambda e: e.dma_start(out=cbrow[:], in_=io.inp("w_corr_bias", [1, 8], F32)), dma=True, wait=(0, 0))
    P.step("gpsimd", lambda e: e.dma_start(out=gw[:], in_=io.inp("w_f_gateT", [D, 8], F32).rearrange("(k p) m -> p k m", p=128)), dma=True, wait=(0, 0))
    P.step("gpsimd", lambda e: e.dma_start(out=fc1[:], in_=io.inp("w_fc1T", [D, 512], F32).rearrange("(k p) m -> p k m", p=128)), dma=True, wait=(0, 0))
    P.step("tensor", lambda e: e.matmul(psc[:], sb["ones_row_f"][:], cbrow[:], start=True, stop=True))
    P.step("vector", lambda e: e.tensor_copy(cbc[:], psc[:]))
    s_ld = P.xsem("ld"); s_st = P.xsem("st")
    st_gate = {}
    for c in range(NCH):
        buf = c % 2
        P.step("sync", lambda e, c=c, buf=buf: e.dma_start(out=x_f[buf][:], in_=slabs_ap(x3T, c * CW, CW)),
               dma=True, wait=(0, 0), inc=s_ld, waits=[(s_st, st_gate.get(c - 2, 0))])
        P.step("sync", lambda e: e.nop(), waits=[(s_ld, s_ld.n)])
        ln_stats_steps(P, sb, ps, x_f[buf], CW, eps_ap, name)
        def ap_fn(e, buf=buf):
            for s in range(NSLAB):
                e.tensor_tensor(tmp_f[:], x_f[buf][:, s, :], ps[name + "pbr"][:], ALU.mult)
                e.drain()
                ins = e.tensor_tensor(xn_sb[buf][:, s, :], tmp_f[:], ps[name + "pbp"][:], ALU.subtract)
                e.drain()
            return ins
        P.step("vector", ap_fn)
        for m in range(4):
            def mml(e, buf=buf, m=m):
                for k in range(NSLAB):
                    ins = e.matmul(plat[:], fc1[:, k, m * 128:(m + 1) * 128],
                                   xn_sb[buf][:, k, :], start=(k == 0), stop=(k == NSLAB - 1))
                return ins
            P.step("tensor", mml)
            P.step("vector", lambda e, m=m: e.tensor_copy(lat_sb[:, m, :], plat[:]))
        P.step("gpsimd", lambda e, c=c: e.dma_start(
            out=latT.rearrange("(s p) t -> p s t", p=128, s=4)[:, :, c * CW:(c + 1) * CW],
            in_=lat_sb[:]), dma=True)
        for t4 in range(NT4):
            def mmsc(e, buf=buf, t4=t4):
                for k in range(NSLAB):
                    ins = e.matmul(psc[:], xn_sb[buf][:, k, t4 * 128:(t4 + 1) * 128],
                                   gw[:, k, :], start=(k == 0), stop=(k == NSLAB - 1))
                return ins
            P.step("tensor", mmsc)
            P.step("scalar", lambda e: e.activation(out=sc_f[:], in_=psc[:], func=AF.Sigmoid))
            def grp_fn(e):
                e.tensor_tensor(sfc[:], sc_f[:], cbc[:], ALU.add)
                e.memset(pad8[:], -3.0e38)
                e.drain()
                e.tensor_copy(pad8[:, 0:4], sfc[:, 0:4])
                e.tensor_copy(pad8[:, 8:12], sfc[:, 4:8])
                e.drain()
                e.max(vals[:], pad8[:, 0:8])
                e.drain()
                e.tensor_tensor(gs[:, 0:1], vals[:, 0:1], vals[:, 1:2], ALU.add)
                e.max(vals[:], pad8[:, 8:16])
                e.drain()
                e.tensor_tensor(gs[:, 1:2], vals[:, 0:1], vals[:, 1:2], ALU.add)
                e.drain()
                e.tensor_tensor(gs[:, 2:3], gs[:, 0:1], gs[:, 1:2], ALU.is_ge)
                e.drain()
                e.tensor_scalar(gs[:, 3:4], gs[:, 2:3], -1.0, 1.0, ALU.mult, ALU.add)
                e.drain()
                e.tensor_scalar(sfc[:, 0:4], sfc[:, 0:4], gs[:, 2:3], None, ALU.mult)
                e.tensor_scalar(sfc[:, 4:8], sfc[:, 4:8], gs[:, 3:4], None, ALU.mult)
                e.drain()
                e.max(vals[:], sfc[:])
                e.drain()
                return e.max_index(idxu[:], vals[:], sfc[:])
            P.step("vector", grp_fn)
            def wsel_fn(e):
                e.tensor_copy(idxf[:], idxu[:])
                e.drain()
                for kk in range(2):
                    e.tensor_scalar(oh[:, kk, :], iota8[:], idxf[:, kk:kk + 1], None, ALU.is_equal)
                e.drain()
                for kk in range(2):
                    e.tensor_tensor(oh[:, kk, :], oh[:, kk, :], sc_f[:], ALU.mult)
                e.drain()
                og3 = og[:].rearrange("p (k o) -> p k o", k=8)
                e.tensor_reduce(og3[:, 0:2, :], oh[:], mybir.AxisListType.X, ALU.add)
                e.drain()
                e.tensor_reduce(og[:, 4:5], og[:, 0:2], mybir.AxisListType.X, ALU.add)
                e.drain()
                e.tensor_scalar_add(og[:, 4:5], og[:, 4:5], 1e-20)
                e.drain()
                return e.reciprocal(og[:, 5:6], og[:, 4:5])
            P.step("vector", wsel_fn)
            def comb_fn(e):
                e.memset(comb_tok[:], 0.0)
                e.drain()
                for kk in range(2):
                    e.scalar_tensor_tensor(comb_tok[:], oh[:, kk, :], og[0:128, 5:6], comb_tok[:],
                                           ALU.mult, ALU.add)
                    e.drain()
                return e.nop()
            P.step("vector", comb_fn)
            P.step("tensor", lambda e: e.transpose(pct[:], comb_tok[:], identf[:]))
            P.step("vector", lambda e, buf=buf, t4=t4: e.tensor_copy(combt_sb[buf][:, t4 * 128:(t4 + 1) * 128], pct[:]))
        P.step("gpsimd", lambda e, c=c, buf=buf: e.dma_start(
            out=slabs_ap(xnT, c * CW, CW), in_=xn_sb[buf][:]), dma=True, inc=s_st)
        P.step("gpsimd", lambda e, c=c, buf=buf: e.dma_start(
            out=combT[:, c * CW:(c + 1) * CW], in_=combt_sb[buf][:]), dma=True, inc=s_st)
        st_gate[c] = s_st.n
    P.emit()


def build_p4b(nc, io, es, name="p4b"):
    latT = io.stream("latT", [512, T], BF16, produce=False)
    combT = io.stream("comb4T", [8, T], F32, produce=False)
    routedT = io.stream("routedT", [D, T], F32, produce=True)
    CW = 512
    NCH = T // CW
    P = Pipe(nc, es, name)
    t = lambda n, sh, dt: es.enter_context(nc.sbuf_tensor(name + "_" + n, sh, dt))
    pt = lambda n, sh: es.enter_context(nc.psum_tensor(name + "_" + n, sh, F32))
    w1 = t("w1", [128, 8, 4, 512], BF16)
    w2 = t("w2", [128, 8, 4, 512], BF16)
    fc2 = t("fc2", [128, 4, D], BF16)
    onesr = t("onesr", [1, 128], F32)
    lat_c = [t(f"lat{i}", [128, 4, CW], BF16) for i in range(2)]
    crow = t("crow", [1, 8 * CW], F32)
    bc_sb = t("bc", [128, CW], F32)
    r_bf = t("rbf", [128, 4, CW], BF16)
    racc = t("racc", [128, 4, CW], F32)
    rq_bf = t("rq", [128, 4, CW], BF16)
    ro_f = t("rof", [128, NSLAB, CW], F32)
    tmp_f = t("tmpq", [128, CW], F32)
    pbc = pt("pbc", [128, CW])
    ph = pt("ph", [128, CW])
    po = pt("po", [128, CW])
    w1ap = io.inp("w_f_w1T", [8, 512, 512], F32)
    w2ap = io.inp("w_f_w2T", [8, 512, 512], F32)
    P.step("sync", lambda e: e.dma_start(out=onesr[:], in_=io.inp("c_ones_row_f", [1, 128], F32)), dma=True, wait=(0, 0))
    for eg in range(8):
        P.step("gpsimd", lambda e, eg=eg: e.dma_start(
            out=w1[:, eg, :, :], in_=w1ap[eg].rearrange("(k p) m -> p k m", p=128)), dma=True, wait=(0, 0))
        P.step("gpsimd", lambda e, eg=eg: e.dma_start(
            out=w2[:, eg, :, :], in_=w2ap[eg].rearrange("(k p) m -> p k m", p=128)), dma=True, wait=(0, 0))
    P.step("gpsimd", lambda e: e.dma_start(
        out=fc2[:], in_=io.inp("w_fc2T", [512, D], F32).rearrange("(k p) m -> p k m", p=128)), dma=True, wait=(0, 0))
    s_ld = P.xsem("ld"); s_st = P.xsem("st")
    st_gate = {}
    for c in range(NCH):
        buf = c % 2
        P.step("sync", lambda e, c=c, buf=buf: e.dma_start(
            out=lat_c[buf][:], in_=latT.rearrange("(s p) t -> p s t", p=128, s=4)[:, :, c * CW:(c + 1) * CW]),
            dma=True, wait=(0, 0), inc=s_ld, waits=[(s_st, st_gate.get(c - 2, 0))])
        P.step("sync", lambda e: e.nop(), waits=[(s_ld, s_ld.n)])
        for eg in range(8):
            P.step("sync", lambda e, c=c, eg=eg: e.dma_start(
                out=crow[:, eg * CW:(eg + 1) * CW], in_=combT[eg:eg + 1, c * CW:(c + 1) * CW]), dma=True)
        P.step("vector", lambda e: e.memset(racc[:], 0.0))
        for eg in range(8):
            def mmbc(e, eg=eg):
                return e.matmul(pbc[:], onesr[:], crow[:, eg * CW:(eg + 1) * CW], start=True, stop=True)
            P.step("tensor", mmbc)
            P.step("vector", lambda e: e.tensor_copy(bc_sb[:], pbc[:]))
            for m in range(4):
                def mmh(e, eg=eg, m=m, buf=buf):
                    for k in range(4):
                        ins = e.matmul(ph[:], w1[:, eg, k, m * 128:(m + 1) * 128],
                                       lat_c[buf][:, k, :], start=(k == 0), stop=(k == 3))
                    return ins
                P.step("tensor", mmh)
                def relu2_fn(e, m=m):
                    e.tensor_relu(tmp_f[:], ph[:])
                    e.drain()
                    return e.tensor_tensor(r_bf[:, m, :], tmp_f[:], tmp_f[:], ALU.mult)
                P.step("vector", relu2_fn)
            for m in range(4):
                def mmo(e, eg=eg, m=m):
                    for k in range(4):
                        ins = e.matmul(po[:], w2[:, eg, k, m * 128:(m + 1) * 128],
                                       r_bf[:, k, :], start=(k == 0), stop=(k == 3))
                    return ins
                P.step("tensor", mmo)
                def racc_fn(e, m=m):
                    e.tensor_tensor(tmp_f[:], po[:], bc_sb[:], ALU.mult)
                    e.drain()
                    sl = racc[:, m, :]
                    return e.tensor_tensor(sl, sl, tmp_f[:], ALU.add)
                P.step("vector", racc_fn)
        P.step("vector", lambda e: e.tensor_copy(rq_bf[:], racc[:]))
        for s in range(NSLAB):
            def mmf(e, s=s):
                for k in range(4):
                    ins = e.matmul(po[:], fc2[:, k, s * 128:(s + 1) * 128],
                                   rq_bf[:, k, :], start=(k == 0), stop=(k == 3))
                return ins
            P.step("tensor", mmf)
            P.step("vector", lambda e, s=s: e.tensor_copy(ro_f[:, s, :], po[:]))
        P.step("gpsimd", lambda e, c=c: e.dma_start(
            out=slabs_ap(routedT, c * CW, CW), in_=ro_f[:]), dma=True, inc=s_st)
        st_gate[c] = s_st.n
    P.emit()


def build_p4c(nc, io, es, name="p4c"):
    x3T = io.stream("x3T", [D, T], F32, produce=False)
    xnT = io.stream("xn4T", [D, T], BF16, produce=False)
    routedT = io.stream("routedT", [D, T], F32, produce=False)
    outT = io.out("outT", [D, T], F32)
    SH = 4096
    NSH = SH // 128
    CW = 256
    NCH = T // CW
    P = Pipe(nc, es, name)
    t = lambda n, sh, dt: es.enter_context(nc.sbuf_tensor(name + "_" + n, sh, dt))
    pt = lambda n, sh: es.enter_context(nc.psum_tensor(name + "_" + n, sh, F32))
    up = t("up", [128, NSLAB, SH], BF16)
    down = t("down", [128, NSH, D], BF16)
    xn_c = [t(f"xn{i}", [128, NSLAB, CW], BF16) for i in range(2)]
    x_f = t("xf", [128, NSLAB, CW], F32)
    r_f = t("rf", [128, NSLAB, CW], F32)
    hs_bf = t("hs", [128, NSH, CW], BF16)
    o_f = t("of", [128, NSLAB, CW], F32)
    tmp_f = t("tmpc", [128, CW], F32)
    ph = pt("ph", [128, CW])
    po = pt("po", [128, CW])
    P.step("gpsimd", lambda e: e.dma_start(
        out=up[:], in_=io.inp("w_upT", [D, SH], F32).rearrange("(k p) m -> p k m", p=128)), dma=True, wait=(0, 0))
    P.step("gpsimd", lambda e: e.dma_start(
        out=down[:], in_=io.inp("w_downT", [SH, D], F32).rearrange("(k p) m -> p k m", p=128)), dma=True, wait=(0, 0))
    s_ld = P.xsem("ld"); s_st = P.xsem("st")
    st_gate = {}
    for c in range(NCH):
        buf = c % 2
        P.step("sync", lambda e, c=c, buf=buf: e.dma_start(out=xn_c[buf][:], in_=slabs_ap(xnT, c * CW, CW)),
               dma=True, wait=(0, 0), inc=s_ld, waits=[(s_st, st_gate.get(c - 2, 0))])
        P.step("sync", lambda e: e.nop(), waits=[(s_ld, s_ld.n)])
        P.step("sync", lambda e, c=c: e.dma_start(out=x_f[:], in_=slabs_ap(x3T, c * CW, CW)), dma=True)
        P.step("sync", lambda e, c=c: e.dma_start(out=r_f[:], in_=slabs_ap(routedT, c * CW, CW)), dma=True)
        for m in range(NSH):
            def mmh(e, m=m, buf=buf):
                for k in range(NSLAB):
                    ins = e.matmul(ph[:], up[:, k, m * 128:(m + 1) * 128],
                                   xn_c[buf][:, k, :], start=(k == 0), stop=(k == NSLAB - 1))
                return ins
            P.step("tensor", mmh)
            def relu2_fn(e, m=m):
                e.tensor_relu(tmp_f[:], ph[:])
                e.drain()
                return e.tensor_tensor(hs_bf[:, m, :], tmp_f[:], tmp_f[:], ALU.mult)
            P.step("vector", relu2_fn)
        for s in range(NSLAB):
            def mmo(e, s=s):
                for k in range(NSH):
                    ins = e.matmul(po[:], down[:, k, s * 128:(s + 1) * 128],
                                   hs_bf[:, k, :], start=(k == 0), stop=(k == NSH - 1))
                return ins
            P.step("tensor", mmo)
            def out_fn(e, s=s):
                e.tensor_tensor(o_f[:, s, :], x_f[:, s, :], r_f[:, s, :], ALU.add)
                e.drain()
                return e.tensor_tensor(o_f[:, s, :], o_f[:, s, :], po[:], ALU.add)
            P.step("vector", out_fn)
        P.step("gpsimd", lambda e, c=c: e.dma_start(
            out=slabs_ap(outT, c * CW, CW), in_=o_f[:]), dma=True, inc=s_st)
        st_gate[c] = s_st.n
    P.emit()


NSLOTS = 96 * 512
NTILES = 96
BIGF = 1.0e9

def build_p2m(nc, io, es, name="p2m"):
    combT = io.stream("combT", [32, T], F32, produce=False)
    selT = io.stream("selT", [T, 4], F32, produce=False)
    wgtT = io.stream("wgtT", [T, 4], F32, produce=False)
    metaT = io.stream("metaT", [NSLOTS, 4], F32, produce=True)
    gidxT = io.stream("gidxT", [128, 2 * NTILES], F32, produce=True)
    w1bf = io.stream("w1bfT", [32, D, 768], BF16, produce=True)
    w3bf = io.stream("w3bfT", [32, D, 768], BF16, produce=True)
    w2bf = io.stream("w2bfT", [32, 768, D], BF16, produce=True)
    P = Pipe(nc, es, name)
    t = lambda n, sh, dt: es.enter_context(nc.sbuf_tensor(name + "_" + n, sh, dt))
    pt = lambda n, sh, dt=F32: es.enter_context(nc.psum_tensor(name + "_" + n, sh, dt))
    comb = t("comb", [32, T], F32)
    m_m = t("mm", [32, T], F32)
    pos = t("pos", [32, T], F32)
    cnt = t("cnt", [32, 8], F32)
    base = t("base", [32, 1], F32)
    ones32 = t("ones32", [32, 1], F32)
    onesT = t("onesT", [32, T], F32)
    ltri = t("ltri", [32, 32], F32)
    tile_iota = t("tiota", [32, NTILES], F32)
    iota_col = t("icol", [128, 1], F32)
    onesr = t("onesr", [1, 128], F32)
    identf = t("identf", [128, 128], F32)
    eb_cmp = t("ebcmp", [32, NTILES], F32)
    eid_row = t("eidr", [1, NTILES], F32)
    gvec = t("gvec", [128, 2 * NTILES], F32)
    base_bc = t("basebc", [128, 32], F32)
    pos_tok = t("postok", [128, 32], F32)
    sel4 = t("sel4", [128, 4], F32)
    wg4 = t("wg4", [128, 4], F32)
    oh4 = t("oh4", [128, 32], F32)
    iota32 = t("iota32b", [128, 32], F32)
    meta_sc = t("metasc", [128, 4], F32)
    slot_i = t("sloti", [128, 1], I32)
    cnt_i = t("cnti", [32, 1], I32)
    red2 = t("red2", [128, 4], F32)
    init_t = t("initt", [128, 384 * 4], F32)
    wconv = [t(f"wconv{i}", [128, NSLAB, 768], BF16) for i in range(2)]
    wconv2 = [t(f"wconv2{i}", [128, 6, D], BF16) for i in range(2)]
    p96 = pt("p96", [1, NTILES])
    pb32 = pt("pb32", [128, 32])
    pbase = pt("pbase", [32, 1])
    ptp = pt("ptp", [128, 128])
    P.step("sync", lambda e: e.dma_start(out=onesr[:], in_=io.inp("c_ones_row_f", [1, 128], F32)), dma=True, wait=(0, 0))
    P.step("sync", lambda e: e.dma_start(out=identf[:], in_=io.inp("c_ident_f", [128, 128], F32)), dma=True, wait=(0, 0))
    P.step("sync", lambda e: e.dma_start(out=ltri[:], in_=io.inp("c_ltri32", [32, 32], F32)), dma=True, wait=(0, 0))
    P.step("sync", lambda e: e.dma_start(out=tile_iota[:], in_=io.inp("c_tile_iota", [32, NTILES], F32)), dma=True, wait=(0, 0))
    P.step("sync", lambda e: e.dma_start(out=iota_col[:], in_=io.inp("c_iota_col", [128, 1], F32)), dma=True, wait=(0, 0))
    P.step("sync", lambda e: e.dma_start(out=ones32[:], in_=io.inp("c_ones32col_f", [32, 1], F32)), dma=True, wait=(0, 0))
    P.step("sync", lambda e: e.dma_start(out=iota32[:], in_=io.inp("c_iota32_bc", [128, 32], F32)), dma=True, wait=(0, 0))
    P.step("sync", lambda e: e.dma_start(out=comb[:], in_=combT[:]), dma=True, wait=(0, 0))
    P.step("vector", lambda e: e.memset(onesT[:], 1.0))
    def scan_fn(e):
        e.tensor_single_scalar(m_m[:], comb[:], 0.0, ALU.is_gt)
        e.drain()
        e.tensor_tensor_scan(pos[:], onesT[:], m_m[:], 0.0, ALU.mult, ALU.add)
        e.drain()
        e.tensor_copy(cnt[:, 0:1], pos[:, T - 1:T])
        e.drain()
        return e.tensor_tensor(pos[:], pos[:], m_m[:], ALU.subtract)
    P.step("vector", scan_fn)
    def pc_fn(e):
        e.tensor_scalar(cnt[:, 1:2], cnt[:, 0:1], 1.0 / 512.0, 255.5 / 512.0, ALU.mult, ALU.add)
        e.drain()
        e.tensor_copy(cnt_i[:], cnt[:, 1:2])
        e.drain()
        e.tensor_copy(cnt[:, 3:4], cnt_i[:])
        e.drain()
        return e.tensor_scalar_mul(cnt[:, 3:4], cnt[:, 3:4], 512.0)
    P.step("vector", pc_fn)
    P.step("tensor", lambda e: e.matmul(pbase[:], ltri[:], cnt[:, 3:4], start=True, stop=True))
    P.step("vector", lambda e: e.tensor_copy(base[:], pbase[:]))
    def eid_fn(e):
        e.tensor_scalar(eb_cmp[:], tile_iota[:], base[:, 0:1], None, ALU.is_ge)
        return e.nop()
    P.step("vector", eid_fn)
    P.step("tensor", lambda e: e.matmul(p96[:], ones32[:], eb_cmp[:], start=True, stop=True))
    def eidr_fn(e):
        return e.tensor_scalar_add(eid_row[:], p96[:], -1.0)
    P.step("vector", eidr_fn)
    P.step("tensor", lambda e: e.matmul(ptp[:, 0:NTILES], onesr[:], eid_row[:], start=True, stop=True))
    def gvec_fn(e):
        e.tensor_scalar(gvec[:, 0:NTILES], ptp[:, 0:NTILES], 1024.0, None, ALU.mult)
        e.tensor_scalar(gvec[:, NTILES:2 * NTILES], ptp[:, 0:NTILES], 768.0, None, ALU.mult)
        e.drain()
        e.tensor_tensor(gvec[:, 0:NTILES], gvec[:, 0:NTILES], iota_col[:, 0:1].broadcast_to((128, NTILES)), ALU.add)
        return e.tensor_tensor(gvec[:, NTILES:2 * NTILES], gvec[:, NTILES:2 * NTILES], iota_col[:, 0:1].broadcast_to((128, NTILES)), ALU.add)
    P.step("vector", gvec_fn)
    P.step("gpsimd", lambda e: e.dma_start(out=gidxT[:], in_=gvec[:]), dma=True)
    baseRow = io.stream("baseRowT", [32, 1], F32, produce=True)
    P.step("gpsimd", lambda e: e.dma_start(out=baseRow[:], in_=base[:]), dma=True)
    P.step("sync", lambda e: e.dma_start(out=m_m[0:1, 0:32], in_=baseRow.rearrange("e o -> o e")), dma=True)
    P.step("tensor", lambda e: e.matmul(pb32[:], onesr[:], m_m[0:1, 0:32], start=True, stop=True))
    P.step("vector", lambda e: e.tensor_copy(base_bc[:], pb32[:]))
    def init_fn(e):
        ini4 = init_t[:].rearrange("p (b c) -> p b c", c=4)
        e.memset(init_t[:], 0.0)
        e.drain()
        e.memset(ini4[:, :, 2:3], float(4 * T))
        return e.nop()
    P.step("vector", init_fn)
    P.step("gpsimd", lambda e: e.dma_start(
        out=metaT.rearrange("(p b) c -> p (b c)", p=128),
        in_=init_t[:]), dma=True)
    s_lw1 = P.xsem("lw1"); s_sw1 = P.xsem("sw1")
    s_lw2 = P.xsem("lw2"); s_sw2 = P.xsem("sw2")
    wpairs = []
    for eg in range(32):
        wpairs.append(("w1", eg))
        wpairs.append(("w3", eg))
        wpairs.append(("w2", eg))
    wstate = {"n1": 0, "n2": 0}
    sw_hist1 = {}
    sw_hist2 = {}

    def emit_wpair():
        kind, eg = wpairs.pop(0)
        if kind == "w2":
            k = wstate["n2"]; wstate["n2"] += 1
            par = k % 2
            P.step("gpsimd", lambda e, eg=eg, par=par: e.dma_start(
                out=wconv2[par][:], in_=io.inp("w_moe_w2", [32, 768, D], F32)[eg].rearrange("(k p) m -> p k m", p=128)),
                dma=True, wait=(0, 0), inc=s_lw2, waits=[(s_sw2, sw_hist2.get(k - 2, 0))])
            lwn = s_lw2.n
            P.step("sync", lambda e, eg=eg, par=par: e.dma_start(
                out=w2bf[eg].rearrange("(k p) m -> p k m", p=128), in_=wconv2[par][:]),
                dma=True, wait=(0, 0), inc=s_sw2, waits=[(s_lw2, lwn)])
            sw_hist2[k] = s_sw2.n
        else:
            src_name = "w_moe_w1" if kind == "w1" else "w_moe_w3"
            dst = w1bf if kind == "w1" else w3bf
            k = wstate["n1"]; wstate["n1"] += 1
            par = k % 2
            P.step("gpsimd", lambda e, eg=eg, par=par, src_name=src_name: e.dma_start(
                out=wconv[par][:], in_=io.inp(src_name, [32, D, 768], F32)[eg].rearrange("(k p) m -> p k m", p=128)),
                dma=True, wait=(0, 0), inc=s_lw1, waits=[(s_sw1, sw_hist1.get(k - 2, 0))])
            lwn = s_lw1.n
            P.step("sync", lambda e, eg=eg, par=par, dst=dst: e.dma_start(
                out=dst[eg].rearrange("(k p) m -> p k m", p=128), in_=wconv[par][:]),
                dma=True, wait=(0, 0), inc=s_sw1, waits=[(s_lw1, lwn)])
            sw_hist1[k] = s_sw1.n

    for st in range(T // 128):
        if wpairs:
            emit_wpair()
        if st % 2 == 0 and wpairs:
            emit_wpair()
        t0 = st * 128
        P.step("sync", lambda e, t0=t0: e.dma_start(out=sel4[:], in_=selT[t0:t0 + 128, :]), dma=True)
        P.step("sync", lambda e, t0=t0: e.dma_start(out=wg4[:], in_=wgtT[t0:t0 + 128, :]), dma=True)
        P.step("tensor", lambda e, t0=t0: e.transpose(ptp[:, 0:32], pos[:, t0:t0 + 128], identf[0:32, 0:32]))
        P.step("vector", lambda e: e.tensor_copy(pos_tok[:], ptp[:, 0:32]))
        for kk in range(4):
            def meta_fn(e, kk=kk, t0=t0):
                e.tensor_scalar(oh4[:], iota32[:], sel4[:, kk:kk + 1], None, ALU.is_equal)
                e.drain()
                e.tensor_tensor(oh4[:], oh4[:], pos_tok[:], ALU.mult)
                e.drain()
                e.tensor_reduce(red2[:, 0:1], oh4[:], mybir.AxisListType.X, ALU.add)
                e.tensor_scalar(oh4[:], iota32[:], sel4[:, kk:kk + 1], None, ALU.is_equal)
                e.drain()
                e.tensor_tensor(oh4[:], oh4[:], base_bc[:], ALU.mult)
                e.drain()
                e.tensor_reduce(red2[:, 1:2], oh4[:], mybir.AxisListType.X, ALU.add)
                e.drain()
                e.tensor_tensor(red2[:, 0:1], red2[:, 0:1], red2[:, 1:2], ALU.add)
                e.drain()
                e.tensor_copy(meta_sc[:, 0:1], wg4[:, kk:kk + 1])
                e.tensor_scalar(meta_sc[:, 1:2], iota_col[:], 1.0, float(t0), ALU.mult, ALU.add)
                e.tensor_scalar(meta_sc[:, 2:3], iota_col[:], 1.0, float(kk * T + t0), ALU.mult, ALU.add)
                e.memset(meta_sc[:, 3:4], 0.0)
                e.drain()
                return e.tensor_copy(slot_i[:], red2[:, 0:1])
            P.step("vector", meta_fn)
            P.step("gpsimd", lambda e: e.indirect_dma_start(
                out=metaT, out_offset=IndirectOffsetOnAxis(slot_i[:, 0:1], 0),
                in_=meta_sc[:], in_offset=None), dma=True)
    while wpairs:
        emit_wpair()
    P.emit()


def build_p2s(nc, io, es, name="p2s"):
    xtokT = io.stream("xn2tokT", [T, D], BF16, produce=False)
    metaT = io.stream("metaT", [NSLOTS, 4], F32, produce=False)
    gidxT = io.stream("gidxT", [128, 2 * NTILES], F32, produce=False)
    w1bf = io.stream("w1bfT", [32, D, 768], BF16, produce=False)
    w3bf = io.stream("w3bfT", [32, D, 768], BF16, produce=False)
    w2bf = io.stream("w2bfT", [32, 768, D], BF16, produce=False)
    moe4 = io.stream("moe4T", [4 * T + 128, D], F32, produce=True)
    NI = 6
    P = Pipe(nc, es, name)
    t = lambda n, sh, dt: es.enter_context(nc.sbuf_tensor(name + "_" + n, sh, dt))
    pt = lambda n, sh, dt=F32: es.enter_context(nc.psum_tensor(name + "_" + n, sh, dt))
    identb = t("identb", [128, 128], BF16)
    gvec = t("gvec", [128, 2 * NTILES], F32)
    idx13f = t("idx13f", [128, NSLAB * NTILES], F32)
    idx13 = t("idx13", [128, NSLAB * NTILES], I32)
    idx2f = t("idx2f", [128, NI * NTILES], F32)
    idx2 = t("idx2", [128, NI * NTILES], I32)
    meta4 = [t(f"meta4{i}", [128, 4, 4], F32) for i in range(2)]
    tok_i = [t(f"toki{i}", [128, 4], I32) for i in range(2)]
    dst_i = [t(f"dsti{i}", [128, 4], I32) for i in range(2)]
    w1g = [t(f"w1g{i}", [128, NSLAB, 768], BF16) for i in range(2)]
    w3g = [t(f"w3g{i}", [128, NSLAB, 768], BF16) for i in range(2)]
    w2g = [t(f"w2g{i}", [128, NI, D], BF16) for i in range(2)]
    xga = [t(f"xga{i}", [128, 4, D], BF16) for i in range(2)]
    xgT = [t(f"xgT{i}", [128, NSLAB, 512], BF16) for i in range(2)]
    sg_bf = t("sg", [128, 512], BF16)
    a_bf = t("abf", [128, NI, 512], BF16)
    y_f = t("yf", [128, D], F32)
    ph = pt("ph", [128, 512])
    pu = pt("pu", [128, 512])
    pyA = pt("pyA", [128, 512])
    pyB = pt("pyB", [128, 512])
    pxt = pt("pxt", [128, 128], BF16)
    P.step("sync", lambda e: e.dma_start(out=identb[:], in_=io.inp("c_ident_bf", [128, 128], BF16)), dma=True, wait=(0, 0))
    P.step("sync", lambda e: e.dma_start(out=gvec[:], in_=gidxT[:]), dma=True, wait=(0, 0))
    def idx_fn(e):
        for k in range(NSLAB):
            e.tensor_scalar(idx13f[:, k * NTILES:(k + 1) * NTILES], gvec[:, 0:NTILES], 1.0, float(k * 128), ALU.mult, ALU.add)
        for k in range(NI):
            e.tensor_scalar(idx2f[:, k * NTILES:(k + 1) * NTILES], gvec[:, NTILES:2 * NTILES], 1.0, float(k * 128), ALU.mult, ALU.add)
        e.drain()
        e.tensor_copy(idx13[:], idx13f[:])
        e.drain()
        return e.tensor_copy(idx2[:], idx2f[:])
    P.step("vector", idx_fn)
    s_ld = P.xsem("ld")
    meta_n = {}
    ld_full = {}

    def G(i):
        wb = i % 2
        P.step("gpsimd", lambda e, i=i, wb=wb: e.dma_start(
            out=meta4[wb][:], in_=metaT[i * 512:(i + 1) * 512, :].rearrange("(s p) c -> p s c", p=128)), dma=True, inc=s_ld)
        meta_n[i] = s_ld.n
        for k in range(NSLAB):
            P.step("gpsimd", lambda e, i=i, k=k, wb=wb: e.indirect_dma_start(
                out=w1g[wb][:, k, :], out_offset=None,
                in_=w1bf.rearrange("e k m -> (e k) m"),
                in_offset=IndirectOffsetOnAxis(idx13[:, k * NTILES + i:k * NTILES + i + 1], 0)), dma=True, inc=s_ld)
            P.step("gpsimd", lambda e, i=i, k=k, wb=wb: e.indirect_dma_start(
                out=w3g[wb][:, k, :], out_offset=None,
                in_=w3bf.rearrange("e k m -> (e k) m"),
                in_offset=IndirectOffsetOnAxis(idx13[:, k * NTILES + i:k * NTILES + i + 1], 0)), dma=True, inc=s_ld)
        for k in range(NI):
            P.step("gpsimd", lambda e, i=i, k=k, wb=wb: e.indirect_dma_start(
                out=w2g[wb][:, k, :], out_offset=None,
                in_=w2bf.rearrange("e k m -> (e k) m"),
                in_offset=IndirectOffsetOnAxis(idx2[:, k * NTILES + i:k * NTILES + i + 1], 0)), dma=True, inc=s_ld)
        def toks_fn(e, wb=wb):
            e.tensor_copy(tok_i[wb][:], meta4[wb][:, :, 1])
            return e.tensor_copy(dst_i[wb][:], meta4[wb][:, :, 2])
        P.step("vector", toks_fn, waits=[(s_ld, meta_n[i])])
        for s4 in range(4):
            P.step("gpsimd", lambda e, s4=s4, wb=wb: e.indirect_dma_start(
                out=xga[wb][:, s4, :], out_offset=None,
                in_=xtokT, in_offset=IndirectOffsetOnAxis(tok_i[wb][:, s4:s4 + 1], 0)), dma=True, inc=s_ld)
        ld_full[i] = s_ld.n

    def C(i):
        wb = i % 2
        first = True
        for s4 in range(4):
            for s in range(NSLAB):
                w = [(s_ld, ld_full[i])] if first else ()
                first = False
                P.step("tensor", lambda e, s4=s4, s=s, wb=wb: e.transpose(
                    pxt[:], xga[wb][:, s4, s * 128:(s + 1) * 128], identb[:]), waits=w)
                P.step("vector", lambda e, s4=s4, s=s, wb=wb: e.tensor_copy(
                    xgT[wb][:, s, s4 * 128:(s4 + 1) * 128], pxt[:]))
        for m in range(NI):
            def mmg(e, m=m, wb=wb):
                for k in range(NSLAB):
                    ins = e.matmul(ph[:], w1g[wb][:, k, m * 128:(m + 1) * 128],
                                   xgT[wb][:, k, :], start=(k == 0), stop=(k == NSLAB - 1))
                return ins
            P.step("tensor", mmg)
            P.step("scalar", lambda e: e.activation(out=sg_bf[:], in_=ph[:], func=AF.Silu))
            def mmu(e, m=m, wb=wb):
                for k in range(NSLAB):
                    ins = e.matmul(pu[:], w3g[wb][:, k, m * 128:(m + 1) * 128],
                                   xgT[wb][:, k, :], start=(k == 0), stop=(k == NSLAB - 1))
                return ins
            P.step("tensor", mmu)
            P.step("vector", lambda e, m=m: e.tensor_tensor(a_bf[:, m, :], sg_bf[:], pu[:], ALU.mult))
        for s4 in range(4):
            def mmy(e, s4=s4, wb=wb):
                for k in range(NI):
                    e.matmul(pyA[:], a_bf[:, k, s4 * 128:(s4 + 1) * 128],
                             w2g[wb][:, k, 0:512], start=(k == 0), stop=(k == NI - 1))
                for k in range(NI):
                    ins = e.matmul(pyB[:], a_bf[:, k, s4 * 128:(s4 + 1) * 128],
                                   w2g[wb][:, k, 512:1024], start=(k == 0), stop=(k == NI - 1))
                return ins
            P.step("tensor", mmy)
            def yev(e, s4=s4, wb=wb):
                e.tensor_scalar(y_f[:, 0:512], pyA[:], meta4[wb][:, s4, 0:1], None, ALU.mult)
                return e.tensor_scalar(y_f[:, 512:1024], pyB[:], meta4[wb][:, s4, 0:1], None, ALU.mult)
            P.step("vector", yev)
            P.step("gpsimd", lambda e, s4=s4, wb=wb: e.indirect_dma_start(
                out=moe4, out_offset=IndirectOffsetOnAxis(dst_i[wb][:, s4:s4 + 1], 0),
                in_=y_f[:], in_offset=None), dma=True)

    G(0)
    G(1)
    for i in range(NTILES):
        C(i)
        if i + 2 < NTILES:
            G(i + 2)
    P.emit()


def build_p2f(nc, io, es, name="p2f"):
    x1T = io.stream("x1T", [D, T], F32, produce=False)
    moe4 = io.stream("moe4T", [4 * T + 128, D], F32, produce=False)
    x2T = io.stream("x2T", [D, T], F32, produce=True)
    P = Pipe(nc, es, name)
    t = lambda n, sh, dt: es.enter_context(nc.sbuf_tensor(name + "_" + n, sh, dt))
    pt = lambda n, sh, dt=F32: es.enter_context(nc.psum_tensor(name + "_" + n, sh, dt))
    identf = t("identf", [128, 128], F32)
    mo = t("mo", [128, 4, D], F32)
    msum = t("msum", [128, D], F32)
    x_f = t("xf", [128, NSLAB, 128], F32)
    o_f = t("of", [128, NSLAB, 128], F32)
    ptp = pt("ptp", [128, 128])
    P.step("sync", lambda e: e.dma_start(out=identf[:], in_=io.inp("c_ident_f", [128, 128], F32)), dma=True, wait=(0, 0))
    for st in range(T // 128):
        t0 = st * 128
        for r in range(4):
            P.step("sync", lambda e, t0=t0, r=r: e.dma_start(
                out=mo[:, r, :], in_=moe4[r * T + t0:r * T + t0 + 128, :]), dma=True)
        P.step("sync", lambda e, t0=t0: e.dma_start(out=x_f[:], in_=slabs_ap(x1T, t0, 128)), dma=True)
        def sum_fn(e):
            e.tensor_tensor(msum[:], mo[:, 0, :], mo[:, 1, :], ALU.add)
            e.tensor_tensor(mo[:, 2, :], mo[:, 2, :], mo[:, 3, :], ALU.add)
            e.drain()
            return e.tensor_tensor(msum[:], msum[:], mo[:, 2, :], ALU.add)
        P.step("vector", sum_fn)
        for s in range(NSLAB):
            P.step("tensor", lambda e, s=s: e.transpose(ptp[:], msum[:, s * 128:(s + 1) * 128], identf[:]))
            def add_fn(e, s=s):
                return e.tensor_tensor(o_f[:, s, :], x_f[:, s, :], ptp[:], ALU.add)
            P.step("vector", add_fn)
        P.step("gpsimd", lambda e, t0=t0: e.dma_start(out=slabs_ap(x2T, t0, 128), in_=o_f[:]), dma=True)
    P.emit()


def _build_launch(build_fns, fused=False):
    nc = bass.Bass("TRN2", target_bir_lowering=False, debug=False)
    from contextlib import ExitStack
    io = IO(nc, fused)
    if not fused:
        with ExitStack() as es:
            for fn in build_fns:
                fn(nc, io, es)
        return nc
    with ExitStack() as es_sem:
        _FUSE["on"] = True
        _FUSE["es_sem"] = es_sem
        _FUSE["bar"] = es_sem.enter_context(nc.semaphore("fusebar"))
        _FUSE["k"] = 0
        try:
            for fn in build_fns:
                with ExitStack() as es:
                    fn(nc, io, es)
        finally:
            _FUSE["on"] = False
            _FUSE["es_sem"] = None
            _FUSE["bar"] = None
            _FUSE["k"] = 0
    return nc


FUSED = True


def _run_fused(in_maps):
    fns = [f for n, (fs, o) in ALL_LAUNCHES for f in fs]
    nc = _build_launch(fns, fused=True)
    need = _declared_inputs(nc)
    maps = [{k: m[k] for k in need} for m in in_maps]
    res = run_bass_kernel_spmd(nc, maps, core_ids=list(range(NCORES)))
    return [{"outT": np.asarray(res.results[c]["outT"])} for c in range(NCORES)]


def _shard_inputs(inputs, consts):
    x = inputs["L0_attn_x"]
    base = {}
    base.update(consts)
    for k in ["Wq", "Wk", "Wv", "Wa", "Wg", "Wo", "W1", "W2"]:
        base["w_" + k] = np.ascontiguousarray(inputs["L0_attn_" + k])
    for k in ["gn_scale", "gn_bias", "ln1_scale", "ln1_bias", "b1", "b2",
              "ln2_scale", "ln2_bias"]:
        base["w_" + k] = np.ascontiguousarray(inputs["L0_attn_" + k])
    base["w_gate_wT"] = np.ascontiguousarray(inputs["L0_ffn_gate_w"].T)
    base["w_moe_w1"] = np.ascontiguousarray(inputs["L0_ffn_w1"])
    base["w_moe_w3"] = np.ascontiguousarray(inputs["L0_ffn_w3"])
    base["w_moe_w2"] = np.ascontiguousarray(inputs["L0_ffn_w2"])
    base["w_e_bias"] = np.ascontiguousarray(inputs["L0_ffn_e_bias"].reshape(1, 32))
    base["w_qkvT"] = np.ascontiguousarray(inputs["L1_attn_w_qkv"].T)
    base["w_gatedT"] = np.ascontiguousarray(inputs["L1_attn_w_gated"].T)
    base["w_projT"] = np.ascontiguousarray(inputs["L1_attn_w_proj"].T)
    base["w_decay"] = np.ascontiguousarray(inputs["L1_attn_decay"].reshape(1, 8))
    base["w_f_gateT"] = np.ascontiguousarray(inputs["L1_ffn_gate"].T)
    base["w_corr_bias"] = np.ascontiguousarray(inputs["L1_ffn_corr_bias"].reshape(1, 8))
    base["w_f_w1T"] = np.ascontiguousarray(inputs["L1_ffn_w1"].transpose(0, 2, 1))
    base["w_f_w2T"] = np.ascontiguousarray(inputs["L1_ffn_w2"].transpose(0, 2, 1))
    base["w_fc1T"] = np.ascontiguousarray(inputs["L1_ffn_fc1_w"].T)
    base["w_fc2T"] = np.ascontiguousarray(inputs["L1_ffn_fc2_w"].T)
    base["w_upT"] = np.ascontiguousarray(inputs["L1_ffn_up_w"].T)
    base["w_downT"] = np.ascontiguousarray(inputs["L1_ffn_down_w"].T)
    in_maps = []
    for c in range(NCORES):
        m = dict(base)
        m["xT"] = np.ascontiguousarray(
            x[B * c:B * (c + 1)].transpose(2, 0, 1).reshape(D, T))
        in_maps.append(m)
    return in_maps


ALL_LAUNCHES = [
    ("p1a", ([lambda nc, io, es: build_p1a(nc, io, es, "xT", "xn1T")], ["xn1T"])),
    ("p1b", ([build_p1b], ["kvT"])),
    ("p1c", ([build_p1c], ["lrT", "thT"])),
    ("p1d", ([build_p1d], ["yrT", "gnab"])),
    ("p1e", ([build_p1e], ["v1T"])),
    ("p1f0", ([lambda nc, io, es: build_p1f(nc, io, es, 0)], ["ffacc"])),
    ("p1f1", ([lambda nc, io, es: build_p1f(nc, io, es, 1)], ["x1T"])),
    ("p2a", ([build_p2a], ["xn2T", "combT", "xn2tokT", "selT", "wgtT"])),
    ("p2m", ([build_p2m], ["metaT", "gidxT", "w1bfT", "w3bfT", "w2bfT", "baseRowT"])),
    ("p2s", ([build_p2s], ["moe4T"])),
    ("p2f", ([build_p2f], ["x2T"])),
    ("p3a", ([build_p3a], ["xn3T", "qxT", "kxT", "vtokT"])),
    ("p3b", ([build_p3b], ["x3T"])),
    ("p4a", ([build_p4a], ["xn4T", "comb4T", "latT"])),
    ("p4b", ([build_p4b], ["routedT"])),
    ("p4c", ([build_p4c], ["outT"])),
]
LAUNCHES = [(n, f, o) for n, (f, o) in ALL_LAUNCHES]


def _declared_inputs(nc):
    names = set()
    for alloc in nc.m.functions[0].allocations:
        if isinstance(alloc, mybir.MemoryLocationSet) and alloc.kind == "ExternalInput":
            names.add(alloc.memorylocations[0].name)
    if nc.partition_id_tensor is not None:
        names.discard(nc.partition_id_tensor.name)
    return names


def _run_launches(launches, in_maps):
    carry = [dict() for _ in range(NCORES)]
    for name, fns, outs in launches:
        nc = _build_launch(fns)
        need = _declared_inputs(nc)
        maps = []
        for c in range(NCORES):
            m = {}
            for k in need:
                if k in carry[c]:
                    m[k] = carry[c][k]
                elif k in in_maps[c]:
                    m[k] = in_maps[c][k]
                else:
                    raise KeyError(f"launch {name}: missing input {k}")
            maps.append(m)
        res = run_bass_kernel_spmd(nc, maps, core_ids=list(range(NCORES)))
        for c in range(NCORES):
            for o in outs:
                carry[c][o] = np.asarray(res.results[c][o])
    return carry


def kernel(**inputs):
    consts = host_consts()
    in_maps = _shard_inputs(inputs, consts)
    if FUSED:
        carry = _run_fused(in_maps)
    else:
        carry = _run_launches(LAUNCHES, in_maps)
    outs = []
    for c in range(NCORES):
        o = np.ascontiguousarray(carry[c]["outT"].astype(np.float32))
        outs.append(o.reshape(D, B, S).transpose(1, 2, 0))
    return np.concatenate(outs, axis=0)
